# version 02
# baseline (speedup 1.0000x reference)
from contextlib import ExitStack

import numpy as np

import concourse.bass as bass
import concourse.mybir as mybir
from concourse.bass_utils import run_bass_kernel_spmd
from concourse.bass_core_rust import IndirectOffsetOnAxis

F32 = mybir.dt.float32
BF16 = mybir.dt.bfloat16
I32 = mybir.dt.int32
U32 = mybir.dt.uint32
AF = mybir.ActivationFunctionType
ALU = mybir.AluOpType
AX = mybir.AxisListType
RG8 = [[0, 1, 2, 3, 4, 5, 6, 7]]

T = 2048
D = 1024
NT = 16
NJ = 8
NP = 8
NE = 64
EK = 6
CAP = 384
EF = 512
SF = 2048
ELOC = 8
NU = 8
UT = 256

DEBUG = False
PHASE_LIMIT = None


class Ctr:
    __slots__ = ("h", "v", "is_dma")

    def __init__(self, h, is_dma=False):
        self.h = h
        self.v = 0
        self.is_dma = is_dma


def build_nc():
    nc = bass.Bass("TRN2", target_bir_lowering=False)

    def inp(name, shape, dt=F32):
        return nc.dram_tensor(name, shape, dt, kind="ExternalInput").ap()

    x_h = inp("x", [T, D])
    wvT_h = inp("wvT", [D, D])
    owT_h = inp("owT", [D, D])
    sw1T_h = inp("sw1T", [D, SF])
    sw2T_h = inp("sw2T", [SF, D])
    rwT_h = inp("rwT", [D, NE])
    watt_h = inp("watt", [NP, 128, 64])
    wvb_h = inp("wvb", [1, D])
    owb_h = inp("owb", [1, D])
    ebias_h = inp("ebias", [1, NE])
    w1_h = inp("w1", [ELOC, D, EF])
    w2_h = inp("w2", [ELOC, EF, D])
    tokid_h = inp("tokid", [T, 1])
    pm_h = inp("pm", [1, 8])
    u128_h = inp("u128", [128, 128])
    uc_h = inp("uc", [64, 64])
    iota_h = inp("iota", [128, NE])
    onesc_h = inp("onesc", [128, 64])

    out_h = nc.dram_tensor("out", [T, D], F32, kind="ExternalOutput").ap()

    def idram(name, shape, dt=F32):
        return nc.dram_tensor(name, shape, dt, kind="Internal").ap()

    d_xn = idram("d_xn", [T, D], BF16)
    d_xnall = idram("d_xnall", [8 * T, D], BF16)
    d_cnt = idram("d_cnt", [1, NE])
    d_cs = idram("d_cs", [8, NE])
    d_ss = idram("d_ss", [16, NE])
    d_choff = idram("d_choff", [NT, NE])
    d_tokmap = idram("d_tokmap", [25088, 1])
    d_tokmy = idram("d_tokmy", [3136, 1])
    d_xs = idram("d_xs", [ELOC * CAP, D], BF16)
    d_ymine = idram("d_ymine", [ELOC * CAP, D], BF16)
    d_yall = idram("d_yall", [NE * CAP, D], BF16)
    d_sh2 = idram("d_sh2", [T, D], BF16)
    d_inv = idram("d_inv", [64, 512])
    d_wv16 = idram("d_wv16", [D, D], BF16)
    d_sw1b = idram("d_sw1b", [D, SF], BF16)
    d_ow16 = idram("d_ow16", [D, D], BF16)
    d_w116 = idram("d_w116", [ELOC, D, EF], BF16)
    d_w216 = idram("d_w216", [ELOC, EF, D], BF16)

    dbg = {}

    def dbgout(name, shape, dt):
        if DEBUG:
            dbg[name] = nc.dram_tensor("dbg_" + name, shape, dt,
                                       kind="ExternalOutput").ap()

    dbgout("xn", [T, D], BF16)
    dbgout("cnt", [1, NE], F32)
    dbgout("choff", [NT, NE], F32)
    dbgout("if6", [128, NT, 8], F32)
    dbgout("pw", [128, NT, 8], F32)
    dbgout("gidx", [128, NT, 8], I32)
    dbgout("tokmy", [3136, 1], F32)
    dbgout("y", [ELOC * CAP, D], BF16)
    dbgout("sh2", [T, D], BF16)
    dbgout("ot", [128, NP, T], BF16)
    dbgout("satt", [T, D], BF16)

    st = ExitStack()
    with st:
        def sb(name, shape, dt):
            return st.enter_context(nc.sbuf_tensor(name, shape, dt))

        def psum(name, shape):
            return st.enter_context(nc.psum_tensor(name, shape, F32))

        c_iota = sb("c_iota", [128, NE], F32)
        c_ones = sb("c_ones", [128, 64], BF16)
        c_u128 = sb("c_u128", [128, 128], BF16)
        c_uc = sb("c_uc", [64, 64], BF16)
        w_watt = sb("w_watt", [128, NP, 64], BF16)
        w_rwT = sb("w_rwT", [128, NJ, NE], BF16)
        w_sw2r = sb("w_sw2r", [128, 16, D], BF16)
        w_sw1s = sb("w_sw1s", [128, 2, NJ, 128], BF16)
        w_vx = sb("w_vx", [128, 2, NJ, 128], BF16)
        w_ow = sb("w_ow", [128, 2, 512], BF16)
        w_w1 = sb("w_w1", [128, NJ, EF], BF16)
        w_w2 = sb("w_w2", [128, 2, D], BF16)
        b_wvb = sb("b_wvb", [128, D], F32)
        b_owb = sb("b_owb", [128, D], F32)
        b_eb = sb("b_eb", [128, NE], F32)
        s_pm = sb("s_pm", [1, 8], F32)
        s_tokid = sb("s_tokid", [128, NT], F32)

        s_xnT = sb("s_xnT", [128, NJ, T], BF16)
        s_oT = sb("s_oT", [128, NP, T], BF16)
        s_vx = sb("s_vx", [128, NT, 128], BF16)
        s_xw = sb("s_xw", [128, 4, 512], BF16)
        s_pT = sb("s_pT", [128, 4, 512], BF16)
        s_hraw = sb("s_hraw", [128, 16, UT], BF16)
        s_xn_st = sb("s_xn_st", [128, 2, D], BF16)
        s_sh2st = sb("s_sh2st", [128, 2, D], BF16)
        s_A = sb("s_A", [128, NT, NE], BF16)
        s_xsT = sb("s_xsT", [128, NJ, CAP], BF16)
        s_h1r = sb("s_h1r", [128, 8, CAP], BF16)
        s_yst = sb("s_yst", [128, 2, D], BF16)
        s_y6 = sb("s_y6", [128, 2, D], BF16)

        s_xt = sb("s_xt", [128, 2, D], F32)
        s_moe = sb("s_moe", [128, D], F32)
        s_lns = sb("s_lns", [128, 8], F32)
        s_scores = sb("s_scores", [128, NE], F32)
        s_biased = sb("s_biased", [128, NE], F32)
        s_m8 = sb("s_m8", [128, 8], F32)
        s_i8 = sb("s_i8", [128, 8], U32)
        s_if6a = sb("s_if6a", [128, NT, 8], F32)
        s_Af = sb("s_Af", [128, NE], F32)
        s_Af2 = sb("s_Af2", [128, NE], F32)
        s_scr64 = sb("s_scr64", [128, NE], F32)
        s_sc6 = sb("s_sc6", [128, 8], F32)
        s_pw = sb("s_pw", [128, NT, 8], F32)
        s_r6 = sb("s_r6", [128, 8], F32)
        s_v6 = sb("s_v6", [128, 8], F32)
        s_t8a = sb("s_t8a", [128, 8], F32)
        s_t8b = sb("s_t8b", [128, 8], F32)
        s_sidx = sb("s_sidx", [128, NT, 8], I32)
        s_gidx = sb("s_gidx", [128, NT, 8], I32)
        s_r6i = sb("s_r6i", [128, 8], I32)
        s_cnt = sb("s_cnt", [1, NE], F32)
        s_csf = sb("s_csf", [1, NE], F32)
        s_co = sb("s_co", [1, NE], F32)
        s_co2 = sb("s_co2", [1, NE], F32)
        s_coi = sb("s_coi", [1, NE], I32)
        s_cq = sb("s_cq", [1, NE], I32)
        s_cqf = sb("s_cqf", [1, NE], F32)
        s_crf = sb("s_crf", [1, NE], F32)
        s_stage = sb("s_stage", [64, 64], BF16)
        s_choff = sb("s_choff", [64, NE], F32)
        s_offb = sb("s_offb", [128, 2, NE], F32)
        s_C = sb("s_C", [128, NE], F32)
        s_Ct = sb("s_Ct", [128, NE], F32)
        s_rstmp = sb("s_rstmp", [128, 512], F32)
        s_invb = sb("s_invb", [128, 512], F32)
        s_tmf = sb("s_tmf", [128, 24], F32)
        s_tmidx = sb("s_tmidx", [128, 24], I32)
        s_zero = sb("s_zero", [128, 256], F32)
        s_junk = sb("s_junk", [128, 32], F32)

        p_att0 = psum("p_att0", [128, 512])
        p_att1 = psum("p_att1", [128, 512])
        p_o0 = psum("p_o0", [128, 512])
        p_o1 = psum("p_o1", [128, 512])
        p_rs0 = psum("p_rs0", [128, 512])
        p_rs1 = psum("p_rs1", [128, 512])
        p_bg0 = psum("p_bg0", [128, 512])
        p_bg1 = psum("p_bg1", [128, 512])

        names = "gq sq mm ex ve sc c_xn c_cnt c_tok c_y vsy ssy".split()
        sems = {nm: Ctr(st.enter_context(nc.semaphore("sem_" + nm)),
                        is_dma=nm in ("gq", "sq"))
                for nm in names}
        gq, sq = sems["gq"], sems["sq"]
        smm, sex, sve, ssc = sems["mm"], sems["ex"], sems["ve"], sems["sc"]

        Q = {k: [] for k in ("sync", "gpsimd", "tensor", "scalar", "vector")}
        marks = []

        def mark():
            _flush("vector")
            _flush("scalar")
            marks.append({k: len(v) for k, v in Q.items()})
        tok = {}

        def q(eng, fn):
            Q[eng].append(fn)

        _fence = {}

        def wait(eng, ctr, val=None):
            v = ctr.v if (val is None or ctr.is_dma) else val
            h = ctr.h
            if ctr.is_dma:
                owner = "sync" if ctr is sems["sq"] else "gpsimd"
                if _fence.get(id(ctr), -1) < v:
                    _fence[id(ctr)] = v
                    if eng != owner:
                        q(owner, lambda e: e.wait_ge(h, v))
            fn = lambda e: e.wait_ge(h, v)
            if eng in ("vector", "scalar") and _stage[eng] is not None:
                _swaits[eng].append(fn)
            else:
                q(eng, fn)

        def guard(eng, key):
            if key in tok:
                ctr, v = tok[key]
                wait(eng, ctr, v)

        def settok(key, ctr):
            tok[key] = (ctr, ctr.v)

        def dma(eng, dst, src, ctr):
            ctr.v += 16
            h = ctr.h
            q(eng, lambda e: e.dma_start(dst, src).then_inc(h, 16))

        def dmaT(eng, dst, src, ctr):
            ctr.v += 16
            h = ctr.h
            q(eng, lambda e: e.dma_start_transpose(dst, src).then_inc(h, 16))

        def dmaslow(eng, dst, src, ctr):
            ctr.v += 16
            h = ctr.h
            q(eng, lambda e: e.dma_start(
                dst, src, allow_slow_non_contiguous=True).then_inc(h, 16))

        def mm(out, lhsT, rhs, start=True, stop=True, tp=None, ctr=None):
            if ctr is not None:
                ctr.v += 1
                h = ctr.h
                q("tensor", lambda e: e.matmul(
                    out, lhsT, rhs, start=start, stop=stop,
                    tile_position=tp).then_inc(h, 1))
            else:
                q("tensor", lambda e: e.matmul(out, lhsT, rhs, start=start,
                                               stop=stop, tile_position=tp))

        _stage = {"scalar": None, "vector": None}
        _swaits = {"scalar": [], "vector": []}
        _prevt = {"scalar": set(), "vector": set()}

        def _names(objs):
            out = set()
            for o in objs:
                t = getattr(o, "tensor", None)
                if t is not None:
                    out.add(t.name)
            return out

        def _flush(eng, attach=False):
            if _stage[eng] is None:
                return
            b, touched, has_ctr = _stage[eng]
            if attach:
                sc_ = sems["vsy"] if eng == "vector" else sems["ssy"]
                sc_.v += 1
                h, v = sc_.h, sc_.v

                def _att(e, b=b, h=h, has_ctr=has_ctr, eng=eng):
                    r = b(e)
                    if r is None or has_ctr:
                        if eng == "vector":
                            e.engine_nop()
                            r = e.engine_nop()
                        else:
                            e.activation(s_junk[:, 0:8], s_junk[:, 0:8],
                                         AF.Identity)
                            r = e.activation(s_junk[:, 8:16], s_junk[:, 8:16],
                                             AF.Identity)
                    r.then_inc(h, 1)
                Q[eng].append(_att)
                for w in _swaits[eng]:
                    Q[eng].append(w)
                Q[eng].append(lambda e, h=h, v=v: e.wait_ge(h, v))
            else:
                Q[eng].append(lambda e, b=b: b(e))
                for w in _swaits[eng]:
                    Q[eng].append(w)
            _swaits[eng] = []
            _prevt[eng] = set() if attach else touched
            _stage[eng] = None

        def _emit_eng(eng, builder, touched, has_ctr=False):
            if _stage[eng] is not None:
                conflict = bool((_stage[eng][1] | _prevt[eng]) & touched)
                _flush(eng, attach=conflict)
            else:
                if _prevt[eng] & touched:
                    sc_ = sems["vsy"] if eng == "vector" else sems["ssy"]
                    sc_.v += 1
                    h, v = sc_.h, sc_.v

                    def _np(e, h=h, eng=eng):
                        if eng == "vector":
                            e.engine_nop()
                            e.engine_nop().then_inc(h, 1)
                        else:
                            e.activation(s_junk[:, 0:8], s_junk[:, 0:8],
                                         AF.Identity)
                            e.activation(s_junk[:, 8:16], s_junk[:, 8:16],
                                         AF.Identity).then_inc(h, 1)
                    Q[eng].append(_np)
                    Q[eng].append(lambda e, h=h, v=v: e.wait_ge(h, v))
                    _prevt[eng] = set()
            _stage[eng] = (builder, touched, has_ctr)

        def act(out, in_, func, ctr=None, **kw):
            touched = _names([out, in_, kw.get("accum_out"), kw.get("bias"),
                              kw.get("scale")])
            if ctr is not None:
                ctr.v += 1
                h = ctr.h
                b = lambda e: e.activation(out, in_, func, **kw).then_inc(h, 1)
                _emit_eng("scalar", b, touched, has_ctr=True)
            else:
                b = lambda e: e.activation(out, in_, func, **kw)
                _emit_eng("scalar", b, touched)

        def vec(meth, *a, ctr=None, **kw):
            touched = _names(list(a) + list(kw.values()))
            if ctr is not None:
                ctr.v += 1
                h = ctr.h
                b = lambda e: getattr(e, meth)(*a, **kw).then_inc(h, 1)
                _emit_eng("vector", b, touched, has_ctr=True)
            else:
                b = lambda e: getattr(e, meth)(*a, **kw)
                _emit_eng("vector", b, touched)

        def coll(kind, op, ins_, outs_, ctr):
            ctr.v += 1
            h = ctr.h
            q("gpsimd", lambda e: e.collective_compute(
                kind, op, replica_groups=RG8, ins=ins_, outs=outs_
            ).then_inc(h, 1))

        _bc_reg = {}

        def iscat(out_ap, idx_ap, in_ap, ctr, bc):
            ctr.v += 16
            h = ctr.h

            def fn(e):
                if bc not in _bc_reg:
                    _bc_reg[bc] = e.to_reg(bc)
                e.indirect_dma_start(
                    out_ap, IndirectOffsetOnAxis(idx_ap, 0), in_ap, None,
                    bounds_check=_bc_reg[bc], oob_is_err=False).then_inc(h, 16)
            q("gpsimd", fn)

        def igat(out_ap, in_ap, idx_ap, ctr):
            ctr.v += 16
            h = ctr.h
            q("gpsimd", lambda e: e.indirect_dma_start(
                out_ap, None, in_ap, IndirectOffsetOnAxis(idx_ap, 0)
            ).then_inc(h, 16))

        satt_flat = s_xnT[:].rearrange("p a t -> p (a t)")

        def satt(tt):
            return satt_flat[:, D * tt:D * (tt + 1)]

        vec("memset", s_junk[:], 0.0)
        vec("memset", s_zero[:], 0.0)
        vec("memset", s_stage[:], 0.0)
        vec("memset", s_pT[:], 0.0, ctr=sve)
        zero_tok = sve.v
        wait("scalar", sve, zero_tok)
        act(s_junk[:, 0:16], s_junk[:, 0:16], AF.Identity)
        act(s_junk[:, 16:32], s_junk[:, 16:32], AF.Identity)

        dma("gpsimd", c_iota[:], iota_h[:], gq)
        dma("gpsimd", c_ones[:], onesc_h[:], gq)
        dma("gpsimd", c_u128[:], u128_h[:], gq)
        dma("gpsimd", c_uc[:], uc_h[:], gq)
        dma("gpsimd", w_watt[:], watt_h.rearrange("a p f -> p a f"), gq)
        dma("gpsimd", w_rwT[:], rwT_h.rearrange("(a p) f -> p a f", p=128), gq)
        for jj in range(16):
            dma("gpsimd", w_sw2r[:, jj, :],
                sw2T_h[128 * jj:128 * (jj + 1), :], gq)
        gq_const = gq.v

        def precast(src2d, dst2d, nrow, buf, key):
            for jj in range(nrow // 128):
                guard("gpsimd", key)
                dma("gpsimd", buf, src2d[128 * jj:128 * (jj + 1), :], gq)
                wait("gpsimd", gq)
                dma("gpsimd", dst2d[128 * jj:128 * (jj + 1), :], buf, gq)
                settok(key, gq)

        pc_full = s_hraw[:].rearrange("p a f -> p (a f)")
        pc_buf = pc_full[:, 0:D]
        precast(sw1T_h, d_sw1b, D, pc_full[:, 0:SF], ("pcb",))
        precast(wvT_h, d_wv16, D, pc_buf, ("pcb",))
        precast(owT_h, d_ow16, D, pc_buf, ("pcb",))
        for e in range(ELOC):
            precast(w1_h[e], d_w116[e], D, pc_buf[:, 0:EF], ("pcb",))
            precast(w2_h[e], d_w216[e], EF, pc_buf, ("pcb",))
        gq_cast = gq.v

        dma("sync", b_wvb[:], wvb_h[0:1, :].to_broadcast([128, D]), sq)
        dma("sync", b_owb[:], owb_h[0:1, :].to_broadcast([128, D]), sq)
        dma("sync", b_eb[:], ebias_h[0:1, :].to_broadcast([128, NE]), sq)
        dma("sync", s_pm[:], pm_h[:], sq)
        dmaslow("sync", s_tokid[:],
                tokid_h.rearrange("(a p) o -> p (a o)", p=128), sq)
        sq_const = sq.v

        for tt in range(NT):
            sl = tt % 2
            guard("sync", ("xt", sl))
            dma("sync", s_xt[:, sl, :], x_h[128 * tt:128 * (tt + 1), :], sq)
            xl = sq.v
            wait("vector", sq, xl)
            vec("reduce_sum", s_lns[:, 0:1], s_xt[:, sl, :], AX.X)
            wait("scalar", sq, xl)
            guard("scalar", ("moe_scr",))
            act(s_moe[:], s_xt[:, sl, :], AF.Square, ctr=ssc,
                accum_out=s_lns[:, 1:2])
            settok(("moe_scr",), ssc)
            wait("vector", ssc)
            vec("tensor_scalar_mul", s_lns[:, 2:3], s_lns[:, 0:1], 1.0 / D)
            vec("tensor_scalar_mul", s_lns[:, 3:4], s_lns[:, 1:2], 1.0 / D)
            vec("tensor_mul", s_lns[:, 4:5], s_lns[:, 2:3], s_lns[:, 2:3])
            vec("tensor_sub", s_lns[:, 5:6], s_lns[:, 3:4], s_lns[:, 4:5],
                ctr=sve)
            vec("tensor_scalar_add", s_lns[:, 5:6], s_lns[:, 5:6], 1e-5,
                ctr=sve)
            wait("scalar", sve)
            act(s_lns[:, 6:7], s_lns[:, 5:6], AF.Sqrt, ctr=ssc)
            wait("vector", ssc)
            vec("reciprocal", s_lns[:, 7:8], s_lns[:, 6:7])
            guard("vector", ("xnst", sl))
            vec("scalar_tensor_tensor", s_xn_st[:, sl, :], s_xt[:, sl, :],
                s_lns[:, 2:3], s_lns[:, 7:8].to_broadcast([128, D]),
                ALU.subtract, ALU.mult, ctr=sve)
            settok(("xt", sl), sve)
            wait("sync", sve)
            dma("sync", d_xn[128 * tt:128 * (tt + 1), :], s_xn_st[:, sl, :], sq)
            settok(("xnst", sl), sq)
        sq_xn = sq.v

        wait("gpsimd", sq, sq_xn)
        coll("AllGather", ALU.bypass, [d_xn[:]], [d_xnall[:]], sems["c_xn"])

        wait("sync", sq, sq_xn)
        for jj in range(NJ):
            dmaT("sync", s_xnT[:, jj, :], d_xn[:, 128 * jj:128 * (jj + 1)], sq)
        sq_xnT = sq.v

        if DEBUG:
            dma("sync", dbg["xn"][:], d_xn[:], sq)

        mark()
        wait("tensor", sq, sq_xnT)
        wait("tensor", gq, gq_const)
        wait("vector", sq, sq_const)
        for tt in range(NT):
            guard("tensor", ("psA",))
            for j in range(NJ):
                mm(p_bg0[:, 0:NE], s_xnT[:, j, 128 * tt:128 * (tt + 1)],
                   w_rwT[:, j, :], start=(j == 0), stop=(j == NJ - 1),
                   ctr=smm if j == NJ - 1 else None)
            wait("scalar", smm)
            act(s_scores[:], p_bg0[:, 0:NE], AF.Sigmoid, ctr=ssc)
            settok(("psA",), ssc)
            wait("vector", ssc)
            vec("tensor_add", s_biased[:], s_scores[:], b_eb[:])
            vec("max", out=s_m8[:], in_=s_biased[:])
            vec("max_index", s_i8[:], s_m8[:], s_biased[:])
            vec("tensor_copy", s_if6a[:, tt, :], s_i8[:])
            vec("memset", s_Af[:], 0.0)
            cur, nxt = s_Af, s_Af2
            for k in range(EK):
                vec("scalar_tensor_tensor", nxt[:], c_iota[:],
                    s_if6a[:, tt, k:k + 1], cur[:], ALU.is_equal, ALU.add)
                cur, nxt = nxt, cur
                vec("scalar_tensor_tensor", s_scr64[:], c_iota[:],
                    s_if6a[:, tt, k:k + 1], s_scores[:], ALU.is_equal,
                    ALU.mult, accum_out=s_sc6[:, k:k + 1])
            vec("tensor_copy", s_A[:, tt, :], cur[:])
            vec("reduce_sum", s_t8a[:, 0:1], s_sc6[:, 0:EK], AX.X)
            vec("tensor_scalar_add", s_t8a[:, 1:2], s_t8a[:, 0:1], 1e-20)
            vec("reciprocal", s_t8a[:, 2:3], s_t8a[:, 1:2])
            vec("tensor_scalar_mul", s_pw[:, tt, 0:EK], s_sc6[:, 0:EK],
                s_t8a[:, 2:3], ctr=sve)
            wait("tensor", sve)
            guard("tensor", ("psB",))
            mm(p_bg1[0:64, 0:NE], c_ones[:], s_A[:, tt, :], ctr=smm)
            wait("vector", smm)
            guard("vector", ("cntst",))
            vec("tensor_copy", s_cnt[:], p_bg1[0:1, 0:NE], ctr=sve)
            settok(("psB",), sve)
            wait("sync", sve)
            dma("sync", d_ss[tt:tt + 1, :], s_cnt[:], sq)
            settok(("cntst",), sq)
        sq_ss = sq.v

        wait("gpsimd", sq, sq_ss)
        wait("gpsimd", sve, zero_tok)
        dma("gpsimd", s_stage[0:16, :], d_ss[:], gq)
        wait("tensor", gq)
        guard("tensor", ("psB",))
        mm(p_bg1[0:64, 0:NE], c_ones[0:64, :], s_stage[:], ctr=smm)
        wait("vector", smm)
        guard("vector", ("cntst",))
        vec("tensor_copy", s_cnt[:], p_bg1[0:1, 0:NE], ctr=sve)
        settok(("psB",), sve)
        wait("sync", sve)
        dma("sync", d_cnt[:], s_cnt[:], sq)
        settok(("cntst",), sq)
        wait("gpsimd", sq, sq.v)
        coll("AllGather", ALU.bypass, [d_cnt[:]], [d_cs[:]], sems["c_cnt"])

        mark()
        wait("vector", sems["c_cnt"])
        vec("memset", s_co[:], 0.0)
        cur, nxt = s_co, s_co2
        for cc in range(8):
            wait("sync", sems["c_cnt"])
            guard("sync", ("csf",))
            dma("sync", s_csf[:], d_cs[cc:cc + 1, :], sq)
            wait("vector", sq)
            vec("scalar_tensor_tensor", nxt[:], s_csf[:],
                s_pm[:, cc:cc + 1], cur[:], ALU.mult, ALU.add, ctr=sve)
            settok(("csf",), sve)
            cur, nxt = nxt, cur
        vec("tensor_copy", s_coi[:], cur[:])
        vec("tensor_scalar", s_cq[:], s_coi[:], 6, None, ALU.arith_shift_right)
        vec("tensor_scalar", s_coi[:], s_coi[:], 63, None, ALU.bitwise_and)
        vec("tensor_copy", s_cqf[:], s_cq[:])
        vec("tensor_copy", s_crf[:], s_coi[:], ctr=sve)
        sve_qr = sve.v
        wait("gpsimd", sve, sve_qr)
        dma("gpsimd", s_stage[0:1, :], s_cqf[:], gq)
        dma("gpsimd", s_stage[1:2, :], s_crf[:], gq)
        dma("gpsimd", s_stage[2:18, :], d_ss[:], gq)
        wait("tensor", gq)
        wait("tensor", sve, sve_qr)
        guard("tensor", ("psA",))
        mm(p_bg0[0:64, 0:NE], c_uc[:], s_stage[:], ctr=smm)
        wait("vector", smm)
        vec("tensor_copy", s_choff[:], p_bg0[0:64, 0:NE], ctr=sve)
        settok(("psA",), sve)
        wait("sync", sve)
        dma("sync", d_choff[:], s_choff[0:16, :], sq)
        sq_choff = sq.v
        if DEBUG:
            dma("sync", dbg["choff"][:], d_choff[:], sq)

        wait("gpsimd", sve, zero_tok)
        dma("gpsimd", d_tokmap[:].rearrange("(p b) o -> p (b o)", p=128),
            s_zero[:, 0:196], gq)
        gq_tmz = gq.v

        for tt in range(NT):
            osl = tt % 2
            guard("sync", ("offb", osl))
            wait("sync", sq, sq_choff)
            dma("sync", s_offb[:, osl, :],
                d_choff[tt:tt + 1, :].to_broadcast([128, NE]), sq)
            off_ld = sq.v
            guard("tensor", ("psB",))
            mm(p_bg1[:, 0:NE], c_u128[:], s_A[:, tt, :], ctr=smm)
            wait("vector", smm)
            wait("vector", sq, off_ld)
            vec("tensor_copy", s_C[:], p_bg1[:, 0:NE], ctr=sve)
            settok(("psB",), sve)
            vec("tensor_add", s_Ct[:], s_C[:], s_offb[:, osl, :], ctr=sve)
            settok(("offb", osl), sve)
            for k in range(EK):
                vec("scalar_tensor_tensor", s_scr64[:], c_iota[:],
                    s_if6a[:, tt, k:k + 1], s_Ct[:], ALU.is_equal, ALU.mult,
                    accum_out=s_r6[:, k:k + 1])
            E6 = slice(0, EK)
            vec("tensor_scalar", s_v6[:, E6], s_r6[:, E6], float(CAP), None,
                ALU.is_lt)
            vec("tensor_mul", s_pw[:, tt, E6], s_pw[:, tt, E6], s_v6[:, E6])
            vec("tensor_scalar", s_t8a[:, E6], s_if6a[:, tt, E6], float(CAP),
                None, ALU.mult)
            vec("tensor_add", s_t8a[:, E6], s_t8a[:, E6], s_r6[:, E6])
            vec("tensor_copy", s_r6i[:, E6], s_if6a[:, tt, E6])
            vec("tensor_scalar", s_r6i[:, E6], s_r6i[:, E6], 3, None,
                ALU.arith_shift_right)
            vec("tensor_copy", s_t8b[:, E6], s_r6i[:, E6])
            vec("tensor_scalar", s_t8b[:, E6], s_t8b[:, E6], 64.0, None,
                ALU.mult)
            vec("tensor_add", s_t8b[:, E6], s_t8b[:, E6], s_t8a[:, E6])
            vec("tensor_scalar", s_scr64[:, 0:EK], s_v6[:, E6], -2e6, 2e6,
                ALU.mult, ALU.add)
            vec("tensor_add", s_t8b[:, E6], s_t8b[:, E6], s_scr64[:, 0:EK])
            vec("tensor_copy", s_sidx[:, tt, E6], s_t8b[:, E6])
            vec("tensor_scalar_min", s_t8a[:, E6], s_t8a[:, E6],
                float(NE * CAP - 1))
            vec("tensor_copy", s_gidx[:, tt, E6], s_t8a[:, E6], ctr=sve)
        sve_route = sve.v

        wait("gpsimd", sve, sve_route)
        wait("gpsimd", gq, gq_tmz)
        for tt in range(NT):
            for k in range(EK):
                iscat(d_tokmap[:], s_sidx[:, tt, k:k + 1],
                      s_tokid[:, tt:tt + 1], gq, 25087)
        wait("gpsimd", gq)
        coll("ReduceScatter", ALU.add, [d_tokmap[:]], [d_tokmy[:]],
             sems["c_tok"])
        if DEBUG:
            wait("sync", sems["c_tok"])
            dma("sync", dbg["tokmy"][:], d_tokmy[:], sq)
            dma("sync", dbg["cnt"][:], d_cnt[:], sq)
            wait("sync", sve, sve_route)
            dma("sync", dbg["if6"][:], s_if6a[:], sq)
            dma("sync", dbg["pw"][:], s_pw[:], sq)
            dma("sync", dbg["gidx"][:], s_gidx[:], sq)

        mark()
        wait("tensor", gq, gq_cast)
        for u in range(NU):
            tsl = slice(UT * u, UT * (u + 1))
            for ft in range(16):
                sl = ft % 2
                guard("sync", ("sw1", sl))
                dma("sync", w_sw1s[:, sl, :, :],
                    d_sw1b.rearrange("(a p) f -> p a f", p=128)
                    [:, :, 128 * ft:128 * (ft + 1)], sq)
                ld = sq.v
                pp = p_bg0 if ft % 2 == 0 else p_bg1
                pk = ("psA",) if ft % 2 == 0 else ("psB",)
                guard("tensor", pk)
                wait("tensor", sq, ld)
                for j in range(NJ):
                    mm(pp[:, 0:UT], w_sw1s[:, sl, j, :], s_xnT[:, j, tsl],
                       start=(j == 0), stop=(j == NJ - 1),
                       ctr=smm if j == NJ - 1 else None)
                settok(("sw1", sl), smm)
                wait("vector", smm)
                vec("tensor_copy", s_hraw[:, ft, :], pp[:, 0:UT], ctr=sve)
                settok(pk, sve)
            wait("scalar", sve)
            act(s_hraw[:].rearrange("p a f -> p (a f)"),
                s_hraw[:].rearrange("p a f -> p (a f)"),
                AF.Gelu_apprx_tanh, ctr=ssc)
            wait("tensor", ssc)
            for tq in range(2):
                tt = 2 * u + tq
                for ih in range(2):
                    pp = p_bg0 if ih == 0 else p_bg1
                    pk = ("psA",) if ih == 0 else ("psB",)
                    guard("tensor", pk)
                    for ft in range(16):
                        mm(pp[:, 0:512],
                           s_hraw[:, ft, 128 * tq:128 * (tq + 1)],
                           w_sw2r[:, ft, 512 * ih:512 * (ih + 1)],
                           start=(ft == 0), stop=(ft == 15),
                           ctr=smm if ft == 15 else None)
                    wait("vector", smm)
                    guard("vector", ("sh2st", tt % 2))
                    vec("tensor_copy",
                        s_sh2st[:, tt % 2, 512 * ih:512 * (ih + 1)],
                        pp[:, 0:512], ctr=sve)
                    settok(pk, sve)
                wait("sync", sve)
                dma("sync", d_sh2[128 * tt:128 * (tt + 1), :],
                    s_sh2st[:, tt % 2, :], sq)
                settok(("sh2st", tt % 2), sq)
        sq_sh2 = sq.v
        if DEBUG:
            wait("sync", sq, sq_sh2)
            dma("sync", dbg["sh2"][:], d_sh2[:], sq)

        mark()
        wait("gpsimd", sems["c_tok"])
        wait("gpsimd", sems["c_xn"])
        dmaslow("gpsimd", s_tmf[:],
                d_tokmy[0:3072, :].rearrange("(j p) o -> p (j o)", p=128), gq)
        wait("vector", gq)
        vec("tensor_copy", s_tmidx[:], s_tmf[:], ctr=sve)
        wait("gpsimd", sve)
        for jj in range(24):
            sl = jj % 2
            guard("gpsimd", ("xsg", sl))
            igat(s_y6[:, sl, :], d_xnall[:], s_tmidx[:, jj:jj + 1], gq)
            wait("gpsimd", gq)
            dma("gpsimd", d_xs[128 * jj:128 * (jj + 1), :], s_y6[:, sl, :], gq)
            settok(("xsg", sl), gq)
        gq_xs = gq.v

        w2view = w_w1[:].rearrange("p a f -> p (a f)").rearrange(
            "p (b g) -> p b g", b=4)

        for eg in range(4):
            for el in range(2):
                e = 2 * eg + el
                guard("sync", ("xsT",))
                wait("sync", gq, gq_xs)
                for j in range(NJ):
                    dmaT("sync", s_xsT[:, j, :],
                         d_xs[CAP * e:CAP * (e + 1), 128 * j:128 * (j + 1)],
                         sq)
                xsl = sq.v
                guard("sync", ("w1",))
                dma("sync", w_w1[:],
                    d_w116[e].rearrange("(a p) f -> p a f", p=128), sq)
                w1l = sq.v
                wait("tensor", sq, w1l)
                for ft in range(4):
                    pp = p_bg0 if ft % 2 == 0 else p_bg1
                    pk = ("psA",) if ft % 2 == 0 else ("psB",)
                    guard("tensor", pk)
                    for j in range(NJ):
                        mm(pp[:, 0:CAP], w_w1[:, j, 128 * ft:128 * (ft + 1)],
                           s_xsT[:, j, :], start=(j == 0), stop=(j == NJ - 1),
                           ctr=smm if j == NJ - 1 else None)
                    wait("vector", smm)
                    vec("tensor_copy", s_h1r[:, 4 * el + ft, :], pp[:, 0:CAP],
                        ctr=sve)
                    settok(pk, sve)
                settok(("xsT",), smm)
                settok(("w1",), smm)
            wait("scalar", sve)
            act(s_h1r[:].rearrange("p a f -> p (a f)"),
                s_h1r[:].rearrange("p a f -> p (a f)"),
                AF.Gelu_apprx_tanh, ctr=ssc)
            gelu_tok = ssc.v
            wait("tensor", ssc, gelu_tok)
            for el in range(2):
                e = 2 * eg + el
                guard("sync", ("w1",))
                dma("sync", w2view,
                    d_w216[e].rearrange("(a p) f -> p a f", p=128), sq)
                w2l = sq.v
                wait("tensor", sq, w2l)
                for ct in range(3):
                    for ih in range(2):
                        pp = p_bg0 if ih == 0 else p_bg1
                        pk = ("psA",) if ih == 0 else ("psB",)
                        guard("tensor", pk)
                        for f in range(4):
                            mm(pp[:, 0:512],
                               s_h1r[:, 4 * el + f, 128 * ct:128 * (ct + 1)],
                               w2view[:, f, 512 * ih:512 * (ih + 1)],
                               start=(f == 0), stop=(f == 3),
                               ctr=smm if f == 3 else None)
                        wait("vector", smm)
                        guard("vector", ("yst", ct % 2))
                        vec("tensor_copy",
                            s_yst[:, ct % 2, 512 * ih:512 * (ih + 1)],
                            pp[:, 0:512], ctr=sve)
                        settok(pk, sve)
                    wait("sync", sve)
                    dma("sync",
                        d_ymine[CAP * e + 128 * ct:CAP * e + 128 * (ct + 1), :],
                        s_yst[:, ct % 2, :], sq)
                    settok(("yst", ct % 2), sq)
                settok(("w1",), smm)
        sq_y = sq.v
        wait("gpsimd", sq, sq_y)
        coll("AllGather", ALU.bypass, [d_ymine[:]], [d_yall[:]], sems["c_y"])
        if DEBUG:
            dma("sync", dbg["y"][:], d_ymine[:], sq)

        mark()
        wait("sync", gq, gq_cast)
        inv_slot = [0]
        for p in range(NP):
            guard("vector", ("vxall",))
            vsl = p % 2
            guard("sync", ("wvx", vsl))
            dma("sync", w_vx[:, vsl, :, :],
                d_wv16.rearrange("(a p) f -> p a f", p=128)
                [:, :, 128 * p:128 * (p + 1)], sq)
            wv_ld = sq.v
            wait("tensor", sq, wv_ld)
            for tt in range(NT):
                guard("tensor", ("psA",))
                for j in range(NJ):
                    mm(p_bg0[:, 0:128], s_xnT[:, j, 128 * tt:128 * (tt + 1)],
                       w_vx[:, vsl, j, :], start=(j == 0), stop=(j == NJ - 1),
                       ctr=smm if j == NJ - 1 else None)
                wait("vector", smm)
                guard("vector", ("vx", tt))
                vec("tensor_add", s_vx[:, tt, :], p_bg0[:, 0:128],
                    b_wvb[:, 128 * p:128 * (p + 1)], ctr=sve)
                settok(("psA",), sve)
                settok(("vx", tt), sve)
            settok(("wvx", vsl), smm)
            for tc in range(4):
                guard("tensor", ("psA",))
                mm(p_bg0[0:64, 0:512], w_watt[0:64, p, :],
                   s_xnT[0:64, p, 512 * tc:512 * (tc + 1)], tp=(0, 0),
                   start=True, stop=True)
                guard("tensor", ("psB",))
                mm(p_bg1[64:128, 0:512], w_watt[64:128, p, :],
                   s_xnT[64:128, p, 512 * tc:512 * (tc + 1)], tp=(64, 64),
                   start=True, stop=True, ctr=smm)
                wait("vector", smm)
                guard("vector", ("xw", tc))
                vec("tensor_copy", s_xw[0:64, tc, :], p_bg0[0:64, 0:512])
                vec("tensor_copy", s_xw[64:128, tc, :], p_bg1[64:128, 0:512],
                    ctr=sve)
                settok(("psA",), sve)
                settok(("psB",), sve)
                settok(("xw", tc), sve)
            wait("tensor", sve)

            for tc in range(4):
                pend = []
                for sch in range(NT):
                    sl0 = (sch % 2) * 2
                    sl1 = sl0 + 1
                    guard("tensor", ("att0",))
                    mm(p_att0[:, 0:512],
                       s_xnT[0:64, p, 128 * sch:128 * (sch + 1)],
                       s_xw[0:64, tc, :], tp=(0, 0), ctr=smm)
                    att0_tok = smm.v
                    guard("tensor", ("att1",))
                    mm(p_att1[:, 0:512],
                       s_xnT[64:128, p, 128 * sch:128 * (sch + 1)],
                       s_xw[64:128, tc, :], tp=(64, 0), ctr=smm)
                    att1_tok = smm.v
                    for ps_ in pend:
                        tok[ps_] = (smm, att1_tok)
                    pend = []
                    guard("scalar", ("pT", sl0))
                    wait("scalar", smm, att0_tok)
                    act(s_pT[:, sl0, :], p_att0[:, 0:512], AF.Exp,
                        scale=0.125, ctr=sex)
                    settok(("att0",), sex)
                    guard("scalar", ("pT", sl1))
                    wait("scalar", smm, att1_tok)
                    act(s_pT[:, sl1, :], p_att1[:, 0:512], AF.Exp,
                        scale=0.125, ctr=sex)
                    settok(("att1",), sex)
                    wait("tensor", sex)
                    first = (sch == 0)
                    last = (sch == NT - 1)
                    mm(p_o0[0:64, :], s_vx[:, sch, 0:64], s_pT[:, sl0, :],
                       tp=(0, 0), start=first, stop=last)
                    mm(p_o1[64:128, :], s_vx[:, sch, 64:128], s_pT[:, sl1, :],
                       tp=(0, 64), start=first, stop=last)
                    mm(p_rs0[0:64, :], c_ones[:], s_pT[:, sl0, :],
                       tp=(0, 0), start=first, stop=last)
                    mm(p_rs1[64:128, :], c_ones[:], s_pT[:, sl1, :],
                       tp=(0, 64), start=first, stop=last,
                       ctr=smm if last else None)
                    pend = [("pT", sl0), ("pT", sl1)]
                o_tok = smm.v
                for ps_ in pend:
                    tok[ps_] = (smm, o_tok)
                pend = []
                wait("vector", smm, o_tok)
                guard("vector", ("rstmp",))
                vec("tensor_copy", s_rstmp[0:64, :], p_rs0[0:64, :])
                vec("tensor_copy", s_rstmp[64:128, :], p_rs1[64:128, :])
                vec("reciprocal", s_rstmp[:], s_rstmp[:], ctr=sve)
                rs_tok = sve.v
                slA = inv_slot[0]
                slB = slA + 1
                inv_slot[0] += 2
                wait("sync", sve, rs_tok)
                dma("sync", d_inv[slA:slA + 1, :], s_rstmp[0:1, :], sq)
                dma("sync", d_inv[slB:slB + 1, :], s_rstmp[64:65, :], sq)
                settok(("rstmp",), sq)
                wait("sync", sq, sq.v)
                guard("sync", ("invb",))
                dma("sync", s_invb[0:64, :],
                    d_inv[slA:slA + 1, :].to_broadcast([64, 512]), sq)
                dma("sync", s_invb[64:128, :],
                    d_inv[slB:slB + 1, :].to_broadcast([64, 512]), sq)
                inv_ld = sq.v
                wait("vector", sq, inv_ld)
                vec("tensor_mul", s_oT[0:64, p, 512 * tc:512 * (tc + 1)],
                    p_o0[0:64, :], s_invb[0:64, :])
                vec("tensor_mul", s_oT[64:128, p, 512 * tc:512 * (tc + 1)],
                    p_o1[64:128, :], s_invb[64:128, :], ctr=sve)
                settok(("invb",), sve)
                settok(("o_rs",), sve)
                guard("tensor", ("o_rs",))
            settok(("vxall",), smm)
        sve_att = sve.v
        if DEBUG:
            wait("sync", sve, sve_att)
            dma("sync", dbg["ot"][:], s_oT[:], sq)

        mark()
        ow16v = d_ow16.rearrange("(a p) f -> p a f", p=128)
        owbuf = w_w1[:]
        wait("vector", sve, sve_att)
        for fh in range(2):
            guard("sync", ("w1",))
            guard("sync", ("owb",))
            for p in range(NP):
                dma("sync", owbuf[:, p, :],
                    ow16v[:, p, 512 * fh:512 * (fh + 1)], sq)
            ow_ld = sq.v
            wait("tensor", sq, ow_ld)
            wait("tensor", sve, sve_att)
            for tg in range(8):
                for ti in range(2):
                    tt = 2 * tg + ti
                    pp = (p_bg0, p_bg1)[ti]
                    pk = (("psA",), ("psB",))[ti]
                    guard("tensor", pk)
                    for p in range(NP):
                        mm(pp[:, 0:512], s_oT[:, p, 128 * tt:128 * (tt + 1)],
                           owbuf[:, p, :], start=(p == 0), stop=(p == NP - 1),
                           ctr=smm if p == NP - 1 else None)
                    wait("vector", smm)
                    vec("tensor_add", satt(tt)[:, 512 * fh:512 * (fh + 1)],
                        pp[:, 0:512], b_owb[:, 512 * fh:512 * (fh + 1)],
                        ctr=sve)
                    settok(pk, sve)
            settok(("owb",), smm)
        sve_op = sve.v
        if DEBUG:
            wait("sync", sve, sve_op)
            for tt in range(NT):
                dma("sync", dbg["satt"][128 * tt:128 * (tt + 1), :],
                    satt(tt), sq)

        mark()
        wait("gpsimd", sems["c_y"])
        wait("vector", sve, sve_op)
        for tt in range(NT):
            for k in range(EK):
                sl = k % 2
                guard("gpsimd", ("y6", sl))
                igat(s_y6[:, sl, :], d_yall[:], s_gidx[:, tt, k:k + 1], gq)
                y6_ld = gq.v
                wait("vector", gq, y6_ld)
                guard("vector", ("moe_scr",)) if k == 0 else None
                if k == 0:
                    vec("tensor_scalar_mul", s_moe[:], s_y6[:, sl, :],
                        s_pw[:, tt, 0:1], ctr=sve)
                else:
                    vec("scalar_tensor_tensor", s_moe[:], s_y6[:, sl, :],
                        s_pw[:, tt, k:k + 1], s_moe[:], ALU.mult, ALU.add,
                        ctr=sve)
                settok(("y6", sl), sve)
            guard("sync", ("sh2ld", tt % 2))
            wait("sync", sq, sq_sh2)
            dma("sync", s_sh2st[:, tt % 2, :],
                d_sh2[128 * tt:128 * (tt + 1), :], sq)
            sh2_ld = sq.v
            wait("vector", sq, sh2_ld)
            vec("tensor_add", s_moe[:], s_moe[:], s_sh2st[:, tt % 2, :],
                ctr=sve)
            settok(("sh2ld", tt % 2), sve)
            wait("scalar", sve)
            guard("scalar", ("gst", tt % 2))
            act(s_xn_st[:, tt % 2, :], s_moe[:], AF.Silu, ctr=ssc)
            settok(("moe_scr",), ssc)
            guard("sync", ("xt2", tt % 2))
            dma("sync", s_xt[:, tt % 2, :], x_h[128 * tt:128 * (tt + 1), :],
                sq)
            x_ld = sq.v
            wait("vector", ssc)
            wait("vector", sq, x_ld)
            vec("tensor_mul", s_moe[:], satt(tt), s_xn_st[:, tt % 2, :],
                ctr=sve)
            settok(("gst", tt % 2), sve)
            vec("tensor_add", s_xt[:, tt % 2, :], s_xt[:, tt % 2, :],
                s_moe[:], ctr=sve)
            wait("sync", sve)
            dma("sync", out_h[128 * tt:128 * (tt + 1), :], s_xt[:, tt % 2, :],
                sq)
            settok(("xt2", tt % 2), sq)

        mark()
        wait("sync", sq, sq.v)
        wait("gpsimd", gq, gq.v)

        _flush("vector")
        _flush("scalar")
        mark()
        if PHASE_LIMIT is not None:
            cut = marks[PHASE_LIMIT]
            for k in Q:
                Q[k] = Q[k][:cut[k]]

        with nc.Block() as block:
            @block.sync
            def _(e):
                for fn in Q["sync"]:
                    fn(e)

            @block.gpsimd
            def _(e):
                for fn in Q["gpsimd"]:
                    fn(e)

            @block.tensor
            def _(e):
                for fn in Q["tensor"]:
                    fn(e)

            @block.scalar
            def _(e):
                for fn in Q["scalar"]:
                    fn(e)

            @block.vector
            def _(e):
                for fn in Q["vector"]:
                    fn(e)

    return nc, dbg


_NC_CACHE = {}


def _get_nc():
    if "nc" not in _NC_CACHE:
        _NC_CACHE["nc"] = build_nc()
    return _NC_CACHE["nc"]


def kernel(L0_attn_x, L0_attn_w_att_val, L0_attn_wv_w, L0_attn_wv_b,
           L0_attn_out_w, L0_attn_out_b, L0_ffn_router_weight,
           L0_ffn_expert_bias, L0_ffn_w1, L0_ffn_w2, L0_ffn_sw1, L0_ffn_sw2):
    x = np.ascontiguousarray(np.asarray(L0_attn_x, dtype=np.float32))
    watt = np.asarray(L0_attn_w_att_val, dtype=np.float32)
    wv = np.asarray(L0_attn_wv_w, dtype=np.float32)
    wvb = np.asarray(L0_attn_wv_b, dtype=np.float32)
    ow = np.asarray(L0_attn_out_w, dtype=np.float32)
    owb = np.asarray(L0_attn_out_b, dtype=np.float32)
    rw = np.asarray(L0_ffn_router_weight, dtype=np.float32)
    eb = np.asarray(L0_ffn_expert_bias, dtype=np.float32)
    w1 = np.asarray(L0_ffn_w1, dtype=np.float32)
    w2 = np.asarray(L0_ffn_w2, dtype=np.float32)
    sw1 = np.asarray(L0_ffn_sw1, dtype=np.float32)
    sw2 = np.asarray(L0_ffn_sw2, dtype=np.float32)

    wvT = np.ascontiguousarray(wv.T)
    owT = np.ascontiguousarray(ow.T)
    sw1T = np.ascontiguousarray(sw1.T)
    sw2T = np.ascontiguousarray(sw2.T)
    rwT = np.ascontiguousarray(rw.T)
    wattp = np.zeros((NP, 128, 64), np.float32)
    for p in range(NP):
        wattp[p, 0:64] = watt[2 * p]
        wattp[p, 64:128] = watt[2 * p + 1]
    u128 = np.triu(np.ones((128, 128), np.float32), 1)
    uc = np.zeros((64, 64), np.float32)
    for tt in range(16):
        uc[0, tt] = 64.0
        uc[1, tt] = 1.0
        for i in range(tt):
            uc[2 + i, tt] = 1.0
    iota = np.tile(np.arange(NE, dtype=np.float32), (128, 1))
    onesc = np.ones((128, 64), np.float32)

    nc, dbg = _get_nc()
    in_maps = []
    for c in range(8):
        in_maps.append(dict(
            x=np.ascontiguousarray(x[c]),
            wvT=wvT, owT=owT, sw1T=sw1T, sw2T=sw2T, rwT=rwT,
            watt=wattp,
            wvb=wvb.reshape(1, D), owb=owb.reshape(1, D),
            ebias=eb.reshape(1, NE),
            w1=np.ascontiguousarray(w1[ELOC * c:ELOC * (c + 1)]),
            w2=np.ascontiguousarray(w2[ELOC * c:ELOC * (c + 1)]),
            tokid=np.arange(T * c, T * (c + 1), dtype=np.float32).reshape(T, 1),
            pm=(np.arange(8, dtype=np.float32) < c).astype(np.float32)
            .reshape(1, 8),
            u128=u128, uc=uc, iota=iota, onesc=onesc,
        ))
    res = run_bass_kernel_spmd(nc, in_maps, core_ids=list(range(8)))
    out = np.stack([np.asarray(res.results[c]["out"]) for c in range(8)],
                   axis=0)
    if DEBUG:
        kernel.last_results = res.results
    return out


# version 03
# speedup vs baseline: 1.1394x; 1.1394x over previous
from contextlib import ExitStack

import numpy as np

import concourse.bass as bass
import concourse.mybir as mybir
from concourse.bass_utils import run_bass_kernel_spmd
from concourse.bass_core_rust import IndirectOffsetOnAxis

F32 = mybir.dt.float32
BF16 = mybir.dt.bfloat16
I32 = mybir.dt.int32
U32 = mybir.dt.uint32
AF = mybir.ActivationFunctionType
ALU = mybir.AluOpType
AX = mybir.AxisListType
RG8 = [[0, 1, 2, 3, 4, 5, 6, 7]]

T = 2048
D = 1024
NT = 16
NJ = 8
NP = 8
NE = 64
EK = 6
CAP = 384
EF = 512
SF = 2048
ELOC = 8
NU = 8
UT = 256

DEBUG = False
PHASE_LIMIT = None


class Ctr:
    __slots__ = ("h", "v", "is_dma", "dedicated")

    def __init__(self, h, is_dma=False, dedicated=False):
        self.h = h
        self.v = 0
        self.is_dma = is_dma
        self.dedicated = dedicated


def build_nc():
    nc = bass.Bass("TRN2", target_bir_lowering=False)

    def inp(name, shape, dt=F32):
        return nc.dram_tensor(name, shape, dt, kind="ExternalInput").ap()

    x_h = inp("x", [T, D])
    wvT_h = inp("wvT", [D, D])
    owT_h = inp("owT", [D, D])
    sw1T_h = inp("sw1T", [D, SF])
    sw2T_h = inp("sw2T", [SF, D])
    rwT_h = inp("rwT", [D, NE])
    watt_h = inp("watt", [NP, 128, 64])
    wvb_h = inp("wvb", [1, D])
    owb_h = inp("owb", [1, D])
    ebias_h = inp("ebias", [1, NE])
    w1_h = inp("w1", [ELOC, D, EF])
    w2_h = inp("w2", [ELOC, EF, D])
    tokid_h = inp("tokid", [T, 1])
    pm_h = inp("pm", [1, 8])
    u128_h = inp("u128", [128, 128])
    uc_h = inp("uc", [64, 64])
    iota_h = inp("iota", [128, NE])
    onesc_h = inp("onesc", [128, 64])

    out_h = nc.dram_tensor("out", [T, D], F32, kind="ExternalOutput").ap()

    def idram(name, shape, dt=F32):
        return nc.dram_tensor(name, shape, dt, kind="Internal").ap()

    d_xn = idram("d_xn", [T, D], BF16)
    d_xnall = idram("d_xnall", [8 * T, D], BF16)
    d_cnt = idram("d_cnt", [1, NE])
    d_cs = idram("d_cs", [8, NE])
    d_ss = idram("d_ss", [16, NE])
    d_choff = idram("d_choff", [NT, NE])
    d_tokmap = idram("d_tokmap", [25088, 1])
    d_tokmy = idram("d_tokmy", [3136, 1])
    d_xs = idram("d_xs", [ELOC * CAP, D], BF16)
    d_ymine = idram("d_ymine", [ELOC * CAP, D], BF16)
    d_yall = idram("d_yall", [NE * CAP, D], BF16)
    d_sh2 = idram("d_sh2", [T, D], BF16)
    d_inv = idram("d_inv", [64, 512])
    d_wv16 = idram("d_wv16", [D, D], BF16)
    d_sw1b = idram("d_sw1b", [D, SF], BF16)
    d_ow16 = idram("d_ow16", [D, D], BF16)
    d_w116 = idram("d_w116", [ELOC, D, EF], BF16)
    d_w216 = idram("d_w216", [ELOC, EF, D], BF16)

    dbg = {}

    def dbgout(name, shape, dt):
        if DEBUG:
            dbg[name] = nc.dram_tensor("dbg_" + name, shape, dt,
                                       kind="ExternalOutput").ap()

    dbgout("xn", [T, D], BF16)
    dbgout("cnt", [1, NE], F32)
    dbgout("choff", [NT, NE], F32)
    dbgout("if6", [128, NT, 8], F32)
    dbgout("pw", [128, NT, 8], F32)
    dbgout("gidx", [128, NT, 8], I32)
    dbgout("tokmy", [3136, 1], F32)
    dbgout("y", [ELOC * CAP, D], BF16)
    dbgout("sh2", [T, D], BF16)
    dbgout("ot", [128, NP, T], BF16)
    dbgout("satt", [T, D], BF16)

    st = ExitStack()
    with st:
        def sb(name, shape, dt):
            return st.enter_context(nc.sbuf_tensor(name, shape, dt))

        def psum(name, shape):
            return st.enter_context(nc.psum_tensor(name, shape, F32))

        c_iota = sb("c_iota", [128, NE], F32)
        c_ones = sb("c_ones", [128, 64], BF16)
        c_u128 = sb("c_u128", [128, 128], BF16)
        c_uc = sb("c_uc", [64, 64], BF16)
        w_watt = sb("w_watt", [128, NP, 64], BF16)
        w_rwT = sb("w_rwT", [128, NJ, NE], BF16)
        w_sw2r = sb("w_sw2r", [128, 16, D], BF16)
        w_sw1s = sb("w_sw1s", [128, 2, NJ, 128], BF16)
        w_vx = sb("w_vx", [128, 2, NJ, 128], BF16)
        w_ow = sb("w_ow", [128, 2, 512], BF16)
        w_w1 = sb("w_w1", [128, NJ, EF], BF16)
        w_w2 = sb("w_w2", [128, 2, D], BF16)
        b_wvb = sb("b_wvb", [128, D], F32)
        b_owb = sb("b_owb", [128, D], F32)
        b_eb = sb("b_eb", [128, NE], F32)
        s_pm = sb("s_pm", [1, 8], F32)
        s_tokid = sb("s_tokid", [128, NT], F32)

        s_xnT = sb("s_xnT", [128, NJ, T], BF16)
        s_oT = sb("s_oT", [128, NP, T], BF16)
        s_vx = sb("s_vx", [128, NT, 128], BF16)
        s_xw = sb("s_xw", [128, 4, 512], BF16)
        s_pT = sb("s_pT", [128, 4, 512], BF16)
        s_hraw = sb("s_hraw", [128, 16, UT], BF16)
        s_xn_st = sb("s_xn_st", [128, 2, D], BF16)
        s_sh2st = sb("s_sh2st", [128, 2, D], BF16)
        s_A = sb("s_A", [128, NT, NE], BF16)
        s_xsT = sb("s_xsT", [128, NJ, CAP], BF16)
        s_h1r = sb("s_h1r", [128, 8, CAP], BF16)
        s_yst = sb("s_yst", [128, 2, D], BF16)
        s_y6 = sb("s_y6", [128, 2, D], BF16)

        s_xt = sb("s_xt", [128, 2, D], F32)
        s_moe = sb("s_moe", [128, D], F32)
        s_lns = sb("s_lns", [128, 8], F32)
        s_scores = sb("s_scores", [128, NE], F32)
        s_biased = sb("s_biased", [128, NE], F32)
        s_m8 = sb("s_m8", [128, 8], F32)
        s_i8 = sb("s_i8", [128, 8], U32)
        s_if6a = sb("s_if6a", [128, NT, 8], F32)
        s_Af = sb("s_Af", [128, NE], F32)
        s_Af2 = sb("s_Af2", [128, NE], F32)
        s_scr64 = sb("s_scr64", [128, NE], F32)
        s_sc6 = sb("s_sc6", [128, 8], F32)
        s_pw = sb("s_pw", [128, NT, 8], F32)
        s_r6 = sb("s_r6", [128, 8], F32)
        s_v6 = sb("s_v6", [128, 8], F32)
        s_t8a = sb("s_t8a", [128, 8], F32)
        s_t8b = sb("s_t8b", [128, 8], F32)
        s_sidx = sb("s_sidx", [128, NT, 8], I32)
        s_gidx = sb("s_gidx", [128, NT, 8], I32)
        s_r6i = sb("s_r6i", [128, 8], I32)
        s_cnt = sb("s_cnt", [1, NE], F32)
        s_csf = sb("s_csf", [1, NE], F32)
        s_co = sb("s_co", [1, NE], F32)
        s_co2 = sb("s_co2", [1, NE], F32)
        s_coi = sb("s_coi", [1, NE], I32)
        s_cq = sb("s_cq", [1, NE], I32)
        s_cqf = sb("s_cqf", [1, NE], F32)
        s_crf = sb("s_crf", [1, NE], F32)
        s_stage = sb("s_stage", [64, 64], BF16)
        s_choff = sb("s_choff", [64, NE], F32)
        s_offb = sb("s_offb", [128, 2, NE], F32)
        s_C = sb("s_C", [128, NE], F32)
        s_Ct = sb("s_Ct", [128, NE], F32)
        s_rstmp = sb("s_rstmp", [128, 512], F32)
        s_invb = sb("s_invb", [128, 512], F32)
        s_tmf = sb("s_tmf", [128, 24], F32)
        s_tmidx = sb("s_tmidx", [128, 24], I32)
        s_zero = sb("s_zero", [128, 256], F32)
        s_junk = sb("s_junk", [128, 32], F32)

        p_att0 = psum("p_att0", [128, 512])
        p_att1 = psum("p_att1", [128, 512])
        p_o0 = psum("p_o0", [128, 512])
        p_o1 = psum("p_o1", [128, 512])
        p_rs0 = psum("p_rs0", [128, 512])
        p_rs1 = psum("p_rs1", [128, 512])
        p_bg0 = psum("p_bg0", [128, 512])
        p_bg1 = psum("p_bg1", [128, 512])

        names = "gq sq mm ex ve sc c_xn c_cnt c_tok c_y vsy ssy".split()
        sems = {nm: Ctr(st.enter_context(nc.semaphore("sem_" + nm)),
                        is_dma=nm in ("gq", "sq"))
                for nm in names}
        dnames = ("dw_sw1a dw_sw1b dw_vxa dw_vxb dw_w1 dw_w2 dw_xsT dw_ow "
                  "dw_xa dw_xb dw_inva dw_invb dw_y6a dw_y6b dw_sh2a dw_sh2b "
                  "dw_gath").split()
        for nm in dnames:
            sems[nm] = Ctr(st.enter_context(nc.semaphore("sem_" + nm)),
                           is_dma=True, dedicated=True)
        gq, sq = sems["gq"], sems["sq"]
        smm, sex, sve, ssc = sems["mm"], sems["ex"], sems["ve"], sems["sc"]

        Q = {k: [] for k in ("sync", "gpsimd", "tensor", "scalar", "vector")}
        marks = []

        def mark():
            _flush("vector")
            _flush("scalar")
            marks.append({k: len(v) for k, v in Q.items()})
        tok = {}

        def q(eng, fn):
            Q[eng].append(fn)

        _fence = {}

        def wait(eng, ctr, val=None):
            if ctr.dedicated:
                v = ctr.v if val is None else val
            else:
                v = ctr.v if (val is None or ctr.is_dma) else val
            h = ctr.h
            if ctr.is_dma and not ctr.dedicated:
                owner = "sync" if ctr is sems["sq"] else "gpsimd"
                if _fence.get(id(ctr), -1) < v:
                    _fence[id(ctr)] = v
                    if eng != owner:
                        q(owner, lambda e: e.wait_ge(h, v))
            fn = lambda e: e.wait_ge(h, v)
            if eng in ("vector", "scalar") and _stage[eng] is not None:
                _swaits[eng].append(fn)
            else:
                q(eng, fn)

        def guard(eng, key):
            if key in tok:
                ctr, v = tok[key]
                wait(eng, ctr, v)

        def settok(key, ctr):
            tok[key] = (ctr, ctr.v)

        def dma(eng, dst, src, ctr):
            ctr.v += 16
            h = ctr.h
            q(eng, lambda e: e.dma_start(dst, src).then_inc(h, 16))

        def dmaT(eng, dst, src, ctr):
            ctr.v += 16
            h = ctr.h
            q(eng, lambda e: e.dma_start_transpose(dst, src).then_inc(h, 16))

        def dmaslow(eng, dst, src, ctr):
            ctr.v += 16
            h = ctr.h
            q(eng, lambda e: e.dma_start(
                dst, src, allow_slow_non_contiguous=True).then_inc(h, 16))

        def mm(out, lhsT, rhs, start=True, stop=True, tp=None, ctr=None):
            if ctr is not None:
                ctr.v += 1
                h = ctr.h
                q("tensor", lambda e: e.matmul(
                    out, lhsT, rhs, start=start, stop=stop,
                    tile_position=tp).then_inc(h, 1))
            else:
                q("tensor", lambda e: e.matmul(out, lhsT, rhs, start=start,
                                               stop=stop, tile_position=tp))

        _stage = {"scalar": None, "vector": None}
        _swaits = {"scalar": [], "vector": []}
        _prevt = {"scalar": set(), "vector": set()}

        def _names(objs):
            out = set()
            for o in objs:
                t = getattr(o, "tensor", None)
                if t is not None:
                    out.add(t.name)
            return out

        def _flush(eng, attach=False):
            if _stage[eng] is None:
                return
            b, touched, has_ctr = _stage[eng]
            if attach:
                sc_ = sems["vsy"] if eng == "vector" else sems["ssy"]
                sc_.v += 1
                h, v = sc_.h, sc_.v

                def _att(e, b=b, h=h, has_ctr=has_ctr, eng=eng):
                    r = b(e)
                    if r is None or has_ctr:
                        if eng == "vector":
                            e.engine_nop()
                            r = e.engine_nop()
                        else:
                            e.activation(s_junk[:, 0:8], s_junk[:, 0:8],
                                         AF.Identity)
                            r = e.activation(s_junk[:, 8:16], s_junk[:, 8:16],
                                             AF.Identity)
                    r.then_inc(h, 1)
                Q[eng].append(_att)
                for w in _swaits[eng]:
                    Q[eng].append(w)
                Q[eng].append(lambda e, h=h, v=v: e.wait_ge(h, v))
            else:
                Q[eng].append(lambda e, b=b: b(e))
                for w in _swaits[eng]:
                    Q[eng].append(w)
            _swaits[eng] = []
            _prevt[eng] = set() if attach else touched
            _stage[eng] = None

        def _emit_eng(eng, builder, touched, has_ctr=False):
            if _stage[eng] is not None:
                conflict = bool((_stage[eng][1] | _prevt[eng]) & touched)
                _flush(eng, attach=conflict)
            else:
                if _prevt[eng] & touched:
                    sc_ = sems["vsy"] if eng == "vector" else sems["ssy"]
                    sc_.v += 1
                    h, v = sc_.h, sc_.v

                    def _np(e, h=h, eng=eng):
                        if eng == "vector":
                            e.engine_nop()
                            e.engine_nop().then_inc(h, 1)
                        else:
                            e.activation(s_junk[:, 0:8], s_junk[:, 0:8],
                                         AF.Identity)
                            e.activation(s_junk[:, 8:16], s_junk[:, 8:16],
                                         AF.Identity).then_inc(h, 1)
                    Q[eng].append(_np)
                    Q[eng].append(lambda e, h=h, v=v: e.wait_ge(h, v))
                    _prevt[eng] = set()
            _stage[eng] = (builder, touched, has_ctr)

        def act(out, in_, func, ctr=None, **kw):
            touched = _names([out, in_, kw.get("accum_out"), kw.get("bias"),
                              kw.get("scale")])
            if ctr is not None:
                ctr.v += 1
                h = ctr.h
                b = lambda e: e.activation(out, in_, func, **kw).then_inc(h, 1)
                _emit_eng("scalar", b, touched, has_ctr=True)
            else:
                b = lambda e: e.activation(out, in_, func, **kw)
                _emit_eng("scalar", b, touched)

        def vec(meth, *a, ctr=None, **kw):
            touched = _names(list(a) + list(kw.values()))
            if ctr is not None:
                ctr.v += 1
                h = ctr.h
                b = lambda e: getattr(e, meth)(*a, **kw).then_inc(h, 1)
                _emit_eng("vector", b, touched, has_ctr=True)
            else:
                b = lambda e: getattr(e, meth)(*a, **kw)
                _emit_eng("vector", b, touched)

        def coll(kind, op, ins_, outs_, ctr):
            ctr.v += 1
            h = ctr.h
            q("gpsimd", lambda e: e.collective_compute(
                kind, op, replica_groups=RG8, ins=ins_, outs=outs_
            ).then_inc(h, 1))

        _bc_reg = {}

        def iscat(out_ap, idx_ap, in_ap, ctr, bc):
            ctr.v += 16
            h = ctr.h

            def fn(e):
                if bc not in _bc_reg:
                    _bc_reg[bc] = e.to_reg(bc)
                e.indirect_dma_start(
                    out_ap, IndirectOffsetOnAxis(idx_ap, 0), in_ap, None,
                    bounds_check=_bc_reg[bc], oob_is_err=False).then_inc(h, 16)
            q("gpsimd", fn)

        def igat(out_ap, in_ap, idx_ap, ctr):
            ctr.v += 16
            h = ctr.h
            q("gpsimd", lambda e: e.indirect_dma_start(
                out_ap, None, in_ap, IndirectOffsetOnAxis(idx_ap, 0)
            ).then_inc(h, 16))

        satt_flat = s_xnT[:].rearrange("p a t -> p (a t)")

        def satt(tt):
            return satt_flat[:, D * tt:D * (tt + 1)]

        vec("memset", s_junk[:], 0.0)
        vec("memset", s_zero[:], 0.0)
        vec("memset", s_stage[:], 0.0)
        vec("memset", s_pT[:], 0.0, ctr=sve)
        zero_tok = sve.v
        wait("scalar", sve, zero_tok)
        act(s_junk[:, 0:16], s_junk[:, 0:16], AF.Identity)
        act(s_junk[:, 16:32], s_junk[:, 16:32], AF.Identity)

        dma("gpsimd", c_iota[:], iota_h[:], gq)
        dma("gpsimd", c_ones[:], onesc_h[:], gq)
        dma("gpsimd", c_u128[:], u128_h[:], gq)
        dma("gpsimd", c_uc[:], uc_h[:], gq)
        dma("gpsimd", w_watt[:], watt_h.rearrange("a p f -> p a f"), gq)
        dma("gpsimd", w_rwT[:], rwT_h.rearrange("(a p) f -> p a f", p=128), gq)
        for jj in range(16):
            dma("gpsimd", w_sw2r[:, jj, :],
                sw2T_h[128 * jj:128 * (jj + 1), :], gq)
        gq_const = gq.v

        gq_cast = gq.v

        dma("sync", b_wvb[:], wvb_h[0:1, :].to_broadcast([128, D]), sq)
        dma("sync", b_owb[:], owb_h[0:1, :].to_broadcast([128, D]), sq)
        dma("sync", b_eb[:], ebias_h[0:1, :].to_broadcast([128, NE]), sq)
        dma("sync", s_pm[:], pm_h[:], sq)
        dmaslow("sync", s_tokid[:],
                tokid_h.rearrange("(a p) o -> p (a o)", p=128), sq)
        sq_const = sq.v

        for tt in range(NT):
            sl = tt % 2
            xsem = sems["dw_xa"] if sl == 0 else sems["dw_xb"]
            guard("sync", ("xt", sl))
            dma("sync", s_xt[:, sl, :], x_h[128 * tt:128 * (tt + 1), :], xsem)
            wait("vector", xsem)
            vec("reduce_sum", s_lns[:, 0:1], s_xt[:, sl, :], AX.X)
            wait("scalar", xsem)
            guard("scalar", ("moe_scr",))
            act(s_moe[:], s_xt[:, sl, :], AF.Square, ctr=ssc,
                accum_out=s_lns[:, 1:2])
            settok(("moe_scr",), ssc)
            wait("vector", ssc)
            vec("tensor_scalar_mul", s_lns[:, 2:3], s_lns[:, 0:1], 1.0 / D)
            vec("tensor_scalar_mul", s_lns[:, 3:4], s_lns[:, 1:2], 1.0 / D)
            vec("tensor_mul", s_lns[:, 4:5], s_lns[:, 2:3], s_lns[:, 2:3])
            vec("tensor_sub", s_lns[:, 5:6], s_lns[:, 3:4], s_lns[:, 4:5],
                ctr=sve)
            vec("tensor_scalar_add", s_lns[:, 5:6], s_lns[:, 5:6], 1e-5,
                ctr=sve)
            wait("scalar", sve)
            act(s_lns[:, 6:7], s_lns[:, 5:6], AF.Sqrt, ctr=ssc)
            wait("vector", ssc)
            vec("reciprocal", s_lns[:, 7:8], s_lns[:, 6:7])
            guard("vector", ("xnst", sl))
            vec("scalar_tensor_tensor", s_xn_st[:, sl, :], s_xt[:, sl, :],
                s_lns[:, 2:3], s_lns[:, 7:8].to_broadcast([128, D]),
                ALU.subtract, ALU.mult, ctr=sve)
            settok(("xt", sl), sve)
            wait("sync", sve)
            dma("sync", d_xn[128 * tt:128 * (tt + 1), :], s_xn_st[:, sl, :], sq)
            settok(("xnst", sl), sq)
        sq_xn = sq.v

        wait("gpsimd", sq, sq_xn)
        coll("AllGather", ALU.bypass, [d_xn[:]], [d_xnall[:]], sems["c_xn"])

        wait("sync", sq, sq_xn)
        for jj in range(NJ):
            dmaT("sync", s_xnT[:, jj, :], d_xn[:, 128 * jj:128 * (jj + 1)], sq)
        sq_xnT = sq.v

        if DEBUG:
            dma("sync", dbg["xn"][:], d_xn[:], sq)

        mark()
        wait("tensor", sq, sq_xnT)
        wait("tensor", gq, gq_const)
        wait("vector", sq, sq_const)
        for tt in range(NT):
            guard("tensor", ("psA",))
            for j in range(NJ):
                mm(p_bg0[:, 0:NE], s_xnT[:, j, 128 * tt:128 * (tt + 1)],
                   w_rwT[:, j, :], start=(j == 0), stop=(j == NJ - 1),
                   ctr=smm if j == NJ - 1 else None)
            wait("scalar", smm)
            act(s_scores[:], p_bg0[:, 0:NE], AF.Sigmoid, ctr=ssc)
            settok(("psA",), ssc)
            wait("vector", ssc)
            vec("tensor_add", s_biased[:], s_scores[:], b_eb[:])
            vec("max", out=s_m8[:], in_=s_biased[:])
            vec("max_index", s_i8[:], s_m8[:], s_biased[:])
            vec("tensor_copy", s_if6a[:, tt, :], s_i8[:])
            vec("memset", s_Af[:], 0.0)
            cur, nxt = s_Af, s_Af2
            for k in range(EK):
                vec("scalar_tensor_tensor", nxt[:], c_iota[:],
                    s_if6a[:, tt, k:k + 1], cur[:], ALU.is_equal, ALU.add)
                cur, nxt = nxt, cur
                vec("scalar_tensor_tensor", s_scr64[:], c_iota[:],
                    s_if6a[:, tt, k:k + 1], s_scores[:], ALU.is_equal,
                    ALU.mult, accum_out=s_sc6[:, k:k + 1])
            vec("tensor_copy", s_A[:, tt, :], cur[:])
            vec("reduce_sum", s_t8a[:, 0:1], s_sc6[:, 0:EK], AX.X)
            vec("tensor_scalar_add", s_t8a[:, 1:2], s_t8a[:, 0:1], 1e-20)
            vec("reciprocal", s_t8a[:, 2:3], s_t8a[:, 1:2])
            vec("tensor_scalar_mul", s_pw[:, tt, 0:EK], s_sc6[:, 0:EK],
                s_t8a[:, 2:3], ctr=sve)
            wait("tensor", sve)
            guard("tensor", ("psB",))
            mm(p_bg1[0:64, 0:NE], c_ones[:], s_A[:, tt, :], ctr=smm)
            wait("vector", smm)
            guard("vector", ("cntst",))
            vec("tensor_copy", s_cnt[:], p_bg1[0:1, 0:NE], ctr=sve)
            settok(("psB",), sve)
            wait("sync", sve)
            dma("sync", d_ss[tt:tt + 1, :], s_cnt[:], sq)
            settok(("cntst",), sq)
        sq_ss = sq.v

        wait("gpsimd", sq, sq_ss)
        wait("gpsimd", sve, zero_tok)
        dma("gpsimd", s_stage[0:16, :], d_ss[:], gq)
        wait("tensor", gq)
        guard("tensor", ("psB",))
        mm(p_bg1[0:64, 0:NE], c_ones[0:64, :], s_stage[:], ctr=smm)
        wait("vector", smm)
        guard("vector", ("cntst",))
        vec("tensor_copy", s_cnt[:], p_bg1[0:1, 0:NE], ctr=sve)
        settok(("psB",), sve)
        wait("sync", sve)
        dma("sync", d_cnt[:], s_cnt[:], sq)
        settok(("cntst",), sq)
        wait("gpsimd", sq, sq.v)
        coll("AllGather", ALU.bypass, [d_cnt[:]], [d_cs[:]], sems["c_cnt"])

        mark()
        wait("vector", sems["c_cnt"])
        vec("memset", s_co[:], 0.0)
        cur, nxt = s_co, s_co2
        for cc in range(8):
            wait("sync", sems["c_cnt"])
            guard("sync", ("csf",))
            dma("sync", s_csf[:], d_cs[cc:cc + 1, :], sq)
            wait("vector", sq)
            vec("scalar_tensor_tensor", nxt[:], s_csf[:],
                s_pm[:, cc:cc + 1], cur[:], ALU.mult, ALU.add, ctr=sve)
            settok(("csf",), sve)
            cur, nxt = nxt, cur
        vec("tensor_copy", s_coi[:], cur[:])
        vec("tensor_scalar", s_cq[:], s_coi[:], 6, None, ALU.arith_shift_right)
        vec("tensor_scalar", s_coi[:], s_coi[:], 63, None, ALU.bitwise_and)
        vec("tensor_copy", s_cqf[:], s_cq[:])
        vec("tensor_copy", s_crf[:], s_coi[:], ctr=sve)
        sve_qr = sve.v
        wait("gpsimd", sve, sve_qr)
        dma("gpsimd", s_stage[0:1, :], s_cqf[:], gq)
        dma("gpsimd", s_stage[1:2, :], s_crf[:], gq)
        dma("gpsimd", s_stage[2:18, :], d_ss[:], gq)
        wait("tensor", gq)
        wait("tensor", sve, sve_qr)
        guard("tensor", ("psA",))
        mm(p_bg0[0:64, 0:NE], c_uc[:], s_stage[:], ctr=smm)
        wait("vector", smm)
        vec("tensor_copy", s_choff[:], p_bg0[0:64, 0:NE], ctr=sve)
        settok(("psA",), sve)
        wait("sync", sve)
        dma("sync", d_choff[:], s_choff[0:16, :], sq)
        sq_choff = sq.v
        if DEBUG:
            dma("sync", dbg["choff"][:], d_choff[:], sq)

        wait("gpsimd", sve, zero_tok)
        dma("gpsimd", d_tokmap[:].rearrange("(p b) o -> p (b o)", p=128),
            s_zero[:, 0:196], gq)
        gq_tmz = gq.v

        for tt in range(NT):
            osl = tt % 2
            guard("sync", ("offb", osl))
            wait("sync", sq, sq_choff)
            dma("sync", s_offb[:, osl, :],
                d_choff[tt:tt + 1, :].to_broadcast([128, NE]), sq)
            off_ld = sq.v
            guard("tensor", ("psB",))
            mm(p_bg1[:, 0:NE], c_u128[:], s_A[:, tt, :], ctr=smm)
            wait("vector", smm)
            wait("vector", sq, off_ld)
            vec("tensor_copy", s_C[:], p_bg1[:, 0:NE], ctr=sve)
            settok(("psB",), sve)
            vec("tensor_add", s_Ct[:], s_C[:], s_offb[:, osl, :], ctr=sve)
            settok(("offb", osl), sve)
            for k in range(EK):
                vec("scalar_tensor_tensor", s_scr64[:], c_iota[:],
                    s_if6a[:, tt, k:k + 1], s_Ct[:], ALU.is_equal, ALU.mult,
                    accum_out=s_r6[:, k:k + 1])
            E6 = slice(0, EK)
            vec("tensor_scalar", s_v6[:, E6], s_r6[:, E6], float(CAP), None,
                ALU.is_lt)
            vec("tensor_mul", s_pw[:, tt, E6], s_pw[:, tt, E6], s_v6[:, E6])
            vec("tensor_scalar", s_t8a[:, E6], s_if6a[:, tt, E6], float(CAP),
                None, ALU.mult)
            vec("tensor_add", s_t8a[:, E6], s_t8a[:, E6], s_r6[:, E6])
            vec("tensor_copy", s_r6i[:, E6], s_if6a[:, tt, E6])
            vec("tensor_scalar", s_r6i[:, E6], s_r6i[:, E6], 3, None,
                ALU.arith_shift_right)
            vec("tensor_copy", s_t8b[:, E6], s_r6i[:, E6])
            vec("tensor_scalar", s_t8b[:, E6], s_t8b[:, E6], 64.0, None,
                ALU.mult)
            vec("tensor_add", s_t8b[:, E6], s_t8b[:, E6], s_t8a[:, E6])
            vec("tensor_scalar", s_scr64[:, 0:EK], s_v6[:, E6], -2e6, 2e6,
                ALU.mult, ALU.add)
            vec("tensor_add", s_t8b[:, E6], s_t8b[:, E6], s_scr64[:, 0:EK])
            vec("tensor_copy", s_sidx[:, tt, E6], s_t8b[:, E6])
            vec("tensor_scalar_min", s_t8a[:, E6], s_t8a[:, E6],
                float(NE * CAP - 1))
            vec("tensor_copy", s_gidx[:, tt, E6], s_t8a[:, E6], ctr=sve)
        sve_route = sve.v

        wait("gpsimd", sve, sve_route)
        wait("gpsimd", gq, gq_tmz)
        for tt in range(NT):
            for k in range(EK):
                iscat(d_tokmap[:], s_sidx[:, tt, k:k + 1],
                      s_tokid[:, tt:tt + 1], gq, 25087)
        wait("gpsimd", gq)
        coll("ReduceScatter", ALU.add, [d_tokmap[:]], [d_tokmy[:]],
             sems["c_tok"])
        if DEBUG:
            wait("sync", sems["c_tok"])
            dma("sync", dbg["tokmy"][:], d_tokmy[:], sq)
            dma("sync", dbg["cnt"][:], d_cnt[:], sq)
            wait("sync", sve, sve_route)
            dma("sync", dbg["if6"][:], s_if6a[:], sq)
            dma("sync", dbg["pw"][:], s_pw[:], sq)
            dma("sync", dbg["gidx"][:], s_gidx[:], sq)

        mark()
        wait("tensor", gq, gq_cast)
        for u in range(NU):
            tsl = slice(UT * u, UT * (u + 1))
            for ft in range(16):
                sl = ft % 2
                dsem = sems["dw_sw1a"] if sl == 0 else sems["dw_sw1b"]
                guard("gpsimd", ("sw1", sl))
                dma("gpsimd", w_sw1s[:, sl, :, :],
                    sw1T_h.rearrange("(a p) f -> p a f", p=128)
                    [:, :, 128 * ft:128 * (ft + 1)], dsem)
                pp = p_bg0 if ft % 2 == 0 else p_bg1
                pk = ("psA",) if ft % 2 == 0 else ("psB",)
                guard("tensor", pk)
                wait("tensor", dsem)
                for j in range(NJ):
                    mm(pp[:, 0:UT], w_sw1s[:, sl, j, :], s_xnT[:, j, tsl],
                       start=(j == 0), stop=(j == NJ - 1),
                       ctr=smm if j == NJ - 1 else None)
                settok(("sw1", sl), smm)
                wait("vector", smm)
                vec("tensor_copy", s_hraw[:, ft, :], pp[:, 0:UT], ctr=sve)
                settok(pk, sve)
            wait("scalar", sve)
            act(s_hraw[:].rearrange("p a f -> p (a f)"),
                s_hraw[:].rearrange("p a f -> p (a f)"),
                AF.Gelu_apprx_tanh, ctr=ssc)
            wait("tensor", ssc)
            for tq in range(2):
                tt = 2 * u + tq
                for ih in range(2):
                    pp = p_bg0 if ih == 0 else p_bg1
                    pk = ("psA",) if ih == 0 else ("psB",)
                    guard("tensor", pk)
                    for ft in range(16):
                        mm(pp[:, 0:512],
                           s_hraw[:, ft, 128 * tq:128 * (tq + 1)],
                           w_sw2r[:, ft, 512 * ih:512 * (ih + 1)],
                           start=(ft == 0), stop=(ft == 15),
                           ctr=smm if ft == 15 else None)
                    wait("vector", smm)
                    guard("vector", ("sh2st", tt % 2))
                    vec("tensor_copy",
                        s_sh2st[:, tt % 2, 512 * ih:512 * (ih + 1)],
                        pp[:, 0:512], ctr=sve)
                    settok(pk, sve)
                wait("sync", sve)
                dma("sync", d_sh2[128 * tt:128 * (tt + 1), :],
                    s_sh2st[:, tt % 2, :], sq)
                settok(("sh2st", tt % 2), sq)
        sq_sh2 = sq.v
        if DEBUG:
            wait("sync", sq, sq_sh2)
            dma("sync", dbg["sh2"][:], d_sh2[:], sq)

        mark()
        wait("gpsimd", sems["c_tok"])
        wait("gpsimd", sems["c_xn"])
        dmaslow("gpsimd", s_tmf[:],
                d_tokmy[0:3072, :].rearrange("(j p) o -> p (j o)", p=128), gq)
        wait("vector", gq)
        vec("tensor_copy", s_tmidx[:], s_tmf[:], ctr=sve)
        wait("gpsimd", sve)
        for jj in range(24):
            sl = jj % 2
            gsem = sems["dw_y6a"] if sl == 0 else sems["dw_y6b"]
            guard("gpsimd", ("xsg", sl))
            igat(s_y6[:, sl, :], d_xnall[:], s_tmidx[:, jj:jj + 1], gsem)
            wait("gpsimd", gsem)
            dma("gpsimd", d_xs[128 * jj:128 * (jj + 1), :], s_y6[:, sl, :], gq)
            settok(("xsg", sl), gq)
        gq_xs = gq.v

        w2view = w_w1[:].rearrange("p a f -> p (a f)").rearrange(
            "p (b g) -> p b g", b=4)

        for eg in range(4):
            for el in range(2):
                e = 2 * eg + el
                guard("sync", ("xsT",))
                wait("sync", gq, gq_xs)
                for j in range(NJ):
                    dmaT("sync", s_xsT[:, j, :],
                         d_xs[CAP * e:CAP * (e + 1), 128 * j:128 * (j + 1)],
                         sems["dw_xsT"])
                wait("tensor", sems["dw_xsT"])
                guard("gpsimd", ("w1",))
                dma("gpsimd", w_w1[:],
                    w1_h[e].rearrange("(a p) f -> p a f", p=128),
                    sems["dw_w1"])
                wait("tensor", sems["dw_w1"])
                for ft in range(4):
                    pp = p_bg0 if ft % 2 == 0 else p_bg1
                    pk = ("psA",) if ft % 2 == 0 else ("psB",)
                    guard("tensor", pk)
                    for j in range(NJ):
                        mm(pp[:, 0:CAP], w_w1[:, j, 128 * ft:128 * (ft + 1)],
                           s_xsT[:, j, :], start=(j == 0), stop=(j == NJ - 1),
                           ctr=smm if j == NJ - 1 else None)
                    wait("vector", smm)
                    vec("tensor_copy", s_h1r[:, 4 * el + ft, :], pp[:, 0:CAP],
                        ctr=sve)
                    settok(pk, sve)
                settok(("xsT",), smm)
                settok(("w1",), smm)
            wait("scalar", sve)
            act(s_h1r[:].rearrange("p a f -> p (a f)"),
                s_h1r[:].rearrange("p a f -> p (a f)"),
                AF.Gelu_apprx_tanh, ctr=ssc)
            gelu_tok = ssc.v
            wait("tensor", ssc, gelu_tok)
            for el in range(2):
                e = 2 * eg + el
                guard("gpsimd", ("w1",))
                dma("gpsimd", w2view,
                    w2_h[e].rearrange("(a p) f -> p a f", p=128),
                    sems["dw_w2"])
                wait("tensor", sems["dw_w2"])
                for ct in range(3):
                    for ih in range(2):
                        pp = p_bg0 if ih == 0 else p_bg1
                        pk = ("psA",) if ih == 0 else ("psB",)
                        guard("tensor", pk)
                        for f in range(4):
                            mm(pp[:, 0:512],
                               s_h1r[:, 4 * el + f, 128 * ct:128 * (ct + 1)],
                               w2view[:, f, 512 * ih:512 * (ih + 1)],
                               start=(f == 0), stop=(f == 3),
                               ctr=smm if f == 3 else None)
                        wait("vector", smm)
                        guard("vector", ("yst", ct % 2))
                        vec("tensor_copy",
                            s_yst[:, ct % 2, 512 * ih:512 * (ih + 1)],
                            pp[:, 0:512], ctr=sve)
                        settok(pk, sve)
                    wait("sync", sve)
                    dma("sync",
                        d_ymine[CAP * e + 128 * ct:CAP * e + 128 * (ct + 1), :],
                        s_yst[:, ct % 2, :], sq)
                    settok(("yst", ct % 2), sq)
                settok(("w1",), smm)
        sq_y = sq.v
        wait("gpsimd", sq, sq_y)
        coll("AllGather", ALU.bypass, [d_ymine[:]], [d_yall[:]], sems["c_y"])
        if DEBUG:
            dma("sync", dbg["y"][:], d_ymine[:], sq)

        mark()
        wait("sync", gq, gq_cast)
        inv_slot = [0]
        for p in range(NP):
            guard("vector", ("vxall",))
            vsl = p % 2
            dsem = sems["dw_vxa"] if vsl == 0 else sems["dw_vxb"]
            guard("gpsimd", ("wvx", vsl))
            dma("gpsimd", w_vx[:, vsl, :, :],
                wvT_h.rearrange("(a p) f -> p a f", p=128)
                [:, :, 128 * p:128 * (p + 1)], dsem)
            wait("tensor", dsem)
            for tt in range(NT):
                guard("tensor", ("psA",))
                for j in range(NJ):
                    mm(p_bg0[:, 0:128], s_xnT[:, j, 128 * tt:128 * (tt + 1)],
                       w_vx[:, vsl, j, :], start=(j == 0), stop=(j == NJ - 1),
                       ctr=smm if j == NJ - 1 else None)
                wait("vector", smm)
                guard("vector", ("vx", tt))
                vec("tensor_add", s_vx[:, tt, :], p_bg0[:, 0:128],
                    b_wvb[:, 128 * p:128 * (p + 1)], ctr=sve)
                settok(("psA",), sve)
                settok(("vx", tt), sve)
            settok(("wvx", vsl), smm)
            for tc in range(4):
                guard("tensor", ("psA",))
                mm(p_bg0[0:64, 0:512], w_watt[0:64, p, :],
                   s_xnT[0:64, p, 512 * tc:512 * (tc + 1)], tp=(0, 0),
                   start=True, stop=True)
                guard("tensor", ("psB",))
                mm(p_bg1[64:128, 0:512], w_watt[64:128, p, :],
                   s_xnT[64:128, p, 512 * tc:512 * (tc + 1)], tp=(64, 64),
                   start=True, stop=True, ctr=smm)
                wait("vector", smm)
                guard("vector", ("xw", tc))
                vec("tensor_copy", s_xw[0:64, tc, :], p_bg0[0:64, 0:512])
                vec("tensor_copy", s_xw[64:128, tc, :], p_bg1[64:128, 0:512],
                    ctr=sve)
                settok(("psA",), sve)
                settok(("psB",), sve)
                settok(("xw", tc), sve)
            wait("tensor", sve)

            for tc in range(4):
                pend = []
                for sch in range(NT):
                    sl0 = (sch % 2) * 2
                    sl1 = sl0 + 1
                    guard("tensor", ("att0",))
                    mm(p_att0[:, 0:512],
                       s_xnT[0:64, p, 128 * sch:128 * (sch + 1)],
                       s_xw[0:64, tc, :], tp=(0, 0), ctr=smm)
                    att0_tok = smm.v
                    guard("tensor", ("att1",))
                    mm(p_att1[:, 0:512],
                       s_xnT[64:128, p, 128 * sch:128 * (sch + 1)],
                       s_xw[64:128, tc, :], tp=(64, 0), ctr=smm)
                    att1_tok = smm.v
                    for ps_ in pend:
                        tok[ps_] = (smm, att1_tok)
                    pend = []
                    guard("scalar", ("pT", sl0))
                    wait("scalar", smm, att0_tok)
                    act(s_pT[:, sl0, :], p_att0[:, 0:512], AF.Exp,
                        scale=0.125, ctr=sex)
                    settok(("att0",), sex)
                    guard("scalar", ("pT", sl1))
                    wait("scalar", smm, att1_tok)
                    act(s_pT[:, sl1, :], p_att1[:, 0:512], AF.Exp,
                        scale=0.125, ctr=sex)
                    settok(("att1",), sex)
                    wait("tensor", sex)
                    first = (sch == 0)
                    last = (sch == NT - 1)
                    mm(p_o0[0:64, :], s_vx[:, sch, 0:64], s_pT[:, sl0, :],
                       tp=(0, 0), start=first, stop=last)
                    mm(p_o1[64:128, :], s_vx[:, sch, 64:128], s_pT[:, sl1, :],
                       tp=(0, 64), start=first, stop=last)
                    mm(p_rs0[0:64, :], c_ones[:], s_pT[:, sl0, :],
                       tp=(0, 0), start=first, stop=last)
                    mm(p_rs1[64:128, :], c_ones[:], s_pT[:, sl1, :],
                       tp=(0, 64), start=first, stop=last,
                       ctr=smm if last else None)
                    pend = [("pT", sl0), ("pT", sl1)]
                o_tok = smm.v
                for ps_ in pend:
                    tok[ps_] = (smm, o_tok)
                pend = []
                wait("vector", smm, o_tok)
                guard("vector", ("rstmp",))
                vec("tensor_copy", s_rstmp[0:64, :], p_rs0[0:64, :])
                vec("tensor_copy", s_rstmp[64:128, :], p_rs1[64:128, :])
                vec("reciprocal", s_rstmp[:], s_rstmp[:], ctr=sve)
                rs_tok = sve.v
                slA = inv_slot[0]
                slB = slA + 1
                inv_slot[0] += 2
                wait("sync", sve, rs_tok)
                dma("sync", d_inv[slA:slA + 1, :], s_rstmp[0:1, :],
                    sems["dw_inva"])
                dma("sync", d_inv[slB:slB + 1, :], s_rstmp[64:65, :],
                    sems["dw_inva"])
                settok(("rstmp",), sems["dw_inva"])
                wait("sync", sems["dw_inva"])
                guard("sync", ("invb",))
                dma("sync", s_invb[0:64, :],
                    d_inv[slA:slA + 1, :].to_broadcast([64, 512]),
                    sems["dw_invb"])
                dma("sync", s_invb[64:128, :],
                    d_inv[slB:slB + 1, :].to_broadcast([64, 512]),
                    sems["dw_invb"])
                wait("vector", sems["dw_invb"])
                vec("tensor_mul", s_oT[0:64, p, 512 * tc:512 * (tc + 1)],
                    p_o0[0:64, :], s_invb[0:64, :])
                vec("tensor_mul", s_oT[64:128, p, 512 * tc:512 * (tc + 1)],
                    p_o1[64:128, :], s_invb[64:128, :], ctr=sve)
                settok(("invb",), sve)
                settok(("o_rs",), sve)
                guard("tensor", ("o_rs",))
            settok(("vxall",), smm)
        sve_att = sve.v
        if DEBUG:
            wait("sync", sve, sve_att)
            dma("sync", dbg["ot"][:], s_oT[:], sq)

        mark()
        ow16v = owT_h.rearrange("(a p) f -> p a f", p=128)
        owbuf = w_w1[:]
        wait("vector", sve, sve_att)
        for fh in range(2):
            guard("gpsimd", ("w1",))
            guard("gpsimd", ("owb",))
            for p in range(NP):
                dma("gpsimd", owbuf[:, p, :],
                    ow16v[:, p, 512 * fh:512 * (fh + 1)], sems["dw_ow"])
            wait("tensor", sems["dw_ow"])
            wait("tensor", sve, sve_att)
            for tg in range(8):
                for ti in range(2):
                    tt = 2 * tg + ti
                    pp = (p_bg0, p_bg1)[ti]
                    pk = (("psA",), ("psB",))[ti]
                    guard("tensor", pk)
                    for p in range(NP):
                        mm(pp[:, 0:512], s_oT[:, p, 128 * tt:128 * (tt + 1)],
                           owbuf[:, p, :], start=(p == 0), stop=(p == NP - 1),
                           ctr=smm if p == NP - 1 else None)
                    wait("vector", smm)
                    vec("tensor_add", satt(tt)[:, 512 * fh:512 * (fh + 1)],
                        pp[:, 0:512], b_owb[:, 512 * fh:512 * (fh + 1)],
                        ctr=sve)
                    settok(pk, sve)
            settok(("owb",), smm)
        sve_op = sve.v
        if DEBUG:
            wait("sync", sve, sve_op)
            for tt in range(NT):
                dma("sync", dbg["satt"][128 * tt:128 * (tt + 1), :],
                    satt(tt), sq)

        mark()
        wait("gpsimd", sems["c_y"])
        wait("vector", sve, sve_op)
        for tt in range(NT):
            for k in range(EK):
                sl = k % 2
                gsem = sems["dw_y6a"] if sl == 0 else sems["dw_y6b"]
                guard("gpsimd", ("y6", sl))
                igat(s_y6[:, sl, :], d_yall[:], s_gidx[:, tt, k:k + 1], gsem)
                wait("vector", gsem)
                guard("vector", ("moe_scr",)) if k == 0 else None
                if k == 0:
                    vec("tensor_scalar_mul", s_moe[:], s_y6[:, sl, :],
                        s_pw[:, tt, 0:1], ctr=sve)
                else:
                    vec("scalar_tensor_tensor", s_moe[:], s_y6[:, sl, :],
                        s_pw[:, tt, k:k + 1], s_moe[:], ALU.mult, ALU.add,
                        ctr=sve)
                settok(("y6", sl), sve)
            shsem = sems["dw_sh2a"] if tt % 2 == 0 else sems["dw_sh2b"]
            guard("sync", ("sh2ld", tt % 2))
            if tt == 0:
                wait("sync", sq, sq_sh2)
            dma("sync", s_sh2st[:, tt % 2, :],
                d_sh2[128 * tt:128 * (tt + 1), :], shsem)
            wait("vector", shsem)
            vec("tensor_add", s_moe[:], s_moe[:], s_sh2st[:, tt % 2, :],
                ctr=sve)
            settok(("sh2ld", tt % 2), sve)
            wait("scalar", sve)
            guard("scalar", ("gst", tt % 2))
            act(s_xn_st[:, tt % 2, :], s_moe[:], AF.Silu, ctr=ssc)
            settok(("moe_scr",), ssc)
            xsem = sems["dw_xa"] if tt % 2 == 0 else sems["dw_xb"]
            guard("sync", ("xt2", tt % 2))
            dma("sync", s_xt[:, tt % 2, :], x_h[128 * tt:128 * (tt + 1), :],
                xsem)
            wait("vector", ssc)
            wait("vector", xsem)
            vec("tensor_mul", s_moe[:], satt(tt), s_xn_st[:, tt % 2, :],
                ctr=sve)
            settok(("gst", tt % 2), sve)
            vec("tensor_add", s_xt[:, tt % 2, :], s_xt[:, tt % 2, :],
                s_moe[:], ctr=sve)
            wait("sync", sve)
            dma("sync", out_h[128 * tt:128 * (tt + 1), :], s_xt[:, tt % 2, :],
                sq)
            settok(("xt2", tt % 2), sq)

        mark()
        wait("sync", sq, sq.v)
        wait("gpsimd", gq, gq.v)

        _flush("vector")
        _flush("scalar")
        mark()
        if PHASE_LIMIT is not None:
            cut = marks[PHASE_LIMIT]
            for k in Q:
                Q[k] = Q[k][:cut[k]]

        with nc.Block() as block:
            @block.sync
            def _(e):
                for fn in Q["sync"]:
                    fn(e)

            @block.gpsimd
            def _(e):
                for fn in Q["gpsimd"]:
                    fn(e)

            @block.tensor
            def _(e):
                for fn in Q["tensor"]:
                    fn(e)

            @block.scalar
            def _(e):
                for fn in Q["scalar"]:
                    fn(e)

            @block.vector
            def _(e):
                for fn in Q["vector"]:
                    fn(e)

    return nc, dbg


_NC_CACHE = {}


def _get_nc():
    if "nc" not in _NC_CACHE:
        _NC_CACHE["nc"] = build_nc()
    return _NC_CACHE["nc"]


def kernel(L0_attn_x, L0_attn_w_att_val, L0_attn_wv_w, L0_attn_wv_b,
           L0_attn_out_w, L0_attn_out_b, L0_ffn_router_weight,
           L0_ffn_expert_bias, L0_ffn_w1, L0_ffn_w2, L0_ffn_sw1, L0_ffn_sw2):
    x = np.ascontiguousarray(np.asarray(L0_attn_x, dtype=np.float32))
    watt = np.asarray(L0_attn_w_att_val, dtype=np.float32)
    wv = np.asarray(L0_attn_wv_w, dtype=np.float32)
    wvb = np.asarray(L0_attn_wv_b, dtype=np.float32)
    ow = np.asarray(L0_attn_out_w, dtype=np.float32)
    owb = np.asarray(L0_attn_out_b, dtype=np.float32)
    rw = np.asarray(L0_ffn_router_weight, dtype=np.float32)
    eb = np.asarray(L0_ffn_expert_bias, dtype=np.float32)
    w1 = np.asarray(L0_ffn_w1, dtype=np.float32)
    w2 = np.asarray(L0_ffn_w2, dtype=np.float32)
    sw1 = np.asarray(L0_ffn_sw1, dtype=np.float32)
    sw2 = np.asarray(L0_ffn_sw2, dtype=np.float32)

    wvT = np.ascontiguousarray(wv.T)
    owT = np.ascontiguousarray(ow.T)
    sw1T = np.ascontiguousarray(sw1.T)
    sw2T = np.ascontiguousarray(sw2.T)
    rwT = np.ascontiguousarray(rw.T)
    wattp = np.zeros((NP, 128, 64), np.float32)
    for p in range(NP):
        wattp[p, 0:64] = watt[2 * p]
        wattp[p, 64:128] = watt[2 * p + 1]
    u128 = np.triu(np.ones((128, 128), np.float32), 1)
    uc = np.zeros((64, 64), np.float32)
    for tt in range(16):
        uc[0, tt] = 64.0
        uc[1, tt] = 1.0
        for i in range(tt):
            uc[2 + i, tt] = 1.0
    iota = np.tile(np.arange(NE, dtype=np.float32), (128, 1))
    onesc = np.ones((128, 64), np.float32)

    nc, dbg = _get_nc()
    in_maps = []
    for c in range(8):
        in_maps.append(dict(
            x=np.ascontiguousarray(x[c]),
            wvT=wvT, owT=owT, sw1T=sw1T, sw2T=sw2T, rwT=rwT,
            watt=wattp,
            wvb=wvb.reshape(1, D), owb=owb.reshape(1, D),
            ebias=eb.reshape(1, NE),
            w1=np.ascontiguousarray(w1[ELOC * c:ELOC * (c + 1)]),
            w2=np.ascontiguousarray(w2[ELOC * c:ELOC * (c + 1)]),
            tokid=np.arange(T * c, T * (c + 1), dtype=np.float32).reshape(T, 1),
            pm=(np.arange(8, dtype=np.float32) < c).astype(np.float32)
            .reshape(1, 8),
            u128=u128, uc=uc, iota=iota, onesc=onesc,
        ))
    res = run_bass_kernel_spmd(nc, in_maps, core_ids=list(range(8)))
    out = np.stack([np.asarray(res.results[c]["out"]) for c in range(8)],
                   axis=0)
    if DEBUG:
        kernel.last_results = res.results
    return out


# version 04
# speedup vs baseline: 1.1662x; 1.0235x over previous
from contextlib import ExitStack

import numpy as np

import concourse.bass as bass
import concourse.mybir as mybir
from concourse.bass_utils import run_bass_kernel_spmd
from concourse.bass_core_rust import IndirectOffsetOnAxis

F32 = mybir.dt.float32
BF16 = mybir.dt.bfloat16
I32 = mybir.dt.int32
U32 = mybir.dt.uint32
AF = mybir.ActivationFunctionType
ALU = mybir.AluOpType
AX = mybir.AxisListType
RG8 = [[0, 1, 2, 3, 4, 5, 6, 7]]

T = 2048
D = 1024
NT = 16
NJ = 8
NP = 8
NE = 64
EK = 6
CAP = 384
EF = 512
SF = 2048
ELOC = 8
NU = 8
UT = 256

DEBUG = False
PHASE_LIMIT = None


class Ctr:
    __slots__ = ("h", "v", "is_dma", "dedicated")

    def __init__(self, h, is_dma=False, dedicated=False):
        self.h = h
        self.v = 0
        self.is_dma = is_dma
        self.dedicated = dedicated


def build_nc():
    nc = bass.Bass("TRN2", target_bir_lowering=False)

    def inp(name, shape, dt=F32):
        return nc.dram_tensor(name, shape, dt, kind="ExternalInput").ap()

    x_h = inp("x", [T, D])
    wvT_h = inp("wvT", [D, D])
    owT_h = inp("owT", [D, D])
    sw1T_h = inp("sw1T", [D, SF])
    sw2T_h = inp("sw2T", [SF, D])
    rwT_h = inp("rwT", [D, NE])
    watt_h = inp("watt", [NP, 128, 64])
    wvb_h = inp("wvb", [1, D])
    owb_h = inp("owb", [1, D])
    ebias_h = inp("ebias", [1, NE])
    w1_h = inp("w1", [ELOC, D, EF])
    w2_h = inp("w2", [ELOC, EF, D])
    tokid_h = inp("tokid", [T, 1])
    pm_h = inp("pm", [1, 8])
    u128_h = inp("u128", [128, 128])
    uc_h = inp("uc", [64, 64])
    iota_h = inp("iota", [128, NE])
    onesc_h = inp("onesc", [128, 64])

    out_h = nc.dram_tensor("out", [T, D], F32, kind="ExternalOutput").ap()

    def idram(name, shape, dt=F32):
        return nc.dram_tensor(name, shape, dt, kind="Internal").ap()

    d_xn = idram("d_xn", [T, D], BF16)
    d_xnall = idram("d_xnall", [8 * T, D], BF16)
    d_cnt = idram("d_cnt", [1, NE])
    d_cs = idram("d_cs", [8, NE])
    d_ss = idram("d_ss", [16, NE])
    d_choff = idram("d_choff", [NT, NE])
    d_tokmap = idram("d_tokmap", [25088, 1])
    d_tokmy = idram("d_tokmy", [3136, 1])
    d_xs = idram("d_xs", [ELOC * CAP, D], BF16)
    d_ymine = idram("d_ymine", [ELOC * CAP, D], BF16)
    d_yall = idram("d_yall", [NE * CAP, D], BF16)
    d_sh2 = idram("d_sh2", [T, D], BF16)
    d_inv = idram("d_inv", [64, 512])
    d_wv16 = idram("d_wv16", [D, D], BF16)
    d_sw1b = idram("d_sw1b", [D, SF], BF16)
    d_ow16 = idram("d_ow16", [D, D], BF16)
    d_w116 = idram("d_w116", [ELOC, D, EF], BF16)
    d_w216 = idram("d_w216", [ELOC, EF, D], BF16)

    dbg = {}

    def dbgout(name, shape, dt):
        if DEBUG:
            dbg[name] = nc.dram_tensor("dbg_" + name, shape, dt,
                                       kind="ExternalOutput").ap()

    dbgout("xn", [T, D], BF16)
    dbgout("cnt", [1, NE], F32)
    dbgout("choff", [NT, NE], F32)
    dbgout("if6", [128, NT, 8], F32)
    dbgout("pw", [128, NT, 8], F32)
    dbgout("gidx", [128, NT, 8], I32)
    dbgout("tokmy", [3136, 1], F32)
    dbgout("y", [ELOC * CAP, D], BF16)
    dbgout("sh2", [T, D], BF16)
    dbgout("ot", [128, NP, T], BF16)
    dbgout("satt", [T, D], BF16)

    st = ExitStack()
    with st:
        def sb(name, shape, dt):
            return st.enter_context(nc.sbuf_tensor(name, shape, dt))

        def psum(name, shape):
            return st.enter_context(nc.psum_tensor(name, shape, F32))

        c_iota = sb("c_iota", [128, NE], F32)
        c_ones = sb("c_ones", [128, 64], BF16)
        c_u128 = sb("c_u128", [128, 128], BF16)
        c_uc = sb("c_uc", [64, 64], BF16)
        w_watt = sb("w_watt", [128, NP, 64], BF16)
        w_rwT = sb("w_rwT", [128, NJ, NE], BF16)
        w_sw2r = sb("w_sw2r", [128, 16, D], BF16)
        w_sw1s = sb("w_sw1s", [128, 2, NJ, 128], BF16)
        w_vx = sb("w_vx", [128, 2, NJ, 128], BF16)
        w_ow = sb("w_ow", [128, 2, 512], BF16)
        w_w1 = sb("w_w1", [128, NJ, EF], BF16)
        s_wf32 = sb("s_wf32", [128, 4, 512], F32)
        b_wvb = sb("b_wvb", [128, D], F32)
        b_owb = sb("b_owb", [128, D], F32)
        b_eb = sb("b_eb", [128, NE], F32)
        s_pm = sb("s_pm", [1, 8], F32)
        s_tokid = sb("s_tokid", [128, NT], F32)

        s_xnT = sb("s_xnT", [128, NJ, T], BF16)
        s_oT = sb("s_oT", [128, NP, T], BF16)
        s_vx = sb("s_vx", [128, NT, 128], BF16)
        s_xw = sb("s_xw", [128, 4, 512], BF16)
        s_pT = sb("s_pT", [128, 4, 512], BF16)
        s_hraw = sb("s_hraw", [128, 16, UT], BF16)
        s_xn_st = sb("s_xn_st", [128, 2, D], BF16)
        s_sh2st = sb("s_sh2st", [128, 2, D], BF16)
        s_A = sb("s_A", [128, NT, NE], BF16)
        s_xsT = sb("s_xsT", [128, NJ, CAP], BF16)
        s_h1r = sb("s_h1r", [128, 8, CAP], BF16)
        s_yst = sb("s_yst", [128, 2, D], BF16)
        s_y6 = sb("s_y6", [128, 2, D], BF16)

        s_xt = sb("s_xt", [128, 2, D], F32)
        s_moe = sb("s_moe", [128, D], F32)
        s_lns = sb("s_lns", [128, 8], F32)
        s_scores = sb("s_scores", [128, NE], F32)
        s_biased = sb("s_biased", [128, NE], F32)
        s_m8 = sb("s_m8", [128, 8], F32)
        s_i8 = sb("s_i8", [128, 8], U32)
        s_if6a = sb("s_if6a", [128, NT, 8], F32)
        s_Af = sb("s_Af", [128, NE], F32)
        s_Af2 = sb("s_Af2", [128, NE], F32)
        s_scr64 = sb("s_scr64", [128, NE], F32)
        s_sc6 = sb("s_sc6", [128, 8], F32)
        s_pw = sb("s_pw", [128, NT, 8], F32)
        s_r6 = sb("s_r6", [128, 8], F32)
        s_v6 = sb("s_v6", [128, 8], F32)
        s_t8a = sb("s_t8a", [128, 8], F32)
        s_t8b = sb("s_t8b", [128, 8], F32)
        s_sidx = sb("s_sidx", [128, NT, 8], I32)
        s_gidx = sb("s_gidx", [128, NT, 8], I32)
        s_r6i = sb("s_r6i", [128, 8], I32)
        s_cnt = sb("s_cnt", [1, NE], F32)
        s_csf = sb("s_csf", [1, NE], F32)
        s_co = sb("s_co", [1, NE], F32)
        s_co2 = sb("s_co2", [1, NE], F32)
        s_coi = sb("s_coi", [1, NE], I32)
        s_cq = sb("s_cq", [1, NE], I32)
        s_cqf = sb("s_cqf", [1, NE], F32)
        s_crf = sb("s_crf", [1, NE], F32)
        s_stage = sb("s_stage", [64, 64], BF16)
        s_choff = sb("s_choff", [64, NE], F32)
        s_offb = sb("s_offb", [128, 2, NE], F32)
        s_C = sb("s_C", [128, NE], F32)
        s_Ct = sb("s_Ct", [128, NE], F32)
        s_rstmp = sb("s_rstmp", [128, 512], F32)
        s_invb = sb("s_invb", [128, 512], F32)
        s_tmf = sb("s_tmf", [128, 24], F32)
        s_tmidx = sb("s_tmidx", [128, 24], I32)
        s_zero = sb("s_zero", [128, 256], F32)
        s_junk = sb("s_junk", [128, 32], F32)

        p_att0 = psum("p_att0", [128, 512])
        p_att1 = psum("p_att1", [128, 512])
        p_o0 = psum("p_o0", [128, 512])
        p_o1 = psum("p_o1", [128, 512])
        p_rs0 = psum("p_rs0", [128, 512])
        p_rs1 = psum("p_rs1", [128, 512])
        p_bg0 = psum("p_bg0", [128, 512])
        p_bg1 = psum("p_bg1", [128, 512])

        names = "gq sq mm ex ve sc c_xn c_cnt c_tok c_y vsy ssy".split()
        sems = {nm: Ctr(st.enter_context(nc.semaphore("sem_" + nm)),
                        is_dma=nm in ("gq", "sq"))
                for nm in names}
        dnames = ("dw_sw1a dw_sw1b dw_vxa dw_vxb dw_w1 dw_w2 dw_xsT dw_ow "
                  "dw_xa dw_xb dw_inva dw_invb dw_y6a dw_y6b dw_sh2a dw_sh2b "
                  "dw_gath").split()
        for nm in dnames:
            sems[nm] = Ctr(st.enter_context(nc.semaphore("sem_" + nm)),
                           is_dma=True, dedicated=True)
        gq, sq = sems["gq"], sems["sq"]
        _wsems = [sems[n] for n in ("dw_w1", "dw_w2", "dw_sw1a", "dw_sw1b")]
        smm, sex, sve, ssc = sems["mm"], sems["ex"], sems["ve"], sems["sc"]

        Q = {k: [] for k in ("sync", "gpsimd", "tensor", "scalar", "vector")}
        marks = []

        def mark():
            _flush("vector")
            _flush("scalar")
            marks.append({k: len(v) for k, v in Q.items()})
        tok = {}

        def q(eng, fn):
            Q[eng].append(fn)

        _fence = {}

        def wait(eng, ctr, val=None):
            if ctr.dedicated:
                v = ctr.v if val is None else val
            else:
                v = ctr.v if (val is None or ctr.is_dma) else val
            h = ctr.h
            if ctr.is_dma and not ctr.dedicated:
                owner = "sync" if ctr is sems["sq"] else "gpsimd"
                if _fence.get(id(ctr), -1) < v:
                    _fence[id(ctr)] = v
                    if eng != owner:
                        q(owner, lambda e: e.wait_ge(h, v))
            fn = lambda e: e.wait_ge(h, v)
            if eng in ("vector", "scalar") and _stage[eng] is not None:
                _swaits[eng].append(fn)
            else:
                q(eng, fn)

        def guard(eng, key):
            if key in tok:
                ctr, v = tok[key]
                wait(eng, ctr, v)

        def settok(key, ctr):
            tok[key] = (ctr, ctr.v)

        def dma(eng, dst, src, ctr):
            ctr.v += 16
            h = ctr.h
            q(eng, lambda e: e.dma_start(dst, src).then_inc(h, 16))

        def dmaT(eng, dst, src, ctr):
            ctr.v += 16
            h = ctr.h
            q(eng, lambda e: e.dma_start_transpose(dst, src).then_inc(h, 16))

        def dmaslow(eng, dst, src, ctr):
            ctr.v += 16
            h = ctr.h
            q(eng, lambda e: e.dma_start(
                dst, src, allow_slow_non_contiguous=True).then_inc(h, 16))

        def mm(out, lhsT, rhs, start=True, stop=True, tp=None, ctr=None):
            if ctr is not None:
                ctr.v += 1
                h = ctr.h
                q("tensor", lambda e: e.matmul(
                    out, lhsT, rhs, start=start, stop=stop,
                    tile_position=tp).then_inc(h, 1))
            else:
                q("tensor", lambda e: e.matmul(out, lhsT, rhs, start=start,
                                               stop=stop, tile_position=tp))

        _stage = {"scalar": None, "vector": None}
        _swaits = {"scalar": [], "vector": []}
        _prevt = {"scalar": set(), "vector": set()}

        def _names(objs):
            out = set()
            for o in objs:
                t = getattr(o, "tensor", None)
                if t is not None:
                    out.add(t.name)
            return out

        def _flush(eng, attach=False):
            if _stage[eng] is None:
                return
            b, touched, has_ctr = _stage[eng]
            if attach:
                sc_ = sems["vsy"] if eng == "vector" else sems["ssy"]
                sc_.v += 1
                h, v = sc_.h, sc_.v

                def _att(e, b=b, h=h, has_ctr=has_ctr, eng=eng):
                    r = b(e)
                    if r is None or has_ctr:
                        if eng == "vector":
                            e.engine_nop()
                            r = e.engine_nop()
                        else:
                            e.activation(s_junk[:, 0:8], s_junk[:, 0:8],
                                         AF.Identity)
                            r = e.activation(s_junk[:, 8:16], s_junk[:, 8:16],
                                             AF.Identity)
                    r.then_inc(h, 1)
                Q[eng].append(_att)
                for w in _swaits[eng]:
                    Q[eng].append(w)
                Q[eng].append(lambda e, h=h, v=v: e.wait_ge(h, v))
            else:
                Q[eng].append(lambda e, b=b: b(e))
                for w in _swaits[eng]:
                    Q[eng].append(w)
            _swaits[eng] = []
            _prevt[eng] = set() if attach else touched
            _stage[eng] = None

        def _emit_eng(eng, builder, touched, has_ctr=False):
            if _stage[eng] is not None:
                conflict = bool((_stage[eng][1] | _prevt[eng]) & touched)
                _flush(eng, attach=conflict)
            else:
                if _prevt[eng] & touched:
                    sc_ = sems["vsy"] if eng == "vector" else sems["ssy"]
                    sc_.v += 1
                    h, v = sc_.h, sc_.v

                    def _np(e, h=h, eng=eng):
                        if eng == "vector":
                            e.engine_nop()
                            e.engine_nop().then_inc(h, 1)
                        else:
                            e.activation(s_junk[:, 0:8], s_junk[:, 0:8],
                                         AF.Identity)
                            e.activation(s_junk[:, 8:16], s_junk[:, 8:16],
                                         AF.Identity).then_inc(h, 1)
                    Q[eng].append(_np)
                    Q[eng].append(lambda e, h=h, v=v: e.wait_ge(h, v))
                    _prevt[eng] = set()
            _stage[eng] = (builder, touched, has_ctr)

        def act(out, in_, func, ctr=None, **kw):
            touched = _names([out, in_, kw.get("accum_out"), kw.get("bias"),
                              kw.get("scale")])
            if ctr is not None:
                ctr.v += 1
                h = ctr.h
                b = lambda e: e.activation(out, in_, func, **kw).then_inc(h, 1)
                _emit_eng("scalar", b, touched, has_ctr=True)
            else:
                b = lambda e: e.activation(out, in_, func, **kw)
                _emit_eng("scalar", b, touched)

        def vec(meth, *a, ctr=None, **kw):
            touched = _names(list(a) + list(kw.values()))
            if ctr is not None:
                ctr.v += 1
                h = ctr.h
                b = lambda e: getattr(e, meth)(*a, **kw).then_inc(h, 1)
                _emit_eng("vector", b, touched, has_ctr=True)
            else:
                b = lambda e: getattr(e, meth)(*a, **kw)
                _emit_eng("vector", b, touched)

        def coll(kind, op, ins_, outs_, ctr):
            ctr.v += 1
            h = ctr.h
            q("gpsimd", lambda e: e.collective_compute(
                kind, op, replica_groups=RG8, ins=ins_, outs=outs_
            ).then_inc(h, 1))

        _bc_reg = {}

        def iscat(out_ap, idx_ap, in_ap, ctr, bc):
            ctr.v += 16
            h = ctr.h

            def fn(e):
                if bc not in _bc_reg:
                    _bc_reg[bc] = e.to_reg(bc)
                e.indirect_dma_start(
                    out_ap, IndirectOffsetOnAxis(idx_ap, 0), in_ap, None,
                    bounds_check=_bc_reg[bc], oob_is_err=False).then_inc(h, 16)
            q("gpsimd", fn)

        def igat(out_ap, in_ap, idx_ap, ctr):
            ctr.v += 16
            h = ctr.h
            q("gpsimd", lambda e: e.indirect_dma_start(
                out_ap, None, in_ap, IndirectOffsetOnAxis(idx_ap, 0)
            ).then_inc(h, 16))

        satt_flat = s_xnT[:].rearrange("p a t -> p (a t)")

        def satt(tt):
            return satt_flat[:, D * tt:D * (tt + 1)]

        _wslot = [0]

        def wload(dst, src, a=None):
            sl = _wslot[0] % 4
            _wslot[0] += 1
            dsem = _wsems[sl]
            stg = s_wf32[:, sl, :]
            if a is not None:
                stg = stg.rearrange("p (a f) -> p a f", a=a)
            _flush("scalar")
            guard("scalar", ("wf", sl))
            dma("scalar", stg, src, dsem)
            wait("vector", dsem)
            vec("tensor_copy", dst, stg, ctr=sve)
            settok(("wf", sl), sve)

        vec("memset", s_junk[:], 0.0)
        vec("memset", s_zero[:], 0.0)
        vec("memset", s_stage[:], 0.0)
        vec("memset", s_pT[:], 0.0, ctr=sve)
        zero_tok = sve.v
        wait("scalar", sve, zero_tok)
        act(s_junk[:, 0:16], s_junk[:, 0:16], AF.Identity)
        act(s_junk[:, 16:32], s_junk[:, 16:32], AF.Identity)

        dma("gpsimd", c_iota[:], iota_h[:], gq)
        dma("gpsimd", c_ones[:], onesc_h[:], gq)
        dma("gpsimd", c_u128[:], u128_h[:], gq)
        dma("gpsimd", c_uc[:], uc_h[:], gq)
        dma("gpsimd", w_watt[:], watt_h.rearrange("a p f -> p a f"), gq)
        dma("gpsimd", w_rwT[:], rwT_h.rearrange("(a p) f -> p a f", p=128), gq)
        gq_const = gq.v

        gq_cast = gq.v

        dma("sync", b_wvb[:], wvb_h[0:1, :].to_broadcast([128, D]), sq)
        dma("sync", b_owb[:], owb_h[0:1, :].to_broadcast([128, D]), sq)
        dma("sync", b_eb[:], ebias_h[0:1, :].to_broadcast([128, NE]), sq)
        dma("sync", s_pm[:], pm_h[:], sq)
        dmaslow("sync", s_tokid[:],
                tokid_h.rearrange("(a p) o -> p (a o)", p=128), sq)
        sq_const = sq.v

        for tt in range(NT):
            sl = tt % 2
            xsem = sems["dw_xa"] if sl == 0 else sems["dw_xb"]
            guard("sync", ("xt", sl))
            dma("sync", s_xt[:, sl, :], x_h[128 * tt:128 * (tt + 1), :], xsem)
            wait("vector", xsem)
            vec("reduce_sum", s_lns[:, 0:1], s_xt[:, sl, :], AX.X)
            wait("scalar", xsem)
            guard("scalar", ("moe_scr",))
            act(s_moe[:], s_xt[:, sl, :], AF.Square, ctr=ssc,
                accum_out=s_lns[:, 1:2])
            settok(("moe_scr",), ssc)
            wait("vector", ssc)
            vec("tensor_scalar_mul", s_lns[:, 2:3], s_lns[:, 0:1], 1.0 / D)
            vec("tensor_scalar_mul", s_lns[:, 3:4], s_lns[:, 1:2], 1.0 / D)
            vec("tensor_mul", s_lns[:, 4:5], s_lns[:, 2:3], s_lns[:, 2:3])
            vec("tensor_sub", s_lns[:, 5:6], s_lns[:, 3:4], s_lns[:, 4:5],
                ctr=sve)
            vec("tensor_scalar_add", s_lns[:, 5:6], s_lns[:, 5:6], 1e-5,
                ctr=sve)
            wait("scalar", sve)
            act(s_lns[:, 6:7], s_lns[:, 5:6], AF.Sqrt, ctr=ssc)
            wait("vector", ssc)
            vec("reciprocal", s_lns[:, 7:8], s_lns[:, 6:7])
            guard("vector", ("xnst", sl))
            vec("scalar_tensor_tensor", s_xn_st[:, sl, :], s_xt[:, sl, :],
                s_lns[:, 2:3], s_lns[:, 7:8].to_broadcast([128, D]),
                ALU.subtract, ALU.mult, ctr=sve)
            settok(("xt", sl), sve)
            wait("sync", sve)
            dma("sync", d_xn[128 * tt:128 * (tt + 1), :], s_xn_st[:, sl, :], sq)
            settok(("xnst", sl), sq)
        sq_xn = sq.v

        wait("gpsimd", sq, sq_xn)
        coll("AllGather", ALU.bypass, [d_xn[:]], [d_xnall[:]], sems["c_xn"])

        wait("sync", sq, sq_xn)
        for jj in range(NJ):
            dmaT("sync", s_xnT[:, jj, :], d_xn[:, 128 * jj:128 * (jj + 1)], sq)
        sq_xnT = sq.v

        if DEBUG:
            dma("sync", dbg["xn"][:], d_xn[:], sq)

        mark()
        wait("tensor", sq, sq_xnT)
        wait("tensor", gq, gq_const)
        wait("vector", sq, sq_const)
        for tt in range(NT):
            guard("tensor", ("psA",))
            for j in range(NJ):
                mm(p_bg0[:, 0:NE], s_xnT[:, j, 128 * tt:128 * (tt + 1)],
                   w_rwT[:, j, :], start=(j == 0), stop=(j == NJ - 1),
                   ctr=smm if j == NJ - 1 else None)
            wait("scalar", smm)
            act(s_scores[:], p_bg0[:, 0:NE], AF.Sigmoid, ctr=ssc)
            settok(("psA",), ssc)
            wait("vector", ssc)
            vec("tensor_add", s_biased[:], s_scores[:], b_eb[:])
            vec("max", out=s_m8[:], in_=s_biased[:])
            vec("max_index", s_i8[:], s_m8[:], s_biased[:])
            vec("tensor_copy", s_if6a[:, tt, :], s_i8[:])
            vec("memset", s_Af[:], 0.0)
            cur, nxt = s_Af, s_Af2
            for k in range(EK):
                vec("scalar_tensor_tensor", nxt[:], c_iota[:],
                    s_if6a[:, tt, k:k + 1], cur[:], ALU.is_equal, ALU.add)
                cur, nxt = nxt, cur
                vec("scalar_tensor_tensor", s_scr64[:], c_iota[:],
                    s_if6a[:, tt, k:k + 1], s_scores[:], ALU.is_equal,
                    ALU.mult, accum_out=s_sc6[:, k:k + 1])
            vec("tensor_copy", s_A[:, tt, :], cur[:])
            vec("reduce_sum", s_t8a[:, 0:1], s_sc6[:, 0:EK], AX.X)
            vec("tensor_scalar_add", s_t8a[:, 1:2], s_t8a[:, 0:1], 1e-20)
            vec("reciprocal", s_t8a[:, 2:3], s_t8a[:, 1:2])
            vec("tensor_scalar_mul", s_pw[:, tt, 0:EK], s_sc6[:, 0:EK],
                s_t8a[:, 2:3], ctr=sve)
            wait("tensor", sve)
            guard("tensor", ("psB",))
            mm(p_bg1[0:64, 0:NE], c_ones[:], s_A[:, tt, :], ctr=smm)
            wait("vector", smm)
            guard("vector", ("cntst",))
            vec("tensor_copy", s_cnt[:], p_bg1[0:1, 0:NE], ctr=sve)
            settok(("psB",), sve)
            wait("sync", sve)
            dma("sync", d_ss[tt:tt + 1, :], s_cnt[:], sq)
            settok(("cntst",), sq)
        sq_ss = sq.v

        wait("gpsimd", sq, sq_ss)
        wait("gpsimd", sve, zero_tok)
        dma("gpsimd", s_stage[0:16, :], d_ss[:], gq)
        wait("tensor", gq)
        guard("tensor", ("psB",))
        mm(p_bg1[0:64, 0:NE], c_ones[0:64, :], s_stage[:], ctr=smm)
        wait("vector", smm)
        guard("vector", ("cntst",))
        vec("tensor_copy", s_cnt[:], p_bg1[0:1, 0:NE], ctr=sve)
        settok(("psB",), sve)
        wait("sync", sve)
        dma("sync", d_cnt[:], s_cnt[:], sq)
        settok(("cntst",), sq)
        wait("gpsimd", sq, sq.v)
        coll("AllGather", ALU.bypass, [d_cnt[:]], [d_cs[:]], sems["c_cnt"])

        mark()
        sw2v = sw2T_h.rearrange("(a p) f -> p a f", p=128)
        for jj in range(16):
            for h2 in range(2):
                wload(w_sw2r[:, jj, 512 * h2:512 * (h2 + 1)],
                      sw2v[:, jj, 512 * h2:512 * (h2 + 1)])
        sw2r_tok = sve.v
        wait("vector", sems["c_cnt"])
        vec("memset", s_co[:], 0.0)
        cur, nxt = s_co, s_co2
        for cc in range(8):
            wait("sync", sems["c_cnt"])
            guard("sync", ("csf",))
            dma("sync", s_csf[:], d_cs[cc:cc + 1, :], sq)
            wait("vector", sq)
            vec("scalar_tensor_tensor", nxt[:], s_csf[:],
                s_pm[:, cc:cc + 1], cur[:], ALU.mult, ALU.add, ctr=sve)
            settok(("csf",), sve)
            cur, nxt = nxt, cur
        vec("tensor_copy", s_coi[:], cur[:])
        vec("tensor_scalar", s_cq[:], s_coi[:], 6, None, ALU.arith_shift_right)
        vec("tensor_scalar", s_coi[:], s_coi[:], 63, None, ALU.bitwise_and)
        vec("tensor_copy", s_cqf[:], s_cq[:])
        vec("tensor_copy", s_crf[:], s_coi[:], ctr=sve)
        sve_qr = sve.v
        wait("gpsimd", sve, sve_qr)
        dma("gpsimd", s_stage[0:1, :], s_cqf[:], gq)
        dma("gpsimd", s_stage[1:2, :], s_crf[:], gq)
        dma("gpsimd", s_stage[2:18, :], d_ss[:], gq)
        wait("tensor", gq)
        wait("tensor", sve, sve_qr)
        guard("tensor", ("psA",))
        mm(p_bg0[0:64, 0:NE], c_uc[:], s_stage[:], ctr=smm)
        wait("vector", smm)
        vec("tensor_copy", s_choff[:], p_bg0[0:64, 0:NE], ctr=sve)
        settok(("psA",), sve)
        wait("sync", sve)
        dma("sync", d_choff[:], s_choff[0:16, :], sq)
        sq_choff = sq.v
        if DEBUG:
            dma("sync", dbg["choff"][:], d_choff[:], sq)

        wait("gpsimd", sve, zero_tok)
        dma("gpsimd", d_tokmap[:].rearrange("(p b) o -> p (b o)", p=128),
            s_zero[:, 0:196], gq)
        gq_tmz = gq.v

        for tt in range(NT):
            osl = tt % 2
            guard("sync", ("offb", osl))
            wait("sync", sq, sq_choff)
            dma("sync", s_offb[:, osl, :],
                d_choff[tt:tt + 1, :].to_broadcast([128, NE]), sq)
            off_ld = sq.v
            guard("tensor", ("psB",))
            mm(p_bg1[:, 0:NE], c_u128[:], s_A[:, tt, :], ctr=smm)
            wait("vector", smm)
            wait("vector", sq, off_ld)
            vec("tensor_copy", s_C[:], p_bg1[:, 0:NE], ctr=sve)
            settok(("psB",), sve)
            vec("tensor_add", s_Ct[:], s_C[:], s_offb[:, osl, :], ctr=sve)
            settok(("offb", osl), sve)
            for k in range(EK):
                vec("scalar_tensor_tensor", s_scr64[:], c_iota[:],
                    s_if6a[:, tt, k:k + 1], s_Ct[:], ALU.is_equal, ALU.mult,
                    accum_out=s_r6[:, k:k + 1])
            E6 = slice(0, EK)
            vec("tensor_scalar", s_v6[:, E6], s_r6[:, E6], float(CAP), None,
                ALU.is_lt)
            vec("tensor_mul", s_pw[:, tt, E6], s_pw[:, tt, E6], s_v6[:, E6])
            vec("tensor_scalar", s_t8a[:, E6], s_if6a[:, tt, E6], float(CAP),
                None, ALU.mult)
            vec("tensor_add", s_t8a[:, E6], s_t8a[:, E6], s_r6[:, E6])
            vec("tensor_copy", s_r6i[:, E6], s_if6a[:, tt, E6])
            vec("tensor_scalar", s_r6i[:, E6], s_r6i[:, E6], 3, None,
                ALU.arith_shift_right)
            vec("tensor_copy", s_t8b[:, E6], s_r6i[:, E6])
            vec("tensor_scalar", s_t8b[:, E6], s_t8b[:, E6], 64.0, None,
                ALU.mult)
            vec("tensor_add", s_t8b[:, E6], s_t8b[:, E6], s_t8a[:, E6])
            vec("tensor_scalar", s_scr64[:, 0:EK], s_v6[:, E6], -2e6, 2e6,
                ALU.mult, ALU.add)
            vec("tensor_add", s_t8b[:, E6], s_t8b[:, E6], s_scr64[:, 0:EK])
            vec("tensor_copy", s_sidx[:, tt, E6], s_t8b[:, E6])
            vec("tensor_scalar_min", s_t8a[:, E6], s_t8a[:, E6],
                float(NE * CAP - 1))
            vec("tensor_copy", s_gidx[:, tt, E6], s_t8a[:, E6], ctr=sve)
        sve_route = sve.v

        wait("gpsimd", sve, sve_route)
        wait("gpsimd", gq, gq_tmz)
        for tt in range(NT):
            for k in range(EK):
                iscat(d_tokmap[:], s_sidx[:, tt, k:k + 1],
                      s_tokid[:, tt:tt + 1], gq, 25087)
        wait("gpsimd", gq)
        coll("ReduceScatter", ALU.add, [d_tokmap[:]], [d_tokmy[:]],
             sems["c_tok"])
        if DEBUG:
            wait("sync", sems["c_tok"])
            dma("sync", dbg["tokmy"][:], d_tokmy[:], sq)
            dma("sync", dbg["cnt"][:], d_cnt[:], sq)
            wait("sync", sve, sve_route)
            dma("sync", dbg["if6"][:], s_if6a[:], sq)
            dma("sync", dbg["pw"][:], s_pw[:], sq)
            dma("sync", dbg["gidx"][:], s_gidx[:], sq)

        mark()
        wait("tensor", sve, sw2r_tok)
        for u in range(NU):
            tsl = slice(UT * u, UT * (u + 1))
            sw1v = sw1T_h.rearrange("(a p) f -> p a f", p=128)
            for ft in range(16):
                slw = ft % 2
                guard("vector", ("sw1", slw))
                for h2 in range(2):
                    wload(w_sw1s[:, slw, 4 * h2:4 * (h2 + 1), :],
                          sw1v[:, 4 * h2:4 * (h2 + 1),
                               128 * ft:128 * (ft + 1)], a=4)
                cast_tok = sve.v
                pp = p_bg0 if ft % 2 == 0 else p_bg1
                pk = ("psA",) if ft % 2 == 0 else ("psB",)
                guard("tensor", pk)
                wait("tensor", sve, cast_tok)
                for j in range(NJ):
                    mm(pp[:, 0:UT], w_sw1s[:, slw, j, :], s_xnT[:, j, tsl],
                       start=(j == 0), stop=(j == NJ - 1),
                       ctr=smm if j == NJ - 1 else None)
                settok(("sw1", slw), smm)
                wait("vector", smm)
                vec("tensor_copy", s_hraw[:, ft, :], pp[:, 0:UT], ctr=sve)
                settok(pk, sve)
            wait("scalar", sve)
            act(s_hraw[:].rearrange("p a f -> p (a f)"),
                s_hraw[:].rearrange("p a f -> p (a f)"),
                AF.Gelu_apprx_tanh, ctr=ssc)
            wait("tensor", ssc)
            for tq in range(2):
                tt = 2 * u + tq
                for ih in range(2):
                    pp = p_bg0 if ih == 0 else p_bg1
                    pk = ("psA",) if ih == 0 else ("psB",)
                    guard("tensor", pk)
                    for ft in range(16):
                        mm(pp[:, 0:512],
                           s_hraw[:, ft, 128 * tq:128 * (tq + 1)],
                           w_sw2r[:, ft, 512 * ih:512 * (ih + 1)],
                           start=(ft == 0), stop=(ft == 15),
                           ctr=smm if ft == 15 else None)
                    wait("vector", smm)
                    guard("vector", ("sh2st", tt % 2))
                    vec("tensor_copy",
                        s_sh2st[:, tt % 2, 512 * ih:512 * (ih + 1)],
                        pp[:, 0:512], ctr=sve)
                    settok(pk, sve)
                wait("sync", sve)
                dma("sync", d_sh2[128 * tt:128 * (tt + 1), :],
                    s_sh2st[:, tt % 2, :], sq)
                settok(("sh2st", tt % 2), sq)
        sq_sh2 = sq.v
        if DEBUG:
            wait("sync", sq, sq_sh2)
            dma("sync", dbg["sh2"][:], d_sh2[:], sq)

        mark()
        wait("gpsimd", sems["c_tok"])
        wait("gpsimd", sems["c_xn"])
        dmaslow("gpsimd", s_tmf[:],
                d_tokmy[0:3072, :].rearrange("(j p) o -> p (j o)", p=128), gq)
        wait("vector", gq)
        vec("tensor_copy", s_tmidx[:], s_tmf[:], ctr=sve)
        wait("gpsimd", sve)
        for jj in range(24):
            sl = jj % 2
            gsem = sems["dw_y6a"] if sl == 0 else sems["dw_y6b"]
            guard("gpsimd", ("xsg", sl))
            igat(s_y6[:, sl, :], d_xnall[:], s_tmidx[:, jj:jj + 1], gsem)
            wait("gpsimd", gsem)
            dma("gpsimd", d_xs[128 * jj:128 * (jj + 1), :], s_y6[:, sl, :], gq)
            settok(("xsg", sl), gq)
        gq_xs = gq.v

        w2view = w_w1[:].rearrange("p a f -> p (a f)").rearrange(
            "p (b g) -> p b g", b=4)

        for eg in range(4):
            for el in range(2):
                e = 2 * eg + el
                guard("sync", ("xsT",))
                wait("sync", gq, gq_xs)
                for j in range(NJ):
                    dmaT("sync", s_xsT[:, j, :],
                         d_xs[CAP * e:CAP * (e + 1), 128 * j:128 * (j + 1)],
                         sems["dw_xsT"])
                wait("tensor", sems["dw_xsT"])
                guard("vector", ("w1",))
                w1v = w1_h[e].rearrange("(a p) f -> p a f", p=128)
                for j in range(NJ):
                    wload(w_w1[:, j, :], w1v[:, j, :])
                wait("tensor", sve)
                for ft in range(4):
                    pp = p_bg0 if ft % 2 == 0 else p_bg1
                    pk = ("psA",) if ft % 2 == 0 else ("psB",)
                    guard("tensor", pk)
                    for j in range(NJ):
                        mm(pp[:, 0:CAP], w_w1[:, j, 128 * ft:128 * (ft + 1)],
                           s_xsT[:, j, :], start=(j == 0), stop=(j == NJ - 1),
                           ctr=smm if j == NJ - 1 else None)
                    wait("vector", smm)
                    vec("tensor_copy", s_h1r[:, 4 * el + ft, :], pp[:, 0:CAP],
                        ctr=sve)
                    settok(pk, sve)
                settok(("xsT",), smm)
                settok(("w1",), smm)
            wait("scalar", sve)
            act(s_h1r[:].rearrange("p a f -> p (a f)"),
                s_h1r[:].rearrange("p a f -> p (a f)"),
                AF.Gelu_apprx_tanh, ctr=ssc)
            gelu_tok = ssc.v
            wait("tensor", ssc, gelu_tok)
            for el in range(2):
                e = 2 * eg + el
                guard("vector", ("w1",))
                w2v = w2_h[e].rearrange("(a p) f -> p a f", p=128)
                for c4 in range(4):
                    for h2 in range(2):
                        wload(w2view[:, c4, 512 * h2:512 * (h2 + 1)],
                              w2v[:, c4, 512 * h2:512 * (h2 + 1)])
                wait("tensor", sve)
                for ct in range(3):
                    for ih in range(2):
                        pp = p_bg0 if ih == 0 else p_bg1
                        pk = ("psA",) if ih == 0 else ("psB",)
                        guard("tensor", pk)
                        for f in range(4):
                            mm(pp[:, 0:512],
                               s_h1r[:, 4 * el + f, 128 * ct:128 * (ct + 1)],
                               w2view[:, f, 512 * ih:512 * (ih + 1)],
                               start=(f == 0), stop=(f == 3),
                               ctr=smm if f == 3 else None)
                        wait("vector", smm)
                        guard("vector", ("yst", ct % 2))
                        vec("tensor_copy",
                            s_yst[:, ct % 2, 512 * ih:512 * (ih + 1)],
                            pp[:, 0:512], ctr=sve)
                        settok(pk, sve)
                    wait("sync", sve)
                    dma("sync",
                        d_ymine[CAP * e + 128 * ct:CAP * e + 128 * (ct + 1), :],
                        s_yst[:, ct % 2, :], sq)
                    settok(("yst", ct % 2), sq)
                settok(("w1",), smm)
        sq_y = sq.v
        wait("gpsimd", sq, sq_y)
        coll("AllGather", ALU.bypass, [d_ymine[:]], [d_yall[:]], sems["c_y"])
        if DEBUG:
            dma("sync", dbg["y"][:], d_ymine[:], sq)

        mark()
        wait("sync", gq, gq_cast)
        inv_slot = [0]
        for p in range(NP):
            guard("vector", ("vxall",))
            vsl = p % 2
            dsem = sems["dw_vxa"] if vsl == 0 else sems["dw_vxb"]
            guard("gpsimd", ("wvx", vsl))
            dma("gpsimd", w_vx[:, vsl, :, :],
                wvT_h.rearrange("(a p) f -> p a f", p=128)
                [:, :, 128 * p:128 * (p + 1)], dsem)
            wait("tensor", dsem)
            for tt in range(NT):
                guard("tensor", ("psA",))
                for j in range(NJ):
                    mm(p_bg0[:, 0:128], s_xnT[:, j, 128 * tt:128 * (tt + 1)],
                       w_vx[:, vsl, j, :], start=(j == 0), stop=(j == NJ - 1),
                       ctr=smm if j == NJ - 1 else None)
                wait("vector", smm)
                guard("vector", ("vx", tt))
                vec("tensor_add", s_vx[:, tt, :], p_bg0[:, 0:128],
                    b_wvb[:, 128 * p:128 * (p + 1)], ctr=sve)
                settok(("psA",), sve)
                settok(("vx", tt), sve)
            settok(("wvx", vsl), smm)
            for tc in range(4):
                guard("tensor", ("psA",))
                mm(p_bg0[0:64, 0:512], w_watt[0:64, p, :],
                   s_xnT[0:64, p, 512 * tc:512 * (tc + 1)], tp=(0, 0),
                   start=True, stop=True)
                guard("tensor", ("psB",))
                mm(p_bg1[64:128, 0:512], w_watt[64:128, p, :],
                   s_xnT[64:128, p, 512 * tc:512 * (tc + 1)], tp=(64, 64),
                   start=True, stop=True, ctr=smm)
                wait("vector", smm)
                guard("vector", ("xw", tc))
                vec("tensor_copy", s_xw[0:64, tc, :], p_bg0[0:64, 0:512])
                vec("tensor_copy", s_xw[64:128, tc, :], p_bg1[64:128, 0:512],
                    ctr=sve)
                settok(("psA",), sve)
                settok(("psB",), sve)
                settok(("xw", tc), sve)
            wait("tensor", sve)

            for tc in range(4):
                pend = []
                for sch in range(NT):
                    sl0 = (sch % 2) * 2
                    sl1 = sl0 + 1
                    guard("tensor", ("att0",))
                    mm(p_att0[:, 0:512],
                       s_xnT[0:64, p, 128 * sch:128 * (sch + 1)],
                       s_xw[0:64, tc, :], tp=(0, 0), ctr=smm)
                    att0_tok = smm.v
                    guard("tensor", ("att1",))
                    mm(p_att1[:, 0:512],
                       s_xnT[64:128, p, 128 * sch:128 * (sch + 1)],
                       s_xw[64:128, tc, :], tp=(64, 0), ctr=smm)
                    att1_tok = smm.v
                    for ps_ in pend:
                        tok[ps_] = (smm, att1_tok)
                    pend = []
                    guard("scalar", ("pT", sl0))
                    wait("scalar", smm, att0_tok)
                    act(s_pT[:, sl0, :], p_att0[:, 0:512], AF.Exp,
                        scale=0.125, ctr=sex)
                    settok(("att0",), sex)
                    guard("scalar", ("pT", sl1))
                    wait("scalar", smm, att1_tok)
                    act(s_pT[:, sl1, :], p_att1[:, 0:512], AF.Exp,
                        scale=0.125, ctr=sex)
                    settok(("att1",), sex)
                    wait("tensor", sex)
                    first = (sch == 0)
                    last = (sch == NT - 1)
                    mm(p_o0[0:64, :], s_vx[:, sch, 0:64], s_pT[:, sl0, :],
                       tp=(0, 0), start=first, stop=last)
                    mm(p_o1[64:128, :], s_vx[:, sch, 64:128], s_pT[:, sl1, :],
                       tp=(0, 64), start=first, stop=last)
                    mm(p_rs0[0:64, :], c_ones[:], s_pT[:, sl0, :],
                       tp=(0, 0), start=first, stop=last)
                    mm(p_rs1[64:128, :], c_ones[:], s_pT[:, sl1, :],
                       tp=(0, 64), start=first, stop=last,
                       ctr=smm if last else None)
                    pend = [("pT", sl0), ("pT", sl1)]
                o_tok = smm.v
                for ps_ in pend:
                    tok[ps_] = (smm, o_tok)
                pend = []
                wait("vector", smm, o_tok)
                guard("vector", ("rstmp",))
                vec("tensor_copy", s_rstmp[0:64, :], p_rs0[0:64, :])
                vec("tensor_copy", s_rstmp[64:128, :], p_rs1[64:128, :])
                vec("reciprocal", s_rstmp[:], s_rstmp[:], ctr=sve)
                rs_tok = sve.v
                slA = inv_slot[0]
                slB = slA + 1
                inv_slot[0] += 2
                wait("sync", sve, rs_tok)
                dma("sync", d_inv[slA:slA + 1, :], s_rstmp[0:1, :],
                    sems["dw_inva"])
                dma("sync", d_inv[slB:slB + 1, :], s_rstmp[64:65, :],
                    sems["dw_inva"])
                settok(("rstmp",), sems["dw_inva"])
                wait("sync", sems["dw_inva"])
                guard("sync", ("invb",))
                dma("sync", s_invb[0:64, :],
                    d_inv[slA:slA + 1, :].to_broadcast([64, 512]),
                    sems["dw_invb"])
                dma("sync", s_invb[64:128, :],
                    d_inv[slB:slB + 1, :].to_broadcast([64, 512]),
                    sems["dw_invb"])
                wait("vector", sems["dw_invb"])
                vec("tensor_mul", s_oT[0:64, p, 512 * tc:512 * (tc + 1)],
                    p_o0[0:64, :], s_invb[0:64, :])
                vec("tensor_mul", s_oT[64:128, p, 512 * tc:512 * (tc + 1)],
                    p_o1[64:128, :], s_invb[64:128, :], ctr=sve)
                settok(("invb",), sve)
                settok(("o_rs",), sve)
                guard("tensor", ("o_rs",))
            settok(("vxall",), smm)
        sve_att = sve.v
        if DEBUG:
            wait("sync", sve, sve_att)
            dma("sync", dbg["ot"][:], s_oT[:], sq)

        mark()
        ow16v = owT_h.rearrange("(a p) f -> p a f", p=128)
        owbuf = w_w1[:]
        wait("vector", sve, sve_att)
        for fh in range(2):
            guard("vector", ("w1",))
            guard("vector", ("owb",))
            for p in range(NP):
                wload(owbuf[:, p, :], ow16v[:, p, 512 * fh:512 * (fh + 1)])
            wait("tensor", sve)
            wait("tensor", sve, sve_att)
            for tg in range(8):
                for ti in range(2):
                    tt = 2 * tg + ti
                    pp = (p_bg0, p_bg1)[ti]
                    pk = (("psA",), ("psB",))[ti]
                    guard("tensor", pk)
                    for p in range(NP):
                        mm(pp[:, 0:512], s_oT[:, p, 128 * tt:128 * (tt + 1)],
                           owbuf[:, p, :], start=(p == 0), stop=(p == NP - 1),
                           ctr=smm if p == NP - 1 else None)
                    wait("vector", smm)
                    vec("tensor_add", satt(tt)[:, 512 * fh:512 * (fh + 1)],
                        pp[:, 0:512], b_owb[:, 512 * fh:512 * (fh + 1)],
                        ctr=sve)
                    settok(pk, sve)
            settok(("owb",), smm)
        sve_op = sve.v
        if DEBUG:
            wait("sync", sve, sve_op)
            for tt in range(NT):
                dma("sync", dbg["satt"][128 * tt:128 * (tt + 1), :],
                    satt(tt), sq)

        mark()
        wait("gpsimd", sems["c_y"])
        wait("vector", sve, sve_op)
        for tt in range(NT):
            for k in range(EK):
                sl = k % 2
                gsem = sems["dw_y6a"] if sl == 0 else sems["dw_y6b"]
                guard("gpsimd", ("y6", sl))
                igat(s_y6[:, sl, :], d_yall[:], s_gidx[:, tt, k:k + 1], gsem)
                wait("vector", gsem)
                guard("vector", ("moe_scr",)) if k == 0 else None
                if k == 0:
                    vec("tensor_scalar_mul", s_moe[:], s_y6[:, sl, :],
                        s_pw[:, tt, 0:1], ctr=sve)
                else:
                    vec("scalar_tensor_tensor", s_moe[:], s_y6[:, sl, :],
                        s_pw[:, tt, k:k + 1], s_moe[:], ALU.mult, ALU.add,
                        ctr=sve)
                settok(("y6", sl), sve)
            shsem = sems["dw_sh2a"] if tt % 2 == 0 else sems["dw_sh2b"]
            guard("sync", ("sh2ld", tt % 2))
            if tt == 0:
                wait("sync", sq, sq_sh2)
            dma("sync", s_sh2st[:, tt % 2, :],
                d_sh2[128 * tt:128 * (tt + 1), :], shsem)
            wait("vector", shsem)
            vec("tensor_add", s_moe[:], s_moe[:], s_sh2st[:, tt % 2, :],
                ctr=sve)
            settok(("sh2ld", tt % 2), sve)
            wait("scalar", sve)
            guard("scalar", ("gst", tt % 2))
            act(s_xn_st[:, tt % 2, :], s_moe[:], AF.Silu, ctr=ssc)
            settok(("moe_scr",), ssc)
            xsem = sems["dw_xa"] if tt % 2 == 0 else sems["dw_xb"]
            guard("sync", ("xt2", tt % 2))
            dma("sync", s_xt[:, tt % 2, :], x_h[128 * tt:128 * (tt + 1), :],
                xsem)
            wait("vector", ssc)
            wait("vector", xsem)
            vec("tensor_mul", s_moe[:], satt(tt), s_xn_st[:, tt % 2, :],
                ctr=sve)
            settok(("gst", tt % 2), sve)
            vec("tensor_add", s_xt[:, tt % 2, :], s_xt[:, tt % 2, :],
                s_moe[:], ctr=sve)
            wait("sync", sve)
            dma("sync", out_h[128 * tt:128 * (tt + 1), :], s_xt[:, tt % 2, :],
                sq)
            settok(("xt2", tt % 2), sq)

        mark()
        wait("sync", sq, sq.v)
        wait("gpsimd", gq, gq.v)

        _flush("vector")
        _flush("scalar")
        mark()
        if PHASE_LIMIT is not None:
            cut = marks[PHASE_LIMIT]
            for k in Q:
                Q[k] = Q[k][:cut[k]]

        with nc.Block() as block:
            @block.sync
            def _(e):
                for fn in Q["sync"]:
                    fn(e)

            @block.gpsimd
            def _(e):
                for fn in Q["gpsimd"]:
                    fn(e)

            @block.tensor
            def _(e):
                for fn in Q["tensor"]:
                    fn(e)

            @block.scalar
            def _(e):
                for fn in Q["scalar"]:
                    fn(e)

            @block.vector
            def _(e):
                for fn in Q["vector"]:
                    fn(e)

    return nc, dbg


_NC_CACHE = {}


def _get_nc():
    if "nc" not in _NC_CACHE:
        _NC_CACHE["nc"] = build_nc()
    return _NC_CACHE["nc"]


def kernel(L0_attn_x, L0_attn_w_att_val, L0_attn_wv_w, L0_attn_wv_b,
           L0_attn_out_w, L0_attn_out_b, L0_ffn_router_weight,
           L0_ffn_expert_bias, L0_ffn_w1, L0_ffn_w2, L0_ffn_sw1, L0_ffn_sw2):
    x = np.ascontiguousarray(np.asarray(L0_attn_x, dtype=np.float32))
    watt = np.asarray(L0_attn_w_att_val, dtype=np.float32)
    wv = np.asarray(L0_attn_wv_w, dtype=np.float32)
    wvb = np.asarray(L0_attn_wv_b, dtype=np.float32)
    ow = np.asarray(L0_attn_out_w, dtype=np.float32)
    owb = np.asarray(L0_attn_out_b, dtype=np.float32)
    rw = np.asarray(L0_ffn_router_weight, dtype=np.float32)
    eb = np.asarray(L0_ffn_expert_bias, dtype=np.float32)
    w1 = np.asarray(L0_ffn_w1, dtype=np.float32)
    w2 = np.asarray(L0_ffn_w2, dtype=np.float32)
    sw1 = np.asarray(L0_ffn_sw1, dtype=np.float32)
    sw2 = np.asarray(L0_ffn_sw2, dtype=np.float32)

    wvT = np.ascontiguousarray(wv.T)
    owT = np.ascontiguousarray(ow.T)
    sw1T = np.ascontiguousarray(sw1.T)
    sw2T = np.ascontiguousarray(sw2.T)
    rwT = np.ascontiguousarray(rw.T)
    wattp = np.zeros((NP, 128, 64), np.float32)
    for p in range(NP):
        wattp[p, 0:64] = watt[2 * p]
        wattp[p, 64:128] = watt[2 * p + 1]
    u128 = np.triu(np.ones((128, 128), np.float32), 1)
    uc = np.zeros((64, 64), np.float32)
    for tt in range(16):
        uc[0, tt] = 64.0
        uc[1, tt] = 1.0
        for i in range(tt):
            uc[2 + i, tt] = 1.0
    iota = np.tile(np.arange(NE, dtype=np.float32), (128, 1))
    onesc = np.ones((128, 64), np.float32)

    nc, dbg = _get_nc()
    in_maps = []
    for c in range(8):
        in_maps.append(dict(
            x=np.ascontiguousarray(x[c]),
            wvT=wvT, owT=owT, sw1T=sw1T, sw2T=sw2T, rwT=rwT,
            watt=wattp,
            wvb=wvb.reshape(1, D), owb=owb.reshape(1, D),
            ebias=eb.reshape(1, NE),
            w1=np.ascontiguousarray(w1[ELOC * c:ELOC * (c + 1)]),
            w2=np.ascontiguousarray(w2[ELOC * c:ELOC * (c + 1)]),
            tokid=np.arange(T * c, T * (c + 1), dtype=np.float32).reshape(T, 1),
            pm=(np.arange(8, dtype=np.float32) < c).astype(np.float32)
            .reshape(1, 8),
            u128=u128, uc=uc, iota=iota, onesc=onesc,
        ))
    res = run_bass_kernel_spmd(nc, in_maps, core_ids=list(range(8)))
    out = np.stack([np.asarray(res.results[c]["out"]) for c in range(8)],
                   axis=0)
    if DEBUG:
        kernel.last_results = res.results
    return out


# version 05
# speedup vs baseline: 1.1889x; 1.0194x over previous
from contextlib import ExitStack

import numpy as np

import concourse.bass as bass
import concourse.mybir as mybir
from concourse.bass_utils import run_bass_kernel_spmd
from concourse.bass_core_rust import IndirectOffsetOnAxis

F32 = mybir.dt.float32
BF16 = mybir.dt.bfloat16
I32 = mybir.dt.int32
U32 = mybir.dt.uint32
AF = mybir.ActivationFunctionType
ALU = mybir.AluOpType
AX = mybir.AxisListType
RG8 = [[0, 1, 2, 3, 4, 5, 6, 7]]

T = 2048
D = 1024
NT = 16
NJ = 8
NP = 8
NE = 64
EK = 6
CAP = 384
EF = 512
SF = 2048
ELOC = 8
NU = 8
UT = 256

DEBUG = False
PHASE_LIMIT = None


class Ctr:
    __slots__ = ("h", "v", "is_dma", "dedicated")

    def __init__(self, h, is_dma=False, dedicated=False):
        self.h = h
        self.v = 0
        self.is_dma = is_dma
        self.dedicated = dedicated


def build_nc():
    nc = bass.Bass("TRN2", target_bir_lowering=False)

    def inp(name, shape, dt=F32):
        return nc.dram_tensor(name, shape, dt, kind="ExternalInput").ap()

    x_h = inp("x", [T, D])
    wvT_h = inp("wvT", [D, D])
    owT_h = inp("owT", [D, D])
    sw1T_h = inp("sw1T", [D, SF])
    sw2T_h = inp("sw2T", [SF, D])
    rwT_h = inp("rwT", [D, NE])
    watt_h = inp("watt", [NP, 128, 64])
    wvb_h = inp("wvb", [1, D])
    owb_h = inp("owb", [1, D])
    ebias_h = inp("ebias", [1, NE])
    w1_h = inp("w1", [ELOC, D, EF])
    w2_h = inp("w2", [ELOC, EF, D])
    tokid_h = inp("tokid", [T, 1])
    pm_h = inp("pm", [1, 8])
    u128_h = inp("u128", [128, 128])
    uc_h = inp("uc", [64, 64])
    iota_h = inp("iota", [128, NE])
    onesc_h = inp("onesc", [128, 64])

    out_h = nc.dram_tensor("out", [T, D], F32, kind="ExternalOutput").ap()

    def idram(name, shape, dt=F32):
        return nc.dram_tensor(name, shape, dt, kind="Internal").ap()

    d_xn = idram("d_xn", [T, D], BF16)
    d_xnall = idram("d_xnall", [8 * T, D], BF16)
    d_cnt = idram("d_cnt", [1, NE])
    d_cs = idram("d_cs", [8, NE])
    d_ss = idram("d_ss", [16, NE])
    d_choff = idram("d_choff", [NT, NE])
    d_tokmap = idram("d_tokmap", [25088, 1])
    d_tokmy = idram("d_tokmy", [3136, 1])
    d_xs = idram("d_xs", [ELOC * CAP, D], BF16)
    d_ymine = idram("d_ymine", [ELOC * CAP, D], BF16)
    d_yall = idram("d_yall", [NE * CAP, D], BF16)
    d_sh2 = idram("d_sh2", [T, D], BF16)
    d_inv = idram("d_inv", [64, 512])
    d_wv16 = idram("d_wv16", [D, D], BF16)
    d_sw1b = idram("d_sw1b", [D, SF], BF16)
    d_ow16 = idram("d_ow16", [D, D], BF16)
    d_w116 = idram("d_w116", [ELOC, D, EF], BF16)
    d_w216 = idram("d_w216", [ELOC, EF, D], BF16)

    dbg = {}

    def dbgout(name, shape, dt):
        if DEBUG:
            dbg[name] = nc.dram_tensor("dbg_" + name, shape, dt,
                                       kind="ExternalOutput").ap()

    dbgout("xn", [T, D], BF16)
    dbgout("cnt", [1, NE], F32)
    dbgout("choff", [NT, NE], F32)
    dbgout("if6", [128, NT, 8], F32)
    dbgout("pw", [128, NT, 8], F32)
    dbgout("gidx", [128, NT, 8], I32)
    dbgout("tokmy", [3136, 1], F32)
    dbgout("y", [ELOC * CAP, D], BF16)
    dbgout("sh2", [T, D], BF16)
    dbgout("ot", [128, NP, T], BF16)
    dbgout("satt", [T, D], BF16)

    st = ExitStack()
    with st:
        def sb(name, shape, dt):
            return st.enter_context(nc.sbuf_tensor(name, shape, dt))

        def psum(name, shape):
            return st.enter_context(nc.psum_tensor(name, shape, F32))

        c_iota = sb("c_iota", [128, NE], F32)
        c_ones = sb("c_ones", [128, 64], BF16)
        c_u128 = sb("c_u128", [128, 128], BF16)
        c_uc = sb("c_uc", [64, 64], BF16)
        w_watt = sb("w_watt", [128, NP, 64], BF16)
        w_rwT = sb("w_rwT", [128, NJ, NE], BF16)
        w_sw2r = sb("w_sw2r", [128, 16, D], BF16)
        w_sw1s = sb("w_sw1s", [128, 2, NJ, 128], BF16)
        w_vx = sb("w_vx", [128, 2, NJ, 128], BF16)
        w_ow = sb("w_ow", [128, 2, 512], BF16)
        w_w1 = sb("w_w1", [128, NJ, EF], BF16)
        s_wf32 = sb("s_wf32", [128, 4, 512], F32)
        b_wvb = sb("b_wvb", [128, D], F32)
        b_owb = sb("b_owb", [128, D], F32)
        b_eb = sb("b_eb", [128, NE], F32)
        s_pm = sb("s_pm", [1, 8], F32)
        s_tokid = sb("s_tokid", [128, NT], F32)

        s_xnT = sb("s_xnT", [128, NJ, T], BF16)
        s_oT = sb("s_oT", [128, NP, T], BF16)
        s_vx = sb("s_vx", [128, NT, 132], BF16)
        s_xw = sb("s_xw", [128, 4, 512], BF16)
        s_p0 = sb("s_p0", [128, 512], BF16)
        s_p1 = sb("s_p1", [128, 512], BF16)
        s_p2 = sb("s_p2", [128, 512], BF16)
        s_p3 = sb("s_p3", [128, 512], BF16)
        s_hraw = sb("s_hraw", [128, 16, UT], BF16)
        s_xn_st = sb("s_xn_st", [128, 2, D], BF16)
        s_sh2st = sb("s_sh2st", [128, 2, D], BF16)
        s_A = sb("s_A", [128, NT, NE], BF16)
        s_xsT = sb("s_xsT", [128, NJ, CAP], BF16)
        s_h1r = sb("s_h1r", [128, 8, CAP], BF16)
        s_yst = sb("s_yst", [128, 2, D], BF16)
        s_y6 = sb("s_y6", [128, 2, D], BF16)

        s_xt = sb("s_xt", [128, 2, D], F32)
        s_moe = sb("s_moe", [128, D], F32)
        s_lns = sb("s_lns", [128, 8], F32)
        s_scores = sb("s_scores", [128, NE], F32)
        s_biased = sb("s_biased", [128, NE], F32)
        s_m8 = sb("s_m8", [128, 8], F32)
        s_i8 = sb("s_i8", [128, 8], U32)
        s_if6a = sb("s_if6a", [128, NT, 8], F32)
        s_Af = sb("s_Af", [128, NE], F32)
        s_Af2 = sb("s_Af2", [128, NE], F32)
        s_scr64 = sb("s_scr64", [128, NE], F32)
        s_sc6 = sb("s_sc6", [128, 8], F32)
        s_pw = sb("s_pw", [128, NT, 8], F32)
        s_r6 = sb("s_r6", [128, 8], F32)
        s_v6 = sb("s_v6", [128, 8], F32)
        s_t8a = sb("s_t8a", [128, 8], F32)
        s_t8b = sb("s_t8b", [128, 8], F32)
        s_sidx = sb("s_sidx", [128, NT, 8], I32)
        s_gidx = sb("s_gidx", [128, NT, 8], I32)
        s_r6i = sb("s_r6i", [128, 8], I32)
        s_cnt = sb("s_cnt", [1, NE], F32)
        s_csf = sb("s_csf", [1, NE], F32)
        s_co = sb("s_co", [1, NE], F32)
        s_co2 = sb("s_co2", [1, NE], F32)
        s_coi = sb("s_coi", [1, NE], I32)
        s_cq = sb("s_cq", [1, NE], I32)
        s_cqf = sb("s_cqf", [1, NE], F32)
        s_crf = sb("s_crf", [1, NE], F32)
        s_stage = sb("s_stage", [64, 64], BF16)
        s_choff = sb("s_choff", [64, NE], F32)
        s_offb = sb("s_offb", [128, 2, NE], F32)
        s_C = sb("s_C", [128, NE], F32)
        s_Ct = sb("s_Ct", [128, NE], F32)
        s_invA = sb("s_invA", [64, 512], F32)
        s_invB = sb("s_invB", [64, 512], F32)
        s_tmf = sb("s_tmf", [128, 24], F32)
        s_tmidx = sb("s_tmidx", [128, 24], I32)
        s_zero = sb("s_zero", [128, 256], F32)
        s_junk = sb("s_junk", [128, 32], F32)

        p_att0 = psum("p_att0", [128, 512])
        p_att1 = psum("p_att1", [128, 512])
        p_o0 = psum("p_o0", [128, 512])
        p_o1 = psum("p_o1", [128, 512])
        p_rs0 = psum("p_rs0", [128, 512])
        p_rs1 = psum("p_rs1", [128, 512])
        p_bg0 = psum("p_bg0", [128, 512])
        p_bg1 = psum("p_bg1", [128, 512])

        names = "gq sq mm ex ve sc c_xn c_cnt c_tok c_y vsy ssy".split()
        sems = {nm: Ctr(st.enter_context(nc.semaphore("sem_" + nm)),
                        is_dma=nm in ("gq", "sq"))
                for nm in names}
        dnames = ("dw_sw1a dw_sw1b dw_vxa dw_vxb dw_w1 dw_w2 dw_xsT dw_ow "
                  "dw_xa dw_xb dw_inva dw_invb dw_y6a dw_y6b dw_sh2a dw_sh2b "
                  "dw_gath").split()
        for nm in dnames:
            sems[nm] = Ctr(st.enter_context(nc.semaphore("sem_" + nm)),
                           is_dma=True, dedicated=True)
        gq, sq = sems["gq"], sems["sq"]
        _wsems = [sems[n] for n in ("dw_w1", "dw_w2", "dw_sw1a", "dw_sw1b")]
        smm, sex, sve, ssc = sems["mm"], sems["ex"], sems["ve"], sems["sc"]

        Q = {k: [] for k in ("sync", "gpsimd", "tensor", "scalar", "vector")}
        marks = []

        def mark():
            _flush("vector")
            _flush("scalar")
            marks.append({k: len(v) for k, v in Q.items()})
        tok = {}

        def q(eng, fn):
            Q[eng].append(fn)

        _fence = {}

        def wait(eng, ctr, val=None):
            if ctr.dedicated:
                v = ctr.v if val is None else val
            else:
                v = ctr.v if (val is None or ctr.is_dma) else val
            h = ctr.h
            if ctr.is_dma and not ctr.dedicated:
                owner = "sync" if ctr is sems["sq"] else "gpsimd"
                if _fence.get(id(ctr), -1) < v:
                    _fence[id(ctr)] = v
                    if eng != owner:
                        q(owner, lambda e: e.wait_ge(h, v))
            fn = lambda e: e.wait_ge(h, v)
            if eng in ("vector", "scalar") and _stage[eng] is not None:
                _swaits[eng].append(fn)
            else:
                q(eng, fn)

        def guard(eng, key):
            if key in tok:
                ctr, v = tok[key]
                wait(eng, ctr, v)

        def settok(key, ctr):
            tok[key] = (ctr, ctr.v)

        def dma(eng, dst, src, ctr):
            ctr.v += 16
            h = ctr.h
            q(eng, lambda e: e.dma_start(dst, src).then_inc(h, 16))

        def dmaT(eng, dst, src, ctr):
            ctr.v += 16
            h = ctr.h
            q(eng, lambda e: e.dma_start_transpose(dst, src).then_inc(h, 16))

        def dmaslow(eng, dst, src, ctr):
            ctr.v += 16
            h = ctr.h
            q(eng, lambda e: e.dma_start(
                dst, src, allow_slow_non_contiguous=True).then_inc(h, 16))

        def mm(out, lhsT, rhs, start=True, stop=True, tp=None, ctr=None):
            if ctr is not None:
                ctr.v += 1
                h = ctr.h
                q("tensor", lambda e: e.matmul(
                    out, lhsT, rhs, start=start, stop=stop,
                    tile_position=tp).then_inc(h, 1))
            else:
                q("tensor", lambda e: e.matmul(out, lhsT, rhs, start=start,
                                               stop=stop, tile_position=tp))

        _stage = {"scalar": None, "vector": None}
        _swaits = {"scalar": [], "vector": []}
        _prevt = {"scalar": set(), "vector": set()}

        def _names(objs):
            out = set()
            for o in objs:
                t = getattr(o, "tensor", None)
                if t is not None:
                    out.add(t.name)
            return out

        def _flush(eng, attach=False):
            if _stage[eng] is None:
                return
            b, touched, has_ctr = _stage[eng]
            if attach:
                sc_ = sems["vsy"] if eng == "vector" else sems["ssy"]
                sc_.v += 1
                h, v = sc_.h, sc_.v

                def _att(e, b=b, h=h, has_ctr=has_ctr, eng=eng):
                    r = b(e)
                    if r is None or has_ctr:
                        if eng == "vector":
                            e.engine_nop()
                            r = e.engine_nop()
                        else:
                            e.activation(s_junk[:, 0:8], s_junk[:, 0:8],
                                         AF.Identity)
                            r = e.activation(s_junk[:, 8:16], s_junk[:, 8:16],
                                             AF.Identity)
                    r.then_inc(h, 1)
                Q[eng].append(_att)
                for w in _swaits[eng]:
                    Q[eng].append(w)
                Q[eng].append(lambda e, h=h, v=v: e.wait_ge(h, v))
            else:
                Q[eng].append(lambda e, b=b: b(e))
                for w in _swaits[eng]:
                    Q[eng].append(w)
            _swaits[eng] = []
            _prevt[eng] = set() if attach else touched
            _stage[eng] = None

        def _emit_eng(eng, builder, touched, has_ctr=False):
            if _stage[eng] is not None:
                conflict = bool((_stage[eng][1] | _prevt[eng]) & touched)
                _flush(eng, attach=conflict)
            else:
                if _prevt[eng] & touched:
                    sc_ = sems["vsy"] if eng == "vector" else sems["ssy"]
                    sc_.v += 1
                    h, v = sc_.h, sc_.v

                    def _np(e, h=h, eng=eng):
                        if eng == "vector":
                            e.engine_nop()
                            e.engine_nop().then_inc(h, 1)
                        else:
                            e.activation(s_junk[:, 0:8], s_junk[:, 0:8],
                                         AF.Identity)
                            e.activation(s_junk[:, 8:16], s_junk[:, 8:16],
                                         AF.Identity).then_inc(h, 1)
                    Q[eng].append(_np)
                    Q[eng].append(lambda e, h=h, v=v: e.wait_ge(h, v))
                    _prevt[eng] = set()
            _stage[eng] = (builder, touched, has_ctr)

        def act(out, in_, func, ctr=None, **kw):
            touched = _names([out, in_, kw.get("accum_out"), kw.get("bias"),
                              kw.get("scale")])
            if ctr is not None:
                ctr.v += 1
                h = ctr.h
                b = lambda e: e.activation(out, in_, func, **kw).then_inc(h, 1)
                _emit_eng("scalar", b, touched, has_ctr=True)
            else:
                b = lambda e: e.activation(out, in_, func, **kw)
                _emit_eng("scalar", b, touched)

        def vec(meth, *a, ctr=None, **kw):
            touched = _names(list(a) + list(kw.values()))
            if ctr is not None:
                ctr.v += 1
                h = ctr.h
                b = lambda e: getattr(e, meth)(*a, **kw).then_inc(h, 1)
                _emit_eng("vector", b, touched, has_ctr=True)
            else:
                b = lambda e: getattr(e, meth)(*a, **kw)
                _emit_eng("vector", b, touched)

        def coll(kind, op, ins_, outs_, ctr):
            ctr.v += 1
            h = ctr.h
            q("gpsimd", lambda e: e.collective_compute(
                kind, op, replica_groups=RG8, ins=ins_, outs=outs_
            ).then_inc(h, 1))

        _bc_reg = {}

        def iscat(out_ap, idx_ap, in_ap, ctr, bc):
            ctr.v += 16
            h = ctr.h

            def fn(e):
                if bc not in _bc_reg:
                    _bc_reg[bc] = e.to_reg(bc)
                e.indirect_dma_start(
                    out_ap, IndirectOffsetOnAxis(idx_ap, 0), in_ap, None,
                    bounds_check=_bc_reg[bc], oob_is_err=False).then_inc(h, 16)
            q("gpsimd", fn)

        def igat(out_ap, in_ap, idx_ap, ctr):
            ctr.v += 16
            h = ctr.h
            q("gpsimd", lambda e: e.indirect_dma_start(
                out_ap, None, in_ap, IndirectOffsetOnAxis(idx_ap, 0)
            ).then_inc(h, 16))

        satt_flat = s_xnT[:].rearrange("p a t -> p (a t)")

        def satt(tt):
            return satt_flat[:, D * tt:D * (tt + 1)]

        _wslot = [0]

        def wload(dst, src, a=None):
            sl = _wslot[0] % 4
            _wslot[0] += 1
            dsem = _wsems[sl]
            stg = s_wf32[:, sl, :]
            if a is not None:
                stg = stg.rearrange("p (a f) -> p a f", a=a)
            _flush("scalar")
            guard("scalar", ("wf", sl))
            dma("scalar", stg, src, dsem)
            wait("vector", dsem)
            vec("tensor_copy", dst, stg, ctr=sve)
            settok(("wf", sl), sve)

        vec("memset", s_junk[:], 0.0)
        vec("memset", s_zero[:], 0.0)
        vec("memset", s_stage[:], 0.0)
        vec("memset", s_p0[:], 0.0)
        vec("memset", s_p1[:], 0.0)
        vec("memset", s_p2[:], 0.0)
        vec("memset", s_p3[:], 0.0, ctr=sve)
        zero_tok = sve.v
        wait("scalar", sve, zero_tok)
        act(s_junk[:, 0:16], s_junk[:, 0:16], AF.Identity)
        act(s_junk[:, 16:32], s_junk[:, 16:32], AF.Identity)

        dma("gpsimd", c_iota[:], iota_h[:], gq)
        dma("gpsimd", c_ones[:], onesc_h[:], gq)
        dma("gpsimd", c_u128[:], u128_h[:], gq)
        dma("gpsimd", c_uc[:], uc_h[:], gq)
        dma("gpsimd", w_watt[:], watt_h.rearrange("a p f -> p a f"), gq)
        dma("gpsimd", w_rwT[:], rwT_h.rearrange("(a p) f -> p a f", p=128), gq)
        gq_const = gq.v

        gq_cast = gq.v

        dma("sync", b_wvb[:], wvb_h[0:1, :].to_broadcast([128, D]), sq)
        dma("sync", b_owb[:], owb_h[0:1, :].to_broadcast([128, D]), sq)
        dma("sync", b_eb[:], ebias_h[0:1, :].to_broadcast([128, NE]), sq)
        dma("sync", s_pm[:], pm_h[:], sq)
        dmaslow("sync", s_tokid[:],
                tokid_h.rearrange("(a p) o -> p (a o)", p=128), sq)
        sq_const = sq.v

        for tt in range(NT):
            sl = tt % 2
            xsem = sems["dw_xa"] if sl == 0 else sems["dw_xb"]
            guard("sync", ("xt", sl))
            dma("sync", s_xt[:, sl, :], x_h[128 * tt:128 * (tt + 1), :], xsem)
            wait("vector", xsem)
            vec("reduce_sum", s_lns[:, 0:1], s_xt[:, sl, :], AX.X)
            wait("scalar", xsem)
            guard("scalar", ("moe_scr",))
            act(s_moe[:], s_xt[:, sl, :], AF.Square, ctr=ssc,
                accum_out=s_lns[:, 1:2])
            settok(("moe_scr",), ssc)
            wait("vector", ssc)
            vec("tensor_scalar_mul", s_lns[:, 2:3], s_lns[:, 0:1], 1.0 / D)
            vec("tensor_scalar_mul", s_lns[:, 3:4], s_lns[:, 1:2], 1.0 / D)
            vec("tensor_mul", s_lns[:, 4:5], s_lns[:, 2:3], s_lns[:, 2:3])
            vec("tensor_sub", s_lns[:, 5:6], s_lns[:, 3:4], s_lns[:, 4:5],
                ctr=sve)
            vec("tensor_scalar_add", s_lns[:, 5:6], s_lns[:, 5:6], 1e-5,
                ctr=sve)
            wait("scalar", sve)
            act(s_lns[:, 6:7], s_lns[:, 5:6], AF.Sqrt, ctr=ssc)
            wait("vector", ssc)
            vec("reciprocal", s_lns[:, 7:8], s_lns[:, 6:7])
            guard("vector", ("xnst", sl))
            vec("scalar_tensor_tensor", s_xn_st[:, sl, :], s_xt[:, sl, :],
                s_lns[:, 2:3], s_lns[:, 7:8].to_broadcast([128, D]),
                ALU.subtract, ALU.mult, ctr=sve)
            settok(("xt", sl), sve)
            wait("sync", sve)
            dma("sync", d_xn[128 * tt:128 * (tt + 1), :], s_xn_st[:, sl, :], sq)
            settok(("xnst", sl), sq)
        sq_xn = sq.v

        wait("gpsimd", sq, sq_xn)
        coll("AllGather", ALU.bypass, [d_xn[:]], [d_xnall[:]], sems["c_xn"])

        wait("sync", sq, sq_xn)
        for jj in range(NJ):
            dmaT("sync", s_xnT[:, jj, :], d_xn[:, 128 * jj:128 * (jj + 1)], sq)
        sq_xnT = sq.v

        if DEBUG:
            dma("sync", dbg["xn"][:], d_xn[:], sq)

        mark()
        wait("tensor", sq, sq_xnT)
        wait("tensor", gq, gq_const)
        wait("vector", sq, sq_const)
        for tt in range(NT):
            guard("tensor", ("psA",))
            for j in range(NJ):
                mm(p_bg0[:, 0:NE], s_xnT[:, j, 128 * tt:128 * (tt + 1)],
                   w_rwT[:, j, :], start=(j == 0), stop=(j == NJ - 1),
                   ctr=smm if j == NJ - 1 else None)
            wait("scalar", smm)
            act(s_scores[:], p_bg0[:, 0:NE], AF.Sigmoid, ctr=ssc)
            settok(("psA",), ssc)
            wait("vector", ssc)
            vec("tensor_add", s_biased[:], s_scores[:], b_eb[:])
            vec("max", out=s_m8[:], in_=s_biased[:])
            vec("max_index", s_i8[:], s_m8[:], s_biased[:])
            vec("tensor_copy", s_if6a[:, tt, :], s_i8[:])
            vec("memset", s_Af[:], 0.0)
            cur, nxt = s_Af, s_Af2
            for k in range(EK):
                vec("scalar_tensor_tensor", nxt[:], c_iota[:],
                    s_if6a[:, tt, k:k + 1], cur[:], ALU.is_equal, ALU.add)
                cur, nxt = nxt, cur
                vec("scalar_tensor_tensor", s_scr64[:], c_iota[:],
                    s_if6a[:, tt, k:k + 1], s_scores[:], ALU.is_equal,
                    ALU.mult, accum_out=s_sc6[:, k:k + 1])
            vec("tensor_copy", s_A[:, tt, :], cur[:])
            vec("reduce_sum", s_t8a[:, 0:1], s_sc6[:, 0:EK], AX.X)
            vec("tensor_scalar_add", s_t8a[:, 1:2], s_t8a[:, 0:1], 1e-20)
            vec("reciprocal", s_t8a[:, 2:3], s_t8a[:, 1:2])
            vec("tensor_scalar_mul", s_pw[:, tt, 0:EK], s_sc6[:, 0:EK],
                s_t8a[:, 2:3], ctr=sve)
            wait("tensor", sve)
            guard("tensor", ("psB",))
            mm(p_bg1[0:64, 0:NE], c_ones[:], s_A[:, tt, :], ctr=smm)
            wait("vector", smm)
            guard("vector", ("cntst",))
            vec("tensor_copy", s_cnt[:], p_bg1[0:1, 0:NE], ctr=sve)
            settok(("psB",), sve)
            wait("sync", sve)
            dma("sync", d_ss[tt:tt + 1, :], s_cnt[:], sq)
            settok(("cntst",), sq)
        sq_ss = sq.v

        wait("gpsimd", sq, sq_ss)
        wait("gpsimd", sve, zero_tok)
        dma("gpsimd", s_stage[0:16, :], d_ss[:], gq)
        wait("tensor", gq)
        guard("tensor", ("psB",))
        mm(p_bg1[0:64, 0:NE], c_ones[0:64, :], s_stage[:], ctr=smm)
        wait("vector", smm)
        guard("vector", ("cntst",))
        vec("tensor_copy", s_cnt[:], p_bg1[0:1, 0:NE], ctr=sve)
        settok(("psB",), sve)
        wait("sync", sve)
        dma("sync", d_cnt[:], s_cnt[:], sq)
        settok(("cntst",), sq)
        wait("gpsimd", sq, sq.v)
        coll("AllGather", ALU.bypass, [d_cnt[:]], [d_cs[:]], sems["c_cnt"])

        mark()
        sw2v = sw2T_h.rearrange("(a p) f -> p a f", p=128)
        for jj in range(16):
            for h2 in range(2):
                wload(w_sw2r[:, jj, 512 * h2:512 * (h2 + 1)],
                      sw2v[:, jj, 512 * h2:512 * (h2 + 1)])
        sw2r_tok = sve.v
        wait("vector", sems["c_cnt"])
        vec("memset", s_co[:], 0.0)
        cur, nxt = s_co, s_co2
        for cc in range(8):
            wait("sync", sems["c_cnt"])
            guard("sync", ("csf",))
            dma("sync", s_csf[:], d_cs[cc:cc + 1, :], sq)
            wait("vector", sq)
            vec("scalar_tensor_tensor", nxt[:], s_csf[:],
                s_pm[:, cc:cc + 1], cur[:], ALU.mult, ALU.add, ctr=sve)
            settok(("csf",), sve)
            cur, nxt = nxt, cur
        vec("tensor_copy", s_coi[:], cur[:])
        vec("tensor_scalar", s_cq[:], s_coi[:], 6, None, ALU.arith_shift_right)
        vec("tensor_scalar", s_coi[:], s_coi[:], 63, None, ALU.bitwise_and)
        vec("tensor_copy", s_cqf[:], s_cq[:])
        vec("tensor_copy", s_crf[:], s_coi[:], ctr=sve)
        sve_qr = sve.v
        wait("gpsimd", sve, sve_qr)
        dma("gpsimd", s_stage[0:1, :], s_cqf[:], gq)
        dma("gpsimd", s_stage[1:2, :], s_crf[:], gq)
        dma("gpsimd", s_stage[2:18, :], d_ss[:], gq)
        wait("tensor", gq)
        wait("tensor", sve, sve_qr)
        guard("tensor", ("psA",))
        mm(p_bg0[0:64, 0:NE], c_uc[:], s_stage[:], ctr=smm)
        wait("vector", smm)
        vec("tensor_copy", s_choff[:], p_bg0[0:64, 0:NE], ctr=sve)
        settok(("psA",), sve)
        wait("sync", sve)
        dma("sync", d_choff[:], s_choff[0:16, :], sq)
        sq_choff = sq.v
        if DEBUG:
            dma("sync", dbg["choff"][:], d_choff[:], sq)

        wait("gpsimd", sve, zero_tok)
        dma("gpsimd", d_tokmap[:].rearrange("(p b) o -> p (b o)", p=128),
            s_zero[:, 0:196], gq)
        gq_tmz = gq.v

        for tt in range(NT):
            osl = tt % 2
            guard("sync", ("offb", osl))
            wait("sync", sq, sq_choff)
            dma("sync", s_offb[:, osl, :],
                d_choff[tt:tt + 1, :].to_broadcast([128, NE]), sq)
            off_ld = sq.v
            guard("tensor", ("psB",))
            mm(p_bg1[:, 0:NE], c_u128[:], s_A[:, tt, :], ctr=smm)
            wait("vector", smm)
            wait("vector", sq, off_ld)
            vec("tensor_copy", s_C[:], p_bg1[:, 0:NE], ctr=sve)
            settok(("psB",), sve)
            vec("tensor_add", s_Ct[:], s_C[:], s_offb[:, osl, :], ctr=sve)
            settok(("offb", osl), sve)
            for k in range(EK):
                vec("scalar_tensor_tensor", s_scr64[:], c_iota[:],
                    s_if6a[:, tt, k:k + 1], s_Ct[:], ALU.is_equal, ALU.mult,
                    accum_out=s_r6[:, k:k + 1])
            E6 = slice(0, EK)
            vec("tensor_scalar", s_v6[:, E6], s_r6[:, E6], float(CAP), None,
                ALU.is_lt)
            vec("tensor_mul", s_pw[:, tt, E6], s_pw[:, tt, E6], s_v6[:, E6])
            vec("tensor_scalar", s_t8a[:, E6], s_if6a[:, tt, E6], float(CAP),
                None, ALU.mult)
            vec("tensor_add", s_t8a[:, E6], s_t8a[:, E6], s_r6[:, E6])
            vec("tensor_copy", s_r6i[:, E6], s_if6a[:, tt, E6])
            vec("tensor_scalar", s_r6i[:, E6], s_r6i[:, E6], 3, None,
                ALU.arith_shift_right)
            vec("tensor_copy", s_t8b[:, E6], s_r6i[:, E6])
            vec("tensor_scalar", s_t8b[:, E6], s_t8b[:, E6], 64.0, None,
                ALU.mult)
            vec("tensor_add", s_t8b[:, E6], s_t8b[:, E6], s_t8a[:, E6])
            vec("tensor_scalar", s_scr64[:, 0:EK], s_v6[:, E6], -2e6, 2e6,
                ALU.mult, ALU.add)
            vec("tensor_add", s_t8b[:, E6], s_t8b[:, E6], s_scr64[:, 0:EK])
            vec("tensor_copy", s_sidx[:, tt, E6], s_t8b[:, E6])
            vec("tensor_scalar_min", s_t8a[:, E6], s_t8a[:, E6],
                float(NE * CAP - 1))
            vec("tensor_copy", s_gidx[:, tt, E6], s_t8a[:, E6], ctr=sve)
        sve_route = sve.v

        wait("gpsimd", sve, sve_route)
        wait("gpsimd", gq, gq_tmz)
        for tt in range(NT):
            for k in range(EK):
                iscat(d_tokmap[:], s_sidx[:, tt, k:k + 1],
                      s_tokid[:, tt:tt + 1], gq, 25087)
        wait("gpsimd", gq)
        coll("ReduceScatter", ALU.add, [d_tokmap[:]], [d_tokmy[:]],
             sems["c_tok"])
        if DEBUG:
            wait("sync", sems["c_tok"])
            dma("sync", dbg["tokmy"][:], d_tokmy[:], sq)
            dma("sync", dbg["cnt"][:], d_cnt[:], sq)
            wait("sync", sve, sve_route)
            dma("sync", dbg["if6"][:], s_if6a[:], sq)
            dma("sync", dbg["pw"][:], s_pw[:], sq)
            dma("sync", dbg["gidx"][:], s_gidx[:], sq)

        mark()
        wait("tensor", sve, sw2r_tok)
        for u in range(NU):
            tsl = slice(UT * u, UT * (u + 1))
            sw1v = sw1T_h.rearrange("(a p) f -> p a f", p=128)
            for ft in range(16):
                slw = ft % 2
                guard("vector", ("sw1", slw))
                for h2 in range(2):
                    wload(w_sw1s[:, slw, 4 * h2:4 * (h2 + 1), :],
                          sw1v[:, 4 * h2:4 * (h2 + 1),
                               128 * ft:128 * (ft + 1)], a=4)
                cast_tok = sve.v
                pp = p_bg0 if ft % 2 == 0 else p_bg1
                pk = ("psA",) if ft % 2 == 0 else ("psB",)
                guard("tensor", pk)
                wait("tensor", sve, cast_tok)
                for j in range(NJ):
                    mm(pp[:, 0:UT], w_sw1s[:, slw, j, :], s_xnT[:, j, tsl],
                       start=(j == 0), stop=(j == NJ - 1),
                       ctr=smm if j == NJ - 1 else None)
                settok(("sw1", slw), smm)
                wait("vector", smm)
                vec("tensor_copy", s_hraw[:, ft, :], pp[:, 0:UT], ctr=sve)
                settok(pk, sve)
            wait("scalar", sve)
            act(s_hraw[:].rearrange("p a f -> p (a f)"),
                s_hraw[:].rearrange("p a f -> p (a f)"),
                AF.Gelu_apprx_tanh, ctr=ssc)
            wait("tensor", ssc)
            for tq in range(2):
                tt = 2 * u + tq
                for ih in range(2):
                    pp = p_bg0 if ih == 0 else p_bg1
                    pk = ("psA",) if ih == 0 else ("psB",)
                    guard("tensor", pk)
                    for ft in range(16):
                        mm(pp[:, 0:512],
                           s_hraw[:, ft, 128 * tq:128 * (tq + 1)],
                           w_sw2r[:, ft, 512 * ih:512 * (ih + 1)],
                           start=(ft == 0), stop=(ft == 15),
                           ctr=smm if ft == 15 else None)
                    wait("vector", smm)
                    guard("vector", ("sh2st", tt % 2))
                    vec("tensor_copy",
                        s_sh2st[:, tt % 2, 512 * ih:512 * (ih + 1)],
                        pp[:, 0:512], ctr=sve)
                    settok(pk, sve)
                wait("sync", sve)
                dma("sync", d_sh2[128 * tt:128 * (tt + 1), :],
                    s_sh2st[:, tt % 2, :], sq)
                settok(("sh2st", tt % 2), sq)
        sq_sh2 = sq.v
        if DEBUG:
            wait("sync", sq, sq_sh2)
            dma("sync", dbg["sh2"][:], d_sh2[:], sq)

        mark()
        wait("gpsimd", sems["c_tok"])
        wait("gpsimd", sems["c_xn"])
        dmaslow("gpsimd", s_tmf[:],
                d_tokmy[0:3072, :].rearrange("(j p) o -> p (j o)", p=128), gq)
        wait("vector", gq)
        vec("tensor_copy", s_tmidx[:], s_tmf[:], ctr=sve)
        wait("gpsimd", sve)
        for jj in range(24):
            sl = jj % 2
            gsem = sems["dw_y6a"] if sl == 0 else sems["dw_y6b"]
            guard("gpsimd", ("xsg", sl))
            igat(s_y6[:, sl, :], d_xnall[:], s_tmidx[:, jj:jj + 1], gsem)
            wait("gpsimd", gsem)
            dma("gpsimd", d_xs[128 * jj:128 * (jj + 1), :], s_y6[:, sl, :], gq)
            settok(("xsg", sl), gq)
        gq_xs = gq.v

        w2view = w_w1[:].rearrange("p a f -> p (a f)").rearrange(
            "p (b g) -> p b g", b=4)

        for eg in range(4):
            for el in range(2):
                e = 2 * eg + el
                guard("sync", ("xsT",))
                wait("sync", gq, gq_xs)
                for j in range(NJ):
                    dmaT("sync", s_xsT[:, j, :],
                         d_xs[CAP * e:CAP * (e + 1), 128 * j:128 * (j + 1)],
                         sems["dw_xsT"])
                wait("tensor", sems["dw_xsT"])
                guard("vector", ("w1",))
                w1v = w1_h[e].rearrange("(a p) f -> p a f", p=128)
                for j in range(NJ):
                    wload(w_w1[:, j, :], w1v[:, j, :])
                wait("tensor", sve)
                for ft in range(4):
                    pp = p_bg0 if ft % 2 == 0 else p_bg1
                    pk = ("psA",) if ft % 2 == 0 else ("psB",)
                    guard("tensor", pk)
                    for j in range(NJ):
                        mm(pp[:, 0:CAP], w_w1[:, j, 128 * ft:128 * (ft + 1)],
                           s_xsT[:, j, :], start=(j == 0), stop=(j == NJ - 1),
                           ctr=smm if j == NJ - 1 else None)
                    wait("vector", smm)
                    vec("tensor_copy", s_h1r[:, 4 * el + ft, :], pp[:, 0:CAP],
                        ctr=sve)
                    settok(pk, sve)
                settok(("xsT",), smm)
                settok(("w1",), smm)
            wait("scalar", sve)
            act(s_h1r[:].rearrange("p a f -> p (a f)"),
                s_h1r[:].rearrange("p a f -> p (a f)"),
                AF.Gelu_apprx_tanh, ctr=ssc)
            gelu_tok = ssc.v
            wait("tensor", ssc, gelu_tok)
            for el in range(2):
                e = 2 * eg + el
                guard("vector", ("w1",))
                w2v = w2_h[e].rearrange("(a p) f -> p a f", p=128)
                for c4 in range(4):
                    for h2 in range(2):
                        wload(w2view[:, c4, 512 * h2:512 * (h2 + 1)],
                              w2v[:, c4, 512 * h2:512 * (h2 + 1)])
                wait("tensor", sve)
                for ct in range(3):
                    for ih in range(2):
                        pp = p_bg0 if ih == 0 else p_bg1
                        pk = ("psA",) if ih == 0 else ("psB",)
                        guard("tensor", pk)
                        for f in range(4):
                            mm(pp[:, 0:512],
                               s_h1r[:, 4 * el + f, 128 * ct:128 * (ct + 1)],
                               w2view[:, f, 512 * ih:512 * (ih + 1)],
                               start=(f == 0), stop=(f == 3),
                               ctr=smm if f == 3 else None)
                        wait("vector", smm)
                        guard("vector", ("yst", ct % 2))
                        vec("tensor_copy",
                            s_yst[:, ct % 2, 512 * ih:512 * (ih + 1)],
                            pp[:, 0:512], ctr=sve)
                        settok(pk, sve)
                    wait("sync", sve)
                    dma("sync",
                        d_ymine[CAP * e + 128 * ct:CAP * e + 128 * (ct + 1), :],
                        s_yst[:, ct % 2, :], sq)
                    settok(("yst", ct % 2), sq)
                settok(("w1",), smm)
        sq_y = sq.v
        wait("gpsimd", sq, sq_y)
        coll("AllGather", ALU.bypass, [d_ymine[:]], [d_yall[:]], sems["c_y"])
        if DEBUG:
            dma("sync", dbg["y"][:], d_ymine[:], sq)

        mark()
        wait("sync", gq, gq_cast)
        vec("memset", s_vx[:, :, 64:65], 1.0)
        vec("memset", s_vx[:, :, 129:130], 1.0, ctr=sve)
        inv_slot = [0]
        for p in range(NP):
            guard("vector", ("vxall",))
            vsl = p % 2
            dsem = sems["dw_vxa"] if vsl == 0 else sems["dw_vxb"]
            guard("gpsimd", ("wvx", vsl))
            dma("gpsimd", w_vx[:, vsl, :, :],
                wvT_h.rearrange("(a p) f -> p a f", p=128)
                [:, :, 128 * p:128 * (p + 1)], dsem)
            wait("tensor", dsem)
            for tt in range(NT):
                guard("tensor", ("psA",))
                for j in range(NJ):
                    mm(p_bg0[:, 0:128], s_xnT[:, j, 128 * tt:128 * (tt + 1)],
                       w_vx[:, vsl, j, :], start=(j == 0), stop=(j == NJ - 1),
                       ctr=smm if j == NJ - 1 else None)
                wait("vector", smm)
                guard("vector", ("vx", tt))
                vec("tensor_add", s_vx[:, tt, 0:64], p_bg0[:, 0:64],
                    b_wvb[:, 128 * p:128 * p + 64])
                vec("tensor_add", s_vx[:, tt, 65:129], p_bg0[:, 64:128],
                    b_wvb[:, 128 * p + 64:128 * (p + 1)], ctr=sve)
                settok(("psA",), sve)
                settok(("vx", tt), sve)
            settok(("wvx", vsl), smm)
            for tc in range(4):
                guard("tensor", ("psA",))
                mm(p_bg0[0:64, 0:512], w_watt[0:64, p, :],
                   s_xnT[0:64, p, 512 * tc:512 * (tc + 1)], tp=(0, 0),
                   start=True, stop=True)
                guard("tensor", ("psB",))
                mm(p_bg1[64:128, 0:512], w_watt[64:128, p, :],
                   s_xnT[64:128, p, 512 * tc:512 * (tc + 1)], tp=(64, 64),
                   start=True, stop=True, ctr=smm)
                wait("vector", smm)
                guard("vector", ("xw", tc))
                vec("tensor_copy", s_xw[0:64, tc, :], p_bg0[0:64, 0:512])
                vec("tensor_copy", s_xw[64:128, tc, :], p_bg1[64:128, 0:512],
                    ctr=sve)
                settok(("psA",), sve)
                settok(("psB",), sve)
                settok(("xw", tc), sve)
            wait("tensor", sve)

            for tc in range(4):
                pend = []
                for sch in range(NT):
                    par = sch % 2
                    sl0 = par * 2
                    sl1 = sl0 + 1
                    pbA = p_att0 if par == 0 else p_rs0
                    pbB = p_att1 if par == 0 else p_rs1
                    pA = (s_p0, s_p1, s_p2, s_p3)[sl0]
                    pB = (s_p0, s_p1, s_p2, s_p3)[sl1]
                    guard("tensor", ("attA", par))
                    mm(pbA[:, 0:512],
                       s_xnT[0:64, p, 128 * sch:128 * (sch + 1)],
                       s_xw[0:64, tc, :], tp=(0, 0), ctr=smm)
                    att0_tok = smm.v
                    guard("tensor", ("attB", par))
                    mm(pbB[:, 0:512],
                       s_xnT[64:128, p, 128 * sch:128 * (sch + 1)],
                       s_xw[64:128, tc, :], tp=(64, 0), ctr=smm)
                    att1_tok = smm.v
                    for ps_ in pend:
                        tok[ps_] = (smm, att1_tok)
                    pend = []
                    guard("scalar", ("pT", sl0))
                    wait("scalar", smm, att0_tok)
                    act(pA[:], pbA[:, 0:512], AF.Exp, scale=0.125, ctr=sex)
                    settok(("attA", par), sex)
                    guard("scalar", ("pT", sl1))
                    wait("scalar", smm, att1_tok)
                    act(pB[:], pbB[:, 0:512], AF.Exp, scale=0.125, ctr=sex)
                    settok(("attB", par), sex)
                    wait("tensor", sex)
                    first = (sch == 0)
                    last = (sch == NT - 1)
                    mm(p_o0[0:65, :], s_vx[:, sch, 0:65], pA[:],
                       tp=(0, 0), start=first, stop=last)
                    mm(p_o1[0:65, :], s_vx[:, sch, 65:130], pB[:],
                       tp=(0, 0), start=first, stop=last,
                       ctr=smm if last else None)
                    pend = [("pT", sl0), ("pT", sl1)]
                o_tok = smm.v
                for ps_ in pend:
                    tok[ps_] = (smm, o_tok)
                pend = []
                slA = inv_slot[0]
                slB = slA + 1
                inv_slot[0] += 2
                wait("vector", smm, o_tok)
                guard("vector", ("xt", 0))
                vec("tensor_copy", s_xt[64:65, 0, 0:512], p_o0[64:65, :])
                vec("tensor_copy", s_xt[64:65, 0, 512:1024], p_o1[64:65, :],
                    ctr=sve)
                wait("sync", sve)
                dma("sync", d_inv[slA:slA + 1, :], s_xt[64:65, 0, 0:512],
                    sems["dw_inva"])
                dma("sync", d_inv[slB:slB + 1, :], s_xt[64:65, 0, 512:1024],
                    sems["dw_inva"])
                settok(("xt", 0), sems["dw_inva"])
                settok(("xt2", 0), sems["dw_inva"])
                wait("sync", sems["dw_inva"])
                guard("sync", ("invb",))
                dma("sync", s_invA[0:64, :],
                    d_inv[slA:slA + 1, :].to_broadcast([64, 512]),
                    sems["dw_invb"])
                dma("sync", s_invB[0:64, :],
                    d_inv[slB:slB + 1, :].to_broadcast([64, 512]),
                    sems["dw_invb"])
                wait("vector", sems["dw_invb"])
                vec("reciprocal", s_invA[:], s_invA[:])
                vec("reciprocal", s_invB[:], s_invB[:])
                vec("tensor_mul", s_oT[0:64, p, 512 * tc:512 * (tc + 1)],
                    p_o0[0:64, :], s_invA[:])
                guard("vector", ("pT", 0))
                vec("tensor_mul", s_p0[0:64, :], p_o1[0:64, :], s_invB[:],
                    ctr=sve)
                settok(("invb",), sve)
                settok(("o_rs",), sve)
                wait("sync", sve)
                dma("sync", s_oT[64:128, p, 512 * tc:512 * (tc + 1)],
                    s_p0[0:64, :], sq)
                settok(("pT", 0), sq)
                guard("tensor", ("o_rs",))
            settok(("vxall",), smm)
        sve_att = sve.v
        sq_oB = sq.v
        if DEBUG:
            wait("sync", sve, sve_att)
            dma("sync", dbg["ot"][:], s_oT[:], sq)

        mark()
        ow16v = owT_h.rearrange("(a p) f -> p a f", p=128)
        owbuf = w_w1[:]
        wait("vector", sve, sve_att)
        wait("tensor", sq, sq_oB)
        for fh in range(2):
            guard("vector", ("w1",))
            guard("vector", ("owb",))
            for p in range(NP):
                wload(owbuf[:, p, :], ow16v[:, p, 512 * fh:512 * (fh + 1)])
            wait("tensor", sve)
            wait("tensor", sve, sve_att)
            for tg in range(8):
                for ti in range(2):
                    tt = 2 * tg + ti
                    pp = (p_bg0, p_bg1)[ti]
                    pk = (("psA",), ("psB",))[ti]
                    guard("tensor", pk)
                    for p in range(NP):
                        mm(pp[:, 0:512], s_oT[:, p, 128 * tt:128 * (tt + 1)],
                           owbuf[:, p, :], start=(p == 0), stop=(p == NP - 1),
                           ctr=smm if p == NP - 1 else None)
                    wait("vector", smm)
                    vec("tensor_add", satt(tt)[:, 512 * fh:512 * (fh + 1)],
                        pp[:, 0:512], b_owb[:, 512 * fh:512 * (fh + 1)],
                        ctr=sve)
                    settok(pk, sve)
            settok(("owb",), smm)
        sve_op = sve.v
        if DEBUG:
            wait("sync", sve, sve_op)
            for tt in range(NT):
                dma("sync", dbg["satt"][128 * tt:128 * (tt + 1), :],
                    satt(tt), sq)

        mark()
        wait("gpsimd", sems["c_y"])
        wait("vector", sve, sve_op)
        for tt in range(NT):
            for k in range(EK):
                sl = k % 2
                gsem = sems["dw_y6a"] if sl == 0 else sems["dw_y6b"]
                guard("gpsimd", ("y6", sl))
                igat(s_y6[:, sl, :], d_yall[:], s_gidx[:, tt, k:k + 1], gsem)
                wait("vector", gsem)
                guard("vector", ("moe_scr",)) if k == 0 else None
                if k == 0:
                    vec("tensor_scalar_mul", s_moe[:], s_y6[:, sl, :],
                        s_pw[:, tt, 0:1], ctr=sve)
                else:
                    vec("scalar_tensor_tensor", s_moe[:], s_y6[:, sl, :],
                        s_pw[:, tt, k:k + 1], s_moe[:], ALU.mult, ALU.add,
                        ctr=sve)
                settok(("y6", sl), sve)
            shsem = sems["dw_sh2a"] if tt % 2 == 0 else sems["dw_sh2b"]
            guard("sync", ("sh2ld", tt % 2))
            if tt == 0:
                wait("sync", sq, sq_sh2)
            dma("sync", s_sh2st[:, tt % 2, :],
                d_sh2[128 * tt:128 * (tt + 1), :], shsem)
            wait("vector", shsem)
            vec("tensor_add", s_moe[:], s_moe[:], s_sh2st[:, tt % 2, :],
                ctr=sve)
            settok(("sh2ld", tt % 2), sve)
            wait("scalar", sve)
            guard("scalar", ("gst", tt % 2))
            act(s_xn_st[:, tt % 2, :], s_moe[:], AF.Silu, ctr=ssc)
            settok(("moe_scr",), ssc)
            xsem = sems["dw_xa"] if tt % 2 == 0 else sems["dw_xb"]
            guard("sync", ("xt2", tt % 2))
            dma("sync", s_xt[:, tt % 2, :], x_h[128 * tt:128 * (tt + 1), :],
                xsem)
            wait("vector", ssc)
            wait("vector", xsem)
            vec("tensor_mul", s_moe[:], satt(tt), s_xn_st[:, tt % 2, :],
                ctr=sve)
            settok(("gst", tt % 2), sve)
            vec("tensor_add", s_xt[:, tt % 2, :], s_xt[:, tt % 2, :],
                s_moe[:], ctr=sve)
            wait("sync", sve)
            dma("sync", out_h[128 * tt:128 * (tt + 1), :], s_xt[:, tt % 2, :],
                sq)
            settok(("xt2", tt % 2), sq)

        mark()
        wait("sync", sq, sq.v)
        wait("gpsimd", gq, gq.v)

        _flush("vector")
        _flush("scalar")
        mark()
        if PHASE_LIMIT is not None:
            cut = marks[PHASE_LIMIT]
            for k in Q:
                Q[k] = Q[k][:cut[k]]

        with nc.Block() as block:
            @block.sync
            def _(e):
                for fn in Q["sync"]:
                    fn(e)

            @block.gpsimd
            def _(e):
                for fn in Q["gpsimd"]:
                    fn(e)

            @block.tensor
            def _(e):
                for fn in Q["tensor"]:
                    fn(e)

            @block.scalar
            def _(e):
                for fn in Q["scalar"]:
                    fn(e)

            @block.vector
            def _(e):
                for fn in Q["vector"]:
                    fn(e)

    return nc, dbg


_NC_CACHE = {}


def _get_nc():
    if "nc" not in _NC_CACHE:
        _NC_CACHE["nc"] = build_nc()
    return _NC_CACHE["nc"]


def kernel(L0_attn_x, L0_attn_w_att_val, L0_attn_wv_w, L0_attn_wv_b,
           L0_attn_out_w, L0_attn_out_b, L0_ffn_router_weight,
           L0_ffn_expert_bias, L0_ffn_w1, L0_ffn_w2, L0_ffn_sw1, L0_ffn_sw2):
    x = np.ascontiguousarray(np.asarray(L0_attn_x, dtype=np.float32))
    watt = np.asarray(L0_attn_w_att_val, dtype=np.float32)
    wv = np.asarray(L0_attn_wv_w, dtype=np.float32)
    wvb = np.asarray(L0_attn_wv_b, dtype=np.float32)
    ow = np.asarray(L0_attn_out_w, dtype=np.float32)
    owb = np.asarray(L0_attn_out_b, dtype=np.float32)
    rw = np.asarray(L0_ffn_router_weight, dtype=np.float32)
    eb = np.asarray(L0_ffn_expert_bias, dtype=np.float32)
    w1 = np.asarray(L0_ffn_w1, dtype=np.float32)
    w2 = np.asarray(L0_ffn_w2, dtype=np.float32)
    sw1 = np.asarray(L0_ffn_sw1, dtype=np.float32)
    sw2 = np.asarray(L0_ffn_sw2, dtype=np.float32)

    wvT = np.ascontiguousarray(wv.T)
    owT = np.ascontiguousarray(ow.T)
    sw1T = np.ascontiguousarray(sw1.T)
    sw2T = np.ascontiguousarray(sw2.T)
    rwT = np.ascontiguousarray(rw.T)
    wattp = np.zeros((NP, 128, 64), np.float32)
    for p in range(NP):
        wattp[p, 0:64] = watt[2 * p]
        wattp[p, 64:128] = watt[2 * p + 1]
    u128 = np.triu(np.ones((128, 128), np.float32), 1)
    uc = np.zeros((64, 64), np.float32)
    for tt in range(16):
        uc[0, tt] = 64.0
        uc[1, tt] = 1.0
        for i in range(tt):
            uc[2 + i, tt] = 1.0
    iota = np.tile(np.arange(NE, dtype=np.float32), (128, 1))
    onesc = np.ones((128, 64), np.float32)

    nc, dbg = _get_nc()
    in_maps = []
    for c in range(8):
        in_maps.append(dict(
            x=np.ascontiguousarray(x[c]),
            wvT=wvT, owT=owT, sw1T=sw1T, sw2T=sw2T, rwT=rwT,
            watt=wattp,
            wvb=wvb.reshape(1, D), owb=owb.reshape(1, D),
            ebias=eb.reshape(1, NE),
            w1=np.ascontiguousarray(w1[ELOC * c:ELOC * (c + 1)]),
            w2=np.ascontiguousarray(w2[ELOC * c:ELOC * (c + 1)]),
            tokid=np.arange(T * c, T * (c + 1), dtype=np.float32).reshape(T, 1),
            pm=(np.arange(8, dtype=np.float32) < c).astype(np.float32)
            .reshape(1, 8),
            u128=u128, uc=uc, iota=iota, onesc=onesc,
        ))
    res = run_bass_kernel_spmd(nc, in_maps, core_ids=list(range(8)))
    out = np.stack([np.asarray(res.results[c]["out"]) for c in range(8)],
                   axis=0)
    if DEBUG:
        kernel.last_results = res.results
    return out


# version 06
# speedup vs baseline: 1.4294x; 1.2023x over previous
from contextlib import ExitStack

import numpy as np

import concourse.bass as bass
import concourse.mybir as mybir
from concourse.bass_utils import run_bass_kernel_spmd
from concourse.bass_core_rust import IndirectOffsetOnAxis

F32 = mybir.dt.float32
BF16 = mybir.dt.bfloat16
I32 = mybir.dt.int32
U32 = mybir.dt.uint32
AF = mybir.ActivationFunctionType
ALU = mybir.AluOpType
AX = mybir.AxisListType
RG8 = [[0, 1, 2, 3, 4, 5, 6, 7]]

T = 2048
D = 1024
NT = 16
NJ = 8
NP = 8
NE = 64
EK = 6
CAP = 384
EF = 512
SF = 2048
ELOC = 8
NU = 8
UT = 256

DEBUG = False
PHASE_LIMIT = None


class Ctr:
    __slots__ = ("h", "v", "is_dma", "dedicated")

    def __init__(self, h, is_dma=False, dedicated=False):
        self.h = h
        self.v = 0
        self.is_dma = is_dma
        self.dedicated = dedicated


def build_nc():
    nc = bass.Bass("TRN2", target_bir_lowering=False)

    def inp(name, shape, dt=F32):
        return nc.dram_tensor(name, shape, dt, kind="ExternalInput").ap()

    x_h = inp("x", [T, D])
    wvT_h = inp("wvT", [D, D])
    owT_h = inp("owT", [D, D])
    sw1T_h = inp("sw1T", [D, SF])
    sw2T_h = inp("sw2T", [SF, D])
    rwT_h = inp("rwT", [D, NE])
    watt_h = inp("watt", [NP, 128, 64])
    wvb_h = inp("wvb", [1, D])
    owb_h = inp("owb", [1, D])
    ebias_h = inp("ebias", [1, NE])
    w1_h = inp("w1", [ELOC, D, EF])
    w2_h = inp("w2", [ELOC, EF, D])
    tokid_h = inp("tokid", [T, 1])
    pm_h = inp("pm", [1, 8])
    u128_h = inp("u128", [128, 128])
    uc_h = inp("uc", [64, 64])
    iota_h = inp("iota", [128, NE])
    onesc_h = inp("onesc", [128, 64])

    out_h = nc.dram_tensor("out", [T, D], F32, kind="ExternalOutput").ap()

    def idram(name, shape, dt=F32):
        return nc.dram_tensor(name, shape, dt, kind="Internal").ap()

    d_xn = idram("d_xn", [T, D], BF16)
    d_xnall = idram("d_xnall", [8 * T, D], BF16)
    d_cnt = idram("d_cnt", [1, NE])
    d_cs = idram("d_cs", [8, NE])
    d_ss = idram("d_ss", [16, NE])
    d_choff = idram("d_choff", [NT, NE])
    d_tokmap = idram("d_tokmap", [25088, 1])
    d_tokmy = idram("d_tokmy", [3136, 1])
    d_xs = idram("d_xs", [ELOC * CAP, D], BF16)
    d_ymine = idram("d_ymine", [ELOC * CAP, D], BF16)
    d_yall = idram("d_yall", [NE * CAP, D], BF16)
    d_sh2 = idram("d_sh2", [T, D], BF16)
    d_inv = idram("d_inv", [64, 512])
    d_wv16 = idram("d_wv16", [D, D], BF16)
    d_sw1b = idram("d_sw1b", [D, SF], BF16)
    d_ow16 = idram("d_ow16", [D, D], BF16)
    d_w116 = idram("d_w116", [ELOC, D, EF], BF16)
    d_w216 = idram("d_w216", [ELOC, EF, D], BF16)

    dbg = {}

    def dbgout(name, shape, dt):
        if DEBUG:
            dbg[name] = nc.dram_tensor("dbg_" + name, shape, dt,
                                       kind="ExternalOutput").ap()

    dbgout("xn", [T, D], BF16)
    dbgout("cnt", [1, NE], F32)
    dbgout("choff", [NT, NE], F32)
    dbgout("if6", [128, NT, 8], F32)
    dbgout("pw", [128, NT, 8], F32)
    dbgout("gidx", [128, NT, 8], I32)
    dbgout("tokmy", [3136, 1], F32)
    dbgout("y", [ELOC * CAP, D], BF16)
    dbgout("sh2", [T, D], BF16)
    dbgout("ot", [128, NP, T], BF16)
    dbgout("satt", [T, D], BF16)

    st = ExitStack()
    with st:
        def sb(name, shape, dt):
            return st.enter_context(nc.sbuf_tensor(name, shape, dt))

        def psum(name, shape):
            return st.enter_context(nc.psum_tensor(name, shape, F32))

        c_iota = sb("c_iota", [128, NE], F32)
        c_ones = sb("c_ones", [128, 64], BF16)
        c_u128 = sb("c_u128", [128, 128], BF16)
        c_uc = sb("c_uc", [64, 64], BF16)
        w_watt = sb("w_watt", [128, NP, 64], BF16)
        w_rwT = sb("w_rwT", [128, NJ, NE], BF16)
        w_sw2r = sb("w_sw2r", [128, 16, D], BF16)
        w_sw1s = sb("w_sw1s", [128, 2, NJ, 128], BF16)
        w_vx = sb("w_vx", [128, 2, NJ, 128], BF16)
        w_ow = sb("w_ow", [128, 2, 512], BF16)
        w_w1 = sb("w_w1", [128, NJ, EF], BF16)
        s_wf32 = sb("s_wf32", [128, 4, 512], F32)
        b_wvb = sb("b_wvb", [128, D], F32)
        b_owb = sb("b_owb", [128, D], F32)
        b_eb = sb("b_eb", [128, NE], F32)
        s_pm = sb("s_pm", [1, 8], F32)
        s_tokid = sb("s_tokid", [128, NT], F32)

        s_xnT = sb("s_xnT", [128, NJ, T], BF16)
        s_oT = sb("s_oT", [128, NP, T], BF16)
        s_vx = sb("s_vx", [128, NT, 132], BF16)
        s_xw = sb("s_xw", [128, 4, 512], BF16)
        s_p0 = sb("s_p0", [128, 512], BF16)
        s_p1 = sb("s_p1", [128, 512], BF16)
        s_p2 = sb("s_p2", [128, 512], BF16)
        s_p3 = sb("s_p3", [128, 512], BF16)
        s_hraw = sb("s_hraw", [128, 16, UT], BF16)
        s_xn_st = sb("s_xn_st", [128, 2, D], BF16)
        s_sh2st = sb("s_sh2st", [128, 2, D], BF16)
        s_A = sb("s_A", [128, NT, NE], BF16)
        s_xsT = sb("s_xsT", [128, NJ, CAP], BF16)
        s_h1r = sb("s_h1r", [128, 8, CAP], BF16)
        s_yst = sb("s_yst", [128, 2, D], BF16)
        s_y6 = sb("s_y6", [128, 2, D], BF16)

        s_xt = sb("s_xt", [128, 2, D], F32)
        s_moe = sb("s_moe", [128, D], F32)
        s_lns = sb("s_lns", [128, 8], F32)
        s_scores = sb("s_scores", [128, NE], F32)
        s_biased = sb("s_biased", [128, NE], F32)
        s_m8 = sb("s_m8", [128, 8], F32)
        s_i8 = sb("s_i8", [128, 8], U32)
        s_if6a = sb("s_if6a", [128, NT, 8], F32)
        s_Af = sb("s_Af", [128, NE], F32)
        s_Af2 = sb("s_Af2", [128, NE], F32)
        s_scr64 = sb("s_scr64", [128, NE], F32)
        s_sc6 = sb("s_sc6", [128, 8], F32)
        s_pw = sb("s_pw", [128, NT, 8], F32)
        s_r6 = sb("s_r6", [128, 8], F32)
        s_v6 = sb("s_v6", [128, 8], F32)
        s_t8a = sb("s_t8a", [128, 8], F32)
        s_t8b = sb("s_t8b", [128, 8], F32)
        s_sidx = sb("s_sidx", [128, NT, 8], I32)
        s_gidx = sb("s_gidx", [128, NT, 8], I32)
        s_r6i = sb("s_r6i", [128, 8], I32)
        s_cnt = sb("s_cnt", [1, NE], F32)
        s_csf = sb("s_csf", [1, NE], F32)
        s_co = sb("s_co", [1, NE], F32)
        s_co2 = sb("s_co2", [1, NE], F32)
        s_coi = sb("s_coi", [1, NE], I32)
        s_cq = sb("s_cq", [1, NE], I32)
        s_cqf = sb("s_cqf", [1, NE], F32)
        s_crf = sb("s_crf", [1, NE], F32)
        s_stage = sb("s_stage", [64, 64], BF16)
        s_choff = sb("s_choff", [64, NE], F32)
        s_offb = sb("s_offb", [128, 2, NE], F32)
        s_C = sb("s_C", [128, NE], F32)
        s_Ct = sb("s_Ct", [128, NE], F32)
        s_invA = sb("s_invA", [64, 512], F32)
        s_invB = sb("s_invB", [64, 512], F32)
        s_tmf = sb("s_tmf", [128, 24], F32)
        s_tmidx = sb("s_tmidx", [128, 24], I32)
        s_zero = sb("s_zero", [128, 256], F32)
        s_junk = sb("s_junk", [128, 32], F32)

        p_att0 = psum("p_att0", [128, 512])
        p_att1 = psum("p_att1", [128, 512])
        p_o0 = psum("p_o0", [128, 512])
        p_o1 = psum("p_o1", [128, 512])
        p_rs0 = psum("p_rs0", [128, 512])
        p_rs1 = psum("p_rs1", [128, 512])
        p_bg0 = psum("p_bg0", [128, 512])
        p_bg1 = psum("p_bg1", [128, 512])

        names = "gq sq mm ex ve sc c_xn c_cnt c_tok c_y vsy ssy".split()
        sems = {nm: Ctr(st.enter_context(nc.semaphore("sem_" + nm)),
                        is_dma=nm in ("gq", "sq"))
                for nm in names}
        dnames = ("dw_sw1a dw_sw1b dw_vxa dw_vxb dw_w1 dw_w2 dw_xsT dw_ow "
                  "dw_xa dw_xb dw_inva dw_invb dw_y6a dw_y6b dw_sh2a dw_sh2b "
                  "dw_gath").split()
        for nm in dnames:
            sems[nm] = Ctr(st.enter_context(nc.semaphore("sem_" + nm)),
                           is_dma=True, dedicated=True)
        gq, sq = sems["gq"], sems["sq"]
        _wsems = [sems[n] for n in ("dw_w1", "dw_w2", "dw_sw1a", "dw_sw1b")]
        smm, sex, sve, ssc = sems["mm"], sems["ex"], sems["ve"], sems["sc"]

        Q = {k: [] for k in ("sync", "gpsimd", "tensor", "scalar", "vector")}
        marks = []

        def mark():
            _flush("vector")
            _flush("scalar")
            marks.append({k: len(v) for k, v in Q.items()})
        tok = {}

        def q(eng, fn):
            Q[eng].append(fn)

        _fence = {}

        def wait(eng, ctr, val=None):
            if ctr.dedicated:
                v = ctr.v if val is None else val
            else:
                v = ctr.v if (val is None or ctr.is_dma) else val
            h = ctr.h
            if ctr.is_dma and not ctr.dedicated:
                owner = "sync" if ctr is sems["sq"] else "gpsimd"
                if _fence.get(id(ctr), -1) < v:
                    _fence[id(ctr)] = v
                    if eng != owner:
                        q(owner, lambda e: e.wait_ge(h, v))
            fn = lambda e: e.wait_ge(h, v)
            if eng in ("vector", "scalar") and _stage[eng] is not None:
                _swaits[eng].append(fn)
            else:
                q(eng, fn)

        def guard(eng, key):
            if key in tok:
                ctr, v = tok[key]
                wait(eng, ctr, v)

        def settok(key, ctr):
            tok[key] = (ctr, ctr.v)

        def dma(eng, dst, src, ctr):
            ctr.v += 16
            h = ctr.h
            q(eng, lambda e: e.dma_start(dst, src).then_inc(h, 16))

        def dmaT(eng, dst, src, ctr):
            ctr.v += 16
            h = ctr.h
            q(eng, lambda e: e.dma_start_transpose(dst, src).then_inc(h, 16))

        def dmaslow(eng, dst, src, ctr):
            ctr.v += 16
            h = ctr.h
            q(eng, lambda e: e.dma_start(
                dst, src, allow_slow_non_contiguous=True).then_inc(h, 16))

        def mm(out, lhsT, rhs, start=True, stop=True, tp=None, ctr=None):
            if ctr is not None:
                ctr.v += 1
                h = ctr.h
                q("tensor", lambda e: e.matmul(
                    out, lhsT, rhs, start=start, stop=stop,
                    tile_position=tp).then_inc(h, 1))
            else:
                q("tensor", lambda e: e.matmul(out, lhsT, rhs, start=start,
                                               stop=stop, tile_position=tp))

        _stage = {"scalar": None, "vector": None}
        _swaits = {"scalar": [], "vector": []}
        _prevt = {"scalar": set(), "vector": set()}

        def _names(objs):
            out = set()
            for o in objs:
                t = getattr(o, "tensor", None)
                if t is not None:
                    out.add(t.name)
            return out

        def _flush(eng, attach=False):
            if _stage[eng] is None:
                return
            b, touched, has_ctr = _stage[eng]
            if attach:
                sc_ = sems["vsy"] if eng == "vector" else sems["ssy"]
                sc_.v += 1
                h, v = sc_.h, sc_.v

                def _att(e, b=b, h=h, has_ctr=has_ctr, eng=eng):
                    r = b(e)
                    if r is None or has_ctr:
                        if eng == "vector":
                            e.engine_nop()
                            r = e.engine_nop()
                        else:
                            e.activation(s_junk[:, 0:8], s_junk[:, 0:8],
                                         AF.Identity)
                            r = e.activation(s_junk[:, 8:16], s_junk[:, 8:16],
                                             AF.Identity)
                    r.then_inc(h, 1)
                Q[eng].append(_att)
                for w in _swaits[eng]:
                    Q[eng].append(w)
                Q[eng].append(lambda e, h=h, v=v: e.wait_ge(h, v))
            else:
                Q[eng].append(lambda e, b=b: b(e))
                for w in _swaits[eng]:
                    Q[eng].append(w)
            _swaits[eng] = []
            _prevt[eng] = set() if attach else touched
            _stage[eng] = None

        def _emit_eng(eng, builder, touched, has_ctr=False):
            if _stage[eng] is not None:
                conflict = bool((_stage[eng][1] | _prevt[eng]) & touched)
                _flush(eng, attach=conflict)
            else:
                if _prevt[eng] & touched:
                    sc_ = sems["vsy"] if eng == "vector" else sems["ssy"]
                    sc_.v += 1
                    h, v = sc_.h, sc_.v

                    def _np(e, h=h, eng=eng):
                        if eng == "vector":
                            e.engine_nop()
                            e.engine_nop().then_inc(h, 1)
                        else:
                            e.activation(s_junk[:, 0:8], s_junk[:, 0:8],
                                         AF.Identity)
                            e.activation(s_junk[:, 8:16], s_junk[:, 8:16],
                                         AF.Identity).then_inc(h, 1)
                    Q[eng].append(_np)
                    Q[eng].append(lambda e, h=h, v=v: e.wait_ge(h, v))
                    _prevt[eng] = set()
            _stage[eng] = (builder, touched, has_ctr)

        def act(out, in_, func, ctr=None, **kw):
            touched = _names([out, in_, kw.get("accum_out"), kw.get("bias"),
                              kw.get("scale")])
            if ctr is not None:
                ctr.v += 1
                h = ctr.h
                b = lambda e: e.activation(out, in_, func, **kw).then_inc(h, 1)
                _emit_eng("scalar", b, touched, has_ctr=True)
            else:
                b = lambda e: e.activation(out, in_, func, **kw)
                _emit_eng("scalar", b, touched)

        def vec(meth, *a, ctr=None, **kw):
            touched = _names(list(a) + list(kw.values()))
            if ctr is not None:
                ctr.v += 1
                h = ctr.h
                b = lambda e: getattr(e, meth)(*a, **kw).then_inc(h, 1)
                _emit_eng("vector", b, touched, has_ctr=True)
            else:
                b = lambda e: getattr(e, meth)(*a, **kw)
                _emit_eng("vector", b, touched)

        def coll(kind, op, ins_, outs_, ctr):
            ctr.v += 1
            h = ctr.h
            q("gpsimd", lambda e: e.collective_compute(
                kind, op, replica_groups=RG8, ins=ins_, outs=outs_
            ).then_inc(h, 1))

        _bc_reg = {}

        def iscat(out_ap, idx_ap, in_ap, ctr, bc):
            ctr.v += 16
            h = ctr.h

            def fn(e):
                if bc not in _bc_reg:
                    _bc_reg[bc] = e.to_reg(bc)
                e.indirect_dma_start(
                    out_ap, IndirectOffsetOnAxis(idx_ap, 0), in_ap, None,
                    bounds_check=_bc_reg[bc], oob_is_err=False).then_inc(h, 16)
            q("gpsimd", fn)

        def igat(out_ap, in_ap, idx_ap, ctr):
            ctr.v += 16
            h = ctr.h
            q("gpsimd", lambda e: e.indirect_dma_start(
                out_ap, None, in_ap, IndirectOffsetOnAxis(idx_ap, 0)
            ).then_inc(h, 16))

        satt_flat = s_xnT[:].rearrange("p a t -> p (a t)")

        def satt(tt):
            return satt_flat[:, D * tt:D * (tt + 1)]

        _wslot = [0]

        def wload(dst, src, a=None):
            sl = _wslot[0] % 4
            _wslot[0] += 1
            dsem = _wsems[sl]
            stg = s_wf32[:, sl, :]
            if a is not None:
                stg = stg.rearrange("p (a f) -> p a f", a=a)
            _flush("scalar")
            guard("scalar", ("wf", sl))
            dma("scalar", stg, src, dsem)
            wait("vector", dsem)
            vec("tensor_copy", dst, stg, ctr=sve)
            settok(("wf", sl), sve)

        vec("memset", s_junk[:], 0.0)
        vec("memset", s_zero[:], 0.0)
        vec("memset", s_stage[:], 0.0)
        vec("memset", s_p0[:], 0.0)
        vec("memset", s_p1[:], 0.0)
        vec("memset", s_p2[:], 0.0)
        vec("memset", s_p3[:], 0.0, ctr=sve)
        zero_tok = sve.v
        wait("scalar", sve, zero_tok)
        act(s_junk[:, 0:16], s_junk[:, 0:16], AF.Identity)
        act(s_junk[:, 16:32], s_junk[:, 16:32], AF.Identity)

        dma("gpsimd", c_iota[:], iota_h[:], gq)
        dma("gpsimd", c_ones[:], onesc_h[:], gq)
        dma("gpsimd", c_u128[:], u128_h[:], gq)
        dma("gpsimd", c_uc[:], uc_h[:], gq)
        dma("gpsimd", w_watt[:], watt_h.rearrange("a p f -> p a f"), gq)
        dma("gpsimd", w_rwT[:], rwT_h.rearrange("(a p) f -> p a f", p=128), gq)
        gq_const = gq.v

        gq_cast = gq.v

        dma("sync", b_wvb[:], wvb_h[0:1, :].to_broadcast([128, D]), sq)
        dma("sync", b_owb[:], owb_h[0:1, :].to_broadcast([128, D]), sq)
        dma("sync", b_eb[:], ebias_h[0:1, :].to_broadcast([128, NE]), sq)
        dma("sync", s_pm[:], pm_h[:], sq)
        dmaslow("sync", s_tokid[:],
                tokid_h.rearrange("(a p) o -> p (a o)", p=128), sq)
        sq_const = sq.v

        for tt in range(NT):
            sl = tt % 2
            xsem = sems["dw_xa"] if sl == 0 else sems["dw_xb"]
            guard("sync", ("xt", sl))
            dma("sync", s_xt[:, sl, :], x_h[128 * tt:128 * (tt + 1), :], xsem)
            wait("vector", xsem)
            vec("reduce_sum", s_lns[:, 0:1], s_xt[:, sl, :], AX.X)
            wait("scalar", xsem)
            guard("scalar", ("moe_scr",))
            act(s_moe[:], s_xt[:, sl, :], AF.Square, ctr=ssc,
                accum_out=s_lns[:, 1:2])
            settok(("moe_scr",), ssc)
            wait("vector", ssc)
            vec("tensor_scalar_mul", s_lns[:, 2:3], s_lns[:, 0:1], 1.0 / D)
            vec("tensor_scalar_mul", s_lns[:, 3:4], s_lns[:, 1:2], 1.0 / D)
            vec("tensor_mul", s_lns[:, 4:5], s_lns[:, 2:3], s_lns[:, 2:3])
            vec("tensor_sub", s_lns[:, 5:6], s_lns[:, 3:4], s_lns[:, 4:5],
                ctr=sve)
            vec("tensor_scalar_add", s_lns[:, 5:6], s_lns[:, 5:6], 1e-5,
                ctr=sve)
            wait("scalar", sve)
            act(s_lns[:, 6:7], s_lns[:, 5:6], AF.Sqrt, ctr=ssc)
            wait("vector", ssc)
            vec("reciprocal", s_lns[:, 7:8], s_lns[:, 6:7])
            guard("vector", ("xnst", sl))
            vec("scalar_tensor_tensor", s_xn_st[:, sl, :], s_xt[:, sl, :],
                s_lns[:, 2:3], s_lns[:, 7:8].to_broadcast([128, D]),
                ALU.subtract, ALU.mult, ctr=sve)
            settok(("xt", sl), sve)
            wait("sync", sve)
            dma("sync", d_xn[128 * tt:128 * (tt + 1), :], s_xn_st[:, sl, :], sq)
            settok(("xnst", sl), sq)
        sq_xn = sq.v

        wait("gpsimd", sq, sq_xn)
        coll("AllGather", ALU.bypass, [d_xn[:]], [d_xnall[:]], sems["c_xn"])

        wait("sync", sq, sq_xn)
        for jj in range(NJ):
            dmaT("sync", s_xnT[:, jj, :], d_xn[:, 128 * jj:128 * (jj + 1)], sq)
        sq_xnT = sq.v

        if DEBUG:
            dma("sync", dbg["xn"][:], d_xn[:], sq)

        mark()
        wait("tensor", sq, sq_xnT)
        wait("tensor", gq, gq_const)
        wait("vector", sq, sq_const)
        for tt in range(NT):
            guard("tensor", ("psA",))
            for j in range(NJ):
                mm(p_bg0[:, 0:NE], s_xnT[:, j, 128 * tt:128 * (tt + 1)],
                   w_rwT[:, j, :], start=(j == 0), stop=(j == NJ - 1),
                   ctr=smm if j == NJ - 1 else None)
            wait("scalar", smm)
            act(s_scores[:], p_bg0[:, 0:NE], AF.Sigmoid, ctr=ssc)
            settok(("psA",), ssc)
            wait("vector", ssc)
            vec("tensor_add", s_biased[:], s_scores[:], b_eb[:])
            vec("max", out=s_m8[:], in_=s_biased[:])
            vec("max_index", s_i8[:], s_m8[:], s_biased[:])
            vec("tensor_copy", s_if6a[:, tt, :], s_i8[:])
            vec("memset", s_Af[:], 0.0)
            cur, nxt = s_Af, s_Af2
            for k in range(EK):
                vec("scalar_tensor_tensor", nxt[:], c_iota[:],
                    s_if6a[:, tt, k:k + 1], cur[:], ALU.is_equal, ALU.add)
                cur, nxt = nxt, cur
                vec("scalar_tensor_tensor", s_scr64[:], c_iota[:],
                    s_if6a[:, tt, k:k + 1], s_scores[:], ALU.is_equal,
                    ALU.mult, accum_out=s_sc6[:, k:k + 1])
            vec("tensor_copy", s_A[:, tt, :], cur[:])
            vec("reduce_sum", s_t8a[:, 0:1], s_sc6[:, 0:EK], AX.X)
            vec("tensor_scalar_add", s_t8a[:, 1:2], s_t8a[:, 0:1], 1e-20)
            vec("reciprocal", s_t8a[:, 2:3], s_t8a[:, 1:2])
            vec("tensor_scalar_mul", s_pw[:, tt, 0:EK], s_sc6[:, 0:EK],
                s_t8a[:, 2:3], ctr=sve)
            wait("tensor", sve)
            guard("tensor", ("psB",))
            mm(p_bg1[0:64, 0:NE], c_ones[:], s_A[:, tt, :], ctr=smm)
            wait("vector", smm)
            guard("vector", ("cntst",))
            vec("tensor_copy", s_cnt[:], p_bg1[0:1, 0:NE], ctr=sve)
            settok(("psB",), sve)
            wait("sync", sve)
            dma("sync", d_ss[tt:tt + 1, :], s_cnt[:], sq)
            settok(("cntst",), sq)
        sq_ss = sq.v

        wait("gpsimd", sq, sq_ss)
        wait("gpsimd", sve, zero_tok)
        dma("gpsimd", s_stage[0:16, :], d_ss[:], gq)
        wait("tensor", gq)
        guard("tensor", ("psB",))
        mm(p_bg1[0:64, 0:NE], c_ones[0:64, :], s_stage[:], ctr=smm)
        wait("vector", smm)
        guard("vector", ("cntst",))
        vec("tensor_copy", s_cnt[:], p_bg1[0:1, 0:NE], ctr=sve)
        settok(("psB",), sve)
        wait("sync", sve)
        dma("sync", d_cnt[:], s_cnt[:], sq)
        settok(("cntst",), sq)
        wait("gpsimd", sq, sq.v)
        coll("AllGather", ALU.bypass, [d_cnt[:]], [d_cs[:]], sems["c_cnt"])

        mark()
        sw2v = sw2T_h.rearrange("(a p) f -> p a f", p=128)
        for jj in range(16):
            for h2 in range(2):
                wload(w_sw2r[:, jj, 512 * h2:512 * (h2 + 1)],
                      sw2v[:, jj, 512 * h2:512 * (h2 + 1)])
        sw2r_tok = sve.v
        wait("vector", sems["c_cnt"])
        vec("memset", s_co[:], 0.0)
        cur, nxt = s_co, s_co2
        for cc in range(8):
            wait("sync", sems["c_cnt"])
            guard("sync", ("csf",))
            dma("sync", s_csf[:], d_cs[cc:cc + 1, :], sq)
            wait("vector", sq)
            vec("scalar_tensor_tensor", nxt[:], s_csf[:],
                s_pm[:, cc:cc + 1], cur[:], ALU.mult, ALU.add, ctr=sve)
            settok(("csf",), sve)
            cur, nxt = nxt, cur
        vec("tensor_copy", s_coi[:], cur[:])
        vec("tensor_scalar", s_cq[:], s_coi[:], 6, None, ALU.arith_shift_right)
        vec("tensor_scalar", s_coi[:], s_coi[:], 63, None, ALU.bitwise_and)
        vec("tensor_copy", s_cqf[:], s_cq[:])
        vec("tensor_copy", s_crf[:], s_coi[:], ctr=sve)
        sve_qr = sve.v
        wait("gpsimd", sve, sve_qr)
        dma("gpsimd", s_stage[0:1, :], s_cqf[:], gq)
        dma("gpsimd", s_stage[1:2, :], s_crf[:], gq)
        dma("gpsimd", s_stage[2:18, :], d_ss[:], gq)
        wait("tensor", gq)
        wait("tensor", sve, sve_qr)
        guard("tensor", ("psA",))
        mm(p_bg0[0:64, 0:NE], c_uc[:], s_stage[:], ctr=smm)
        wait("vector", smm)
        vec("tensor_copy", s_choff[:], p_bg0[0:64, 0:NE], ctr=sve)
        settok(("psA",), sve)
        wait("sync", sve)
        dma("sync", d_choff[:], s_choff[0:16, :], sq)
        sq_choff = sq.v
        if DEBUG:
            dma("sync", dbg["choff"][:], d_choff[:], sq)

        wait("gpsimd", sve, zero_tok)
        dma("gpsimd", d_tokmap[:].rearrange("(p b) o -> p (b o)", p=128),
            s_zero[:, 0:196], gq)
        gq_tmz = gq.v

        for tt in range(NT):
            osl = tt % 2
            guard("sync", ("offb", osl))
            wait("sync", sq, sq_choff)
            dma("sync", s_offb[:, osl, :],
                d_choff[tt:tt + 1, :].to_broadcast([128, NE]), sq)
            off_ld = sq.v
            guard("tensor", ("psB",))
            mm(p_bg1[:, 0:NE], c_u128[:], s_A[:, tt, :], ctr=smm)
            wait("vector", smm)
            wait("vector", sq, off_ld)
            vec("tensor_copy", s_C[:], p_bg1[:, 0:NE], ctr=sve)
            settok(("psB",), sve)
            vec("tensor_add", s_Ct[:], s_C[:], s_offb[:, osl, :], ctr=sve)
            settok(("offb", osl), sve)
            for k in range(EK):
                vec("scalar_tensor_tensor", s_scr64[:], c_iota[:],
                    s_if6a[:, tt, k:k + 1], s_Ct[:], ALU.is_equal, ALU.mult,
                    accum_out=s_r6[:, k:k + 1])
            E6 = slice(0, EK)
            vec("tensor_scalar", s_v6[:, E6], s_r6[:, E6], float(CAP), None,
                ALU.is_lt)
            vec("tensor_mul", s_pw[:, tt, E6], s_pw[:, tt, E6], s_v6[:, E6])
            vec("tensor_scalar", s_t8a[:, E6], s_if6a[:, tt, E6], float(CAP),
                None, ALU.mult)
            vec("tensor_add", s_t8a[:, E6], s_t8a[:, E6], s_r6[:, E6])
            vec("tensor_copy", s_r6i[:, E6], s_if6a[:, tt, E6])
            vec("tensor_scalar", s_r6i[:, E6], s_r6i[:, E6], 3, None,
                ALU.arith_shift_right)
            vec("tensor_copy", s_t8b[:, E6], s_r6i[:, E6])
            vec("tensor_scalar", s_t8b[:, E6], s_t8b[:, E6], 64.0, None,
                ALU.mult)
            vec("tensor_add", s_t8b[:, E6], s_t8b[:, E6], s_t8a[:, E6])
            vec("tensor_scalar", s_scr64[:, 0:EK], s_v6[:, E6], -2e6, 2e6,
                ALU.mult, ALU.add)
            vec("tensor_add", s_t8b[:, E6], s_t8b[:, E6], s_scr64[:, 0:EK])
            vec("tensor_copy", s_sidx[:, tt, E6], s_t8b[:, E6])
            vec("tensor_scalar_min", s_t8a[:, E6], s_t8a[:, E6],
                float(NE * CAP - 1))
            vec("tensor_copy", s_gidx[:, tt, E6], s_t8a[:, E6], ctr=sve)
        sve_route = sve.v

        wait("gpsimd", sve, sve_route)
        wait("gpsimd", gq, gq_tmz)
        for tt in range(NT):
            for k in range(EK):
                iscat(d_tokmap[:], s_sidx[:, tt, k:k + 1],
                      s_tokid[:, tt:tt + 1], gq, 25087)
        wait("gpsimd", gq)
        coll("ReduceScatter", ALU.add, [d_tokmap[:]], [d_tokmy[:]],
             sems["c_tok"])
        if DEBUG:
            wait("sync", sems["c_tok"])
            dma("sync", dbg["tokmy"][:], d_tokmy[:], sq)
            dma("sync", dbg["cnt"][:], d_cnt[:], sq)
            wait("sync", sve, sve_route)
            dma("sync", dbg["if6"][:], s_if6a[:], sq)
            dma("sync", dbg["pw"][:], s_pw[:], sq)
            dma("sync", dbg["gidx"][:], s_gidx[:], sq)

        mark()
        wait("tensor", sve, sw2r_tok)
        for u in range(NU):
            tsl = slice(UT * u, UT * (u + 1))
            sw1v = sw1T_h.rearrange("(a p) f -> p a f", p=128)
            for ft in range(16):
                slw = ft % 2
                guard("vector", ("sw1", slw))
                for h2 in range(2):
                    wload(w_sw1s[:, slw, 4 * h2:4 * (h2 + 1), :],
                          sw1v[:, 4 * h2:4 * (h2 + 1),
                               128 * ft:128 * (ft + 1)], a=4)
                cast_tok = sve.v
                pp = p_bg0 if ft % 2 == 0 else p_bg1
                pk = ("psA",) if ft % 2 == 0 else ("psB",)
                guard("tensor", pk)
                wait("tensor", sve, cast_tok)
                for j in range(NJ):
                    mm(pp[:, 0:UT], w_sw1s[:, slw, j, :], s_xnT[:, j, tsl],
                       start=(j == 0), stop=(j == NJ - 1),
                       ctr=smm if j == NJ - 1 else None)
                settok(("sw1", slw), smm)
                wait("vector", smm)
                vec("tensor_copy", s_hraw[:, ft, :], pp[:, 0:UT], ctr=sve)
                settok(pk, sve)
            wait("scalar", sve)
            act(s_hraw[:].rearrange("p a f -> p (a f)"),
                s_hraw[:].rearrange("p a f -> p (a f)"),
                AF.Gelu_apprx_tanh, ctr=ssc)
            wait("tensor", ssc)
            for tq in range(2):
                tt = 2 * u + tq
                for ih in range(2):
                    pp = p_bg0 if ih == 0 else p_bg1
                    pk = ("psA",) if ih == 0 else ("psB",)
                    guard("tensor", pk)
                    for ft in range(16):
                        mm(pp[:, 0:512],
                           s_hraw[:, ft, 128 * tq:128 * (tq + 1)],
                           w_sw2r[:, ft, 512 * ih:512 * (ih + 1)],
                           start=(ft == 0), stop=(ft == 15),
                           ctr=smm if ft == 15 else None)
                    wait("vector", smm)
                    guard("vector", ("sh2st", tt % 2))
                    vec("tensor_copy",
                        s_sh2st[:, tt % 2, 512 * ih:512 * (ih + 1)],
                        pp[:, 0:512], ctr=sve)
                    settok(pk, sve)
                wait("sync", sve)
                dma("sync", d_sh2[128 * tt:128 * (tt + 1), :],
                    s_sh2st[:, tt % 2, :], sq)
                settok(("sh2st", tt % 2), sq)
        sq_sh2 = sq.v
        if DEBUG:
            wait("sync", sq, sq_sh2)
            dma("sync", dbg["sh2"][:], d_sh2[:], sq)

        mark()
        wait("gpsimd", sems["c_tok"])
        wait("gpsimd", sems["c_xn"])
        dmaslow("gpsimd", s_tmf[:],
                d_tokmy[0:3072, :].rearrange("(j p) o -> p (j o)", p=128), gq)
        wait("vector", gq)
        vec("tensor_copy", s_tmidx[:], s_tmf[:], ctr=sve)
        wait("gpsimd", sve)
        for jj in range(24):
            sl = jj % 2
            gsem = sems["dw_y6a"] if sl == 0 else sems["dw_y6b"]
            guard("gpsimd", ("xsg", sl))
            igat(s_y6[:, sl, :], d_xnall[:], s_tmidx[:, jj:jj + 1], gsem)
            wait("gpsimd", gsem)
            dma("gpsimd", d_xs[128 * jj:128 * (jj + 1), :], s_y6[:, sl, :], gq)
            settok(("xsg", sl), gq)
        gq_xs = gq.v

        w2view = w_w1[:].rearrange("p a f -> p (a f)").rearrange(
            "p (b g) -> p b g", b=4)

        for eg in range(4):
            for el in range(2):
                e = 2 * eg + el
                guard("sync", ("xsT",))
                wait("sync", gq, gq_xs)
                for j in range(NJ):
                    dmaT("sync", s_xsT[:, j, :],
                         d_xs[CAP * e:CAP * (e + 1), 128 * j:128 * (j + 1)],
                         sems["dw_xsT"])
                wait("tensor", sems["dw_xsT"])
                guard("vector", ("w1",))
                w1v = w1_h[e].rearrange("(a p) f -> p a f", p=128)
                for j in range(NJ):
                    wload(w_w1[:, j, :], w1v[:, j, :])
                wait("tensor", sve)
                for ft in range(4):
                    pp = p_bg0 if ft % 2 == 0 else p_bg1
                    pk = ("psA",) if ft % 2 == 0 else ("psB",)
                    guard("tensor", pk)
                    for j in range(NJ):
                        mm(pp[:, 0:CAP], w_w1[:, j, 128 * ft:128 * (ft + 1)],
                           s_xsT[:, j, :], start=(j == 0), stop=(j == NJ - 1),
                           ctr=smm if j == NJ - 1 else None)
                    wait("vector", smm)
                    vec("tensor_copy", s_h1r[:, 4 * el + ft, :], pp[:, 0:CAP],
                        ctr=sve)
                    settok(pk, sve)
                settok(("xsT",), smm)
                settok(("w1",), smm)
            wait("scalar", sve)
            act(s_h1r[:].rearrange("p a f -> p (a f)"),
                s_h1r[:].rearrange("p a f -> p (a f)"),
                AF.Gelu_apprx_tanh, ctr=ssc)
            gelu_tok = ssc.v
            wait("tensor", ssc, gelu_tok)
            for el in range(2):
                e = 2 * eg + el
                guard("vector", ("w1",))
                w2v = w2_h[e].rearrange("(a p) f -> p a f", p=128)
                for c4 in range(4):
                    for h2 in range(2):
                        wload(w2view[:, c4, 512 * h2:512 * (h2 + 1)],
                              w2v[:, c4, 512 * h2:512 * (h2 + 1)])
                wait("tensor", sve)
                for ct in range(3):
                    for ih in range(2):
                        pp = p_bg0 if ih == 0 else p_bg1
                        pk = ("psA",) if ih == 0 else ("psB",)
                        guard("tensor", pk)
                        for f in range(4):
                            mm(pp[:, 0:512],
                               s_h1r[:, 4 * el + f, 128 * ct:128 * (ct + 1)],
                               w2view[:, f, 512 * ih:512 * (ih + 1)],
                               start=(f == 0), stop=(f == 3),
                               ctr=smm if f == 3 else None)
                        wait("vector", smm)
                        guard("vector", ("yst", ct % 2))
                        vec("tensor_copy",
                            s_yst[:, ct % 2, 512 * ih:512 * (ih + 1)],
                            pp[:, 0:512], ctr=sve)
                        settok(pk, sve)
                    wait("sync", sve)
                    dma("sync",
                        d_ymine[CAP * e + 128 * ct:CAP * e + 128 * (ct + 1), :],
                        s_yst[:, ct % 2, :], sq)
                    settok(("yst", ct % 2), sq)
                settok(("w1",), smm)
        sq_y = sq.v
        wait("gpsimd", sq, sq_y)
        coll("AllGather", ALU.bypass, [d_ymine[:]], [d_yall[:]], sems["c_y"])
        if DEBUG:
            dma("sync", dbg["y"][:], d_ymine[:], sq)

        mark()
        wait("sync", gq, gq_cast)
        vec("memset", s_vx[:, :, 64:65], 1.0)
        vec("memset", s_vx[:, :, 129:130], 1.0, ctr=sve)
        inv_slot = [0]
        for p in range(NP):
            guard("vector", ("vxall",))
            vsl = p % 2
            dsem = sems["dw_vxa"] if vsl == 0 else sems["dw_vxb"]
            guard("gpsimd", ("wvx", vsl))
            dma("gpsimd", w_vx[:, vsl, :, :],
                wvT_h.rearrange("(a p) f -> p a f", p=128)
                [:, :, 128 * p:128 * (p + 1)], dsem)
            wait("tensor", dsem)
            for tt in range(NT):
                guard("tensor", ("psA",))
                for j in range(NJ):
                    mm(p_bg0[:, 0:128], s_xnT[:, j, 128 * tt:128 * (tt + 1)],
                       w_vx[:, vsl, j, :], start=(j == 0), stop=(j == NJ - 1),
                       ctr=smm if j == NJ - 1 else None)
                wait("vector", smm)
                guard("vector", ("vx", tt))
                vec("tensor_add", s_vx[:, tt, 0:64], p_bg0[:, 0:64],
                    b_wvb[:, 128 * p:128 * p + 64])
                vec("tensor_add", s_vx[:, tt, 65:129], p_bg0[:, 64:128],
                    b_wvb[:, 128 * p + 64:128 * (p + 1)], ctr=sve)
                settok(("psA",), sve)
                settok(("vx", tt), sve)
            settok(("wvx", vsl), smm)
            for tc in range(4):
                guard("tensor", ("psA",))
                mm(p_bg0[0:64, 0:512], w_watt[0:64, p, :],
                   s_xnT[0:64, p, 512 * tc:512 * (tc + 1)], tp=(0, 0),
                   start=True, stop=True)
                guard("tensor", ("psB",))
                mm(p_bg1[64:128, 0:512], w_watt[64:128, p, :],
                   s_xnT[64:128, p, 512 * tc:512 * (tc + 1)], tp=(64, 64),
                   start=True, stop=True, ctr=smm)
                wait("vector", smm)
                guard("vector", ("xw", tc))
                vec("tensor_copy", s_xw[0:64, tc, :], p_bg0[0:64, 0:512])
                vec("tensor_copy", s_xw[64:128, tc, :], p_bg1[64:128, 0:512],
                    ctr=sve)
                settok(("psA",), sve)
                settok(("psB",), sve)
                settok(("xw", tc), sve)
            wait("tensor", sve)

            for tc in range(4):
                prev = None

                def emit_o(prev):
                    psch, psex = prev
                    ppar = psch % 2
                    ppA = (s_p0, s_p1, s_p2, s_p3)[ppar * 2]
                    ppB = (s_p0, s_p1, s_p2, s_p3)[ppar * 2 + 1]
                    if psch == 0:
                        guard("tensor", ("o_rs",))
                    wait("tensor", sex, psex)
                    first = (psch == 0)
                    last = (psch == NT - 1)
                    mm(p_o0[0:65, :], s_vx[:, psch, 0:65], ppA[:],
                       tp=(0, 0), start=first, stop=last)
                    mm(p_o1[0:65, :], s_vx[:, psch, 65:130], ppB[:],
                       tp=(0, 0), start=first, stop=last, ctr=smm)
                    tok[("pT", ppar * 2)] = (smm, smm.v)
                    tok[("pT", ppar * 2 + 1)] = (smm, smm.v)

                for sch in range(NT):
                    par = sch % 2
                    sl0 = par * 2
                    sl1 = sl0 + 1
                    pbA = p_att0 if par == 0 else p_rs0
                    pbB = p_att1 if par == 0 else p_rs1
                    pA = (s_p0, s_p1, s_p2, s_p3)[sl0]
                    pB = (s_p0, s_p1, s_p2, s_p3)[sl1]
                    guard("tensor", ("attA", par))
                    mm(pbA[:, 0:512],
                       s_xnT[0:64, p, 128 * sch:128 * (sch + 1)],
                       s_xw[0:64, tc, :], tp=(0, 0), ctr=smm)
                    att0_tok = smm.v
                    guard("tensor", ("attB", par))
                    mm(pbB[:, 0:512],
                       s_xnT[64:128, p, 128 * sch:128 * (sch + 1)],
                       s_xw[64:128, tc, :], tp=(64, 0), ctr=smm)
                    att1_tok = smm.v
                    if prev is not None:
                        emit_o(prev)
                    guard("scalar", ("pT", sl0))
                    wait("scalar", smm, att0_tok)
                    act(pA[:], pbA[:, 0:512], AF.Exp, scale=0.125, ctr=sex)
                    settok(("attA", par), sex)
                    guard("scalar", ("pT", sl1))
                    wait("scalar", smm, att1_tok)
                    act(pB[:], pbB[:, 0:512], AF.Exp, scale=0.125, ctr=sex)
                    settok(("attB", par), sex)
                    prev = (sch, sex.v)
                emit_o(prev)
                o_tok = smm.v
                slA = inv_slot[0]
                slB = slA + 1
                inv_slot[0] += 2
                wait("vector", smm, o_tok)
                guard("vector", ("xt", 0))
                vec("tensor_copy", s_xt[64:65, 0, 0:512], p_o0[64:65, :])
                vec("tensor_copy", s_xt[64:65, 0, 512:1024], p_o1[64:65, :],
                    ctr=sve)
                wait("sync", sve)
                dma("sync", d_inv[slA:slA + 1, :], s_xt[64:65, 0, 0:512],
                    sems["dw_inva"])
                dma("sync", d_inv[slB:slB + 1, :], s_xt[64:65, 0, 512:1024],
                    sems["dw_inva"])
                settok(("xt", 0), sems["dw_inva"])
                settok(("xt2", 0), sems["dw_inva"])
                wait("sync", sems["dw_inva"])
                guard("sync", ("invb",))
                dma("sync", s_invA[0:64, :],
                    d_inv[slA:slA + 1, :].to_broadcast([64, 512]),
                    sems["dw_invb"])
                dma("sync", s_invB[0:64, :],
                    d_inv[slB:slB + 1, :].to_broadcast([64, 512]),
                    sems["dw_invb"])
                wait("vector", sems["dw_invb"])
                vec("reciprocal", s_invA[:], s_invA[:])
                vec("reciprocal", s_invB[:], s_invB[:])
                vec("tensor_mul", s_oT[0:64, p, 512 * tc:512 * (tc + 1)],
                    p_o0[0:64, :], s_invA[:])
                guard("vector", ("pT", 0))
                vec("tensor_mul", s_p0[0:64, :], p_o1[0:64, :], s_invB[:],
                    ctr=sve)
                settok(("invb",), sve)
                settok(("o_rs",), sve)
                wait("sync", sve)
                dma("sync", s_oT[64:128, p, 512 * tc:512 * (tc + 1)],
                    s_p0[0:64, :], sq)
                settok(("pT", 0), sq)
            settok(("vxall",), smm)
        sve_att = sve.v
        sq_oB = sq.v
        if DEBUG:
            wait("sync", sve, sve_att)
            dma("sync", dbg["ot"][:], s_oT[:], sq)

        mark()
        ow16v = owT_h.rearrange("(a p) f -> p a f", p=128)
        owbuf = w_w1[:]
        wait("vector", sve, sve_att)
        wait("tensor", sq, sq_oB)
        for fh in range(2):
            guard("vector", ("w1",))
            guard("vector", ("owb",))
            for p in range(NP):
                wload(owbuf[:, p, :], ow16v[:, p, 512 * fh:512 * (fh + 1)])
            wait("tensor", sve)
            wait("tensor", sve, sve_att)
            for tg in range(8):
                for ti in range(2):
                    tt = 2 * tg + ti
                    pp = (p_bg0, p_bg1)[ti]
                    pk = (("psA",), ("psB",))[ti]
                    guard("tensor", pk)
                    for p in range(NP):
                        mm(pp[:, 0:512], s_oT[:, p, 128 * tt:128 * (tt + 1)],
                           owbuf[:, p, :], start=(p == 0), stop=(p == NP - 1),
                           ctr=smm if p == NP - 1 else None)
                    wait("vector", smm)
                    vec("tensor_add", satt(tt)[:, 512 * fh:512 * (fh + 1)],
                        pp[:, 0:512], b_owb[:, 512 * fh:512 * (fh + 1)],
                        ctr=sve)
                    settok(pk, sve)
            settok(("owb",), smm)
        sve_op = sve.v
        if DEBUG:
            wait("sync", sve, sve_op)
            for tt in range(NT):
                dma("sync", dbg["satt"][128 * tt:128 * (tt + 1), :],
                    satt(tt), sq)

        mark()
        wait("gpsimd", sems["c_y"])
        wait("vector", sve, sve_op)
        for tt in range(NT):
            for k in range(EK):
                sl = k % 2
                gsem = sems["dw_y6a"] if sl == 0 else sems["dw_y6b"]
                guard("gpsimd", ("y6", sl))
                igat(s_y6[:, sl, :], d_yall[:], s_gidx[:, tt, k:k + 1], gsem)
                wait("vector", gsem)
                guard("vector", ("moe_scr",)) if k == 0 else None
                if k == 0:
                    vec("tensor_scalar_mul", s_moe[:], s_y6[:, sl, :],
                        s_pw[:, tt, 0:1], ctr=sve)
                else:
                    vec("scalar_tensor_tensor", s_moe[:], s_y6[:, sl, :],
                        s_pw[:, tt, k:k + 1], s_moe[:], ALU.mult, ALU.add,
                        ctr=sve)
                settok(("y6", sl), sve)
            shsem = sems["dw_sh2a"] if tt % 2 == 0 else sems["dw_sh2b"]
            guard("sync", ("sh2ld", tt % 2))
            if tt == 0:
                wait("sync", sq, sq_sh2)
            dma("sync", s_sh2st[:, tt % 2, :],
                d_sh2[128 * tt:128 * (tt + 1), :], shsem)
            wait("vector", shsem)
            vec("tensor_add", s_moe[:], s_moe[:], s_sh2st[:, tt % 2, :],
                ctr=sve)
            settok(("sh2ld", tt % 2), sve)
            wait("scalar", sve)
            guard("scalar", ("gst", tt % 2))
            act(s_xn_st[:, tt % 2, :], s_moe[:], AF.Silu, ctr=ssc)
            settok(("moe_scr",), ssc)
            xsem = sems["dw_xa"] if tt % 2 == 0 else sems["dw_xb"]
            guard("sync", ("xt2", tt % 2))
            dma("sync", s_xt[:, tt % 2, :], x_h[128 * tt:128 * (tt + 1), :],
                xsem)
            wait("vector", ssc)
            wait("vector", xsem)
            vec("tensor_mul", s_moe[:], satt(tt), s_xn_st[:, tt % 2, :],
                ctr=sve)
            settok(("gst", tt % 2), sve)
            vec("tensor_add", s_xt[:, tt % 2, :], s_xt[:, tt % 2, :],
                s_moe[:], ctr=sve)
            wait("sync", sve)
            dma("sync", out_h[128 * tt:128 * (tt + 1), :], s_xt[:, tt % 2, :],
                sq)
            settok(("xt2", tt % 2), sq)

        mark()
        wait("sync", sq, sq.v)
        wait("gpsimd", gq, gq.v)

        _flush("vector")
        _flush("scalar")
        mark()
        if PHASE_LIMIT is not None:
            cut = marks[PHASE_LIMIT]
            for k in Q:
                Q[k] = Q[k][:cut[k]]

        with nc.Block() as block:
            @block.sync
            def _(e):
                for fn in Q["sync"]:
                    fn(e)

            @block.gpsimd
            def _(e):
                for fn in Q["gpsimd"]:
                    fn(e)

            @block.tensor
            def _(e):
                for fn in Q["tensor"]:
                    fn(e)

            @block.scalar
            def _(e):
                for fn in Q["scalar"]:
                    fn(e)

            @block.vector
            def _(e):
                for fn in Q["vector"]:
                    fn(e)

    return nc, dbg


_NC_CACHE = {}


def _get_nc():
    if "nc" not in _NC_CACHE:
        _NC_CACHE["nc"] = build_nc()
    return _NC_CACHE["nc"]


def kernel(L0_attn_x, L0_attn_w_att_val, L0_attn_wv_w, L0_attn_wv_b,
           L0_attn_out_w, L0_attn_out_b, L0_ffn_router_weight,
           L0_ffn_expert_bias, L0_ffn_w1, L0_ffn_w2, L0_ffn_sw1, L0_ffn_sw2):
    x = np.ascontiguousarray(np.asarray(L0_attn_x, dtype=np.float32))
    watt = np.asarray(L0_attn_w_att_val, dtype=np.float32)
    wv = np.asarray(L0_attn_wv_w, dtype=np.float32)
    wvb = np.asarray(L0_attn_wv_b, dtype=np.float32)
    ow = np.asarray(L0_attn_out_w, dtype=np.float32)
    owb = np.asarray(L0_attn_out_b, dtype=np.float32)
    rw = np.asarray(L0_ffn_router_weight, dtype=np.float32)
    eb = np.asarray(L0_ffn_expert_bias, dtype=np.float32)
    w1 = np.asarray(L0_ffn_w1, dtype=np.float32)
    w2 = np.asarray(L0_ffn_w2, dtype=np.float32)
    sw1 = np.asarray(L0_ffn_sw1, dtype=np.float32)
    sw2 = np.asarray(L0_ffn_sw2, dtype=np.float32)

    wvT = np.ascontiguousarray(wv.T)
    owT = np.ascontiguousarray(ow.T)
    sw1T = np.ascontiguousarray(sw1.T)
    sw2T = np.ascontiguousarray(sw2.T)
    rwT = np.ascontiguousarray(rw.T)
    wattp = np.zeros((NP, 128, 64), np.float32)
    for p in range(NP):
        wattp[p, 0:64] = watt[2 * p]
        wattp[p, 64:128] = watt[2 * p + 1]
    u128 = np.triu(np.ones((128, 128), np.float32), 1)
    uc = np.zeros((64, 64), np.float32)
    for tt in range(16):
        uc[0, tt] = 64.0
        uc[1, tt] = 1.0
        for i in range(tt):
            uc[2 + i, tt] = 1.0
    iota = np.tile(np.arange(NE, dtype=np.float32), (128, 1))
    onesc = np.ones((128, 64), np.float32)

    nc, dbg = _get_nc()
    in_maps = []
    for c in range(8):
        in_maps.append(dict(
            x=np.ascontiguousarray(x[c]),
            wvT=wvT, owT=owT, sw1T=sw1T, sw2T=sw2T, rwT=rwT,
            watt=wattp,
            wvb=wvb.reshape(1, D), owb=owb.reshape(1, D),
            ebias=eb.reshape(1, NE),
            w1=np.ascontiguousarray(w1[ELOC * c:ELOC * (c + 1)]),
            w2=np.ascontiguousarray(w2[ELOC * c:ELOC * (c + 1)]),
            tokid=np.arange(T * c, T * (c + 1), dtype=np.float32).reshape(T, 1),
            pm=(np.arange(8, dtype=np.float32) < c).astype(np.float32)
            .reshape(1, 8),
            u128=u128, uc=uc, iota=iota, onesc=onesc,
        ))
    res = run_bass_kernel_spmd(nc, in_maps, core_ids=list(range(8)))
    out = np.stack([np.asarray(res.results[c]["out"]) for c in range(8)],
                   axis=0)
    if DEBUG:
        kernel.last_results = res.results
    return out


# version 07
# speedup vs baseline: 1.4709x; 1.0291x over previous
from contextlib import ExitStack

import numpy as np

import concourse.bass as bass
import concourse.mybir as mybir
from concourse.bass_utils import run_bass_kernel_spmd
from concourse.bass_core_rust import IndirectOffsetOnAxis

F32 = mybir.dt.float32
BF16 = mybir.dt.bfloat16
I32 = mybir.dt.int32
U32 = mybir.dt.uint32
AF = mybir.ActivationFunctionType
ALU = mybir.AluOpType
AX = mybir.AxisListType
RG8 = [[0, 1, 2, 3, 4, 5, 6, 7]]

T = 2048
D = 1024
NT = 16
NJ = 8
NP = 8
NE = 64
EK = 6
CAP = 384
EF = 512
SF = 2048
ELOC = 8
NU = 8
UT = 256

DEBUG = False
PHASE_LIMIT = None


class Ctr:
    __slots__ = ("h", "v", "is_dma", "dedicated")

    def __init__(self, h, is_dma=False, dedicated=False):
        self.h = h
        self.v = 0
        self.is_dma = is_dma
        self.dedicated = dedicated


def build_nc():
    nc = bass.Bass("TRN2", target_bir_lowering=False)

    def inp(name, shape, dt=F32):
        return nc.dram_tensor(name, shape, dt, kind="ExternalInput").ap()

    x_h = inp("x", [T, D])
    wvT_h = inp("wvT", [D, D])
    owT_h = inp("owT", [D, D])
    sw1T_h = inp("sw1T", [D, SF])
    sw2T_h = inp("sw2T", [SF, D])
    rwT_h = inp("rwT", [D, NE])
    watt_h = inp("watt", [NP, 128, 64])
    wvb_h = inp("wvb", [1, D])
    owb_h = inp("owb", [1, D])
    ebias_h = inp("ebias", [1, NE])
    w1_h = inp("w1", [ELOC, D, EF])
    w2_h = inp("w2", [ELOC, EF, D])
    tokid_h = inp("tokid", [T, 1])
    pm_h = inp("pm", [1, 8])
    u128_h = inp("u128", [128, 128])
    uc_h = inp("uc", [64, 64])
    iota_h = inp("iota", [128, NE])
    onesc_h = inp("onesc", [128, 64])

    out_h = nc.dram_tensor("out", [T, D], F32, kind="ExternalOutput").ap()

    def idram(name, shape, dt=F32):
        return nc.dram_tensor(name, shape, dt, kind="Internal").ap()

    d_xn = idram("d_xn", [T, D], BF16)
    d_xnall = idram("d_xnall", [8 * T, D], BF16)
    d_cnt = idram("d_cnt", [1, NE])
    d_cs = idram("d_cs", [8, NE])
    d_ss = idram("d_ss", [16, NE])
    d_choff = idram("d_choff", [NT, NE])
    d_tokmap = idram("d_tokmap", [25088, 1])
    d_tokmy = idram("d_tokmy", [3136, 1])
    d_xs = idram("d_xs", [ELOC * CAP, D], BF16)
    d_ymine = idram("d_ymine", [ELOC * CAP, D], BF16)
    d_yall = idram("d_yall", [NE * CAP, D], BF16)
    d_sh2 = idram("d_sh2", [T, D], BF16)
    d_inv = idram("d_inv", [64, 512])
    d_wv16 = idram("d_wv16", [D, D], BF16)
    d_sw1b = idram("d_sw1b", [D, SF], BF16)
    d_ow16 = idram("d_ow16", [D, D], BF16)
    d_w116 = idram("d_w116", [ELOC, D, EF], BF16)
    d_w216 = idram("d_w216", [ELOC, EF, D], BF16)

    dbg = {}

    def dbgout(name, shape, dt):
        if DEBUG:
            dbg[name] = nc.dram_tensor("dbg_" + name, shape, dt,
                                       kind="ExternalOutput").ap()

    dbgout("xn", [T, D], BF16)
    dbgout("cnt", [1, NE], F32)
    dbgout("choff", [NT, NE], F32)
    dbgout("if6", [128, NT, 8], F32)
    dbgout("pw", [128, NT, 8], F32)
    dbgout("gidx", [128, NT, 8], I32)
    dbgout("tokmy", [3136, 1], F32)
    dbgout("y", [ELOC * CAP, D], BF16)
    dbgout("sh2", [T, D], BF16)
    dbgout("ot", [128, NP, T], BF16)
    dbgout("satt", [T, D], BF16)

    st = ExitStack()
    with st:
        def sb(name, shape, dt):
            return st.enter_context(nc.sbuf_tensor(name, shape, dt))

        def psum(name, shape):
            return st.enter_context(nc.psum_tensor(name, shape, F32))

        c_iota = sb("c_iota", [128, NE], F32)
        c_ones = sb("c_ones", [128, 64], BF16)
        c_u128 = sb("c_u128", [128, 128], BF16)
        c_uc = sb("c_uc", [64, 64], BF16)
        w_watt = sb("w_watt", [128, NP, 64], BF16)
        w_rwT = sb("w_rwT", [128, NJ, NE], BF16)
        w_sw2r = sb("w_sw2r", [128, 16, D], BF16)
        w_sw1s = sb("w_sw1s", [128, 2, NJ, 128], BF16)
        w_vx = sb("w_vx", [128, 2, NJ, 128], BF16)
        w_ow = sb("w_ow", [128, 2, 512], BF16)
        w_w1 = sb("w_w1", [128, NJ, EF], BF16)
        s_wf32 = sb("s_wf32", [128, 4, 512], F32)
        b_wvb = sb("b_wvb", [128, D], F32)
        b_owb = sb("b_owb", [128, D], F32)
        b_eb = sb("b_eb", [128, NE], F32)
        s_pm = sb("s_pm", [1, 8], F32)
        s_tokid = sb("s_tokid", [128, NT], F32)

        s_xnT = sb("s_xnT", [128, NJ, T], BF16)
        s_oT = sb("s_oT", [128, NP, T], BF16)
        s_vx = sb("s_vx", [128, NT, 132], BF16)
        s_xw = sb("s_xw", [128, 4, 512], BF16)
        s_p0 = sb("s_p0", [128, 512], BF16)
        s_p1 = sb("s_p1", [128, 512], BF16)
        s_p2 = sb("s_p2", [128, 512], BF16)
        s_p3 = sb("s_p3", [128, 512], BF16)
        s_hraw = sb("s_hraw", [128, 16, UT], BF16)
        s_xn_st = sb("s_xn_st", [128, 2, D], BF16)
        s_sh2st = sb("s_sh2st", [128, 2, D], BF16)
        s_A = sb("s_A", [128, NT, NE], BF16)
        s_xsT = sb("s_xsT", [128, NJ, CAP], BF16)
        s_h1r = sb("s_h1r", [128, 8, CAP], BF16)
        s_yst = sb("s_yst", [128, 2, D], BF16)
        s_y6 = sb("s_y6", [128, 2, D], BF16)

        s_xt = sb("s_xt", [128, 2, D], F32)
        s_moe = sb("s_moe", [128, D], F32)
        s_lns = sb("s_lns", [128, 8], F32)
        s_scores = sb("s_scores", [128, NE], F32)
        s_biased = sb("s_biased", [128, NE], F32)
        s_m8 = sb("s_m8", [128, 8], F32)
        s_i8 = sb("s_i8", [128, 8], U32)
        s_if6a = sb("s_if6a", [128, NT, 8], F32)
        s_Af = sb("s_Af", [128, NE], F32)
        s_Af2 = sb("s_Af2", [128, NE], F32)
        s_scr64 = sb("s_scr64", [128, NE], F32)
        s_sc6 = sb("s_sc6", [128, 8], F32)
        s_pw = sb("s_pw", [128, NT, 8], F32)
        s_r6 = sb("s_r6", [128, 8], F32)
        s_v6 = sb("s_v6", [128, 8], F32)
        s_t8a = sb("s_t8a", [128, 8], F32)
        s_t8b = sb("s_t8b", [128, 8], F32)
        s_sidx = sb("s_sidx", [128, NT, 8], I32)
        s_gidx = sb("s_gidx", [128, NT, 8], I32)
        s_r6i = sb("s_r6i", [128, 8], I32)
        s_cnt = sb("s_cnt", [1, NE], F32)
        s_csf = sb("s_csf", [1, NE], F32)
        s_co = sb("s_co", [1, NE], F32)
        s_co2 = sb("s_co2", [1, NE], F32)
        s_coi = sb("s_coi", [1, NE], I32)
        s_cq = sb("s_cq", [1, NE], I32)
        s_cqf = sb("s_cqf", [1, NE], F32)
        s_crf = sb("s_crf", [1, NE], F32)
        s_stage = sb("s_stage", [64, 64], BF16)
        s_choff = sb("s_choff", [64, NE], F32)
        s_offb = sb("s_offb", [128, 2, NE], F32)
        s_C = sb("s_C", [128, NE], F32)
        s_Ct = sb("s_Ct", [128, NE], F32)
        s_invA = sb("s_invA", [64, 512], F32)
        s_invB = sb("s_invB", [64, 512], F32)
        s_tmf = sb("s_tmf", [128, 24], F32)
        s_tmidx = sb("s_tmidx", [128, 24], I32)
        s_zero = sb("s_zero", [128, 256], F32)
        s_junk = sb("s_junk", [128, 32], F32)

        p_att0 = psum("p_att0", [128, 512])
        p_att1 = psum("p_att1", [128, 512])
        p_o0 = psum("p_o0", [128, 512])
        p_o1 = psum("p_o1", [128, 512])
        p_rs0 = psum("p_rs0", [128, 512])
        p_rs1 = psum("p_rs1", [128, 512])
        p_bg0 = psum("p_bg0", [128, 512])
        p_bg1 = psum("p_bg1", [128, 512])

        names = "gq sq mm ex ve sc c_xn c_cnt c_tok c_y vsy ssy".split()
        sems = {nm: Ctr(st.enter_context(nc.semaphore("sem_" + nm)),
                        is_dma=nm in ("gq", "sq"))
                for nm in names}
        dnames = ("dw_sw1a dw_sw1b dw_vxa dw_vxb dw_w1 dw_w2 dw_xsT dw_ow "
                  "dw_xa dw_xb dw_inva dw_invb dw_y6a dw_y6b dw_sh2a dw_sh2b "
                  "dw_gath").split()
        for nm in dnames:
            sems[nm] = Ctr(st.enter_context(nc.semaphore("sem_" + nm)),
                           is_dma=True, dedicated=True)
        gq, sq = sems["gq"], sems["sq"]
        _wsems = [sems[n] for n in ("dw_w1", "dw_w2", "dw_sw1a", "dw_sw1b")]
        smm, sex, sve, ssc = sems["mm"], sems["ex"], sems["ve"], sems["sc"]

        Q = {k: [] for k in ("sync", "gpsimd", "tensor", "scalar", "vector")}
        marks = []

        def mark():
            _flush("vector")
            _flush("scalar")
            marks.append({k: len(v) for k, v in Q.items()})
        tok = {}

        def q(eng, fn):
            Q[eng].append(fn)

        _fence = {}

        def wait(eng, ctr, val=None):
            if ctr.dedicated:
                v = ctr.v if val is None else val
            else:
                v = ctr.v if (val is None or ctr.is_dma) else val
            h = ctr.h
            if ctr.is_dma and not ctr.dedicated:
                owner = "sync" if ctr is sems["sq"] else "gpsimd"
                if _fence.get(id(ctr), -1) < v:
                    _fence[id(ctr)] = v
                    if eng != owner:
                        q(owner, lambda e: e.wait_ge(h, v))
            fn = lambda e: e.wait_ge(h, v)
            if eng in ("vector", "scalar") and _stage[eng] is not None:
                _swaits[eng].append(fn)
            else:
                q(eng, fn)

        def guard(eng, key):
            if key in tok:
                ctr, v = tok[key]
                wait(eng, ctr, v)

        def settok(key, ctr):
            tok[key] = (ctr, ctr.v)

        def dma(eng, dst, src, ctr):
            ctr.v += 16
            h = ctr.h
            q(eng, lambda e: e.dma_start(dst, src).then_inc(h, 16))

        def dmaT(eng, dst, src, ctr):
            ctr.v += 16
            h = ctr.h
            q(eng, lambda e: e.dma_start_transpose(dst, src).then_inc(h, 16))

        def dmaslow(eng, dst, src, ctr):
            ctr.v += 16
            h = ctr.h
            q(eng, lambda e: e.dma_start(
                dst, src, allow_slow_non_contiguous=True).then_inc(h, 16))

        def mm(out, lhsT, rhs, start=True, stop=True, tp=None, ctr=None):
            if ctr is not None:
                ctr.v += 1
                h = ctr.h
                q("tensor", lambda e: e.matmul(
                    out, lhsT, rhs, start=start, stop=stop,
                    tile_position=tp).then_inc(h, 1))
            else:
                q("tensor", lambda e: e.matmul(out, lhsT, rhs, start=start,
                                               stop=stop, tile_position=tp))

        _stage = {"scalar": None, "vector": None}
        _swaits = {"scalar": [], "vector": []}
        _prevt = {"scalar": set(), "vector": set()}

        def _names(objs):
            out = set()
            for o in objs:
                t = getattr(o, "tensor", None)
                if t is not None:
                    out.add(t.name)
            return out

        def _flush(eng, attach=False):
            if _stage[eng] is None:
                return
            b, touched, has_ctr = _stage[eng]
            if attach:
                sc_ = sems["vsy"] if eng == "vector" else sems["ssy"]
                sc_.v += 1
                h, v = sc_.h, sc_.v

                def _att(e, b=b, h=h, has_ctr=has_ctr, eng=eng):
                    r = b(e)
                    if r is None or has_ctr:
                        if eng == "vector":
                            e.engine_nop()
                            r = e.engine_nop()
                        else:
                            e.activation(s_junk[:, 0:8], s_junk[:, 0:8],
                                         AF.Identity)
                            r = e.activation(s_junk[:, 8:16], s_junk[:, 8:16],
                                             AF.Identity)
                    r.then_inc(h, 1)
                Q[eng].append(_att)
                for w in _swaits[eng]:
                    Q[eng].append(w)
                Q[eng].append(lambda e, h=h, v=v: e.wait_ge(h, v))
            else:
                Q[eng].append(lambda e, b=b: b(e))
                for w in _swaits[eng]:
                    Q[eng].append(w)
            _swaits[eng] = []
            _prevt[eng] = set() if attach else touched
            _stage[eng] = None

        def _emit_eng(eng, builder, touched, has_ctr=False):
            if _stage[eng] is not None:
                conflict = bool((_stage[eng][1] | _prevt[eng]) & touched)
                _flush(eng, attach=conflict)
            else:
                if _prevt[eng] & touched:
                    sc_ = sems["vsy"] if eng == "vector" else sems["ssy"]
                    sc_.v += 1
                    h, v = sc_.h, sc_.v

                    def _np(e, h=h, eng=eng):
                        if eng == "vector":
                            e.engine_nop()
                            e.engine_nop().then_inc(h, 1)
                        else:
                            e.activation(s_junk[:, 0:8], s_junk[:, 0:8],
                                         AF.Identity)
                            e.activation(s_junk[:, 8:16], s_junk[:, 8:16],
                                         AF.Identity).then_inc(h, 1)
                    Q[eng].append(_np)
                    Q[eng].append(lambda e, h=h, v=v: e.wait_ge(h, v))
                    _prevt[eng] = set()
            _stage[eng] = (builder, touched, has_ctr)

        def act(out, in_, func, ctr=None, **kw):
            touched = _names([out, in_, kw.get("accum_out"), kw.get("bias"),
                              kw.get("scale")])
            if ctr is not None:
                ctr.v += 1
                h = ctr.h
                b = lambda e: e.activation(out, in_, func, **kw).then_inc(h, 1)
                _emit_eng("scalar", b, touched, has_ctr=True)
            else:
                b = lambda e: e.activation(out, in_, func, **kw)
                _emit_eng("scalar", b, touched)

        def vec(meth, *a, ctr=None, **kw):
            touched = _names(list(a) + list(kw.values()))
            if ctr is not None:
                ctr.v += 1
                h = ctr.h
                b = lambda e: getattr(e, meth)(*a, **kw).then_inc(h, 1)
                _emit_eng("vector", b, touched, has_ctr=True)
            else:
                b = lambda e: getattr(e, meth)(*a, **kw)
                _emit_eng("vector", b, touched)

        def coll(kind, op, ins_, outs_, ctr):
            ctr.v += 1
            h = ctr.h
            q("gpsimd", lambda e: e.collective_compute(
                kind, op, replica_groups=RG8, ins=ins_, outs=outs_
            ).then_inc(h, 1))

        _bc_reg = {}

        def iscat(out_ap, idx_ap, in_ap, ctr, bc):
            ctr.v += 16
            h = ctr.h

            def fn(e):
                if bc not in _bc_reg:
                    _bc_reg[bc] = e.to_reg(bc)
                e.indirect_dma_start(
                    out_ap, IndirectOffsetOnAxis(idx_ap, 0), in_ap, None,
                    bounds_check=_bc_reg[bc], oob_is_err=False).then_inc(h, 16)
            q("gpsimd", fn)

        def igat(out_ap, in_ap, idx_ap, ctr):
            ctr.v += 16
            h = ctr.h
            q("gpsimd", lambda e: e.indirect_dma_start(
                out_ap, None, in_ap, IndirectOffsetOnAxis(idx_ap, 0)
            ).then_inc(h, 16))

        satt_flat = s_xnT[:].rearrange("p a t -> p (a t)")

        def satt(tt):
            return satt_flat[:, D * tt:D * (tt + 1)]

        _wslot = [0]

        def wload(dst, src, a=None):
            sl = _wslot[0] % 4
            _wslot[0] += 1
            dsem = _wsems[sl]
            stg = s_wf32[:, sl, :]
            if a is not None:
                stg = stg.rearrange("p (a f) -> p a f", a=a)
            _flush("scalar")
            guard("scalar", ("wf", sl))
            dma("scalar", stg, src, dsem)
            wait("vector", dsem)
            vec("tensor_copy", dst, stg, ctr=sve)
            settok(("wf", sl), sve)

        vec("memset", s_junk[:], 0.0)
        vec("memset", s_zero[:], 0.0)
        vec("memset", s_stage[:], 0.0)
        vec("memset", s_p0[:], 0.0)
        vec("memset", s_p1[:], 0.0)
        vec("memset", s_p2[:], 0.0)
        vec("memset", s_p3[:], 0.0, ctr=sve)
        zero_tok = sve.v
        wait("scalar", sve, zero_tok)
        act(s_junk[:, 0:16], s_junk[:, 0:16], AF.Identity)
        act(s_junk[:, 16:32], s_junk[:, 16:32], AF.Identity)

        dma("gpsimd", c_iota[:], iota_h[:], gq)
        dma("gpsimd", c_ones[:], onesc_h[:], gq)
        dma("gpsimd", c_u128[:], u128_h[:], gq)
        dma("gpsimd", c_uc[:], uc_h[:], gq)
        dma("gpsimd", w_watt[:], watt_h.rearrange("a p f -> p a f"), gq)
        dma("gpsimd", w_rwT[:], rwT_h.rearrange("(a p) f -> p a f", p=128), gq)
        gq_const = gq.v

        gq_cast = gq.v

        dma("sync", b_wvb[:], wvb_h[0:1, :].to_broadcast([128, D]), sq)
        dma("sync", b_owb[:], owb_h[0:1, :].to_broadcast([128, D]), sq)
        dma("sync", b_eb[:], ebias_h[0:1, :].to_broadcast([128, NE]), sq)
        dma("sync", s_pm[:], pm_h[:], sq)
        dmaslow("sync", s_tokid[:],
                tokid_h.rearrange("(a p) o -> p (a o)", p=128), sq)
        sq_const = sq.v

        for tt in range(NT):
            sl = tt % 2
            xsem = sems["dw_xa"] if sl == 0 else sems["dw_xb"]
            guard("sync", ("xt", sl))
            dma("sync", s_xt[:, sl, :], x_h[128 * tt:128 * (tt + 1), :], xsem)
            wait("vector", xsem)
            vec("reduce_sum", s_lns[:, 0:1], s_xt[:, sl, :], AX.X)
            wait("scalar", xsem)
            guard("scalar", ("moe_scr",))
            act(s_moe[:], s_xt[:, sl, :], AF.Square, ctr=ssc,
                accum_out=s_lns[:, 1:2])
            settok(("moe_scr",), ssc)
            wait("vector", ssc)
            vec("tensor_scalar_mul", s_lns[:, 2:3], s_lns[:, 0:1], 1.0 / D)
            vec("tensor_scalar_mul", s_lns[:, 3:4], s_lns[:, 1:2], 1.0 / D)
            vec("tensor_mul", s_lns[:, 4:5], s_lns[:, 2:3], s_lns[:, 2:3])
            vec("tensor_sub", s_lns[:, 5:6], s_lns[:, 3:4], s_lns[:, 4:5],
                ctr=sve)
            vec("tensor_scalar_add", s_lns[:, 5:6], s_lns[:, 5:6], 1e-5,
                ctr=sve)
            wait("scalar", sve)
            act(s_lns[:, 6:7], s_lns[:, 5:6], AF.Sqrt, ctr=ssc)
            wait("vector", ssc)
            vec("reciprocal", s_lns[:, 7:8], s_lns[:, 6:7])
            guard("vector", ("xnst", sl))
            vec("scalar_tensor_tensor", s_xn_st[:, sl, :], s_xt[:, sl, :],
                s_lns[:, 2:3], s_lns[:, 7:8].to_broadcast([128, D]),
                ALU.subtract, ALU.mult, ctr=sve)
            settok(("xt", sl), sve)
            wait("sync", sve)
            dma("sync", d_xn[128 * tt:128 * (tt + 1), :], s_xn_st[:, sl, :], sq)
            settok(("xnst", sl), sq)
        sq_xn = sq.v

        wait("gpsimd", sq, sq_xn)
        coll("AllGather", ALU.bypass, [d_xn[:]], [d_xnall[:]], sems["c_xn"])

        wait("sync", sq, sq_xn)
        for jj in range(NJ):
            dmaT("sync", s_xnT[:, jj, :], d_xn[:, 128 * jj:128 * (jj + 1)], sq)
        sq_xnT = sq.v

        if DEBUG:
            dma("sync", dbg["xn"][:], d_xn[:], sq)

        mark()
        wait("tensor", sq, sq_xnT)
        wait("tensor", gq, gq_const)
        wait("vector", sq, sq_const)
        for tt in range(NT):
            guard("tensor", ("psA",))
            for j in range(NJ):
                mm(p_bg0[:, 0:NE], s_xnT[:, j, 128 * tt:128 * (tt + 1)],
                   w_rwT[:, j, :], start=(j == 0), stop=(j == NJ - 1),
                   ctr=smm if j == NJ - 1 else None)
            wait("scalar", smm)
            act(s_scores[:], p_bg0[:, 0:NE], AF.Sigmoid, ctr=ssc)
            settok(("psA",), ssc)
            wait("vector", ssc)
            vec("tensor_add", s_biased[:], s_scores[:], b_eb[:])
            vec("max", out=s_m8[:], in_=s_biased[:])
            vec("max_index", s_i8[:], s_m8[:], s_biased[:])
            vec("tensor_copy", s_if6a[:, tt, :], s_i8[:])
            vec("memset", s_Af[:], 0.0)
            cur, nxt = s_Af, s_Af2
            for k in range(EK):
                vec("scalar_tensor_tensor", nxt[:], c_iota[:],
                    s_if6a[:, tt, k:k + 1], cur[:], ALU.is_equal, ALU.add)
                cur, nxt = nxt, cur
                vec("scalar_tensor_tensor", s_scr64[:], c_iota[:],
                    s_if6a[:, tt, k:k + 1], s_scores[:], ALU.is_equal,
                    ALU.mult, accum_out=s_sc6[:, k:k + 1])
            vec("tensor_copy", s_A[:, tt, :], cur[:])
            vec("reduce_sum", s_t8a[:, 0:1], s_sc6[:, 0:EK], AX.X)
            vec("tensor_scalar_add", s_t8a[:, 1:2], s_t8a[:, 0:1], 1e-20)
            vec("reciprocal", s_t8a[:, 2:3], s_t8a[:, 1:2])
            vec("tensor_scalar_mul", s_pw[:, tt, 0:EK], s_sc6[:, 0:EK],
                s_t8a[:, 2:3], ctr=sve)
            wait("tensor", sve)
            guard("tensor", ("psB",))
            mm(p_bg1[0:64, 0:NE], c_ones[:], s_A[:, tt, :], ctr=smm)
            wait("vector", smm)
            guard("vector", ("cntst",))
            vec("tensor_copy", s_cnt[:], p_bg1[0:1, 0:NE], ctr=sve)
            settok(("psB",), sve)
            wait("sync", sve)
            dma("sync", d_ss[tt:tt + 1, :], s_cnt[:], sq)
            settok(("cntst",), sq)
        sq_ss = sq.v

        wait("gpsimd", sq, sq_ss)
        wait("gpsimd", sve, zero_tok)
        dma("gpsimd", s_stage[0:16, :], d_ss[:], gq)
        wait("tensor", gq)
        guard("tensor", ("psB",))
        mm(p_bg1[0:64, 0:NE], c_ones[0:64, :], s_stage[:], ctr=smm)
        wait("vector", smm)
        guard("vector", ("cntst",))
        vec("tensor_copy", s_cnt[:], p_bg1[0:1, 0:NE], ctr=sve)
        settok(("psB",), sve)
        wait("sync", sve)
        dma("sync", d_cnt[:], s_cnt[:], sq)
        settok(("cntst",), sq)
        wait("gpsimd", sq, sq.v)
        coll("AllGather", ALU.bypass, [d_cnt[:]], [d_cs[:]], sems["c_cnt"])

        mark()
        sw2v = sw2T_h.rearrange("(a p) f -> p a f", p=128)
        for jj in range(16):
            for h2 in range(2):
                wload(w_sw2r[:, jj, 512 * h2:512 * (h2 + 1)],
                      sw2v[:, jj, 512 * h2:512 * (h2 + 1)])
        sw2r_tok = sve.v
        wait("vector", sems["c_cnt"])
        vec("memset", s_co[:], 0.0)
        cur, nxt = s_co, s_co2
        for cc in range(8):
            wait("sync", sems["c_cnt"])
            guard("sync", ("csf",))
            dma("sync", s_csf[:], d_cs[cc:cc + 1, :], sq)
            wait("vector", sq)
            vec("scalar_tensor_tensor", nxt[:], s_csf[:],
                s_pm[:, cc:cc + 1], cur[:], ALU.mult, ALU.add, ctr=sve)
            settok(("csf",), sve)
            cur, nxt = nxt, cur
        vec("tensor_copy", s_coi[:], cur[:])
        vec("tensor_scalar", s_cq[:], s_coi[:], 6, None, ALU.arith_shift_right)
        vec("tensor_scalar", s_coi[:], s_coi[:], 63, None, ALU.bitwise_and)
        vec("tensor_copy", s_cqf[:], s_cq[:])
        vec("tensor_copy", s_crf[:], s_coi[:], ctr=sve)
        sve_qr = sve.v
        wait("gpsimd", sve, sve_qr)
        dma("gpsimd", s_stage[0:1, :], s_cqf[:], gq)
        dma("gpsimd", s_stage[1:2, :], s_crf[:], gq)
        dma("gpsimd", s_stage[2:18, :], d_ss[:], gq)
        wait("tensor", gq)
        wait("tensor", sve, sve_qr)
        guard("tensor", ("psA",))
        mm(p_bg0[0:64, 0:NE], c_uc[:], s_stage[:], ctr=smm)
        wait("vector", smm)
        vec("tensor_copy", s_choff[:], p_bg0[0:64, 0:NE], ctr=sve)
        settok(("psA",), sve)
        wait("sync", sve)
        dma("sync", d_choff[:], s_choff[0:16, :], sq)
        sq_choff = sq.v
        if DEBUG:
            dma("sync", dbg["choff"][:], d_choff[:], sq)

        wait("gpsimd", sve, zero_tok)
        dma("gpsimd", d_tokmap[:].rearrange("(p b) o -> p (b o)", p=128),
            s_zero[:, 0:196], gq)
        gq_tmz = gq.v

        for tt in range(NT):
            osl = tt % 2
            guard("sync", ("offb", osl))
            wait("sync", sq, sq_choff)
            dma("sync", s_offb[:, osl, :],
                d_choff[tt:tt + 1, :].to_broadcast([128, NE]), sq)
            off_ld = sq.v
            guard("tensor", ("psB",))
            mm(p_bg1[:, 0:NE], c_u128[:], s_A[:, tt, :], ctr=smm)
            wait("vector", smm)
            wait("vector", sq, off_ld)
            vec("tensor_copy", s_C[:], p_bg1[:, 0:NE], ctr=sve)
            settok(("psB",), sve)
            vec("tensor_add", s_Ct[:], s_C[:], s_offb[:, osl, :], ctr=sve)
            settok(("offb", osl), sve)
            for k in range(EK):
                vec("scalar_tensor_tensor", s_scr64[:], c_iota[:],
                    s_if6a[:, tt, k:k + 1], s_Ct[:], ALU.is_equal, ALU.mult,
                    accum_out=s_r6[:, k:k + 1])
            E6 = slice(0, EK)
            vec("tensor_scalar", s_v6[:, E6], s_r6[:, E6], float(CAP), None,
                ALU.is_lt)
            vec("tensor_mul", s_pw[:, tt, E6], s_pw[:, tt, E6], s_v6[:, E6])
            vec("tensor_scalar", s_t8a[:, E6], s_if6a[:, tt, E6], float(CAP),
                None, ALU.mult)
            vec("tensor_add", s_t8a[:, E6], s_t8a[:, E6], s_r6[:, E6])
            vec("tensor_copy", s_r6i[:, E6], s_if6a[:, tt, E6])
            vec("tensor_scalar", s_r6i[:, E6], s_r6i[:, E6], 3, None,
                ALU.arith_shift_right)
            vec("tensor_copy", s_t8b[:, E6], s_r6i[:, E6])
            vec("tensor_scalar", s_t8b[:, E6], s_t8b[:, E6], 64.0, None,
                ALU.mult)
            vec("tensor_add", s_t8b[:, E6], s_t8b[:, E6], s_t8a[:, E6])
            vec("tensor_scalar", s_scr64[:, 0:EK], s_v6[:, E6], -2e6, 2e6,
                ALU.mult, ALU.add)
            vec("tensor_add", s_t8b[:, E6], s_t8b[:, E6], s_scr64[:, 0:EK])
            vec("tensor_copy", s_sidx[:, tt, E6], s_t8b[:, E6])
            vec("tensor_scalar_min", s_t8a[:, E6], s_t8a[:, E6],
                float(NE * CAP - 1))
            vec("tensor_copy", s_gidx[:, tt, E6], s_t8a[:, E6], ctr=sve)
        sve_route = sve.v

        wait("gpsimd", sve, sve_route)
        wait("gpsimd", gq, gq_tmz)
        for tt in range(NT):
            for k in range(EK):
                iscat(d_tokmap[:], s_sidx[:, tt, k:k + 1],
                      s_tokid[:, tt:tt + 1], gq, 25087)
        wait("gpsimd", gq)
        coll("ReduceScatter", ALU.add, [d_tokmap[:]], [d_tokmy[:]],
             sems["c_tok"])
        if DEBUG:
            wait("sync", sems["c_tok"])
            dma("sync", dbg["tokmy"][:], d_tokmy[:], sq)
            dma("sync", dbg["cnt"][:], d_cnt[:], sq)
            wait("sync", sve, sve_route)
            dma("sync", dbg["if6"][:], s_if6a[:], sq)
            dma("sync", dbg["pw"][:], s_pw[:], sq)
            dma("sync", dbg["gidx"][:], s_gidx[:], sq)

        mark()
        wait("tensor", sve, sw2r_tok)
        for u in range(NU):
            tsl = slice(UT * u, UT * (u + 1))
            sw1v = sw1T_h.rearrange("(a p) f -> p a f", p=128)
            for ft in range(16):
                slw = ft % 2
                guard("vector", ("sw1", slw))
                for h2 in range(2):
                    wload(w_sw1s[:, slw, 4 * h2:4 * (h2 + 1), :],
                          sw1v[:, 4 * h2:4 * (h2 + 1),
                               128 * ft:128 * (ft + 1)], a=4)
                cast_tok = sve.v
                pp = p_bg0 if ft % 2 == 0 else p_bg1
                pk = ("psA",) if ft % 2 == 0 else ("psB",)
                guard("tensor", pk)
                wait("tensor", sve, cast_tok)
                for j in range(NJ):
                    mm(pp[:, 0:UT], w_sw1s[:, slw, j, :], s_xnT[:, j, tsl],
                       start=(j == 0), stop=(j == NJ - 1),
                       ctr=smm if j == NJ - 1 else None)
                settok(("sw1", slw), smm)
                wait("vector", smm)
                vec("tensor_copy", s_hraw[:, ft, :], pp[:, 0:UT], ctr=sve)
                settok(pk, sve)
            wait("scalar", sve)
            act(s_hraw[:].rearrange("p a f -> p (a f)"),
                s_hraw[:].rearrange("p a f -> p (a f)"),
                AF.Gelu_apprx_tanh, ctr=ssc)
            wait("tensor", ssc)
            for tq in range(2):
                tt = 2 * u + tq
                for ih in range(2):
                    pp = p_bg0 if ih == 0 else p_bg1
                    pk = ("psA",) if ih == 0 else ("psB",)
                    guard("tensor", pk)
                    for ft in range(16):
                        mm(pp[:, 0:512],
                           s_hraw[:, ft, 128 * tq:128 * (tq + 1)],
                           w_sw2r[:, ft, 512 * ih:512 * (ih + 1)],
                           start=(ft == 0), stop=(ft == 15),
                           ctr=smm if ft == 15 else None)
                    wait("vector", smm)
                    guard("vector", ("sh2st", tt % 2))
                    vec("tensor_copy",
                        s_sh2st[:, tt % 2, 512 * ih:512 * (ih + 1)],
                        pp[:, 0:512], ctr=sve)
                    settok(pk, sve)
                wait("sync", sve)
                dma("sync", d_sh2[128 * tt:128 * (tt + 1), :],
                    s_sh2st[:, tt % 2, :], sq)
                settok(("sh2st", tt % 2), sq)
        sq_sh2 = sq.v
        if DEBUG:
            wait("sync", sq, sq_sh2)
            dma("sync", dbg["sh2"][:], d_sh2[:], sq)

        mark()
        wait("gpsimd", sems["c_tok"])
        wait("gpsimd", sems["c_xn"])
        dmaslow("gpsimd", s_tmf[:],
                d_tokmy[0:3072, :].rearrange("(j p) o -> p (j o)", p=128), gq)
        wait("vector", gq)
        vec("tensor_copy", s_tmidx[:], s_tmf[:], ctr=sve)
        wait("gpsimd", sve)
        for jj in range(24):
            sl = jj % 2
            gsem = sems["dw_y6a"] if sl == 0 else sems["dw_y6b"]
            guard("gpsimd", ("xsg", sl))
            igat(s_y6[:, sl, :], d_xnall[:], s_tmidx[:, jj:jj + 1], gsem)
            wait("gpsimd", gsem)
            dma("gpsimd", d_xs[128 * jj:128 * (jj + 1), :], s_y6[:, sl, :], gq)
            settok(("xsg", sl), gq)
        gq_xs = gq.v

        w2view = w_w1[:].rearrange("p a f -> p (a f)").rearrange(
            "p (b g) -> p b g", b=4)

        for eg in range(4):
            for el in range(2):
                e = 2 * eg + el
                guard("sync", ("xsT",))
                wait("sync", gq, gq_xs)
                for j in range(NJ):
                    dmaT("sync", s_xsT[:, j, :],
                         d_xs[CAP * e:CAP * (e + 1), 128 * j:128 * (j + 1)],
                         sems["dw_xsT"])
                wait("tensor", sems["dw_xsT"])
                guard("vector", ("w1",))
                w1v = w1_h[e].rearrange("(a p) f -> p a f", p=128)
                for j in range(NJ):
                    wload(w_w1[:, j, :], w1v[:, j, :])
                wait("tensor", sve)
                for ft in range(4):
                    pp = p_bg0 if ft % 2 == 0 else p_bg1
                    pk = ("psA",) if ft % 2 == 0 else ("psB",)
                    guard("tensor", pk)
                    for j in range(NJ):
                        mm(pp[:, 0:CAP], w_w1[:, j, 128 * ft:128 * (ft + 1)],
                           s_xsT[:, j, :], start=(j == 0), stop=(j == NJ - 1),
                           ctr=smm if j == NJ - 1 else None)
                    wait("vector", smm)
                    vec("tensor_copy", s_h1r[:, 4 * el + ft, :], pp[:, 0:CAP],
                        ctr=sve)
                    settok(pk, sve)
                settok(("xsT",), smm)
                settok(("w1",), smm)
            wait("scalar", sve)
            act(s_h1r[:].rearrange("p a f -> p (a f)"),
                s_h1r[:].rearrange("p a f -> p (a f)"),
                AF.Gelu_apprx_tanh, ctr=ssc)
            gelu_tok = ssc.v
            wait("tensor", ssc, gelu_tok)
            for el in range(2):
                e = 2 * eg + el
                guard("vector", ("w1",))
                w2v = w2_h[e].rearrange("(a p) f -> p a f", p=128)
                for c4 in range(4):
                    for h2 in range(2):
                        wload(w2view[:, c4, 512 * h2:512 * (h2 + 1)],
                              w2v[:, c4, 512 * h2:512 * (h2 + 1)])
                wait("tensor", sve)
                for ct in range(3):
                    for ih in range(2):
                        pp = p_bg0 if ih == 0 else p_bg1
                        pk = ("psA",) if ih == 0 else ("psB",)
                        guard("tensor", pk)
                        for f in range(4):
                            mm(pp[:, 0:512],
                               s_h1r[:, 4 * el + f, 128 * ct:128 * (ct + 1)],
                               w2view[:, f, 512 * ih:512 * (ih + 1)],
                               start=(f == 0), stop=(f == 3),
                               ctr=smm if f == 3 else None)
                        wait("vector", smm)
                        guard("vector", ("yst", ct % 2))
                        vec("tensor_copy",
                            s_yst[:, ct % 2, 512 * ih:512 * (ih + 1)],
                            pp[:, 0:512], ctr=sve)
                        settok(pk, sve)
                    wait("sync", sve)
                    dma("sync",
                        d_ymine[CAP * e + 128 * ct:CAP * e + 128 * (ct + 1), :],
                        s_yst[:, ct % 2, :], sq)
                    settok(("yst", ct % 2), sq)
                settok(("w1",), smm)
        sq_y = sq.v
        wait("gpsimd", sq, sq_y)
        coll("AllGather", ALU.bypass, [d_ymine[:]], [d_yall[:]], sems["c_y"])
        if DEBUG:
            dma("sync", dbg["y"][:], d_ymine[:], sq)

        mark()
        wait("sync", gq, gq_cast)
        vec("memset", s_vx[:, :, 64:65], 1.0)
        vec("memset", s_vx[:, :, 129:130], 1.0, ctr=sve)
        inv_slot = [0]
        for p in range(NP):
            guard("vector", ("vxall",))
            vsl = p % 2
            dsem = sems["dw_vxa"] if vsl == 0 else sems["dw_vxb"]
            guard("gpsimd", ("wvx", vsl))
            dma("gpsimd", w_vx[:, vsl, :, :],
                wvT_h.rearrange("(a p) f -> p a f", p=128)
                [:, :, 128 * p:128 * (p + 1)], dsem)
            wait("tensor", dsem)
            for tt in range(NT):
                vpp = p_bg0 if tt % 2 == 0 else p_bg1
                vpk = ("psA",) if tt % 2 == 0 else ("psB",)
                guard("tensor", vpk)
                for j in range(NJ):
                    mm(vpp[:, 0:128], s_xnT[:, j, 128 * tt:128 * (tt + 1)],
                       w_vx[:, vsl, j, :], start=(j == 0), stop=(j == NJ - 1),
                       ctr=smm if j == NJ - 1 else None)
                wait("vector", smm)
                guard("vector", ("vx", tt))
                vec("tensor_add", s_vx[:, tt, 0:64], vpp[:, 0:64],
                    b_wvb[:, 128 * p:128 * p + 64])
                vec("tensor_add", s_vx[:, tt, 65:129], vpp[:, 64:128],
                    b_wvb[:, 128 * p + 64:128 * (p + 1)], ctr=sve)
                settok(vpk, sve)
                settok(("vx", tt), sve)
            settok(("wvx", vsl), smm)
            for tc in range(4):
                guard("tensor", ("psA",))
                mm(p_bg0[0:64, 0:512], w_watt[0:64, p, :],
                   s_xnT[0:64, p, 512 * tc:512 * (tc + 1)], tp=(0, 0),
                   start=True, stop=True)
                guard("tensor", ("psB",))
                mm(p_bg1[64:128, 0:512], w_watt[64:128, p, :],
                   s_xnT[64:128, p, 512 * tc:512 * (tc + 1)], tp=(64, 64),
                   start=True, stop=True, ctr=smm)
                wait("vector", smm)
                guard("vector", ("xw", tc))
                vec("tensor_copy", s_xw[0:64, tc, :], p_bg0[0:64, 0:512])
                vec("tensor_copy", s_xw[64:128, tc, :], p_bg1[64:128, 0:512],
                    ctr=sve)
                settok(("psA",), sve)
                settok(("psB",), sve)
                settok(("xw", tc), sve)
            wait("tensor", sve)

            for tc in range(4):
                prev = None

                def emit_o(prev):
                    psch, psex = prev
                    ppar = psch % 2
                    ppA = (s_p0, s_p1, s_p2, s_p3)[ppar * 2]
                    ppB = (s_p0, s_p1, s_p2, s_p3)[ppar * 2 + 1]
                    if psch == 0:
                        guard("tensor", ("o_rs",))
                    wait("tensor", sex, psex)
                    first = (psch == 0)
                    last = (psch == NT - 1)
                    mm(p_o0[0:65, :], s_vx[:, psch, 0:65], ppA[:],
                       tp=(0, 0), start=first, stop=last)
                    mm(p_o1[0:65, :], s_vx[:, psch, 65:130], ppB[:],
                       tp=(0, 0), start=first, stop=last, ctr=smm)
                    tok[("pT", ppar * 2)] = (smm, smm.v)
                    tok[("pT", ppar * 2 + 1)] = (smm, smm.v)

                for sch in range(NT):
                    par = sch % 2
                    sl0 = par * 2
                    sl1 = sl0 + 1
                    pbA = p_att0 if par == 0 else p_rs0
                    pbB = p_att1 if par == 0 else p_rs1
                    pA = (s_p0, s_p1, s_p2, s_p3)[sl0]
                    pB = (s_p0, s_p1, s_p2, s_p3)[sl1]
                    guard("tensor", ("attA", par))
                    mm(pbA[:, 0:512],
                       s_xnT[0:64, p, 128 * sch:128 * (sch + 1)],
                       s_xw[0:64, tc, :], tp=(0, 0), ctr=smm)
                    att0_tok = smm.v
                    guard("tensor", ("attB", par))
                    mm(pbB[:, 0:512],
                       s_xnT[64:128, p, 128 * sch:128 * (sch + 1)],
                       s_xw[64:128, tc, :], tp=(64, 0), ctr=smm)
                    att1_tok = smm.v
                    if prev is not None:
                        emit_o(prev)
                    guard("scalar", ("pT", sl0))
                    wait("scalar", smm, att0_tok)
                    act(pA[:], pbA[:, 0:512], AF.Exp, scale=0.125, ctr=sex)
                    settok(("attA", par), sex)
                    guard("scalar", ("pT", sl1))
                    wait("scalar", smm, att1_tok)
                    act(pB[:], pbB[:, 0:512], AF.Exp, scale=0.125, ctr=sex)
                    settok(("attB", par), sex)
                    prev = (sch, sex.v)
                emit_o(prev)
                o_tok = smm.v
                slA = inv_slot[0]
                slB = slA + 1
                inv_slot[0] += 2
                wait("vector", smm, o_tok)
                guard("vector", ("xt", 0))
                vec("tensor_copy", s_xt[64:65, 0, 0:512], p_o0[64:65, :])
                vec("tensor_copy", s_xt[64:65, 0, 512:1024], p_o1[64:65, :],
                    ctr=sve)
                wait("sync", sve)
                dma("sync", d_inv[slA:slA + 1, :], s_xt[64:65, 0, 0:512],
                    sems["dw_inva"])
                dma("sync", d_inv[slB:slB + 1, :], s_xt[64:65, 0, 512:1024],
                    sems["dw_inva"])
                settok(("xt", 0), sems["dw_inva"])
                settok(("xt2", 0), sems["dw_inva"])
                wait("sync", sems["dw_inva"])
                guard("sync", ("invb",))
                dma("sync", s_invA[0:64, :],
                    d_inv[slA:slA + 1, :].to_broadcast([64, 512]),
                    sems["dw_invb"])
                dma("sync", s_invB[0:64, :],
                    d_inv[slB:slB + 1, :].to_broadcast([64, 512]),
                    sems["dw_invb"])
                wait("vector", sems["dw_invb"])
                vec("reciprocal", s_invA[:], s_invA[:])
                vec("reciprocal", s_invB[:], s_invB[:])
                vec("tensor_mul", s_oT[0:64, p, 512 * tc:512 * (tc + 1)],
                    p_o0[0:64, :], s_invA[:])
                guard("vector", ("pT", 0))
                vec("tensor_mul", s_p0[0:64, :], p_o1[0:64, :], s_invB[:],
                    ctr=sve)
                settok(("invb",), sve)
                settok(("o_rs",), sve)
                wait("sync", sve)
                dma("sync", s_oT[64:128, p, 512 * tc:512 * (tc + 1)],
                    s_p0[0:64, :], sq)
                settok(("pT", 0), sq)
            settok(("vxall",), smm)
        sve_att = sve.v
        sq_oB = sq.v
        if DEBUG:
            wait("sync", sve, sve_att)
            dma("sync", dbg["ot"][:], s_oT[:], sq)

        mark()
        ow16v = owT_h.rearrange("(a p) f -> p a f", p=128)
        owbuf = w_w1[:]
        wait("vector", sve, sve_att)
        wait("tensor", sq, sq_oB)
        for fh in range(2):
            guard("vector", ("w1",))
            guard("vector", ("owb",))
            for p in range(NP):
                wload(owbuf[:, p, :], ow16v[:, p, 512 * fh:512 * (fh + 1)])
            wait("tensor", sve)
            wait("tensor", sve, sve_att)
            for tg in range(8):
                for ti in range(2):
                    tt = 2 * tg + ti
                    pp = (p_bg0, p_bg1)[ti]
                    pk = (("psA",), ("psB",))[ti]
                    guard("tensor", pk)
                    for p in range(NP):
                        mm(pp[:, 0:512], s_oT[:, p, 128 * tt:128 * (tt + 1)],
                           owbuf[:, p, :], start=(p == 0), stop=(p == NP - 1),
                           ctr=smm if p == NP - 1 else None)
                    wait("vector", smm)
                    vec("tensor_add", satt(tt)[:, 512 * fh:512 * (fh + 1)],
                        pp[:, 0:512], b_owb[:, 512 * fh:512 * (fh + 1)],
                        ctr=sve)
                    settok(pk, sve)
            settok(("owb",), smm)
        sve_op = sve.v
        if DEBUG:
            wait("sync", sve, sve_op)
            for tt in range(NT):
                dma("sync", dbg["satt"][128 * tt:128 * (tt + 1), :],
                    satt(tt), sq)

        mark()
        wait("gpsimd", sems["c_y"])
        wait("vector", sve, sve_op)
        for tt in range(NT):
            for k in range(EK):
                sl = k % 2
                gsem = sems["dw_y6a"] if sl == 0 else sems["dw_y6b"]
                guard("gpsimd", ("y6", sl))
                igat(s_y6[:, sl, :], d_yall[:], s_gidx[:, tt, k:k + 1], gsem)
                wait("vector", gsem)
                guard("vector", ("moe_scr",)) if k == 0 else None
                if k == 0:
                    vec("tensor_scalar_mul", s_moe[:], s_y6[:, sl, :],
                        s_pw[:, tt, 0:1], ctr=sve)
                else:
                    vec("scalar_tensor_tensor", s_moe[:], s_y6[:, sl, :],
                        s_pw[:, tt, k:k + 1], s_moe[:], ALU.mult, ALU.add,
                        ctr=sve)
                settok(("y6", sl), sve)
            shsem = sems["dw_sh2a"] if tt % 2 == 0 else sems["dw_sh2b"]
            guard("sync", ("sh2ld", tt % 2))
            if tt == 0:
                wait("sync", sq, sq_sh2)
            dma("sync", s_sh2st[:, tt % 2, :],
                d_sh2[128 * tt:128 * (tt + 1), :], shsem)
            wait("vector", shsem)
            vec("tensor_add", s_moe[:], s_moe[:], s_sh2st[:, tt % 2, :],
                ctr=sve)
            settok(("sh2ld", tt % 2), sve)
            wait("scalar", sve)
            guard("scalar", ("gst", tt % 2))
            act(s_xn_st[:, tt % 2, :], s_moe[:], AF.Silu, ctr=ssc)
            settok(("moe_scr",), ssc)
            xsem = sems["dw_xa"] if tt % 2 == 0 else sems["dw_xb"]
            guard("sync", ("xt2", tt % 2))
            dma("sync", s_xt[:, tt % 2, :], x_h[128 * tt:128 * (tt + 1), :],
                xsem)
            wait("vector", ssc)
            wait("vector", xsem)
            vec("tensor_mul", s_moe[:], satt(tt), s_xn_st[:, tt % 2, :],
                ctr=sve)
            settok(("gst", tt % 2), sve)
            vec("tensor_add", s_xt[:, tt % 2, :], s_xt[:, tt % 2, :],
                s_moe[:], ctr=sve)
            wait("sync", sve)
            dma("sync", out_h[128 * tt:128 * (tt + 1), :], s_xt[:, tt % 2, :],
                sq)
            settok(("xt2", tt % 2), sq)

        mark()
        wait("sync", sq, sq.v)
        wait("gpsimd", gq, gq.v)

        _flush("vector")
        _flush("scalar")
        mark()
        if PHASE_LIMIT is not None:
            cut = marks[PHASE_LIMIT]
            for k in Q:
                Q[k] = Q[k][:cut[k]]

        with nc.Block() as block:
            @block.sync
            def _(e):
                for fn in Q["sync"]:
                    fn(e)

            @block.gpsimd
            def _(e):
                for fn in Q["gpsimd"]:
                    fn(e)

            @block.tensor
            def _(e):
                for fn in Q["tensor"]:
                    fn(e)

            @block.scalar
            def _(e):
                for fn in Q["scalar"]:
                    fn(e)

            @block.vector
            def _(e):
                for fn in Q["vector"]:
                    fn(e)

    return nc, dbg


_NC_CACHE = {}


def _get_nc():
    if "nc" not in _NC_CACHE:
        _NC_CACHE["nc"] = build_nc()
    return _NC_CACHE["nc"]


def kernel(L0_attn_x, L0_attn_w_att_val, L0_attn_wv_w, L0_attn_wv_b,
           L0_attn_out_w, L0_attn_out_b, L0_ffn_router_weight,
           L0_ffn_expert_bias, L0_ffn_w1, L0_ffn_w2, L0_ffn_sw1, L0_ffn_sw2):
    x = np.ascontiguousarray(np.asarray(L0_attn_x, dtype=np.float32))
    watt = np.asarray(L0_attn_w_att_val, dtype=np.float32)
    wv = np.asarray(L0_attn_wv_w, dtype=np.float32)
    wvb = np.asarray(L0_attn_wv_b, dtype=np.float32)
    ow = np.asarray(L0_attn_out_w, dtype=np.float32)
    owb = np.asarray(L0_attn_out_b, dtype=np.float32)
    rw = np.asarray(L0_ffn_router_weight, dtype=np.float32)
    eb = np.asarray(L0_ffn_expert_bias, dtype=np.float32)
    w1 = np.asarray(L0_ffn_w1, dtype=np.float32)
    w2 = np.asarray(L0_ffn_w2, dtype=np.float32)
    sw1 = np.asarray(L0_ffn_sw1, dtype=np.float32)
    sw2 = np.asarray(L0_ffn_sw2, dtype=np.float32)

    wvT = np.ascontiguousarray(wv.T)
    owT = np.ascontiguousarray(ow.T)
    sw1T = np.ascontiguousarray(sw1.T)
    sw2T = np.ascontiguousarray(sw2.T)
    rwT = np.ascontiguousarray(rw.T)
    wattp = np.zeros((NP, 128, 64), np.float32)
    for p in range(NP):
        wattp[p, 0:64] = watt[2 * p]
        wattp[p, 64:128] = watt[2 * p + 1]
    u128 = np.triu(np.ones((128, 128), np.float32), 1)
    uc = np.zeros((64, 64), np.float32)
    for tt in range(16):
        uc[0, tt] = 64.0
        uc[1, tt] = 1.0
        for i in range(tt):
            uc[2 + i, tt] = 1.0
    iota = np.tile(np.arange(NE, dtype=np.float32), (128, 1))
    onesc = np.ones((128, 64), np.float32)

    nc, dbg = _get_nc()
    in_maps = []
    for c in range(8):
        in_maps.append(dict(
            x=np.ascontiguousarray(x[c]),
            wvT=wvT, owT=owT, sw1T=sw1T, sw2T=sw2T, rwT=rwT,
            watt=wattp,
            wvb=wvb.reshape(1, D), owb=owb.reshape(1, D),
            ebias=eb.reshape(1, NE),
            w1=np.ascontiguousarray(w1[ELOC * c:ELOC * (c + 1)]),
            w2=np.ascontiguousarray(w2[ELOC * c:ELOC * (c + 1)]),
            tokid=np.arange(T * c, T * (c + 1), dtype=np.float32).reshape(T, 1),
            pm=(np.arange(8, dtype=np.float32) < c).astype(np.float32)
            .reshape(1, 8),
            u128=u128, uc=uc, iota=iota, onesc=onesc,
        ))
    res = run_bass_kernel_spmd(nc, in_maps, core_ids=list(range(8)))
    out = np.stack([np.asarray(res.results[c]["out"]) for c in range(8)],
                   axis=0)
    if DEBUG:
        kernel.last_results = res.results
    return out


# version 08
# speedup vs baseline: 1.5001x; 1.0198x over previous
from contextlib import ExitStack

import numpy as np

import concourse.bass as bass
import concourse.mybir as mybir
from concourse.bass_utils import run_bass_kernel_spmd
from concourse.bass_core_rust import IndirectOffsetOnAxis

F32 = mybir.dt.float32
BF16 = mybir.dt.bfloat16
I32 = mybir.dt.int32
U32 = mybir.dt.uint32
AF = mybir.ActivationFunctionType
ALU = mybir.AluOpType
AX = mybir.AxisListType
RG8 = [[0, 1, 2, 3, 4, 5, 6, 7]]

T = 2048
D = 1024
NT = 16
NJ = 8
NP = 8
NE = 64
EK = 6
CAP = 384
EF = 512
SF = 2048
ELOC = 8
NU = 8
UT = 256

DEBUG = False
PHASE_LIMIT = None


class Ctr:
    __slots__ = ("h", "v", "is_dma", "dedicated")

    def __init__(self, h, is_dma=False, dedicated=False):
        self.h = h
        self.v = 0
        self.is_dma = is_dma
        self.dedicated = dedicated


def build_nc():
    nc = bass.Bass("TRN2", target_bir_lowering=False)

    def inp(name, shape, dt=F32):
        return nc.dram_tensor(name, shape, dt, kind="ExternalInput").ap()

    x_h = inp("x", [T, D])
    wvT_h = inp("wvT", [D, D])
    owT_h = inp("owT", [D, D])
    sw1T_h = inp("sw1T", [D, SF])
    sw2T_h = inp("sw2T", [SF, D])
    rwT_h = inp("rwT", [D, NE])
    watt_h = inp("watt", [NP, 128, 64])
    wvb_h = inp("wvb", [1, D])
    owb_h = inp("owb", [1, D])
    ebias_h = inp("ebias", [1, NE])
    w1_h = inp("w1", [ELOC, D, EF])
    w2_h = inp("w2", [ELOC, EF, D])
    tokid_h = inp("tokid", [T, 1])
    pm_h = inp("pm", [1, 8])
    u128_h = inp("u128", [128, 128])
    uc_h = inp("uc", [64, 64])
    iota_h = inp("iota", [128, NE])
    onesc_h = inp("onesc", [128, 64])

    out_h = nc.dram_tensor("out", [T, D], F32, kind="ExternalOutput").ap()

    def idram(name, shape, dt=F32):
        return nc.dram_tensor(name, shape, dt, kind="Internal").ap()

    d_xn = idram("d_xn", [T, D], BF16)
    d_xnall = idram("d_xnall", [8 * T, D], BF16)
    d_cnt = idram("d_cnt", [1, NE])
    d_cs = idram("d_cs", [8, NE])
    d_ss = idram("d_ss", [16, NE])
    d_choff = idram("d_choff", [NT, NE])
    d_tokmap = idram("d_tokmap", [25088, 1])
    d_tokmy = idram("d_tokmy", [3136, 1])
    d_xs = idram("d_xs", [ELOC * CAP, D], BF16)
    d_ymine = idram("d_ymine", [ELOC * CAP, D], BF16)
    d_yall = idram("d_yall", [NE * CAP, D], BF16)
    d_sh2 = idram("d_sh2", [T, D], BF16)
    d_inv = idram("d_inv", [64, 512])
    d_wv16 = idram("d_wv16", [D, D], BF16)
    d_sw1b = idram("d_sw1b", [D, SF], BF16)
    d_ow16 = idram("d_ow16", [D, D], BF16)
    d_w116 = idram("d_w116", [ELOC, D, EF], BF16)
    d_w216 = idram("d_w216", [ELOC, EF, D], BF16)

    dbg = {}

    def dbgout(name, shape, dt):
        if DEBUG:
            dbg[name] = nc.dram_tensor("dbg_" + name, shape, dt,
                                       kind="ExternalOutput").ap()

    dbgout("xn", [T, D], BF16)
    dbgout("cnt", [1, NE], F32)
    dbgout("choff", [NT, NE], F32)
    dbgout("if6", [128, NT, 8], F32)
    dbgout("pw", [128, NT, 8], F32)
    dbgout("gidx", [128, NT, 8], I32)
    dbgout("tokmy", [3136, 1], F32)
    dbgout("y", [ELOC * CAP, D], BF16)
    dbgout("sh2", [T, D], BF16)
    dbgout("ot", [128, NP, T], BF16)
    dbgout("satt", [T, D], BF16)

    st = ExitStack()
    with st:
        def sb(name, shape, dt):
            return st.enter_context(nc.sbuf_tensor(name, shape, dt))

        def psum(name, shape):
            return st.enter_context(nc.psum_tensor(name, shape, F32))

        c_iota = sb("c_iota", [128, NE], F32)
        c_ones = sb("c_ones", [128, 64], BF16)
        c_u128 = sb("c_u128", [128, 128], BF16)
        c_uc = sb("c_uc", [64, 64], BF16)
        w_watt = sb("w_watt", [128, NP, 64], BF16)
        w_rwT = sb("w_rwT", [128, NJ, NE], BF16)
        w_sw2r = sb("w_sw2r", [128, 16, D], BF16)
        w_sw1s = sb("w_sw1s", [128, 2, NJ, 128], BF16)
        w_vx = sb("w_vx", [128, 2, NJ, 128], BF16)
        w_ow = sb("w_ow", [128, 2, 512], BF16)
        w_w1 = sb("w_w1", [128, NJ, EF], BF16)
        s_wf32 = sb("s_wf32", [128, 4, 512], F32)
        b_wvb = sb("b_wvb", [128, D], F32)
        b_owb = sb("b_owb", [128, D], F32)
        b_eb = sb("b_eb", [128, NE], F32)
        s_pm = sb("s_pm", [1, 8], F32)
        s_tokid = sb("s_tokid", [128, NT], F32)

        s_xnT = sb("s_xnT", [128, NJ, T], BF16)
        s_oT = sb("s_oT", [128, NP, T], BF16)
        s_vx = sb("s_vx", [128, NT, 132], BF16)
        s_xw = sb("s_xw", [128, 4, 512], BF16)
        s_p0 = sb("s_p0", [128, 512], BF16)
        s_p1 = sb("s_p1", [128, 512], BF16)
        s_p2 = sb("s_p2", [128, 512], BF16)
        s_p3 = sb("s_p3", [128, 512], BF16)
        s_hraw = sb("s_hraw", [128, 16, UT], BF16)
        s_xn_st = sb("s_xn_st", [128, 2, D], BF16)
        s_sh2st = sb("s_sh2st", [128, 2, D], BF16)
        s_A = sb("s_A", [128, NT, NE], BF16)
        s_xsT = sb("s_xsT", [128, NJ, CAP], BF16)
        s_h1r = sb("s_h1r", [128, 8, CAP], BF16)
        s_yst = sb("s_yst", [128, 2, D], BF16)
        s_y6 = sb("s_y6", [128, 2, D], BF16)

        s_xt = sb("s_xt", [128, 2, D], F32)
        s_moe = sb("s_moe", [128, D], F32)
        s_lns = sb("s_lns", [128, 8], F32)
        s_scores = sb("s_scores", [128, NE], F32)
        s_biased = sb("s_biased", [128, NE], F32)
        s_m8 = sb("s_m8", [128, 8], F32)
        s_i8 = sb("s_i8", [128, 8], U32)
        s_if6a = sb("s_if6a", [128, NT, 8], F32)
        s_Af = sb("s_Af", [128, NE], F32)
        s_Af2 = sb("s_Af2", [128, NE], F32)
        s_scr64 = sb("s_scr64", [128, NE], F32)
        s_sc6 = sb("s_sc6", [128, 8], F32)
        s_pw = sb("s_pw", [128, NT, 8], F32)
        s_r6 = sb("s_r6", [128, 8], F32)
        s_v6 = sb("s_v6", [128, 8], F32)
        s_t8a = sb("s_t8a", [128, 8], F32)
        s_t8b = sb("s_t8b", [128, 8], F32)
        s_sidx = sb("s_sidx", [128, NT, 8], I32)
        s_gidx = sb("s_gidx", [128, NT, 8], I32)
        s_r6i = sb("s_r6i", [128, 8], I32)
        s_cnt = sb("s_cnt", [1, NE], F32)
        s_csf = sb("s_csf", [1, NE], F32)
        s_co = sb("s_co", [1, NE], F32)
        s_co2 = sb("s_co2", [1, NE], F32)
        s_coi = sb("s_coi", [1, NE], I32)
        s_cq = sb("s_cq", [1, NE], I32)
        s_cqf = sb("s_cqf", [1, NE], F32)
        s_crf = sb("s_crf", [1, NE], F32)
        s_stage = sb("s_stage", [64, 64], BF16)
        s_choff = sb("s_choff", [64, NE], F32)
        s_offb = sb("s_offb", [128, 2, NE], F32)
        s_C = sb("s_C", [128, NE], F32)
        s_Ct = sb("s_Ct", [128, NE], F32)
        s_invA = sb("s_invA", [64, 512], F32)
        s_invB = sb("s_invB", [64, 512], F32)
        s_tmf = sb("s_tmf", [128, 24], F32)
        s_tmidx = sb("s_tmidx", [128, 24], I32)
        s_zero = sb("s_zero", [128, 256], F32)
        s_junk = sb("s_junk", [128, 32], F32)

        p_att0 = psum("p_att0", [128, 512])
        p_att1 = psum("p_att1", [128, 512])
        p_o0 = psum("p_o0", [128, 512])
        p_o1 = psum("p_o1", [128, 512])
        p_rs0 = psum("p_rs0", [128, 512])
        p_rs1 = psum("p_rs1", [128, 512])
        p_bg0 = psum("p_bg0", [128, 512])
        p_bg1 = psum("p_bg1", [128, 512])

        names = "gq sq mm ex ve sc c_xn c_cnt c_tok c_y vsy ssy".split()
        sems = {nm: Ctr(st.enter_context(nc.semaphore("sem_" + nm)),
                        is_dma=nm in ("gq", "sq"))
                for nm in names}
        dnames = ("dw_sw1a dw_sw1b dw_vxa dw_vxb dw_w1 dw_w2 dw_xsT dw_ow "
                  "dw_xa dw_xb dw_inva dw_invb dw_y6a dw_y6b dw_sh2a dw_sh2b "
                  "dw_gath").split()
        for nm in dnames:
            sems[nm] = Ctr(st.enter_context(nc.semaphore("sem_" + nm)),
                           is_dma=True, dedicated=True)
        gq, sq = sems["gq"], sems["sq"]
        _wsems = [sems[n] for n in ("dw_w1", "dw_w2", "dw_sw1a", "dw_sw1b")]
        smm, sex, sve, ssc = sems["mm"], sems["ex"], sems["ve"], sems["sc"]

        Q = {k: [] for k in ("sync", "gpsimd", "tensor", "scalar", "vector")}
        marks = []

        def mark():
            _flush("vector")
            _flush("scalar")
            marks.append({k: len(v) for k, v in Q.items()})
        tok = {}

        def q(eng, fn):
            Q[eng].append(fn)

        _fence = {}

        def wait(eng, ctr, val=None):
            if ctr.dedicated:
                v = ctr.v if val is None else val
            else:
                v = ctr.v if (val is None or ctr.is_dma) else val
            h = ctr.h
            if ctr.is_dma and not ctr.dedicated:
                owner = "sync" if ctr is sems["sq"] else "gpsimd"
                if _fence.get(id(ctr), -1) < v:
                    _fence[id(ctr)] = v
                    if eng != owner:
                        q(owner, lambda e: e.wait_ge(h, v))
            fn = lambda e: e.wait_ge(h, v)
            if eng in ("vector", "scalar") and _stage[eng] is not None:
                _swaits[eng].append(fn)
            else:
                q(eng, fn)

        def guard(eng, key):
            if key in tok:
                ctr, v = tok[key]
                wait(eng, ctr, v)

        def settok(key, ctr):
            tok[key] = (ctr, ctr.v)

        def dma(eng, dst, src, ctr):
            ctr.v += 16
            h = ctr.h
            q(eng, lambda e: e.dma_start(dst, src).then_inc(h, 16))

        def dmaT(eng, dst, src, ctr):
            ctr.v += 16
            h = ctr.h
            q(eng, lambda e: e.dma_start_transpose(dst, src).then_inc(h, 16))

        def dmaslow(eng, dst, src, ctr):
            ctr.v += 16
            h = ctr.h
            q(eng, lambda e: e.dma_start(
                dst, src, allow_slow_non_contiguous=True).then_inc(h, 16))

        def mm(out, lhsT, rhs, start=True, stop=True, tp=None, ctr=None):
            if ctr is not None:
                ctr.v += 1
                h = ctr.h
                q("tensor", lambda e: e.matmul(
                    out, lhsT, rhs, start=start, stop=stop,
                    tile_position=tp).then_inc(h, 1))
            else:
                q("tensor", lambda e: e.matmul(out, lhsT, rhs, start=start,
                                               stop=stop, tile_position=tp))

        _stage = {"scalar": None, "vector": None}
        _swaits = {"scalar": [], "vector": []}
        _prevt = {"scalar": set(), "vector": set()}

        def _names(objs):
            out = set()
            for o in objs:
                t = getattr(o, "tensor", None)
                if t is not None:
                    out.add(t.name)
            return out

        def _flush(eng, attach=False):
            if _stage[eng] is None:
                return
            b, touched, has_ctr = _stage[eng]
            if attach:
                sc_ = sems["vsy"] if eng == "vector" else sems["ssy"]
                sc_.v += 1
                h, v = sc_.h, sc_.v

                def _att(e, b=b, h=h, has_ctr=has_ctr, eng=eng):
                    r = b(e)
                    if r is None or has_ctr:
                        if eng == "vector":
                            e.engine_nop()
                            r = e.engine_nop()
                        else:
                            e.activation(s_junk[:, 0:8], s_junk[:, 0:8],
                                         AF.Identity)
                            r = e.activation(s_junk[:, 8:16], s_junk[:, 8:16],
                                             AF.Identity)
                    r.then_inc(h, 1)
                Q[eng].append(_att)
                for w in _swaits[eng]:
                    Q[eng].append(w)
                Q[eng].append(lambda e, h=h, v=v: e.wait_ge(h, v))
            else:
                Q[eng].append(lambda e, b=b: b(e))
                for w in _swaits[eng]:
                    Q[eng].append(w)
            _swaits[eng] = []
            _prevt[eng] = set() if attach else touched
            _stage[eng] = None

        def _emit_eng(eng, builder, touched, has_ctr=False):
            if _stage[eng] is not None:
                conflict = bool((_stage[eng][1] | _prevt[eng]) & touched)
                _flush(eng, attach=conflict)
            else:
                if _prevt[eng] & touched:
                    sc_ = sems["vsy"] if eng == "vector" else sems["ssy"]
                    sc_.v += 1
                    h, v = sc_.h, sc_.v

                    def _np(e, h=h, eng=eng):
                        if eng == "vector":
                            e.engine_nop()
                            e.engine_nop().then_inc(h, 1)
                        else:
                            e.activation(s_junk[:, 0:8], s_junk[:, 0:8],
                                         AF.Identity)
                            e.activation(s_junk[:, 8:16], s_junk[:, 8:16],
                                         AF.Identity).then_inc(h, 1)
                    Q[eng].append(_np)
                    Q[eng].append(lambda e, h=h, v=v: e.wait_ge(h, v))
                    _prevt[eng] = set()
            _stage[eng] = (builder, touched, has_ctr)

        def act(out, in_, func, ctr=None, **kw):
            touched = _names([out, in_, kw.get("accum_out"), kw.get("bias"),
                              kw.get("scale")])
            if ctr is not None:
                ctr.v += 1
                h = ctr.h
                b = lambda e: e.activation(out, in_, func, **kw).then_inc(h, 1)
                _emit_eng("scalar", b, touched, has_ctr=True)
            else:
                b = lambda e: e.activation(out, in_, func, **kw)
                _emit_eng("scalar", b, touched)

        def vec(meth, *a, ctr=None, **kw):
            touched = _names(list(a) + list(kw.values()))
            if ctr is not None:
                ctr.v += 1
                h = ctr.h
                b = lambda e: getattr(e, meth)(*a, **kw).then_inc(h, 1)
                _emit_eng("vector", b, touched, has_ctr=True)
            else:
                b = lambda e: getattr(e, meth)(*a, **kw)
                _emit_eng("vector", b, touched)

        def coll(kind, op, ins_, outs_, ctr):
            ctr.v += 1
            h = ctr.h
            q("gpsimd", lambda e: e.collective_compute(
                kind, op, replica_groups=RG8, ins=ins_, outs=outs_
            ).then_inc(h, 1))

        _bc_reg = {}

        def iscat(out_ap, idx_ap, in_ap, ctr, bc):
            ctr.v += 16
            h = ctr.h

            def fn(e):
                if bc not in _bc_reg:
                    _bc_reg[bc] = e.to_reg(bc)
                e.indirect_dma_start(
                    out_ap, IndirectOffsetOnAxis(idx_ap, 0), in_ap, None,
                    bounds_check=_bc_reg[bc], oob_is_err=False).then_inc(h, 16)
            q("gpsimd", fn)

        def igat(out_ap, in_ap, idx_ap, ctr):
            ctr.v += 16
            h = ctr.h
            q("gpsimd", lambda e: e.indirect_dma_start(
                out_ap, None, in_ap, IndirectOffsetOnAxis(idx_ap, 0)
            ).then_inc(h, 16))

        satt_flat = s_xnT[:].rearrange("p a t -> p (a t)")

        def satt(tt):
            return satt_flat[:, D * tt:D * (tt + 1)]

        _wslot = [0]

        def wload(dst, src, a=None):
            sl = _wslot[0] % 4
            _wslot[0] += 1
            dsem = _wsems[sl]
            stg = s_wf32[:, sl, :]
            if a is not None:
                stg = stg.rearrange("p (a f) -> p a f", a=a)
            _flush("scalar")
            guard("scalar", ("wf", sl))
            dma("scalar", stg, src, dsem)
            wait("vector", dsem)
            vec("tensor_copy", dst, stg, ctr=sve)
            settok(("wf", sl), sve)

        vec("memset", s_junk[:], 0.0)
        vec("memset", s_zero[:], 0.0)
        vec("memset", s_stage[:], 0.0)
        vec("memset", s_p0[:], 0.0)
        vec("memset", s_p1[:], 0.0)
        vec("memset", s_p2[:], 0.0)
        vec("memset", s_p3[:], 0.0, ctr=sve)
        zero_tok = sve.v
        wait("scalar", sve, zero_tok)
        act(s_junk[:, 0:16], s_junk[:, 0:16], AF.Identity)
        act(s_junk[:, 16:32], s_junk[:, 16:32], AF.Identity)

        dma("gpsimd", c_iota[:], iota_h[:], gq)
        dma("gpsimd", c_ones[:], onesc_h[:], gq)
        dma("gpsimd", c_u128[:], u128_h[:], gq)
        dma("gpsimd", c_uc[:], uc_h[:], gq)
        dma("gpsimd", w_watt[:], watt_h.rearrange("a p f -> p a f"), gq)
        dma("gpsimd", w_rwT[:], rwT_h.rearrange("(a p) f -> p a f", p=128), gq)
        gq_const = gq.v

        gq_cast = gq.v

        dma("sync", b_wvb[:], wvb_h[0:1, :].to_broadcast([128, D]), sq)
        dma("sync", b_owb[:], owb_h[0:1, :].to_broadcast([128, D]), sq)
        dma("sync", b_eb[:], ebias_h[0:1, :].to_broadcast([128, NE]), sq)
        dma("sync", s_pm[:], pm_h[:], sq)
        dmaslow("sync", s_tokid[:],
                tokid_h.rearrange("(a p) o -> p (a o)", p=128), sq)
        sq_const = sq.v

        for tt in range(NT):
            sl = tt % 2
            xsem = sems["dw_xa"] if sl == 0 else sems["dw_xb"]
            guard("sync", ("xt", sl))
            dma("sync", s_xt[:, sl, :], x_h[128 * tt:128 * (tt + 1), :], xsem)
            wait("vector", xsem)
            vec("reduce_sum", s_lns[:, 0:1], s_xt[:, sl, :], AX.X)
            wait("scalar", xsem)
            guard("scalar", ("moe_scr",))
            act(s_moe[:], s_xt[:, sl, :], AF.Square, ctr=ssc,
                accum_out=s_lns[:, 1:2])
            settok(("moe_scr",), ssc)
            wait("vector", ssc)
            vec("tensor_scalar_mul", s_lns[:, 2:3], s_lns[:, 0:1], 1.0 / D)
            vec("tensor_scalar_mul", s_lns[:, 3:4], s_lns[:, 1:2], 1.0 / D)
            vec("tensor_mul", s_lns[:, 4:5], s_lns[:, 2:3], s_lns[:, 2:3])
            vec("tensor_sub", s_lns[:, 5:6], s_lns[:, 3:4], s_lns[:, 4:5],
                ctr=sve)
            vec("tensor_scalar_add", s_lns[:, 5:6], s_lns[:, 5:6], 1e-5,
                ctr=sve)
            wait("scalar", sve)
            act(s_lns[:, 6:7], s_lns[:, 5:6], AF.Sqrt, ctr=ssc)
            wait("vector", ssc)
            vec("reciprocal", s_lns[:, 7:8], s_lns[:, 6:7])
            guard("vector", ("xnst", sl))
            vec("scalar_tensor_tensor", s_xn_st[:, sl, :], s_xt[:, sl, :],
                s_lns[:, 2:3], s_lns[:, 7:8].to_broadcast([128, D]),
                ALU.subtract, ALU.mult, ctr=sve)
            settok(("xt", sl), sve)
            wait("sync", sve)
            dma("sync", d_xn[128 * tt:128 * (tt + 1), :], s_xn_st[:, sl, :], sq)
            settok(("xnst", sl), sq)
        sq_xn = sq.v

        wait("gpsimd", sq, sq_xn)
        coll("AllGather", ALU.bypass, [d_xn[:]], [d_xnall[:]], sems["c_xn"])

        wait("sync", sq, sq_xn)
        for jj in range(NJ):
            dmaT("sync", s_xnT[:, jj, :], d_xn[:, 128 * jj:128 * (jj + 1)], sq)
        sq_xnT = sq.v

        if DEBUG:
            dma("sync", dbg["xn"][:], d_xn[:], sq)

        mark()
        wait("tensor", sq, sq_xnT)
        wait("tensor", gq, gq_const)
        wait("vector", sq, sq_const)
        for tt in range(NT):
            guard("tensor", ("psA",))
            for j in range(NJ):
                mm(p_bg0[:, 0:NE], s_xnT[:, j, 128 * tt:128 * (tt + 1)],
                   w_rwT[:, j, :], start=(j == 0), stop=(j == NJ - 1),
                   ctr=smm if j == NJ - 1 else None)
            wait("scalar", smm)
            act(s_scores[:], p_bg0[:, 0:NE], AF.Sigmoid, ctr=ssc)
            settok(("psA",), ssc)
            wait("vector", ssc)
            vec("tensor_add", s_biased[:], s_scores[:], b_eb[:])
            vec("max", out=s_m8[:], in_=s_biased[:])
            vec("max_index", s_i8[:], s_m8[:], s_biased[:])
            vec("tensor_copy", s_if6a[:, tt, :], s_i8[:])
            vec("memset", s_Af[:], 0.0)
            cur, nxt = s_Af, s_Af2
            for k in range(EK):
                vec("scalar_tensor_tensor", nxt[:], c_iota[:],
                    s_if6a[:, tt, k:k + 1], cur[:], ALU.is_equal, ALU.add)
                cur, nxt = nxt, cur
                vec("scalar_tensor_tensor", s_scr64[:], c_iota[:],
                    s_if6a[:, tt, k:k + 1], s_scores[:], ALU.is_equal,
                    ALU.mult, accum_out=s_sc6[:, k:k + 1])
            vec("tensor_copy", s_A[:, tt, :], cur[:])
            vec("reduce_sum", s_t8a[:, 0:1], s_sc6[:, 0:EK], AX.X)
            vec("tensor_scalar_add", s_t8a[:, 1:2], s_t8a[:, 0:1], 1e-20)
            vec("reciprocal", s_t8a[:, 2:3], s_t8a[:, 1:2])
            vec("tensor_scalar_mul", s_pw[:, tt, 0:EK], s_sc6[:, 0:EK],
                s_t8a[:, 2:3], ctr=sve)
            wait("tensor", sve)
            guard("tensor", ("psB",))
            mm(p_bg1[0:64, 0:NE], c_ones[:], s_A[:, tt, :], ctr=smm)
            wait("vector", smm)
            guard("vector", ("cntst",))
            vec("tensor_copy", s_cnt[:], p_bg1[0:1, 0:NE], ctr=sve)
            settok(("psB",), sve)
            wait("sync", sve)
            dma("sync", d_ss[tt:tt + 1, :], s_cnt[:], sq)
            settok(("cntst",), sq)
        sq_ss = sq.v

        wait("gpsimd", sq, sq_ss)
        wait("gpsimd", sve, zero_tok)
        dma("gpsimd", s_stage[0:16, :], d_ss[:], gq)
        wait("tensor", gq)
        guard("tensor", ("psB",))
        mm(p_bg1[0:64, 0:NE], c_ones[0:64, :], s_stage[:], ctr=smm)
        wait("vector", smm)
        guard("vector", ("cntst",))
        vec("tensor_copy", s_cnt[:], p_bg1[0:1, 0:NE], ctr=sve)
        settok(("psB",), sve)
        wait("sync", sve)
        dma("sync", d_cnt[:], s_cnt[:], sq)
        settok(("cntst",), sq)
        wait("gpsimd", sq, sq.v)
        coll("AllGather", ALU.bypass, [d_cnt[:]], [d_cs[:]], sems["c_cnt"])

        mark()
        sw2v = sw2T_h.rearrange("(a p) f -> p a f", p=128)
        for jj in range(16):
            for h2 in range(2):
                wload(w_sw2r[:, jj, 512 * h2:512 * (h2 + 1)],
                      sw2v[:, jj, 512 * h2:512 * (h2 + 1)])
        sw2r_tok = sve.v
        wait("vector", sems["c_cnt"])
        vec("memset", s_co[:], 0.0)
        cur, nxt = s_co, s_co2
        for cc in range(8):
            wait("sync", sems["c_cnt"])
            guard("sync", ("csf",))
            dma("sync", s_csf[:], d_cs[cc:cc + 1, :], sq)
            wait("vector", sq)
            vec("scalar_tensor_tensor", nxt[:], s_csf[:],
                s_pm[:, cc:cc + 1], cur[:], ALU.mult, ALU.add, ctr=sve)
            settok(("csf",), sve)
            cur, nxt = nxt, cur
        vec("tensor_copy", s_coi[:], cur[:])
        vec("tensor_scalar", s_cq[:], s_coi[:], 6, None, ALU.arith_shift_right)
        vec("tensor_scalar", s_coi[:], s_coi[:], 63, None, ALU.bitwise_and)
        vec("tensor_copy", s_cqf[:], s_cq[:])
        vec("tensor_copy", s_crf[:], s_coi[:], ctr=sve)
        sve_qr = sve.v
        wait("gpsimd", sve, sve_qr)
        dma("gpsimd", s_stage[0:1, :], s_cqf[:], gq)
        dma("gpsimd", s_stage[1:2, :], s_crf[:], gq)
        dma("gpsimd", s_stage[2:18, :], d_ss[:], gq)
        wait("tensor", gq)
        wait("tensor", sve, sve_qr)
        guard("tensor", ("psA",))
        mm(p_bg0[0:64, 0:NE], c_uc[:], s_stage[:], ctr=smm)
        wait("vector", smm)
        vec("tensor_copy", s_choff[:], p_bg0[0:64, 0:NE], ctr=sve)
        settok(("psA",), sve)
        wait("sync", sve)
        dma("sync", d_choff[:], s_choff[0:16, :], sq)
        sq_choff = sq.v
        if DEBUG:
            dma("sync", dbg["choff"][:], d_choff[:], sq)

        wait("gpsimd", sve, zero_tok)
        dma("gpsimd", d_tokmap[:].rearrange("(p b) o -> p (b o)", p=128),
            s_zero[:, 0:196], gq)
        gq_tmz = gq.v

        for tt in range(NT):
            osl = tt % 2
            guard("sync", ("offb", osl))
            wait("sync", sq, sq_choff)
            dma("sync", s_offb[:, osl, :],
                d_choff[tt:tt + 1, :].to_broadcast([128, NE]), sq)
            off_ld = sq.v
            guard("tensor", ("psB",))
            mm(p_bg1[:, 0:NE], c_u128[:], s_A[:, tt, :], ctr=smm)
            wait("vector", smm)
            wait("vector", sq, off_ld)
            vec("tensor_copy", s_C[:], p_bg1[:, 0:NE], ctr=sve)
            settok(("psB",), sve)
            vec("tensor_add", s_Ct[:], s_C[:], s_offb[:, osl, :], ctr=sve)
            settok(("offb", osl), sve)
            for k in range(EK):
                vec("scalar_tensor_tensor", s_scr64[:], c_iota[:],
                    s_if6a[:, tt, k:k + 1], s_Ct[:], ALU.is_equal, ALU.mult,
                    accum_out=s_r6[:, k:k + 1])
            E6 = slice(0, EK)
            vec("tensor_scalar", s_v6[:, E6], s_r6[:, E6], float(CAP), None,
                ALU.is_lt)
            vec("tensor_mul", s_pw[:, tt, E6], s_pw[:, tt, E6], s_v6[:, E6])
            vec("tensor_scalar", s_t8a[:, E6], s_if6a[:, tt, E6], float(CAP),
                None, ALU.mult)
            vec("tensor_add", s_t8a[:, E6], s_t8a[:, E6], s_r6[:, E6])
            vec("tensor_copy", s_r6i[:, E6], s_if6a[:, tt, E6])
            vec("tensor_scalar", s_r6i[:, E6], s_r6i[:, E6], 3, None,
                ALU.arith_shift_right)
            vec("tensor_copy", s_t8b[:, E6], s_r6i[:, E6])
            vec("tensor_scalar", s_t8b[:, E6], s_t8b[:, E6], 64.0, None,
                ALU.mult)
            vec("tensor_add", s_t8b[:, E6], s_t8b[:, E6], s_t8a[:, E6])
            vec("tensor_scalar", s_scr64[:, 0:EK], s_v6[:, E6], -2e6, 2e6,
                ALU.mult, ALU.add)
            vec("tensor_add", s_t8b[:, E6], s_t8b[:, E6], s_scr64[:, 0:EK])
            vec("tensor_copy", s_sidx[:, tt, E6], s_t8b[:, E6])
            vec("tensor_scalar_min", s_t8a[:, E6], s_t8a[:, E6],
                float(NE * CAP - 1))
            vec("tensor_copy", s_gidx[:, tt, E6], s_t8a[:, E6], ctr=sve)
        sve_route = sve.v

        wait("gpsimd", sve, sve_route)
        wait("gpsimd", gq, gq_tmz)
        for tt in range(NT):
            for k in range(EK):
                iscat(d_tokmap[:], s_sidx[:, tt, k:k + 1],
                      s_tokid[:, tt:tt + 1], gq, 25087)
        wait("gpsimd", gq)
        coll("ReduceScatter", ALU.add, [d_tokmap[:]], [d_tokmy[:]],
             sems["c_tok"])
        if DEBUG:
            wait("sync", sems["c_tok"])
            dma("sync", dbg["tokmy"][:], d_tokmy[:], sq)
            dma("sync", dbg["cnt"][:], d_cnt[:], sq)
            wait("sync", sve, sve_route)
            dma("sync", dbg["if6"][:], s_if6a[:], sq)
            dma("sync", dbg["pw"][:], s_pw[:], sq)
            dma("sync", dbg["gidx"][:], s_gidx[:], sq)

        mark()
        wait("tensor", sve, sw2r_tok)
        for u in range(NU):
            tsl = slice(UT * u, UT * (u + 1))
            sw1v = sw1T_h.rearrange("(a p) f -> p a f", p=128)
            for ft in range(16):
                slw = ft % 2
                guard("vector", ("sw1", slw))
                for h2 in range(2):
                    wload(w_sw1s[:, slw, 4 * h2:4 * (h2 + 1), :],
                          sw1v[:, 4 * h2:4 * (h2 + 1),
                               128 * ft:128 * (ft + 1)], a=4)
                cast_tok = sve.v
                pp = p_bg0 if ft % 2 == 0 else p_bg1
                pk = ("psA",) if ft % 2 == 0 else ("psB",)
                guard("tensor", pk)
                wait("tensor", sve, cast_tok)
                for j in range(NJ):
                    mm(pp[:, 0:UT], w_sw1s[:, slw, j, :], s_xnT[:, j, tsl],
                       start=(j == 0), stop=(j == NJ - 1),
                       ctr=smm if j == NJ - 1 else None)
                settok(("sw1", slw), smm)
                wait("vector", smm)
                vec("tensor_copy", s_hraw[:, ft, :], pp[:, 0:UT], ctr=sve)
                settok(pk, sve)
            wait("scalar", sve)
            act(s_hraw[:].rearrange("p a f -> p (a f)"),
                s_hraw[:].rearrange("p a f -> p (a f)"),
                AF.Gelu_apprx_tanh, ctr=ssc)
            wait("tensor", ssc)
            for tq in range(2):
                tt = 2 * u + tq
                for ih in range(2):
                    pp = p_bg0 if ih == 0 else p_bg1
                    pk = ("psA",) if ih == 0 else ("psB",)
                    guard("tensor", pk)
                    for ft in range(16):
                        mm(pp[:, 0:512],
                           s_hraw[:, ft, 128 * tq:128 * (tq + 1)],
                           w_sw2r[:, ft, 512 * ih:512 * (ih + 1)],
                           start=(ft == 0), stop=(ft == 15),
                           ctr=smm if ft == 15 else None)
                    wait("vector", smm)
                    guard("vector", ("sh2st", tt % 2))
                    vec("tensor_copy",
                        s_sh2st[:, tt % 2, 512 * ih:512 * (ih + 1)],
                        pp[:, 0:512], ctr=sve)
                    settok(pk, sve)
                wait("sync", sve)
                dma("sync", d_sh2[128 * tt:128 * (tt + 1), :],
                    s_sh2st[:, tt % 2, :], sq)
                settok(("sh2st", tt % 2), sq)
        sq_sh2 = sq.v
        if DEBUG:
            wait("sync", sq, sq_sh2)
            dma("sync", dbg["sh2"][:], d_sh2[:], sq)

        mark()
        wait("gpsimd", sems["c_tok"])
        wait("gpsimd", sems["c_xn"])
        dmaslow("gpsimd", s_tmf[:],
                d_tokmy[0:3072, :].rearrange("(j p) o -> p (j o)", p=128), gq)
        wait("vector", gq)
        vec("tensor_copy", s_tmidx[:], s_tmf[:], ctr=sve)
        wait("gpsimd", sve)
        for jj in range(24):
            sl = jj % 2
            gsem = sems["dw_y6a"] if sl == 0 else sems["dw_y6b"]
            guard("gpsimd", ("xsg", sl))
            igat(s_y6[:, sl, :], d_xnall[:], s_tmidx[:, jj:jj + 1], gsem)
            wait("sync", gsem)
            dma("sync", d_xs[128 * jj:128 * (jj + 1), :], s_y6[:, sl, :], sq)
            settok(("xsg", sl), sq)
        gq_xs = sq.v

        w2view = w_w1[:].rearrange("p a f -> p (a f)").rearrange(
            "p (b g) -> p b g", b=4)

        for eg in range(4):
            for el in range(2):
                e = 2 * eg + el
                guard("sync", ("xsT",))
                wait("sync", sq, gq_xs)
                for j in range(NJ):
                    dmaT("sync", s_xsT[:, j, :],
                         d_xs[CAP * e:CAP * (e + 1), 128 * j:128 * (j + 1)],
                         sems["dw_xsT"])
                wait("tensor", sems["dw_xsT"])
                guard("vector", ("w1",))
                w1v = w1_h[e].rearrange("(a p) f -> p a f", p=128)
                for j in range(NJ):
                    wload(w_w1[:, j, :], w1v[:, j, :])
                wait("tensor", sve)
                for ft in range(4):
                    pp = p_bg0 if ft % 2 == 0 else p_bg1
                    pk = ("psA",) if ft % 2 == 0 else ("psB",)
                    guard("tensor", pk)
                    for j in range(NJ):
                        mm(pp[:, 0:CAP], w_w1[:, j, 128 * ft:128 * (ft + 1)],
                           s_xsT[:, j, :], start=(j == 0), stop=(j == NJ - 1),
                           ctr=smm if j == NJ - 1 else None)
                    wait("vector", smm)
                    vec("tensor_copy", s_h1r[:, 4 * el + ft, :], pp[:, 0:CAP],
                        ctr=sve)
                    settok(pk, sve)
                settok(("xsT",), smm)
                settok(("w1",), smm)
            wait("scalar", sve)
            act(s_h1r[:].rearrange("p a f -> p (a f)"),
                s_h1r[:].rearrange("p a f -> p (a f)"),
                AF.Gelu_apprx_tanh, ctr=ssc)
            gelu_tok = ssc.v
            wait("tensor", ssc, gelu_tok)
            for el in range(2):
                e = 2 * eg + el
                guard("vector", ("w1",))
                w2v = w2_h[e].rearrange("(a p) f -> p a f", p=128)
                for c4 in range(4):
                    for h2 in range(2):
                        wload(w2view[:, c4, 512 * h2:512 * (h2 + 1)],
                              w2v[:, c4, 512 * h2:512 * (h2 + 1)])
                wait("tensor", sve)
                for ct in range(3):
                    for ih in range(2):
                        pp = p_bg0 if ih == 0 else p_bg1
                        pk = ("psA",) if ih == 0 else ("psB",)
                        guard("tensor", pk)
                        for f in range(4):
                            mm(pp[:, 0:512],
                               s_h1r[:, 4 * el + f, 128 * ct:128 * (ct + 1)],
                               w2view[:, f, 512 * ih:512 * (ih + 1)],
                               start=(f == 0), stop=(f == 3),
                               ctr=smm if f == 3 else None)
                        wait("vector", smm)
                        guard("vector", ("yst", ct % 2))
                        vec("tensor_copy",
                            s_yst[:, ct % 2, 512 * ih:512 * (ih + 1)],
                            pp[:, 0:512], ctr=sve)
                        settok(pk, sve)
                    wait("sync", sve)
                    dma("sync",
                        d_ymine[CAP * e + 128 * ct:CAP * e + 128 * (ct + 1), :],
                        s_yst[:, ct % 2, :], sq)
                    settok(("yst", ct % 2), sq)
                settok(("w1",), smm)
        sq_y = sq.v
        wait("gpsimd", sq, sq_y)
        coll("AllGather", ALU.bypass, [d_ymine[:]], [d_yall[:]], sems["c_y"])
        if DEBUG:
            dma("sync", dbg["y"][:], d_ymine[:], sq)

        mark()
        wait("sync", gq, gq_cast)
        vec("memset", s_vx[:, :, 64:65], 1.0)
        vec("memset", s_vx[:, :, 129:130], 1.0, ctr=sve)
        inv_slot = [0]
        for p in range(NP):
            guard("vector", ("vxall",))
            vsl = p % 2
            dsem = sems["dw_vxa"] if vsl == 0 else sems["dw_vxb"]
            guard("gpsimd", ("wvx", vsl))
            dma("gpsimd", w_vx[:, vsl, :, :],
                wvT_h.rearrange("(a p) f -> p a f", p=128)
                [:, :, 128 * p:128 * (p + 1)], dsem)
            wait("tensor", dsem)
            for tt in range(NT):
                vpp = p_bg0 if tt % 2 == 0 else p_bg1
                vpk = ("psA",) if tt % 2 == 0 else ("psB",)
                guard("tensor", vpk)
                for j in range(NJ):
                    mm(vpp[:, 0:128], s_xnT[:, j, 128 * tt:128 * (tt + 1)],
                       w_vx[:, vsl, j, :], start=(j == 0), stop=(j == NJ - 1),
                       ctr=smm if j == NJ - 1 else None)
                wait("vector", smm)
                guard("vector", ("vx", tt))
                vec("tensor_add", s_vx[:, tt, 0:64], vpp[:, 0:64],
                    b_wvb[:, 128 * p:128 * p + 64])
                vec("tensor_add", s_vx[:, tt, 65:129], vpp[:, 64:128],
                    b_wvb[:, 128 * p + 64:128 * (p + 1)], ctr=sve)
                settok(vpk, sve)
                settok(("vx", tt), sve)
            settok(("wvx", vsl), smm)
            for tc in range(4):
                guard("tensor", ("psA",))
                mm(p_bg0[0:64, 0:512], w_watt[0:64, p, :],
                   s_xnT[0:64, p, 512 * tc:512 * (tc + 1)], tp=(0, 0),
                   start=True, stop=True)
                guard("tensor", ("psB",))
                mm(p_bg1[64:128, 0:512], w_watt[64:128, p, :],
                   s_xnT[64:128, p, 512 * tc:512 * (tc + 1)], tp=(64, 64),
                   start=True, stop=True, ctr=smm)
                wait("vector", smm)
                guard("vector", ("xw", tc))
                vec("tensor_copy", s_xw[0:64, tc, :], p_bg0[0:64, 0:512])
                vec("tensor_copy", s_xw[64:128, tc, :], p_bg1[64:128, 0:512],
                    ctr=sve)
                settok(("psA",), sve)
                settok(("psB",), sve)
                settok(("xw", tc), sve)
            wait("tensor", sve)

            for tc in range(4):
                prev = None

                def emit_o(prev):
                    psch, psex = prev
                    ppar = psch % 2
                    ppA = (s_p0, s_p1, s_p2, s_p3)[ppar * 2]
                    ppB = (s_p0, s_p1, s_p2, s_p3)[ppar * 2 + 1]
                    if psch == 0:
                        guard("tensor", ("o_rs",))
                    wait("tensor", sex, psex)
                    first = (psch == 0)
                    last = (psch == NT - 1)
                    mm(p_o0[0:65, :], s_vx[:, psch, 0:65], ppA[:],
                       tp=(0, 0), start=first, stop=last)
                    mm(p_o1[0:65, :], s_vx[:, psch, 65:130], ppB[:],
                       tp=(0, 0), start=first, stop=last, ctr=smm)
                    tok[("pT", ppar * 2)] = (smm, smm.v)
                    tok[("pT", ppar * 2 + 1)] = (smm, smm.v)

                for sch in range(NT):
                    par = sch % 2
                    sl0 = par * 2
                    sl1 = sl0 + 1
                    pbA = p_att0 if par == 0 else p_rs0
                    pbB = p_att1 if par == 0 else p_rs1
                    pA = (s_p0, s_p1, s_p2, s_p3)[sl0]
                    pB = (s_p0, s_p1, s_p2, s_p3)[sl1]
                    guard("tensor", ("attA", par))
                    mm(pbA[:, 0:512],
                       s_xnT[0:64, p, 128 * sch:128 * (sch + 1)],
                       s_xw[0:64, tc, :], tp=(0, 0), ctr=smm)
                    att0_tok = smm.v
                    guard("tensor", ("attB", par))
                    mm(pbB[:, 0:512],
                       s_xnT[64:128, p, 128 * sch:128 * (sch + 1)],
                       s_xw[64:128, tc, :], tp=(64, 0), ctr=smm)
                    att1_tok = smm.v
                    if prev is not None:
                        emit_o(prev)
                    guard("scalar", ("pT", sl0))
                    wait("scalar", smm, att0_tok)
                    act(pA[:], pbA[:, 0:512], AF.Exp, scale=0.125, ctr=sex)
                    settok(("attA", par), sex)
                    guard("scalar", ("pT", sl1))
                    wait("scalar", smm, att1_tok)
                    act(pB[:], pbB[:, 0:512], AF.Exp, scale=0.125, ctr=sex)
                    settok(("attB", par), sex)
                    prev = (sch, sex.v)
                emit_o(prev)
                o_tok = smm.v
                slA = inv_slot[0]
                slB = slA + 1
                inv_slot[0] += 2
                wait("vector", smm, o_tok)
                guard("vector", ("xt", 0))
                vec("tensor_copy", s_xt[64:65, 0, 0:512], p_o0[64:65, :])
                vec("tensor_copy", s_xt[64:65, 0, 512:1024], p_o1[64:65, :],
                    ctr=sve)
                wait("sync", sve)
                dma("sync", d_inv[slA:slA + 1, :], s_xt[64:65, 0, 0:512],
                    sems["dw_inva"])
                dma("sync", d_inv[slB:slB + 1, :], s_xt[64:65, 0, 512:1024],
                    sems["dw_inva"])
                settok(("xt", 0), sems["dw_inva"])
                settok(("xt2", 0), sems["dw_inva"])
                wait("sync", sems["dw_inva"])
                guard("sync", ("invb",))
                dma("sync", s_invA[0:64, :],
                    d_inv[slA:slA + 1, :].to_broadcast([64, 512]),
                    sems["dw_invb"])
                dma("sync", s_invB[0:64, :],
                    d_inv[slB:slB + 1, :].to_broadcast([64, 512]),
                    sems["dw_invb"])
                wait("vector", sems["dw_invb"])
                vec("reciprocal", s_invA[:], s_invA[:])
                vec("reciprocal", s_invB[:], s_invB[:])
                vec("tensor_mul", s_oT[0:64, p, 512 * tc:512 * (tc + 1)],
                    p_o0[0:64, :], s_invA[:])
                guard("vector", ("pT", 0))
                vec("tensor_mul", s_p0[0:64, :], p_o1[0:64, :], s_invB[:],
                    ctr=sve)
                settok(("invb",), sve)
                settok(("o_rs",), sve)
                wait("sync", sve)
                dma("sync", s_oT[64:128, p, 512 * tc:512 * (tc + 1)],
                    s_p0[0:64, :], sq)
                settok(("pT", 0), sq)
            settok(("vxall",), smm)
        sve_att = sve.v
        sq_oB = sq.v
        if DEBUG:
            wait("sync", sve, sve_att)
            dma("sync", dbg["ot"][:], s_oT[:], sq)

        mark()
        ow16v = owT_h.rearrange("(a p) f -> p a f", p=128)
        owbuf = w_w1[:]
        wait("vector", sve, sve_att)
        wait("tensor", sq, sq_oB)
        for fh in range(2):
            guard("vector", ("w1",))
            guard("vector", ("owb",))
            for p in range(NP):
                wload(owbuf[:, p, :], ow16v[:, p, 512 * fh:512 * (fh + 1)])
            wait("tensor", sve)
            wait("tensor", sve, sve_att)
            for tg in range(8):
                for ti in range(2):
                    tt = 2 * tg + ti
                    pp = (p_bg0, p_bg1)[ti]
                    pk = (("psA",), ("psB",))[ti]
                    guard("tensor", pk)
                    for p in range(NP):
                        mm(pp[:, 0:512], s_oT[:, p, 128 * tt:128 * (tt + 1)],
                           owbuf[:, p, :], start=(p == 0), stop=(p == NP - 1),
                           ctr=smm if p == NP - 1 else None)
                    wait("vector", smm)
                    vec("tensor_add", satt(tt)[:, 512 * fh:512 * (fh + 1)],
                        pp[:, 0:512], b_owb[:, 512 * fh:512 * (fh + 1)],
                        ctr=sve)
                    settok(pk, sve)
            settok(("owb",), smm)
        sve_op = sve.v
        if DEBUG:
            wait("sync", sve, sve_op)
            for tt in range(NT):
                dma("sync", dbg["satt"][128 * tt:128 * (tt + 1), :],
                    satt(tt), sq)

        mark()
        wait("gpsimd", sems["c_y"])
        wait("vector", sve, sve_op)
        for tt in range(NT):
            for k in range(EK):
                sl = k % 2
                gsem = sems["dw_y6a"] if sl == 0 else sems["dw_y6b"]
                guard("gpsimd", ("y6", sl))
                igat(s_y6[:, sl, :], d_yall[:], s_gidx[:, tt, k:k + 1], gsem)
                wait("vector", gsem)
                guard("vector", ("moe_scr",)) if k == 0 else None
                if k == 0:
                    vec("tensor_scalar_mul", s_moe[:], s_y6[:, sl, :],
                        s_pw[:, tt, 0:1], ctr=sve)
                else:
                    vec("scalar_tensor_tensor", s_moe[:], s_y6[:, sl, :],
                        s_pw[:, tt, k:k + 1], s_moe[:], ALU.mult, ALU.add,
                        ctr=sve)
                settok(("y6", sl), sve)
            shsem = sems["dw_sh2a"] if tt % 2 == 0 else sems["dw_sh2b"]
            guard("sync", ("sh2ld", tt % 2))
            if tt == 0:
                wait("sync", sq, sq_sh2)
            dma("sync", s_sh2st[:, tt % 2, :],
                d_sh2[128 * tt:128 * (tt + 1), :], shsem)
            wait("vector", shsem)
            vec("tensor_add", s_moe[:], s_moe[:], s_sh2st[:, tt % 2, :],
                ctr=sve)
            settok(("sh2ld", tt % 2), sve)
            wait("scalar", sve)
            guard("scalar", ("gst", tt % 2))
            act(s_xn_st[:, tt % 2, :], s_moe[:], AF.Silu, ctr=ssc)
            settok(("moe_scr",), ssc)
            xsem = sems["dw_xa"] if tt % 2 == 0 else sems["dw_xb"]
            guard("sync", ("xt2", tt % 2))
            dma("sync", s_xt[:, tt % 2, :], x_h[128 * tt:128 * (tt + 1), :],
                xsem)
            wait("vector", ssc)
            wait("vector", xsem)
            vec("tensor_mul", s_moe[:], satt(tt), s_xn_st[:, tt % 2, :],
                ctr=sve)
            settok(("gst", tt % 2), sve)
            vec("tensor_add", s_xt[:, tt % 2, :], s_xt[:, tt % 2, :],
                s_moe[:], ctr=sve)
            wait("sync", sve)
            dma("sync", out_h[128 * tt:128 * (tt + 1), :], s_xt[:, tt % 2, :],
                sq)
            settok(("xt2", tt % 2), sq)

        mark()
        wait("sync", sq, sq.v)
        wait("gpsimd", gq, gq.v)

        _flush("vector")
        _flush("scalar")
        mark()
        if PHASE_LIMIT is not None:
            cut = marks[PHASE_LIMIT]
            for k in Q:
                Q[k] = Q[k][:cut[k]]

        with nc.Block() as block:
            @block.sync
            def _(e):
                for fn in Q["sync"]:
                    fn(e)

            @block.gpsimd
            def _(e):
                for fn in Q["gpsimd"]:
                    fn(e)

            @block.tensor
            def _(e):
                for fn in Q["tensor"]:
                    fn(e)

            @block.scalar
            def _(e):
                for fn in Q["scalar"]:
                    fn(e)

            @block.vector
            def _(e):
                for fn in Q["vector"]:
                    fn(e)

    return nc, dbg


_NC_CACHE = {}


def _get_nc():
    if "nc" not in _NC_CACHE:
        _NC_CACHE["nc"] = build_nc()
    return _NC_CACHE["nc"]


def kernel(L0_attn_x, L0_attn_w_att_val, L0_attn_wv_w, L0_attn_wv_b,
           L0_attn_out_w, L0_attn_out_b, L0_ffn_router_weight,
           L0_ffn_expert_bias, L0_ffn_w1, L0_ffn_w2, L0_ffn_sw1, L0_ffn_sw2):
    x = np.ascontiguousarray(np.asarray(L0_attn_x, dtype=np.float32))
    watt = np.asarray(L0_attn_w_att_val, dtype=np.float32)
    wv = np.asarray(L0_attn_wv_w, dtype=np.float32)
    wvb = np.asarray(L0_attn_wv_b, dtype=np.float32)
    ow = np.asarray(L0_attn_out_w, dtype=np.float32)
    owb = np.asarray(L0_attn_out_b, dtype=np.float32)
    rw = np.asarray(L0_ffn_router_weight, dtype=np.float32)
    eb = np.asarray(L0_ffn_expert_bias, dtype=np.float32)
    w1 = np.asarray(L0_ffn_w1, dtype=np.float32)
    w2 = np.asarray(L0_ffn_w2, dtype=np.float32)
    sw1 = np.asarray(L0_ffn_sw1, dtype=np.float32)
    sw2 = np.asarray(L0_ffn_sw2, dtype=np.float32)

    wvT = np.ascontiguousarray(wv.T)
    owT = np.ascontiguousarray(ow.T)
    sw1T = np.ascontiguousarray(sw1.T)
    sw2T = np.ascontiguousarray(sw2.T)
    rwT = np.ascontiguousarray(rw.T)
    wattp = np.zeros((NP, 128, 64), np.float32)
    for p in range(NP):
        wattp[p, 0:64] = watt[2 * p]
        wattp[p, 64:128] = watt[2 * p + 1]
    u128 = np.triu(np.ones((128, 128), np.float32), 1)
    uc = np.zeros((64, 64), np.float32)
    for tt in range(16):
        uc[0, tt] = 64.0
        uc[1, tt] = 1.0
        for i in range(tt):
            uc[2 + i, tt] = 1.0
    iota = np.tile(np.arange(NE, dtype=np.float32), (128, 1))
    onesc = np.ones((128, 64), np.float32)

    nc, dbg = _get_nc()
    in_maps = []
    for c in range(8):
        in_maps.append(dict(
            x=np.ascontiguousarray(x[c]),
            wvT=wvT, owT=owT, sw1T=sw1T, sw2T=sw2T, rwT=rwT,
            watt=wattp,
            wvb=wvb.reshape(1, D), owb=owb.reshape(1, D),
            ebias=eb.reshape(1, NE),
            w1=np.ascontiguousarray(w1[ELOC * c:ELOC * (c + 1)]),
            w2=np.ascontiguousarray(w2[ELOC * c:ELOC * (c + 1)]),
            tokid=np.arange(T * c, T * (c + 1), dtype=np.float32).reshape(T, 1),
            pm=(np.arange(8, dtype=np.float32) < c).astype(np.float32)
            .reshape(1, 8),
            u128=u128, uc=uc, iota=iota, onesc=onesc,
        ))
    res = run_bass_kernel_spmd(nc, in_maps, core_ids=list(range(8)))
    out = np.stack([np.asarray(res.results[c]["out"]) for c in range(8)],
                   axis=0)
    if DEBUG:
        kernel.last_results = res.results
    return out


# version 09 (final)
# speedup vs baseline: 1.5082x; 1.0054x over previous
from contextlib import ExitStack

import numpy as np

import concourse.bass as bass
import concourse.mybir as mybir
from concourse.bass_utils import run_bass_kernel_spmd
from concourse.bass_core_rust import IndirectOffsetOnAxis

F32 = mybir.dt.float32
BF16 = mybir.dt.bfloat16
I32 = mybir.dt.int32
U32 = mybir.dt.uint32
AF = mybir.ActivationFunctionType
ALU = mybir.AluOpType
AX = mybir.AxisListType
RG8 = [[0, 1, 2, 3, 4, 5, 6, 7]]

T = 2048
D = 1024
NT = 16
NJ = 8
NP = 8
NE = 64
EK = 6
CAP = 384
EF = 512
SF = 2048
ELOC = 8
NU = 8
UT = 256

DEBUG = False
PHASE_LIMIT = None


class Ctr:
    __slots__ = ("h", "v", "is_dma", "dedicated")

    def __init__(self, h, is_dma=False, dedicated=False):
        self.h = h
        self.v = 0
        self.is_dma = is_dma
        self.dedicated = dedicated


def build_nc():
    nc = bass.Bass("TRN2", target_bir_lowering=False)

    def inp(name, shape, dt=F32):
        return nc.dram_tensor(name, shape, dt, kind="ExternalInput").ap()

    x_h = inp("x", [T, D])
    wvT_h = inp("wvT", [D, D])
    owT_h = inp("owT", [D, D])
    sw1T_h = inp("sw1T", [D, SF])
    sw2T_h = inp("sw2T", [SF, D])
    rwT_h = inp("rwT", [D, NE])
    watt_h = inp("watt", [NP, 128, 64])
    wvb_h = inp("wvb", [1, D])
    owb_h = inp("owb", [1, D])
    ebias_h = inp("ebias", [1, NE])
    w1_h = inp("w1", [ELOC, D, EF])
    w2_h = inp("w2", [ELOC, EF, D])
    tokid_h = inp("tokid", [T, 1])
    pm_h = inp("pm", [1, 8])
    u128_h = inp("u128", [128, 128])
    uc_h = inp("uc", [64, 64])
    iota_h = inp("iota", [128, NE])
    onesc_h = inp("onesc", [128, 64])

    out_h = nc.dram_tensor("out", [T, D], F32, kind="ExternalOutput").ap()

    def idram(name, shape, dt=F32):
        return nc.dram_tensor(name, shape, dt, kind="Internal").ap()

    d_xn = idram("d_xn", [T, D], BF16)
    d_xnall = idram("d_xnall", [8 * T, D], BF16)
    d_cnt = idram("d_cnt", [1, NE])
    d_cs = idram("d_cs", [8, NE])
    d_ss = idram("d_ss", [16, NE])
    d_choff = idram("d_choff", [NT, NE])
    d_tokmap = idram("d_tokmap", [25088, 1])
    d_tokmy = idram("d_tokmy", [3136, 1])
    d_xs = idram("d_xs", [ELOC * CAP, D], BF16)
    d_ymine = idram("d_ymine", [ELOC * CAP, D], BF16)
    d_yall = idram("d_yall", [NE * CAP, D], BF16)
    d_sh2 = idram("d_sh2", [T, D], BF16)
    d_inv = idram("d_inv", [64, 512])
    d_wv16 = idram("d_wv16", [D, D], BF16)
    d_sw1b = idram("d_sw1b", [D, SF], BF16)
    d_ow16 = idram("d_ow16", [D, D], BF16)
    d_w116 = idram("d_w116", [ELOC, D, EF], BF16)
    d_w216 = idram("d_w216", [ELOC, EF, D], BF16)

    dbg = {}

    def dbgout(name, shape, dt):
        if DEBUG:
            dbg[name] = nc.dram_tensor("dbg_" + name, shape, dt,
                                       kind="ExternalOutput").ap()

    dbgout("xn", [T, D], BF16)
    dbgout("cnt", [1, NE], F32)
    dbgout("choff", [NT, NE], F32)
    dbgout("if6", [128, NT, 8], F32)
    dbgout("pw", [128, NT, 8], F32)
    dbgout("gidx", [128, NT, 8], I32)
    dbgout("tokmy", [3136, 1], F32)
    dbgout("y", [ELOC * CAP, D], BF16)
    dbgout("sh2", [T, D], BF16)
    dbgout("ot", [128, NP, T], BF16)
    dbgout("satt", [T, D], BF16)

    st = ExitStack()
    with st:
        def sb(name, shape, dt):
            return st.enter_context(nc.sbuf_tensor(name, shape, dt))

        def psum(name, shape):
            return st.enter_context(nc.psum_tensor(name, shape, F32))

        c_iota = sb("c_iota", [128, NE], F32)
        c_ones = sb("c_ones", [128, 64], BF16)
        c_u128 = sb("c_u128", [128, 128], BF16)
        c_uc = sb("c_uc", [64, 64], BF16)
        w_watt = sb("w_watt", [128, NP, 64], BF16)
        w_rwT = sb("w_rwT", [128, NJ, NE], BF16)
        w_sw2r = sb("w_sw2r", [128, 16, D], BF16)
        w_sw1s = sb("w_sw1s", [128, 2, NJ, 128], BF16)
        w_vx = sb("w_vx", [128, 2, NJ, 128], BF16)
        w_ow = sb("w_ow", [128, 2, 512], BF16)
        w_w1 = sb("w_w1", [128, NJ, EF], BF16)
        s_wf32 = sb("s_wf32", [128, 4, 512], F32)
        b_wvb = sb("b_wvb", [128, D], F32)
        b_owb = sb("b_owb", [128, D], F32)
        b_eb = sb("b_eb", [128, NE], F32)
        s_pm = sb("s_pm", [1, 8], F32)
        s_tokid = sb("s_tokid", [128, NT], F32)

        s_xnT = sb("s_xnT", [128, NJ, T], BF16)
        s_oT = sb("s_oT", [128, NP, T], BF16)
        s_vx = sb("s_vx", [128, NT, 132], BF16)
        s_xw = sb("s_xw", [128, 4, 512], BF16)
        s_p0 = sb("s_p0", [128, 512], BF16)
        s_p1 = sb("s_p1", [128, 512], BF16)
        s_p2 = sb("s_p2", [128, 512], BF16)
        s_p3 = sb("s_p3", [128, 512], BF16)
        s_hraw = sb("s_hraw", [128, 16, UT], BF16)
        s_xn_st = sb("s_xn_st", [128, 2, D], BF16)
        s_sh2st = sb("s_sh2st", [128, 2, D], BF16)
        s_A = sb("s_A", [128, NT, NE], BF16)
        s_xsT = sb("s_xsT", [128, NJ, CAP], BF16)
        s_h1r = sb("s_h1r", [128, 8, CAP], BF16)
        s_yst = sb("s_yst", [128, 2, D], BF16)
        s_y6 = sb("s_y6", [128, 2, D], BF16)

        s_xt = sb("s_xt", [128, 2, D], F32)
        s_moe = sb("s_moe", [128, D], F32)
        s_lns = sb("s_lns", [128, 8], F32)
        s_scores = sb("s_scores", [128, NE], F32)
        s_biased = sb("s_biased", [128, NE], F32)
        s_m8 = sb("s_m8", [128, 8], F32)
        s_i8 = sb("s_i8", [128, 8], U32)
        s_if6a = sb("s_if6a", [128, NT, 8], F32)
        s_Af = sb("s_Af", [128, NE], F32)
        s_Af2 = sb("s_Af2", [128, NE], F32)
        s_scr64 = sb("s_scr64", [128, NE], F32)
        s_sc6 = sb("s_sc6", [128, 8], F32)
        s_pw = sb("s_pw", [128, NT, 8], F32)
        s_r6 = sb("s_r6", [128, 8], F32)
        s_v6 = sb("s_v6", [128, 8], F32)
        s_t8a = sb("s_t8a", [128, 8], F32)
        s_t8b = sb("s_t8b", [128, 8], F32)
        s_sidx = sb("s_sidx", [128, NT, 8], I32)
        s_gidx = sb("s_gidx", [128, NT, 8], I32)
        s_r6i = sb("s_r6i", [128, 8], I32)
        s_cnt = sb("s_cnt", [1, NE], F32)
        s_csf = sb("s_csf", [1, NE], F32)
        s_co = sb("s_co", [1, NE], F32)
        s_co2 = sb("s_co2", [1, NE], F32)
        s_coi = sb("s_coi", [1, NE], I32)
        s_cq = sb("s_cq", [1, NE], I32)
        s_cqf = sb("s_cqf", [1, NE], F32)
        s_crf = sb("s_crf", [1, NE], F32)
        s_stage = sb("s_stage", [64, 64], BF16)
        s_choff = sb("s_choff", [64, NE], F32)
        s_offb = sb("s_offb", [128, 2, NE], F32)
        s_C = sb("s_C", [128, NE], F32)
        s_Ct = sb("s_Ct", [128, NE], F32)
        s_invA = sb("s_invA", [64, 512], F32)
        s_invB = sb("s_invB", [64, 512], F32)
        s_tmf = sb("s_tmf", [128, 24], F32)
        s_tmidx = sb("s_tmidx", [128, 24], I32)
        s_zero = sb("s_zero", [128, 256], F32)
        s_junk = sb("s_junk", [128, 32], F32)

        p_att0 = psum("p_att0", [128, 512])
        p_att1 = psum("p_att1", [128, 512])
        p_o0 = psum("p_o0", [128, 512])
        p_o1 = psum("p_o1", [128, 512])
        p_rs0 = psum("p_rs0", [128, 512])
        p_rs1 = psum("p_rs1", [128, 512])
        p_bg0 = psum("p_bg0", [128, 512])
        p_bg1 = psum("p_bg1", [128, 512])

        names = "gq sq mm ex ve sc c_xn c_cnt c_tok c_y vsy ssy".split()
        sems = {nm: Ctr(st.enter_context(nc.semaphore("sem_" + nm)),
                        is_dma=nm in ("gq", "sq"))
                for nm in names}
        dnames = ("dw_sw1a dw_sw1b dw_vxa dw_vxb dw_w1 dw_w2 dw_xsT dw_ow "
                  "dw_xa dw_xb dw_inva dw_invb dw_y6a dw_y6b dw_sh2a dw_sh2b "
                  "dw_gath").split()
        for nm in dnames:
            sems[nm] = Ctr(st.enter_context(nc.semaphore("sem_" + nm)),
                           is_dma=True, dedicated=True)
        gq, sq = sems["gq"], sems["sq"]
        _wsems = [sems[n] for n in ("dw_w1", "dw_w2", "dw_sw1a", "dw_sw1b")]
        smm, sex, sve, ssc = sems["mm"], sems["ex"], sems["ve"], sems["sc"]

        Q = {k: [] for k in ("sync", "gpsimd", "tensor", "scalar", "vector")}
        marks = []

        def mark():
            _flush("vector")
            _flush("scalar")
            marks.append({k: len(v) for k, v in Q.items()})
        tok = {}

        def q(eng, fn):
            Q[eng].append(fn)

        _fence = {}

        def wait(eng, ctr, val=None):
            if ctr.dedicated:
                v = ctr.v if val is None else val
            else:
                v = ctr.v if (val is None or ctr.is_dma) else val
            h = ctr.h
            if ctr.is_dma and not ctr.dedicated:
                owner = "sync" if ctr is sems["sq"] else "gpsimd"
                if _fence.get(id(ctr), -1) < v:
                    _fence[id(ctr)] = v
                    if eng != owner:
                        q(owner, lambda e: e.wait_ge(h, v))
            fn = lambda e: e.wait_ge(h, v)
            if eng in ("vector", "scalar") and _stage[eng] is not None:
                _swaits[eng].append(fn)
            else:
                q(eng, fn)

        def guard(eng, key):
            if key in tok:
                ctr, v = tok[key]
                wait(eng, ctr, v)

        def settok(key, ctr):
            tok[key] = (ctr, ctr.v)

        def dma(eng, dst, src, ctr):
            ctr.v += 16
            h = ctr.h
            q(eng, lambda e: e.dma_start(dst, src).then_inc(h, 16))

        def dmaT(eng, dst, src, ctr):
            ctr.v += 16
            h = ctr.h
            q(eng, lambda e: e.dma_start_transpose(dst, src).then_inc(h, 16))

        def dmaslow(eng, dst, src, ctr):
            ctr.v += 16
            h = ctr.h
            q(eng, lambda e: e.dma_start(
                dst, src, allow_slow_non_contiguous=True).then_inc(h, 16))

        def mm(out, lhsT, rhs, start=True, stop=True, tp=None, ctr=None):
            if ctr is not None:
                ctr.v += 1
                h = ctr.h
                q("tensor", lambda e: e.matmul(
                    out, lhsT, rhs, start=start, stop=stop,
                    tile_position=tp).then_inc(h, 1))
            else:
                q("tensor", lambda e: e.matmul(out, lhsT, rhs, start=start,
                                               stop=stop, tile_position=tp))

        _stage = {"scalar": None, "vector": None}
        _swaits = {"scalar": [], "vector": []}
        _prevt = {"scalar": set(), "vector": set()}

        def _names(objs):
            out = set()
            for o in objs:
                t = getattr(o, "tensor", None)
                if t is not None:
                    out.add(t.name)
            return out

        def _flush(eng, attach=False):
            if _stage[eng] is None:
                return
            b, touched, has_ctr = _stage[eng]
            if attach:
                sc_ = sems["vsy"] if eng == "vector" else sems["ssy"]
                sc_.v += 1
                h, v = sc_.h, sc_.v

                def _att(e, b=b, h=h, has_ctr=has_ctr, eng=eng):
                    r = b(e)
                    if r is None or has_ctr:
                        if eng == "vector":
                            e.engine_nop()
                            r = e.engine_nop()
                        else:
                            e.activation(s_junk[:, 0:8], s_junk[:, 0:8],
                                         AF.Identity)
                            r = e.activation(s_junk[:, 8:16], s_junk[:, 8:16],
                                             AF.Identity)
                    r.then_inc(h, 1)
                Q[eng].append(_att)
                for w in _swaits[eng]:
                    Q[eng].append(w)
                Q[eng].append(lambda e, h=h, v=v: e.wait_ge(h, v))
            else:
                Q[eng].append(lambda e, b=b: b(e))
                for w in _swaits[eng]:
                    Q[eng].append(w)
            _swaits[eng] = []
            _prevt[eng] = set() if attach else touched
            _stage[eng] = None

        def _emit_eng(eng, builder, touched, has_ctr=False):
            if _stage[eng] is not None:
                conflict = bool((_stage[eng][1] | _prevt[eng]) & touched)
                _flush(eng, attach=conflict)
            else:
                if _prevt[eng] & touched:
                    sc_ = sems["vsy"] if eng == "vector" else sems["ssy"]
                    sc_.v += 1
                    h, v = sc_.h, sc_.v

                    def _np(e, h=h, eng=eng):
                        if eng == "vector":
                            e.engine_nop()
                            e.engine_nop().then_inc(h, 1)
                        else:
                            e.activation(s_junk[:, 0:8], s_junk[:, 0:8],
                                         AF.Identity)
                            e.activation(s_junk[:, 8:16], s_junk[:, 8:16],
                                         AF.Identity).then_inc(h, 1)
                    Q[eng].append(_np)
                    Q[eng].append(lambda e, h=h, v=v: e.wait_ge(h, v))
                    _prevt[eng] = set()
            _stage[eng] = (builder, touched, has_ctr)

        def act(out, in_, func, ctr=None, **kw):
            touched = _names([out, in_, kw.get("accum_out"), kw.get("bias"),
                              kw.get("scale")])
            if ctr is not None:
                ctr.v += 1
                h = ctr.h
                b = lambda e: e.activation(out, in_, func, **kw).then_inc(h, 1)
                _emit_eng("scalar", b, touched, has_ctr=True)
            else:
                b = lambda e: e.activation(out, in_, func, **kw)
                _emit_eng("scalar", b, touched)

        def vec(meth, *a, ctr=None, **kw):
            touched = _names(list(a) + list(kw.values()))
            if ctr is not None:
                ctr.v += 1
                h = ctr.h
                b = lambda e: getattr(e, meth)(*a, **kw).then_inc(h, 1)
                _emit_eng("vector", b, touched, has_ctr=True)
            else:
                b = lambda e: getattr(e, meth)(*a, **kw)
                _emit_eng("vector", b, touched)

        def coll(kind, op, ins_, outs_, ctr):
            ctr.v += 1
            h = ctr.h
            q("gpsimd", lambda e: e.collective_compute(
                kind, op, replica_groups=RG8, ins=ins_, outs=outs_
            ).then_inc(h, 1))

        _bc_reg = {}

        def iscat(out_ap, idx_ap, in_ap, ctr, bc):
            ctr.v += 16
            h = ctr.h

            def fn(e):
                if bc not in _bc_reg:
                    _bc_reg[bc] = e.to_reg(bc)
                e.indirect_dma_start(
                    out_ap, IndirectOffsetOnAxis(idx_ap, 0), in_ap, None,
                    bounds_check=_bc_reg[bc], oob_is_err=False).then_inc(h, 16)
            q("gpsimd", fn)

        def igat(out_ap, in_ap, idx_ap, ctr):
            ctr.v += 16
            h = ctr.h
            q("gpsimd", lambda e: e.indirect_dma_start(
                out_ap, None, in_ap, IndirectOffsetOnAxis(idx_ap, 0)
            ).then_inc(h, 16))

        satt_flat = s_xnT[:].rearrange("p a t -> p (a t)")

        def satt(tt):
            return satt_flat[:, D * tt:D * (tt + 1)]

        _wslot = [0]

        def wload(dst, src, a=None):
            sl = _wslot[0] % 4
            _wslot[0] += 1
            dsem = _wsems[sl]
            stg = s_wf32[:, sl, :]
            if a is not None:
                stg = stg.rearrange("p (a f) -> p a f", a=a)
            _flush("scalar")
            guard("scalar", ("wf", sl))
            dma("scalar", stg, src, dsem)
            wait("vector", dsem)
            vec("tensor_copy", dst, stg, ctr=sve)
            settok(("wf", sl), sve)

        vec("memset", s_junk[:], 0.0)
        vec("memset", s_zero[:], 0.0)
        vec("memset", s_stage[:], 0.0)
        vec("memset", s_p0[:], 0.0)
        vec("memset", s_p1[:], 0.0)
        vec("memset", s_p2[:], 0.0)
        vec("memset", s_p3[:], 0.0, ctr=sve)
        zero_tok = sve.v
        wait("scalar", sve, zero_tok)
        act(s_junk[:, 0:16], s_junk[:, 0:16], AF.Identity)
        act(s_junk[:, 16:32], s_junk[:, 16:32], AF.Identity)

        dma("gpsimd", c_iota[:], iota_h[:], gq)
        dma("gpsimd", c_ones[:], onesc_h[:], gq)
        dma("gpsimd", c_u128[:], u128_h[:], gq)
        dma("gpsimd", c_uc[:], uc_h[:], gq)
        dma("gpsimd", w_watt[:], watt_h.rearrange("a p f -> p a f"), gq)
        dma("gpsimd", w_rwT[:], rwT_h.rearrange("(a p) f -> p a f", p=128), gq)
        gq_const = gq.v

        gq_cast = gq.v

        dma("sync", b_wvb[:], wvb_h[0:1, :].to_broadcast([128, D]), sq)
        dma("sync", b_owb[:], owb_h[0:1, :].to_broadcast([128, D]), sq)
        dma("sync", b_eb[:], ebias_h[0:1, :].to_broadcast([128, NE]), sq)
        dma("sync", s_pm[:], pm_h[:], sq)
        dmaslow("sync", s_tokid[:],
                tokid_h.rearrange("(a p) o -> p (a o)", p=128), sq)
        sq_const = sq.v

        for tt in range(NT):
            sl = tt % 2
            xsem = sems["dw_xa"] if sl == 0 else sems["dw_xb"]
            guard("sync", ("xt", sl))
            dma("sync", s_xt[:, sl, :], x_h[128 * tt:128 * (tt + 1), :], xsem)
            wait("vector", xsem)
            vec("reduce_sum", s_lns[:, 0:1], s_xt[:, sl, :], AX.X)
            wait("scalar", xsem)
            guard("scalar", ("moe_scr",))
            act(s_moe[:], s_xt[:, sl, :], AF.Square, ctr=ssc,
                accum_out=s_lns[:, 1:2])
            settok(("moe_scr",), ssc)
            wait("vector", ssc)
            vec("tensor_scalar_mul", s_lns[:, 2:3], s_lns[:, 0:1], 1.0 / D)
            vec("tensor_scalar_mul", s_lns[:, 3:4], s_lns[:, 1:2], 1.0 / D)
            vec("tensor_mul", s_lns[:, 4:5], s_lns[:, 2:3], s_lns[:, 2:3])
            vec("tensor_sub", s_lns[:, 5:6], s_lns[:, 3:4], s_lns[:, 4:5],
                ctr=sve)
            vec("tensor_scalar_add", s_lns[:, 5:6], s_lns[:, 5:6], 1e-5,
                ctr=sve)
            wait("scalar", sve)
            act(s_lns[:, 6:7], s_lns[:, 5:6], AF.Sqrt, ctr=ssc)
            wait("vector", ssc)
            vec("reciprocal", s_lns[:, 7:8], s_lns[:, 6:7])
            guard("vector", ("xnst", sl))
            vec("scalar_tensor_tensor", s_xn_st[:, sl, :], s_xt[:, sl, :],
                s_lns[:, 2:3], s_lns[:, 7:8].to_broadcast([128, D]),
                ALU.subtract, ALU.mult, ctr=sve)
            settok(("xt", sl), sve)
            wait("sync", sve)
            dma("sync", d_xn[128 * tt:128 * (tt + 1), :], s_xn_st[:, sl, :], sq)
            settok(("xnst", sl), sq)
        sq_xn = sq.v

        wait("gpsimd", sq, sq_xn)
        coll("AllGather", ALU.bypass, [d_xn[:]], [d_xnall[:]], sems["c_xn"])

        wait("sync", sq, sq_xn)
        for jj in range(NJ):
            dmaT("sync", s_xnT[:, jj, :], d_xn[:, 128 * jj:128 * (jj + 1)], sq)
        sq_xnT = sq.v

        if DEBUG:
            dma("sync", dbg["xn"][:], d_xn[:], sq)

        mark()
        wait("tensor", sq, sq_xnT)
        wait("tensor", gq, gq_const)
        wait("vector", sq, sq_const)
        for tt in range(NT):
            guard("tensor", ("psA",))
            for j in range(NJ):
                mm(p_bg0[:, 0:NE], s_xnT[:, j, 128 * tt:128 * (tt + 1)],
                   w_rwT[:, j, :], start=(j == 0), stop=(j == NJ - 1),
                   ctr=smm if j == NJ - 1 else None)
            wait("scalar", smm)
            act(s_scores[:], p_bg0[:, 0:NE], AF.Sigmoid, ctr=ssc)
            settok(("psA",), ssc)
            wait("vector", ssc)
            vec("tensor_add", s_biased[:], s_scores[:], b_eb[:])
            vec("max", out=s_m8[:], in_=s_biased[:])
            vec("max_index", s_i8[:], s_m8[:], s_biased[:])
            vec("tensor_copy", s_if6a[:, tt, :], s_i8[:])
            vec("memset", s_Af[:], 0.0)
            cur, nxt = s_Af, s_Af2
            for k in range(EK):
                vec("scalar_tensor_tensor", nxt[:], c_iota[:],
                    s_if6a[:, tt, k:k + 1], cur[:], ALU.is_equal, ALU.add)
                cur, nxt = nxt, cur
                vec("scalar_tensor_tensor", s_scr64[:], c_iota[:],
                    s_if6a[:, tt, k:k + 1], s_scores[:], ALU.is_equal,
                    ALU.mult, accum_out=s_sc6[:, k:k + 1])
            vec("tensor_copy", s_A[:, tt, :], cur[:])
            vec("reduce_sum", s_t8a[:, 0:1], s_sc6[:, 0:EK], AX.X)
            vec("tensor_scalar_add", s_t8a[:, 1:2], s_t8a[:, 0:1], 1e-20)
            vec("reciprocal", s_t8a[:, 2:3], s_t8a[:, 1:2])
            vec("tensor_scalar_mul", s_pw[:, tt, 0:EK], s_sc6[:, 0:EK],
                s_t8a[:, 2:3], ctr=sve)
            wait("tensor", sve)
            guard("tensor", ("psB",))
            mm(p_bg1[0:64, 0:NE], c_ones[:], s_A[:, tt, :], ctr=smm)
            wait("vector", smm)
            guard("vector", ("cntst",))
            vec("tensor_copy", s_cnt[:], p_bg1[0:1, 0:NE], ctr=sve)
            settok(("psB",), sve)
            wait("sync", sve)
            dma("sync", d_ss[tt:tt + 1, :], s_cnt[:], sq)
            settok(("cntst",), sq)
        sq_ss = sq.v

        wait("gpsimd", sq, sq_ss)
        wait("gpsimd", sve, zero_tok)
        dma("gpsimd", s_stage[0:16, :], d_ss[:], gq)
        wait("tensor", gq)
        guard("tensor", ("psB",))
        mm(p_bg1[0:64, 0:NE], c_ones[0:64, :], s_stage[:], ctr=smm)
        wait("vector", smm)
        guard("vector", ("cntst",))
        vec("tensor_copy", s_cnt[:], p_bg1[0:1, 0:NE], ctr=sve)
        settok(("psB",), sve)
        wait("sync", sve)
        dma("sync", d_cnt[:], s_cnt[:], sq)
        settok(("cntst",), sq)
        wait("gpsimd", sq, sq.v)
        coll("AllGather", ALU.bypass, [d_cnt[:]], [d_cs[:]], sems["c_cnt"])

        mark()
        sw2v = sw2T_h.rearrange("(a p) f -> p a f", p=128)
        for jj in range(16):
            for h2 in range(2):
                wload(w_sw2r[:, jj, 512 * h2:512 * (h2 + 1)],
                      sw2v[:, jj, 512 * h2:512 * (h2 + 1)])
        sw2r_tok = sve.v
        wait("vector", sems["c_cnt"])
        vec("memset", s_co[:], 0.0)
        cur, nxt = s_co, s_co2
        for cc in range(8):
            wait("sync", sems["c_cnt"])
            guard("sync", ("csf",))
            dma("sync", s_csf[:], d_cs[cc:cc + 1, :], sq)
            wait("vector", sq)
            vec("scalar_tensor_tensor", nxt[:], s_csf[:],
                s_pm[:, cc:cc + 1], cur[:], ALU.mult, ALU.add, ctr=sve)
            settok(("csf",), sve)
            cur, nxt = nxt, cur
        vec("tensor_copy", s_coi[:], cur[:])
        vec("tensor_scalar", s_cq[:], s_coi[:], 6, None, ALU.arith_shift_right)
        vec("tensor_scalar", s_coi[:], s_coi[:], 63, None, ALU.bitwise_and)
        vec("tensor_copy", s_cqf[:], s_cq[:])
        vec("tensor_copy", s_crf[:], s_coi[:], ctr=sve)
        sve_qr = sve.v
        wait("gpsimd", sve, sve_qr)
        dma("gpsimd", s_stage[0:1, :], s_cqf[:], gq)
        dma("gpsimd", s_stage[1:2, :], s_crf[:], gq)
        dma("gpsimd", s_stage[2:18, :], d_ss[:], gq)
        wait("tensor", gq)
        wait("tensor", sve, sve_qr)
        guard("tensor", ("psA",))
        mm(p_bg0[0:64, 0:NE], c_uc[:], s_stage[:], ctr=smm)
        wait("vector", smm)
        vec("tensor_copy", s_choff[:], p_bg0[0:64, 0:NE], ctr=sve)
        settok(("psA",), sve)
        wait("sync", sve)
        dma("sync", d_choff[:], s_choff[0:16, :], sq)
        sq_choff = sq.v
        if DEBUG:
            dma("sync", dbg["choff"][:], d_choff[:], sq)

        wait("gpsimd", sve, zero_tok)
        dma("gpsimd", d_tokmap[:].rearrange("(p b) o -> p (b o)", p=128),
            s_zero[:, 0:196], gq)
        gq_tmz = gq.v

        for tt in range(NT):
            osl = tt % 2
            guard("sync", ("offb", osl))
            wait("sync", sq, sq_choff)
            dma("sync", s_offb[:, osl, :],
                d_choff[tt:tt + 1, :].to_broadcast([128, NE]), sq)
            off_ld = sq.v
            guard("tensor", ("psB",))
            mm(p_bg1[:, 0:NE], c_u128[:], s_A[:, tt, :], ctr=smm)
            wait("vector", smm)
            wait("vector", sq, off_ld)
            vec("tensor_copy", s_C[:], p_bg1[:, 0:NE], ctr=sve)
            settok(("psB",), sve)
            vec("tensor_add", s_Ct[:], s_C[:], s_offb[:, osl, :], ctr=sve)
            settok(("offb", osl), sve)
            for k in range(EK):
                vec("scalar_tensor_tensor", s_scr64[:], c_iota[:],
                    s_if6a[:, tt, k:k + 1], s_Ct[:], ALU.is_equal, ALU.mult,
                    accum_out=s_r6[:, k:k + 1])
            E6 = slice(0, EK)
            vec("tensor_scalar", s_v6[:, E6], s_r6[:, E6], float(CAP), None,
                ALU.is_lt)
            vec("tensor_mul", s_pw[:, tt, E6], s_pw[:, tt, E6], s_v6[:, E6])
            vec("tensor_scalar", s_t8a[:, E6], s_if6a[:, tt, E6], float(CAP),
                None, ALU.mult)
            vec("tensor_add", s_t8a[:, E6], s_t8a[:, E6], s_r6[:, E6])
            vec("tensor_copy", s_r6i[:, E6], s_if6a[:, tt, E6])
            vec("tensor_scalar", s_r6i[:, E6], s_r6i[:, E6], 3, None,
                ALU.arith_shift_right)
            vec("tensor_copy", s_t8b[:, E6], s_r6i[:, E6])
            vec("tensor_scalar", s_t8b[:, E6], s_t8b[:, E6], 64.0, None,
                ALU.mult)
            vec("tensor_add", s_t8b[:, E6], s_t8b[:, E6], s_t8a[:, E6])
            vec("tensor_scalar", s_scr64[:, 0:EK], s_v6[:, E6], -2e6, 2e6,
                ALU.mult, ALU.add)
            vec("tensor_add", s_t8b[:, E6], s_t8b[:, E6], s_scr64[:, 0:EK])
            vec("tensor_copy", s_sidx[:, tt, E6], s_t8b[:, E6])
            vec("tensor_scalar_min", s_t8a[:, E6], s_t8a[:, E6],
                float(NE * CAP - 1))
            vec("tensor_copy", s_gidx[:, tt, E6], s_t8a[:, E6], ctr=sve)
        sve_route = sve.v

        wait("gpsimd", sve, sve_route)
        wait("gpsimd", gq, gq_tmz)
        for tt in range(NT):
            for k in range(EK):
                iscat(d_tokmap[:], s_sidx[:, tt, k:k + 1],
                      s_tokid[:, tt:tt + 1], gq, 25087)
        wait("gpsimd", gq)
        coll("ReduceScatter", ALU.add, [d_tokmap[:]], [d_tokmy[:]],
             sems["c_tok"])
        if DEBUG:
            wait("sync", sems["c_tok"])
            dma("sync", dbg["tokmy"][:], d_tokmy[:], sq)
            dma("sync", dbg["cnt"][:], d_cnt[:], sq)
            wait("sync", sve, sve_route)
            dma("sync", dbg["if6"][:], s_if6a[:], sq)
            dma("sync", dbg["pw"][:], s_pw[:], sq)
            dma("sync", dbg["gidx"][:], s_gidx[:], sq)

        mark()
        wait("tensor", sve, sw2r_tok)
        for u in range(NU):
            tsl = slice(UT * u, UT * (u + 1))
            sw1v = sw1T_h.rearrange("(a p) f -> p a f", p=128)
            for ft in range(16):
                slw = ft % 2
                guard("vector", ("sw1", slw))
                for h2 in range(2):
                    wload(w_sw1s[:, slw, 4 * h2:4 * (h2 + 1), :],
                          sw1v[:, 4 * h2:4 * (h2 + 1),
                               128 * ft:128 * (ft + 1)], a=4)
                cast_tok = sve.v
                pp = p_bg0 if ft % 2 == 0 else p_bg1
                pk = ("psA",) if ft % 2 == 0 else ("psB",)
                guard("tensor", pk)
                wait("tensor", sve, cast_tok)
                for j in range(NJ):
                    mm(pp[:, 0:UT], w_sw1s[:, slw, j, :], s_xnT[:, j, tsl],
                       start=(j == 0), stop=(j == NJ - 1),
                       ctr=smm if j == NJ - 1 else None)
                settok(("sw1", slw), smm)
                wait("vector", smm)
                vec("tensor_copy", s_hraw[:, ft, :], pp[:, 0:UT], ctr=sve)
                settok(pk, sve)
            wait("scalar", sve)
            act(s_hraw[:].rearrange("p a f -> p (a f)"),
                s_hraw[:].rearrange("p a f -> p (a f)"),
                AF.Gelu_apprx_tanh, ctr=ssc)
            wait("tensor", ssc)
            for tq in range(2):
                tt = 2 * u + tq
                for ih in range(2):
                    pp = p_bg0 if ih == 0 else p_bg1
                    pk = ("psA",) if ih == 0 else ("psB",)
                    guard("tensor", pk)
                    for ft in range(16):
                        mm(pp[:, 0:512],
                           s_hraw[:, ft, 128 * tq:128 * (tq + 1)],
                           w_sw2r[:, ft, 512 * ih:512 * (ih + 1)],
                           start=(ft == 0), stop=(ft == 15),
                           ctr=smm if ft == 15 else None)
                    wait("vector", smm)
                    guard("vector", ("sh2st", tt % 2))
                    vec("tensor_copy",
                        s_sh2st[:, tt % 2, 512 * ih:512 * (ih + 1)],
                        pp[:, 0:512], ctr=sve)
                    settok(pk, sve)
                wait("sync", sve)
                dma("sync", d_sh2[128 * tt:128 * (tt + 1), :],
                    s_sh2st[:, tt % 2, :], sq)
                settok(("sh2st", tt % 2), sq)
        sq_sh2 = sq.v
        if DEBUG:
            wait("sync", sq, sq_sh2)
            dma("sync", dbg["sh2"][:], d_sh2[:], sq)

        mark()
        wait("gpsimd", sems["c_tok"])
        wait("gpsimd", sems["c_xn"])
        dmaslow("gpsimd", s_tmf[:],
                d_tokmy[0:3072, :].rearrange("(j p) o -> p (j o)", p=128), gq)
        wait("vector", gq)
        vec("tensor_copy", s_tmidx[:], s_tmf[:], ctr=sve)
        wait("gpsimd", sve)
        xs_tok = {}
        for jj in range(24):
            sl = jj % 2
            gsem = sems["dw_y6a"] if sl == 0 else sems["dw_y6b"]
            guard("gpsimd", ("xsg", sl))
            igat(s_y6[:, sl, :], d_xnall[:], s_tmidx[:, jj:jj + 1], gsem)
            wait("sync", gsem)
            dma("sync", d_xs[128 * jj:128 * (jj + 1), :], s_y6[:, sl, :], sq)
            settok(("xsg", sl), sq)
            xs_tok[jj] = sq.v

        w2view = w_w1[:].rearrange("p a f -> p (a f)").rearrange(
            "p (b g) -> p b g", b=4)

        for eg in range(4):
            for el in range(2):
                e = 2 * eg + el
                guard("sync", ("xsT",))
                wait("sync", sq, xs_tok[3 * e + 2])
                for j in range(NJ):
                    dmaT("sync", s_xsT[:, j, :],
                         d_xs[CAP * e:CAP * (e + 1), 128 * j:128 * (j + 1)],
                         sems["dw_xsT"])
                wait("tensor", sems["dw_xsT"])
                guard("vector", ("w1",))
                w1v = w1_h[e].rearrange("(a p) f -> p a f", p=128)
                for j in range(NJ):
                    wload(w_w1[:, j, :], w1v[:, j, :])
                wait("tensor", sve)
                for ft in range(4):
                    pp = p_bg0 if ft % 2 == 0 else p_bg1
                    pk = ("psA",) if ft % 2 == 0 else ("psB",)
                    guard("tensor", pk)
                    for j in range(NJ):
                        mm(pp[:, 0:CAP], w_w1[:, j, 128 * ft:128 * (ft + 1)],
                           s_xsT[:, j, :], start=(j == 0), stop=(j == NJ - 1),
                           ctr=smm if j == NJ - 1 else None)
                    wait("vector", smm)
                    vec("tensor_copy", s_h1r[:, 4 * el + ft, :], pp[:, 0:CAP],
                        ctr=sve)
                    settok(pk, sve)
                settok(("xsT",), smm)
                settok(("w1",), smm)
            wait("scalar", sve)
            act(s_h1r[:].rearrange("p a f -> p (a f)"),
                s_h1r[:].rearrange("p a f -> p (a f)"),
                AF.Gelu_apprx_tanh, ctr=ssc)
            gelu_tok = ssc.v
            wait("tensor", ssc, gelu_tok)
            for el in range(2):
                e = 2 * eg + el
                guard("vector", ("w1",))
                w2v = w2_h[e].rearrange("(a p) f -> p a f", p=128)
                for c4 in range(4):
                    for h2 in range(2):
                        wload(w2view[:, c4, 512 * h2:512 * (h2 + 1)],
                              w2v[:, c4, 512 * h2:512 * (h2 + 1)])
                wait("tensor", sve)
                for ct in range(3):
                    for ih in range(2):
                        pp = p_bg0 if ih == 0 else p_bg1
                        pk = ("psA",) if ih == 0 else ("psB",)
                        guard("tensor", pk)
                        for f in range(4):
                            mm(pp[:, 0:512],
                               s_h1r[:, 4 * el + f, 128 * ct:128 * (ct + 1)],
                               w2view[:, f, 512 * ih:512 * (ih + 1)],
                               start=(f == 0), stop=(f == 3),
                               ctr=smm if f == 3 else None)
                        wait("vector", smm)
                        guard("vector", ("yst", ct % 2))
                        vec("tensor_copy",
                            s_yst[:, ct % 2, 512 * ih:512 * (ih + 1)],
                            pp[:, 0:512], ctr=sve)
                        settok(pk, sve)
                    wait("sync", sve)
                    dma("sync",
                        d_ymine[CAP * e + 128 * ct:CAP * e + 128 * (ct + 1), :],
                        s_yst[:, ct % 2, :], sq)
                    settok(("yst", ct % 2), sq)
                settok(("w1",), smm)
        sq_y = sq.v
        wait("gpsimd", sq, sq_y)
        coll("AllGather", ALU.bypass, [d_ymine[:]], [d_yall[:]], sems["c_y"])
        if DEBUG:
            dma("sync", dbg["y"][:], d_ymine[:], sq)

        mark()
        wait("sync", gq, gq_cast)
        vec("memset", s_vx[:, :, 64:65], 1.0)
        vec("memset", s_vx[:, :, 129:130], 1.0, ctr=sve)
        inv_slot = [0]
        for p in range(NP):
            guard("vector", ("vxall",))
            vsl = p % 2
            dsem = sems["dw_vxa"] if vsl == 0 else sems["dw_vxb"]
            guard("gpsimd", ("wvx", vsl))
            dma("gpsimd", w_vx[:, vsl, :, :],
                wvT_h.rearrange("(a p) f -> p a f", p=128)
                [:, :, 128 * p:128 * (p + 1)], dsem)
            wait("tensor", dsem)
            for tt in range(NT):
                vpp = p_bg0 if tt % 2 == 0 else p_bg1
                vpk = ("psA",) if tt % 2 == 0 else ("psB",)
                guard("tensor", vpk)
                for j in range(NJ):
                    mm(vpp[:, 0:128], s_xnT[:, j, 128 * tt:128 * (tt + 1)],
                       w_vx[:, vsl, j, :], start=(j == 0), stop=(j == NJ - 1),
                       ctr=smm if j == NJ - 1 else None)
                wait("vector", smm)
                guard("vector", ("vx", tt))
                vec("tensor_add", s_vx[:, tt, 0:64], vpp[:, 0:64],
                    b_wvb[:, 128 * p:128 * p + 64])
                vec("tensor_add", s_vx[:, tt, 65:129], vpp[:, 64:128],
                    b_wvb[:, 128 * p + 64:128 * (p + 1)], ctr=sve)
                settok(vpk, sve)
                settok(("vx", tt), sve)
            settok(("wvx", vsl), smm)
            for tc in range(4):
                guard("tensor", ("psA",))
                mm(p_bg0[0:64, 0:512], w_watt[0:64, p, :],
                   s_xnT[0:64, p, 512 * tc:512 * (tc + 1)], tp=(0, 0),
                   start=True, stop=True)
                guard("tensor", ("psB",))
                mm(p_bg1[64:128, 0:512], w_watt[64:128, p, :],
                   s_xnT[64:128, p, 512 * tc:512 * (tc + 1)], tp=(64, 64),
                   start=True, stop=True, ctr=smm)
                wait("vector", smm)
                guard("vector", ("xw", tc))
                vec("tensor_copy", s_xw[0:64, tc, :], p_bg0[0:64, 0:512])
                vec("tensor_copy", s_xw[64:128, tc, :], p_bg1[64:128, 0:512],
                    ctr=sve)
                settok(("psA",), sve)
                settok(("psB",), sve)
                settok(("xw", tc), sve)
            wait("tensor", sve)

            for tc in range(4):
                prev = None

                def emit_o(prev):
                    psch, psex = prev
                    ppar = psch % 2
                    ppA = (s_p0, s_p1, s_p2, s_p3)[ppar * 2]
                    ppB = (s_p0, s_p1, s_p2, s_p3)[ppar * 2 + 1]
                    if psch == 0:
                        guard("tensor", ("o_rs",))
                    wait("tensor", sex, psex)
                    first = (psch == 0)
                    last = (psch == NT - 1)
                    mm(p_o0[0:65, :], s_vx[:, psch, 0:65], ppA[:],
                       tp=(0, 0), start=first, stop=last)
                    mm(p_o1[0:65, :], s_vx[:, psch, 65:130], ppB[:],
                       tp=(0, 0), start=first, stop=last, ctr=smm)
                    tok[("pT", ppar * 2)] = (smm, smm.v)
                    tok[("pT", ppar * 2 + 1)] = (smm, smm.v)

                for sch in range(NT):
                    par = sch % 2
                    sl0 = par * 2
                    sl1 = sl0 + 1
                    pbA = p_att0 if par == 0 else p_rs0
                    pbB = p_att1 if par == 0 else p_rs1
                    pA = (s_p0, s_p1, s_p2, s_p3)[sl0]
                    pB = (s_p0, s_p1, s_p2, s_p3)[sl1]
                    guard("tensor", ("attA", par))
                    mm(pbA[:, 0:512],
                       s_xnT[0:64, p, 128 * sch:128 * (sch + 1)],
                       s_xw[0:64, tc, :], tp=(0, 0), ctr=smm)
                    att0_tok = smm.v
                    guard("tensor", ("attB", par))
                    mm(pbB[:, 0:512],
                       s_xnT[64:128, p, 128 * sch:128 * (sch + 1)],
                       s_xw[64:128, tc, :], tp=(64, 0), ctr=smm)
                    att1_tok = smm.v
                    if prev is not None:
                        emit_o(prev)
                    guard("scalar", ("pT", sl0))
                    wait("scalar", smm, att0_tok)
                    act(pA[:], pbA[:, 0:512], AF.Exp, scale=0.125, ctr=sex)
                    settok(("attA", par), sex)
                    guard("scalar", ("pT", sl1))
                    wait("scalar", smm, att1_tok)
                    act(pB[:], pbB[:, 0:512], AF.Exp, scale=0.125, ctr=sex)
                    settok(("attB", par), sex)
                    prev = (sch, sex.v)
                emit_o(prev)
                o_tok = smm.v
                slA = inv_slot[0]
                slB = slA + 1
                inv_slot[0] += 2
                wait("vector", smm, o_tok)
                guard("vector", ("xt", 0))
                vec("tensor_copy", s_xt[64:65, 0, 0:512], p_o0[64:65, :])
                vec("tensor_copy", s_xt[64:65, 0, 512:1024], p_o1[64:65, :],
                    ctr=sve)
                wait("sync", sve)
                dma("sync", d_inv[slA:slA + 1, :], s_xt[64:65, 0, 0:512],
                    sems["dw_inva"])
                dma("sync", d_inv[slB:slB + 1, :], s_xt[64:65, 0, 512:1024],
                    sems["dw_inva"])
                settok(("xt", 0), sems["dw_inva"])
                settok(("xt2", 0), sems["dw_inva"])
                wait("sync", sems["dw_inva"])
                guard("sync", ("invb",))
                dma("sync", s_invA[0:64, :],
                    d_inv[slA:slA + 1, :].to_broadcast([64, 512]),
                    sems["dw_invb"])
                dma("sync", s_invB[0:64, :],
                    d_inv[slB:slB + 1, :].to_broadcast([64, 512]),
                    sems["dw_invb"])
                wait("vector", sems["dw_invb"])
                vec("reciprocal", s_invA[:], s_invA[:])
                vec("reciprocal", s_invB[:], s_invB[:])
                vec("tensor_mul", s_oT[0:64, p, 512 * tc:512 * (tc + 1)],
                    p_o0[0:64, :], s_invA[:])
                guard("vector", ("pT", 0))
                vec("tensor_mul", s_p0[0:64, :], p_o1[0:64, :], s_invB[:],
                    ctr=sve)
                settok(("invb",), sve)
                settok(("o_rs",), sve)
                wait("sync", sve)
                dma("sync", s_oT[64:128, p, 512 * tc:512 * (tc + 1)],
                    s_p0[0:64, :], sq)
                settok(("pT", 0), sq)
            settok(("vxall",), smm)
        sve_att = sve.v
        sq_oB = sq.v
        if DEBUG:
            wait("sync", sve, sve_att)
            dma("sync", dbg["ot"][:], s_oT[:], sq)

        mark()
        ow16v = owT_h.rearrange("(a p) f -> p a f", p=128)
        owbuf = w_w1[:]
        wait("vector", sve, sve_att)
        wait("tensor", sq, sq_oB)
        for fh in range(2):
            guard("vector", ("w1",))
            guard("vector", ("owb",))
            for p in range(NP):
                wload(owbuf[:, p, :], ow16v[:, p, 512 * fh:512 * (fh + 1)])
            wait("tensor", sve)
            wait("tensor", sve, sve_att)
            for tg in range(8):
                for ti in range(2):
                    tt = 2 * tg + ti
                    pp = (p_bg0, p_bg1)[ti]
                    pk = (("psA",), ("psB",))[ti]
                    guard("tensor", pk)
                    for p in range(NP):
                        mm(pp[:, 0:512], s_oT[:, p, 128 * tt:128 * (tt + 1)],
                           owbuf[:, p, :], start=(p == 0), stop=(p == NP - 1),
                           ctr=smm if p == NP - 1 else None)
                    wait("vector", smm)
                    vec("tensor_add", satt(tt)[:, 512 * fh:512 * (fh + 1)],
                        pp[:, 0:512], b_owb[:, 512 * fh:512 * (fh + 1)],
                        ctr=sve)
                    settok(pk, sve)
            settok(("owb",), smm)
        sve_op = sve.v
        if DEBUG:
            wait("sync", sve, sve_op)
            for tt in range(NT):
                dma("sync", dbg["satt"][128 * tt:128 * (tt + 1), :],
                    satt(tt), sq)

        mark()
        wait("gpsimd", sems["c_y"])
        wait("vector", sve, sve_op)
        for tt in range(NT):
            for k in range(EK):
                sl = k % 2
                gsem = sems["dw_y6a"] if sl == 0 else sems["dw_y6b"]
                guard("gpsimd", ("y6", sl))
                igat(s_y6[:, sl, :], d_yall[:], s_gidx[:, tt, k:k + 1], gsem)
                wait("vector", gsem)
                guard("vector", ("moe_scr",)) if k == 0 else None
                if k == 0:
                    vec("tensor_scalar_mul", s_moe[:], s_y6[:, sl, :],
                        s_pw[:, tt, 0:1], ctr=sve)
                else:
                    vec("scalar_tensor_tensor", s_moe[:], s_y6[:, sl, :],
                        s_pw[:, tt, k:k + 1], s_moe[:], ALU.mult, ALU.add,
                        ctr=sve)
                settok(("y6", sl), sve)
            shsem = sems["dw_sh2a"] if tt % 2 == 0 else sems["dw_sh2b"]
            guard("sync", ("sh2ld", tt % 2))
            if tt == 0:
                wait("sync", sq, sq_sh2)
            dma("sync", s_sh2st[:, tt % 2, :],
                d_sh2[128 * tt:128 * (tt + 1), :], shsem)
            wait("vector", shsem)
            vec("tensor_add", s_moe[:], s_moe[:], s_sh2st[:, tt % 2, :],
                ctr=sve)
            settok(("sh2ld", tt % 2), sve)
            wait("scalar", sve)
            guard("scalar", ("gst", tt % 2))
            act(s_xn_st[:, tt % 2, :], s_moe[:], AF.Silu, ctr=ssc)
            settok(("moe_scr",), ssc)
            xsem = sems["dw_xa"] if tt % 2 == 0 else sems["dw_xb"]
            guard("sync", ("xt2", tt % 2))
            dma("sync", s_xt[:, tt % 2, :], x_h[128 * tt:128 * (tt + 1), :],
                xsem)
            wait("vector", ssc)
            wait("vector", xsem)
            vec("tensor_mul", s_moe[:], satt(tt), s_xn_st[:, tt % 2, :],
                ctr=sve)
            settok(("gst", tt % 2), sve)
            vec("tensor_add", s_xt[:, tt % 2, :], s_xt[:, tt % 2, :],
                s_moe[:], ctr=sve)
            wait("sync", sve)
            dma("sync", out_h[128 * tt:128 * (tt + 1), :], s_xt[:, tt % 2, :],
                sq)
            settok(("xt2", tt % 2), sq)

        mark()
        wait("sync", sq, sq.v)
        wait("gpsimd", gq, gq.v)

        _flush("vector")
        _flush("scalar")
        mark()
        if PHASE_LIMIT is not None:
            cut = marks[PHASE_LIMIT]
            for k in Q:
                Q[k] = Q[k][:cut[k]]

        with nc.Block() as block:
            @block.sync
            def _(e):
                for fn in Q["sync"]:
                    fn(e)

            @block.gpsimd
            def _(e):
                for fn in Q["gpsimd"]:
                    fn(e)

            @block.tensor
            def _(e):
                for fn in Q["tensor"]:
                    fn(e)

            @block.scalar
            def _(e):
                for fn in Q["scalar"]:
                    fn(e)

            @block.vector
            def _(e):
                for fn in Q["vector"]:
                    fn(e)

    return nc, dbg


_NC_CACHE = {}


def _get_nc():
    if "nc" not in _NC_CACHE:
        _NC_CACHE["nc"] = build_nc()
    return _NC_CACHE["nc"]


def kernel(L0_attn_x, L0_attn_w_att_val, L0_attn_wv_w, L0_attn_wv_b,
           L0_attn_out_w, L0_attn_out_b, L0_ffn_router_weight,
           L0_ffn_expert_bias, L0_ffn_w1, L0_ffn_w2, L0_ffn_sw1, L0_ffn_sw2):
    x = np.ascontiguousarray(np.asarray(L0_attn_x, dtype=np.float32))
    watt = np.asarray(L0_attn_w_att_val, dtype=np.float32)
    wv = np.asarray(L0_attn_wv_w, dtype=np.float32)
    wvb = np.asarray(L0_attn_wv_b, dtype=np.float32)
    ow = np.asarray(L0_attn_out_w, dtype=np.float32)
    owb = np.asarray(L0_attn_out_b, dtype=np.float32)
    rw = np.asarray(L0_ffn_router_weight, dtype=np.float32)
    eb = np.asarray(L0_ffn_expert_bias, dtype=np.float32)
    w1 = np.asarray(L0_ffn_w1, dtype=np.float32)
    w2 = np.asarray(L0_ffn_w2, dtype=np.float32)
    sw1 = np.asarray(L0_ffn_sw1, dtype=np.float32)
    sw2 = np.asarray(L0_ffn_sw2, dtype=np.float32)

    wvT = np.ascontiguousarray(wv.T)
    owT = np.ascontiguousarray(ow.T)
    sw1T = np.ascontiguousarray(sw1.T)
    sw2T = np.ascontiguousarray(sw2.T)
    rwT = np.ascontiguousarray(rw.T)
    wattp = np.zeros((NP, 128, 64), np.float32)
    for p in range(NP):
        wattp[p, 0:64] = watt[2 * p]
        wattp[p, 64:128] = watt[2 * p + 1]
    u128 = np.triu(np.ones((128, 128), np.float32), 1)
    uc = np.zeros((64, 64), np.float32)
    for tt in range(16):
        uc[0, tt] = 64.0
        uc[1, tt] = 1.0
        for i in range(tt):
            uc[2 + i, tt] = 1.0
    iota = np.tile(np.arange(NE, dtype=np.float32), (128, 1))
    onesc = np.ones((128, 64), np.float32)

    nc, dbg = _get_nc()
    in_maps = []
    for c in range(8):
        in_maps.append(dict(
            x=np.ascontiguousarray(x[c]),
            wvT=wvT, owT=owT, sw1T=sw1T, sw2T=sw2T, rwT=rwT,
            watt=wattp,
            wvb=wvb.reshape(1, D), owb=owb.reshape(1, D),
            ebias=eb.reshape(1, NE),
            w1=np.ascontiguousarray(w1[ELOC * c:ELOC * (c + 1)]),
            w2=np.ascontiguousarray(w2[ELOC * c:ELOC * (c + 1)]),
            tokid=np.arange(T * c, T * (c + 1), dtype=np.float32).reshape(T, 1),
            pm=(np.arange(8, dtype=np.float32) < c).astype(np.float32)
            .reshape(1, 8),
            u128=u128, uc=uc, iota=iota, onesc=onesc,
        ))
    res = run_bass_kernel_spmd(nc, in_maps, core_ids=list(range(8)))
    out = np.stack([np.asarray(res.results[c]["out"]) for c in range(8)],
                   axis=0)
    if DEBUG:
        kernel.last_results = res.results
    return out
